# version 21
# baseline (speedup 1.0000x reference)
import numpy as np
import ml_dtypes
import concourse.bass as bass
import concourse.mybir as mybir
from concourse.bass_utils import run_bass_kernel_spmd

F32 = mybir.dt.float32
BF16 = mybir.dt.bfloat16
U8 = mybir.dt.uint8
AF = mybir.ActivationFunctionType
ALU = mybir.AluOpType
AX = mybir.AxisListType
BF = ml_dtypes.bfloat16

T = 4096
D = 1024
NH = 16
HD = 64
FF = 2816
HS = 2048
NE = 8
NFC = D // 128
NTC = T // 512
KC = T // 128
CW = T // NE

def set_T(t):
    global T, NTC, KC, CW
    T = t
    NTC = T // 512
    KC = T // 128
    CW = T // NE
SB_BYTES = 203 * 1024

ENGS = ("t", "v", "s", "g", "q")
ENGMAP = {"t": "tensor", "v": "vector", "s": "scalar", "g": "gpsimd", "q": "sync"}


class Buf:
    __slots__ = ("name", "ap", "aliases", "rng")
    def __init__(self, name, ap=None):
        self.name = name
        self.ap = ap
        self.aliases = []
        self.rng = None

class Op:
    __slots__ = ("eng", "fn", "seq", "waits", "inc_needed", "num", "sem", "incby",
                 "chan", "slot", "dbg", "blast")
    def __init__(self, eng, fn, seq, dbg=""):
        self.eng = eng
        self.fn = fn
        self.seq = seq
        self.waits = []
        self.inc_needed = False
        self.num = None
        self.sem = None
        self.incby = 1
        self.chan = None
        self.slot = None
        self.dbg = dbg
        self.blast = None

class Chan:
    def __init__(self, prog, name, queue, nslots=2):
        self.p = prog
        self.name = name
        self.queue = queue
        self.nslots = nslots
        self.sems = [prog.new_sem(f"{name}{i}") for i in range(nslots)]
        self.count = [0] * nslots
        self.batch = 0
        self.waiters = [[] for _ in range(nslots)]

class Arena:
    def __init__(self, base_ap_f32):
        self.base = base_ap_f32
        self.off = 0
        self.marks = []
        self.top = SB_BYTES
        self.tmarks = []
        self.hi = 0
        self.live = []
        self.freed = []
    def alloc(self, name, rows, cols, dt):
        esz = 4 if dt == F32 else (1 if dt == U8 else 2)
        off = (self.off + 63) & ~63
        nbytes = cols * esz
        assert off + nbytes <= self.top, f"SBUF arena overflow at {name}: {(off+nbytes)/1024:.1f}KB (top {self.top/1024:.1f}KB)"
        self.off = off + nbytes
        self.hi = max(self.hi, self.off)
        ap = self.base.bitcast(dt)[0:rows, off // esz: off // esz + cols]
        b = Buf(name, ap)
        b.rng = (off, off + nbytes)
        b.aliases = [f for f in self.freed if f.rng[0] < b.rng[1] and b.rng[0] < f.rng[1]]
        self.live.append(b)
        return b
    def alloc_top(self, name, rows, cols, dt):
        esz = 4 if dt == F32 else (1 if dt == U8 else 2)
        nbytes = cols * esz
        top = (self.top - nbytes) & ~63
        assert top >= self.off, f"SBUF arena overflow (top) at {name}"
        self.top = top
        ap = self.base.bitcast(dt)[0:rows, top // esz: top // esz + cols]
        b = Buf(name, ap)
        b.rng = (top, top + nbytes)
        b.aliases = [f for f in self.freed if f.rng[0] < b.rng[1] and b.rng[0] < f.rng[1]]
        self.live.append(b)
        return b
    def mark_top(self):
        self.tmarks.append(self.top)
    def release_top(self):
        hi = self.tmarks.pop()
        still = []
        for b in self.live:
            if b.rng[0] >= self.top and b.rng[1] <= hi:
                self.freed.append(b)
            else:
                still.append(b)
        self.live = still
        self.top = hi
    def mark(self):
        self.marks.append(self.off)
    def release(self):
        lo = self.marks.pop()
        still = []
        for b in self.live:
            if b.rng[0] >= lo:
                self.freed.append(b)
            else:
                still.append(b)
        self.live = still
        self.freed = [f for f in self.freed if f.rng[1] > 0]
        self.off = lo

class Prog:
    def __init__(self, nc):
        self.nc = nc
        self.E = {e: [] for e in ENGS}
        self.seq = 0
        self.state = {}
        self.sem_ctx = []
        self.esem = {}
        for e in ENGS:
            self.esem[e] = self.new_sem(f"eng_{e}")
    def new_sem(self, name):
        h = self.nc.semaphore(name=f"sm_{name}")
        sem = h.__enter__()
        self.sem_ctx.append(h)
        return sem
    def chan(self, name, queue, nslots=2):
        return Chan(self, name, queue, nslots)
    def _dep(self, o, producers):
        for p in producers:
            if p is None or p is o:
                continue
            if p.chan is not None and p.blast is not None and p.blast is not p:
                p = p.blast
                if p is o:
                    continue
            if p.eng == o.eng and p.eng == "t":
                continue
            if p not in o.waits:
                o.waits.append(p)
                p.inc_needed = True
                if p.chan is not None:
                    p.chan.waiters[p.slot].append(o)
    def _track(self, o, r, w):
        for b in w:
            if b.aliases and id(b) not in self.state:
                for ab in b.aliases:
                    st = self.state.get(id(ab))
                    if st:
                        self._dep(o, [st["w"]])
                        self._dep(o, st["r"])
                b.aliases = []
        for b in r:
            st = self.state.get(id(b))
            if st:
                self._dep(o, [st["w"]])
        for b in w:
            st = self.state.get(id(b))
            if st:
                self._dep(o, [st["w"]])
                self._dep(o, st["r"])
        for b in r:
            self.state.setdefault(id(b), {"w": None, "r": []})["r"].append(o)
        for b in w:
            self.state[id(b)] = {"w": o, "r": []}
    def op(self, eng, fn, r=(), w=(), dbg=""):
        o = Op(eng, fn, self.seq, dbg); self.seq += 1
        self._track(o, r, w)
        self.E[eng].append(o)
        return o
    def dma(self, chan, fns, r=(), w=(), dbg=""):
        s = chan.batch % chan.nslots
        gate = chan.waiters[s]
        chan.waiters[s] = []
        chan.batch += 1
        sem = chan.sems[s]
        ops = []
        for i, fn in enumerate(fns):
            o = Op(chan.queue, fn, self.seq, dbg); self.seq += 1
            o.sem = sem
            o.incby = 16
            chan.count[s] += 1
            o.num = chan.count[s]
            o.chan = chan
            o.slot = s
            if i == 0:
                for g0 in gate:
                    self._dep(o, [g0])
            ops.append(o)
            self.E[chan.queue].append(o)
        for o in ops:
            o.blast = ops[-1]
        o0, oL = ops[0], ops[-1]
        for b in w:
            if b.aliases and id(b) not in self.state:
                for ab in b.aliases:
                    st = self.state.get(id(ab))
                    if st:
                        self._dep(o0, [st["w"]])
                        self._dep(o0, st["r"])
                b.aliases = []
        for b in r:
            st = self.state.get(id(b))
            if st:
                self._dep(o0, [st["w"]])
        for b in w:
            st = self.state.get(id(b))
            if st:
                self._dep(o0, [st["w"]])
                self._dep(o0, st["r"])
        for b in r:
            self.state.setdefault(id(b), {"w": None, "r": []})["r"].append(oL)
        for b in w:
            self.state[id(b)] = {"w": oL, "r": []}
        return oL
    def finalize(self):
        for e in ("t", "v", "s", "g", "q"):
            n = 0
            for o in self.E[e]:
                if o.sem is None and o.inc_needed:
                    n += 1
                    o.num = n
                    o.sem = self.esem[e]
                    o.incby = 1
    def replay(self):
        nc = self.nc
        with nc.Block() as blk:
            for e in ENGS:
                ops = self.E[e]
                def body(eng, ops=ops):
                    waited = {}
                    for o in ops:
                        for p in o.waits:
                            tgt = p.num * p.incby
                            key = id(p.sem)
                            if waited.get(key, -1) >= tgt:
                                continue
                            waited[key] = tgt
                            eng.wait_ge(p.sem, tgt)
                        inst = o.fn(eng)
                        if o.sem is not None and (o.incby == 16 or o.inc_needed):
                            inst.then_inc(o.sem, o.incby)
                getattr(blk, ENGMAP[e])(body)
    def close(self):
        for h in reversed(self.sem_ctx):
            h.__exit__(None, None, None)



def setup_consts(P, A, io, C):
    X = {}
    X["ident"] = A.alloc("ident", 128, 128, BF16)
    X["onesb"] = A.alloc("onesb", 128, 128, BF16)
    X["onesf"] = A.alloc("onesf", 1, 128, F32)
    X["onesfc"] = A.alloc("onesfc", 128, 1, F32)
    X["eps"] = A.alloc("eps", 128, 1, F32)
    X["identf"] = A.alloc("identf", 128, 128, F32)
    X["e2"] = A.alloc("e2", 128, 128, BF16)
    X["e3"] = A.alloc("e3", 128, 64, BF16)
    ch = C["const"]
    P.dma(ch, [
        lambda e: e.dma_start(out=X["ident"].ap, in_=io["c_ident"]),
        lambda e: e.dma_start(out=X["onesb"].ap, in_=io["c_onesb"]),
        lambda e: e.dma_start(out=X["onesf"].ap, in_=io["c_onesf"]),
        lambda e: e.dma_start(out=X["onesfc"].ap, in_=io["c_onesfc"]),
        lambda e: e.dma_start(out=X["eps"].ap, in_=io["c_eps"]),
        lambda e: e.dma_start(out=X["identf"].ap, in_=io["c_identf"]),
        lambda e: e.dma_start(out=X["e2"].ap, in_=io["c_e2"]),
        lambda e: e.dma_start(out=X["e3"].ap, in_=io["c_e3"]),
    ], w=[X["ident"], X["onesb"], X["onesf"], X["onesfc"], X["eps"], X["identf"], X["e2"], X["e3"]], dbg="consts")
    return X


def emit_outproj_ln(P, A, io, C, X, banks, wname, xin, xout, aospill, bas, bxout):
    A.mark()
    wo = A.alloc("wo", 128, NFC * NFC * 128, BF16)
    P.dma(C["w"], [lambda e: e.dma_start(
            out=wo.ap.rearrange("p (c j o) -> p c j o", c=NFC, j=NFC),
            in_=io[wname].rearrange("(c p) (j o) -> p c j o", p=128, j=NFC))],
          w=[wo], dbg="wo")
    ao_in = [A.alloc(f"aoin{i}", 128, NFC * 512, BF16) for i in range(2)]
    xr_in = [A.alloc(f"xrin{i}", 128, NFC * 512, F32) for i in range(2)]
    zt = A.alloc("zt", 128, NFC * 512, F32)
    zsq = A.alloc("zsq", 128, 512, BF16)
    mu = A.alloc("mu", 1, 512, F32)
    ms = A.alloc("ms", 1, 512, F32)
    aa = A.alloc("aa", 1, 512, F32)
    bb2 = A.alloc("bb2", 1, 512, F32)
    yst = [A.alloc(f"yst{i}", 128, NFC * 512, F32) for i in range(2)]
    stA, stB = banks[6], banks[5]
    abank, bbank = banks[7], banks[3]

    for tc in range(NTC):
        sl = tc % 2
        ai = ao_in[sl]
        xr = xr_in[sl]
        P.dma(C["aoin"], [lambda e, ai=ai, tc=tc: e.dma_start(
                out=ai.ap.rearrange("p (c t) -> p c t", c=NFC),
                in_=aospill[:, tc * 512:(tc + 1) * 512].rearrange("(c p) t -> p c t", p=128))],
              r=[bas], w=[ai], dbg=f"aoin{tc}")
        P.dma(C["xin"], [lambda e, xr=xr, tc=tc: e.dma_start(
                out=xr.ap.rearrange("p (c t) -> p c t", c=NFC),
                in_=xin[:, tc * 512:(tc + 1) * 512].rearrange("(c p) t -> p c t", p=128))],
              w=[xr], dbg=f"xrin{tc}")
        for j in range(NFC):
            pb = banks[(j % 2)]
            for c in range(NFC):
                P.op("t", lambda e, j=j, c=c, ai=ai, pb=pb: e.matmul(
                        pb.ap, wo.ap[:, (c * NFC + j) * 128:(c * NFC + j + 1) * 128],
                        ai.ap[:, c * 512:(c + 1) * 512], start=(c == 0), stop=(c == NFC - 1)),
                     r=[wo, ai], w=[pb], dbg=f"wout{tc}_{j}_{c}")
            P.op("v", lambda e, pb=pb, xr=xr, j=j: e.tensor_tensor(
                    zt.ap[:, j * 512:(j + 1) * 512], pb.ap, xr.ap[:, j * 512:(j + 1) * 512], ALU.add),
                 r=[pb, xr], w=[zt], dbg=f"zadd{tc}_{j}")
            P.op("s", lambda e, j=j: e.activation(zsq.ap, zt.ap[:, j * 512:(j + 1) * 512], AF.Square),
                 r=[zt], w=[zsq], dbg=f"zsq{tc}_{j}")
            P.op("t", lambda e, j=j: e.matmul(stB.ap[0:1, :], X["onesb"].ap[:, 0:1], zsq.ap,
                                              start=(j == 0), stop=(j == NFC - 1)),
                 r=[zsq, X["onesb"]], w=[stB], dbg=f"s2st{tc}_{j}")
            P.op("t", lambda e, j=j: e.matmul(stA.ap[0:1, :], X["onesfc"].ap,
                                              zt.ap[:, j * 512:(j + 1) * 512],
                                              start=(j == 0), stop=(j == NFC - 1)),
                 r=[zt, X["onesfc"]], w=[stA], dbg=f"s1st{tc}_{j}")
        P.op("v", lambda e: e.tensor_scalar_mul(mu.ap, stA.ap[0:1, :], 1.0 / D), r=[stA], w=[mu])
        P.op("v", lambda e: e.tensor_scalar_mul(ms.ap, stB.ap[0:1, :], 1.0 / D), r=[stB], w=[ms])
        P.op("v", lambda e: e.tensor_tensor(bb2.ap, mu.ap, mu.ap, ALU.mult), r=[mu], w=[bb2])
        P.op("v", lambda e: e.tensor_tensor(ms.ap, ms.ap, bb2.ap, ALU.subtract),
             r=[ms, bb2], w=[ms], dbg="var")
        P.op("s", lambda e: e.activation(ms.ap, ms.ap, AF.Sqrt, bias=X["eps"].ap[0:1, :]), r=[ms, X["eps"]], w=[ms])
        P.op("v", lambda e: e.reciprocal(aa.ap, ms.ap), r=[ms], w=[aa])
        P.op("v", lambda e: e.tensor_tensor(bb2.ap, mu.ap, aa.ap, ALU.mult), r=[mu, aa], w=[bb2])
        P.op("v", lambda e: e.tensor_scalar_mul(bb2.ap, bb2.ap, -1.0), r=[bb2], w=[bb2])
        P.op("t", lambda e: e.matmul(abank.ap, X["onesf"].ap[0:1, :], aa.ap, start=True, stop=True),
             r=[aa, X["onesf"]], w=[abank], dbg=f"abc{tc}")
        P.op("t", lambda e: e.matmul(bbank.ap, X["onesf"].ap[0:1, :], bb2.ap, start=True, stop=True),
             r=[bb2, X["onesf"]], w=[bbank], dbg=f"bbc{tc}")
        ys = yst[sl]
        for j in range(NFC):
            P.op("v", lambda e, j=j, ys=ys: e.tensor_tensor(
                    ys.ap[:, j * 512:(j + 1) * 512], zt.ap[:, j * 512:(j + 1) * 512],
                    abank.ap, ALU.mult),
                 r=[zt, abank], w=[ys], dbg=f"lnm{tc}_{j}")
            P.op("v", lambda e, j=j, ys=ys: e.tensor_tensor(
                    ys.ap[:, j * 512:(j + 1) * 512], ys.ap[:, j * 512:(j + 1) * 512],
                    bbank.ap, ALU.add),
                 r=[ys, bbank], w=[ys], dbg=f"lna{tc}_{j}")
        P.dma(C["xout"], [lambda e, ys=ys, tc=tc: e.dma_start(
                out=xout[:, tc * 512:(tc + 1) * 512].rearrange("(c p) t -> p c t", p=128),
                in_=ys.ap.rearrange("p (c t) -> p c t", c=NFC))],
              r=[ys], w=[bxout], dbg=f"xout{tc}")
    A.release()
    return bxout


def build_attn(P, A, io, C, X, banks, layer):
    nc = P.nc
    xin, xout = io["xin"], io["xout"]
    qspill, kspill, aospill = io["qspill"], io["kspill"], io["aospill"]
    bqs, bks, bas = Buf("qspill"), Buf("kspill"), Buf("aospill")
    bxout = Buf("xout")

    A.mark()
    negBB = A.alloc("negBB", 128, NH * NTC, F32)
    g32 = A.alloc("g32", 128, NFC, F32)
    gsb = A.alloc("gsb", 128, NFC, F32)
    qn2m = A.alloc("qn2m", 1, NH * NTC, F32)
    kn2m = A.alloc("kn2m", 1, NH * NTC, F32)
    kn2h = A.alloc("kn2h", 1, NH, F32)
    negB = A.alloc("negB", 1, NH * NTC, F32)
    A.mark()
    vP = A.alloc("vP", 128, NH * KC * 65, BF16)
    A.mark()
    xnb = [A.alloc(f"xn{c}", 128, T, BF16) for c in range(NFC)]

    P.op("v", lambda e: e.memset(vP.ap, 1.0), w=[vP], dbg="vp_ones")
    if layer == 0:
        P.dma(C["const"], [lambda e: e.dma_start(out=gsb.ap, in_=io["gamma"])], w=[gsb])
        P.op("v", lambda e: e.tensor_scalar(g32.ap, gsb.ap, 1.0, 32.0, ALU.add, ALU.mult),
             r=[gsb], w=[g32], dbg="g32")

    A.mark()
    xin_t = [A.alloc(f"xint{i}", 128, NFC * 512, F32) for i in range(2)]
    xsq = [A.alloc(f"xsq{i}", 128, NFC * 512, BF16) for i in range(2)]
    snt = A.alloc("snt", 1, 512, F32)
    stt = A.alloc("stt", 1, 512, F32)
    stat_bank = banks[6]
    bc_bank = banks[7]
    for tc in range(NTC):
        sl = tc % 2
        xt = xin_t[sl]
        src = xin[:, tc * 512:(tc + 1) * 512].rearrange("(c p) t -> p c t", p=128)
        P.dma(C["xin"], [lambda e, xt=xt, src=src: e.dma_start(
                out=xt.ap.rearrange("p (c t) -> p c t", c=NFC), in_=src)],
              w=[xt], dbg=f"xin{tc}")
        if layer == 0:
            xq = xsq[sl]
            P.op("s", lambda e, xt=xt, xq=xq: e.activation(xq.ap, xt.ap, AF.Square),
                 r=[xt], w=[xq], dbg=f"xsq{tc}")
            for c in range(NFC):
                P.op("t", lambda e, xq=xq, c=c: e.matmul(
                        stat_bank.ap[0:1, :], X["onesb"].ap[:, 0:1],
                        xq.ap[:, c * 512:(c + 1) * 512], start=(c == 0), stop=(c == NFC - 1)),
                     r=[xq, X["onesb"]], w=[stat_bank], dbg=f"stat{tc}_{c}")
            P.op("v", lambda e: e.tensor_copy(snt.ap, stat_bank.ap[0:1, :]),
                 r=[stat_bank], w=[snt], dbg=f"snev{tc}")
            P.op("s", lambda e: e.activation(snt.ap, snt.ap, AF.Sqrt),
                 r=[snt], w=[snt], dbg=f"sqrt{tc}")
            P.op("v", lambda e: e.tensor_scalar_max(snt.ap, snt.ap, 1e-12),
                 r=[snt], w=[snt])
            P.op("v", lambda e: e.reciprocal(stt.ap, snt.ap),
                 r=[snt], w=[stt], dbg=f"recip{tc}")
            P.op("t", lambda e: e.matmul(bc_bank.ap, X["onesf"].ap[0:1, :],
                                         stt.ap, start=True, stop=True),
                 r=[stt, X["onesf"]], w=[bc_bank], dbg=f"stokB{tc}")
            for c in range(NFC):
                P.op("v", lambda e, xt=xt, c=c, tc=tc: e.tensor_tensor(
                        xnb[c].ap[:, tc * 512:(tc + 1) * 512],
                        xt.ap[:, c * 512:(c + 1) * 512], bc_bank.ap, ALU.mult),
                     r=[xt, bc_bank], w=[xnb[c]], dbg=f"xn{tc}_{c}")
        else:
            for c in range(NFC):
                P.op("v", lambda e, xt=xt, c=c, tc=tc: e.tensor_copy(
                        xnb[c].ap[:, tc * 512:(tc + 1) * 512], xt.ap[:, c * 512:(c + 1) * 512]),
                     r=[xt], w=[xnb[c]], dbg=f"xcast{tc}_{c}")
    A.release()

    A.mark()
    wqb = [A.alloc(f"wq{i}", 128, NFC * 128, BF16) for i in range(2)]
    qstage = [A.alloc(f"qst{i}", 128, 512, BF16) for i in range(2)]
    sqstage = [A.alloc(f"sqst{i}", 128, 512, BF16) for i in range(2)]
    vstage = [A.alloc(f"vst{i}", 128, T, BF16) for i in range(2)]
    n2bank = banks[5]

    def w_src(j):
        if layer == 0:
            return io["w_qkv"].rearrange("(c p) o -> p c o", p=128)[:, :, j * 128:(j + 1) * 128]
        kind, jj = divmod(j, NFC)
        wsrc = io[["wqT", "wkT", "wvT"][kind]]
        return wsrc.rearrange("(c p) o -> p c o", p=128)[:, :, jj * 128:(jj + 1) * 128]

    for j in range(3 * NFC):
        sl = j % 2
        wq = wqb[sl]
        P.dma(C["w"], [lambda e, wq=wq, j=j: e.dma_start(
                out=wq.ap.rearrange("p (c o) -> p c o", c=NFC), in_=w_src(j))],
              w=[wq], dbg=f"wq{j}")
        kind, jj = divmod(j, NFC)
        if layer == 0:
            isq = (j < NFC)
            for c in range(NFC):
                if isq:
                    P.op("v", lambda e, wq=wq, c=c: e.tensor_scalar(
                            wq.ap[:, c * 128:(c + 1) * 128], wq.ap[:, c * 128:(c + 1) * 128],
                            g32.ap[:, c:c + 1], 0.125, ALU.mult, ALU.mult),
                         r=[wq, g32], w=[wq], dbg=f"wfix{j}_{c}")
                else:
                    P.op("v", lambda e, wq=wq, c=c: e.tensor_scalar_mul(
                            wq.ap[:, c * 128:(c + 1) * 128], wq.ap[:, c * 128:(c + 1) * 128],
                            g32.ap[:, c:c + 1]),
                         r=[wq, g32], w=[wq], dbg=f"wfix{j}_{c}")
        elif j < NFC:
            P.op("v", lambda e, wq=wq: e.tensor_scalar_mul(wq.ap, wq.ap, 0.125),
                 r=[wq], w=[wq], dbg=f"wq_scale{j}")
        for tc in range(NTC):
            pb = banks[(j * NTC + tc) % 2]
            for c in range(NFC):
                P.op("t", lambda e, wq=wq, c=c, tc=tc, pb=pb: e.matmul(
                        pb.ap, wq.ap[:, c * 128:(c + 1) * 128],
                        xnb[c].ap[:, tc * 512:(tc + 1) * 512],
                        start=(c == 0), stop=(c == NFC - 1)),
                     r=[wq, xnb[c]], w=[pb], dbg=f"qkv{j}_{tc}_{c}")
            if kind < 2:
                st = qstage[tc % 2]
                sq = sqstage[tc % 2]
                P.op("s", lambda e, pb=pb, st=st: e.activation(st.ap, pb.ap, AF.Copy),
                     r=[pb], w=[st], dbg=f"qkev{j}_{tc}")
                P.op("s", lambda e, st=st, sq=sq: e.activation(sq.ap, st.ap, AF.Square),
                     r=[st], w=[sq], dbg=f"qsq{j}_{tc}")
                for half in range(2):
                    h = jj * 2 + half
                    P.op("t", lambda e, sq=sq, half=half: e.matmul(
                            n2bank.ap[0:1, :], X["onesb"].ap[64 * half:64 * half + 64, 0:1],
                            sq.ap[64 * half:64 * half + 64, :], start=True, stop=True),
                         r=[sq, X["onesb"]], w=[n2bank], dbg=f"n2{j}_{tc}_{half}")
                    dst = (qn2m if kind == 0 else kn2m)
                    P.op("v", lambda e, dst=dst, h=h, tc=tc: e.reduce_max(
                            dst.ap[:, h * NTC + tc: h * NTC + tc + 1], n2bank.ap[0:1, :], AX.X),
                         r=[n2bank], w=[dst], dbg=f"n2m{j}_{tc}_{half}")
                spill = qspill if kind == 0 else kspill
                bsp = bqs if kind == 0 else bks
                P.dma(C["spillout"], [lambda e, st=st, spill=spill, jj=jj, tc=tc: e.dma_start(
                        out=spill[jj * 128:(jj + 1) * 128, tc * 512:(tc + 1) * 512], in_=st.ap)],
                      r=[st], w=[bsp], dbg=f"spill{j}_{tc}")
            else:
                vs = vstage[jj % 2]
                P.op("v", lambda e, pb=pb, vs=vs, tc=tc: e.tensor_copy(
                        vs.ap[:, tc * 512:(tc + 1) * 512], pb.ap),
                     r=[pb], w=[vs], dbg=f"vev{j}_{tc}")
        if kind == 2:
            vs = vstage[jj % 2]
            for tb in range(KC):
                tb_bank = banks[2 + (tb % 2)]
                P.op("t", lambda e, vs=vs, tb=tb, tb_bank=tb_bank: e.transpose(
                        tb_bank.ap.bitcast(BF16)[:, 0:128], vs.ap[:, tb * 128:(tb + 1) * 128],
                        X["ident"].ap),
                     r=[vs, X["ident"]], w=[tb_bank], dbg=f"vT{jj}_{tb}")
                def cp(e, tb_bank=tb_bank, jj=jj, tb=tb):
                    srcv = tb_bank.ap.bitcast(BF16)[:, 0:128].rearrange("p (h d) -> p h d", h=2)
                    dstv = vP.ap.rearrange("p (h k o) -> p h k o", h=NH, k=KC)[:, 2 * jj:2 * jj + 2, tb, 0:64]
                    return e.tensor_copy(dstv, srcv)
                P.op("v", cp, r=[tb_bank], w=[vP], dbg=f"vPcp{jj}_{tb}")
    A.release()
    A.release()

    bc_bank = banks[7]
    P.op("v", lambda e: e.tensor_reduce(kn2h.ap[:, 0:1], kn2m.ap, AX.X, ALU.max),
         r=[kn2m], w=[kn2h], dbg="kn2h")
    P.op("v", lambda e: e.tensor_reduce(negB.ap[:, 0:1], qn2m.ap, AX.X, ALU.max),
         r=[qn2m], w=[negB], dbg="qn2g")
    P.op("s", lambda e: e.activation(negB.ap[:, 0:1], negB.ap[:, 0:1], AF.Sqrt), r=[negB], w=[negB])
    P.op("s", lambda e: e.activation(kn2h.ap[:, 0:1], kn2h.ap[:, 0:1], AF.Sqrt), r=[kn2h], w=[kn2h])
    P.op("v", lambda e: e.tensor_tensor(negB.ap[:, 0:1], negB.ap[:, 0:1], kn2h.ap[:, 0:1], ALU.mult),
         r=[negB, kn2h], w=[negB], dbg="negB")
    P.op("v", lambda e: e.tensor_scalar_mul(negB.ap[:, 0:1], negB.ap[:, 0:1], -1.0), r=[negB], w=[negB])
    P.op("t", lambda e: e.matmul(bc_bank.ap[:, 0:1], X["onesf"].ap[0:1, :], negB.ap[:, 0:1],
                                 start=True, stop=True),
         r=[negB, X["onesf"]], w=[bc_bank], dbg="negBbc")
    P.op("v", lambda e: e.tensor_copy(negBB.ap[:, 0:1], bc_bank.ap[:, 0:1]), r=[bc_bank], w=[negBB])

    A.mark()
    psall = banks[8]
    ktile = [A.alloc(f"kt{i}", 128, T, BF16) for i in range(2)]
    qtile = [A.alloc(f"qt{i}", 128, 512, BF16) for i in range(4)]
    pt = [A.alloc(f"pt{i}", 128, 1536, BF16) for i in range(2)]
    rdsb = A.alloc("rdsb", 1, 1024, F32)
    rdbb = A.alloc("rdbb", 64, 512, F32)
    aostage = [A.alloc(f"aost{i}", 128, 512, BF16) for i in range(2)]
    osbuf = [A.alloc(f"osb{i}", 65, 1024, F32) for i in range(2)]
    obank = {0: banks[6], 1: banks[7]}

    units = [(kc, head) for kc in range(KC) for head in (0, 1)]
    NB = (len(units) + 2) // 3
    for hp in range(NFC):
        kt = ktile[hp % 2]
        P.dma(C["kin"], [lambda e, kt=kt, hp=hp: e.dma_start(
                out=kt.ap, in_=kspill[hp * 128:(hp + 1) * 128, :])],
              r=[bks], w=[kt], dbg=f"kin{hp}")
        for qc in range(NTC):
            qt = qtile[qc % 4]
            P.dma(C["qin"], [lambda e, qt=qt, hp=hp, qc=qc: e.dma_start(
                    out=qt.ap, in_=qspill[hp * 128:(hp + 1) * 128, qc * 512:(qc + 1) * 512])],
                  r=[bqs], w=[qt], dbg=f"qin{hp}_{qc}")

            def emit_pv(j):
                us = units[3 * j:3 * j + 3]
                p = pt[j % 2]
                for i, (kc, head) in enumerate(us):
                    ob = obank[head]
                    def pv(e, p=p, i=i, kc=kc, head=head, ob=ob, hp=hp):
                        lhs = vP.ap.rearrange("p (h k o) -> p h k o", h=NH, k=KC)[:, 2 * hp + head, kc, :]
                        return e.matmul(ob.ap[0:65, :], lhs, p.ap[:, i * 512:(i + 1) * 512],
                                        start=(kc == 0), stop=(kc == KC - 1))
                    P.op("t", pv, r=[p, vP], w=[ob], dbg=f"pv{hp}_{qc}_{j}_{i}")

            for j in range(NB):
                us = units[3 * j:3 * j + 3]
                sb0 = (3 * j) % 6
                bks_used = [banks[sb0 + i] for i in range(len(us))]
                for i, (kc, head) in enumerate(us):
                    bk = bks_used[i]
                    P.op("t", lambda e, kt=kt, qt=qt, kc=kc, head=head, bk=bk: e.matmul(
                            bk.ap, kt.ap[64 * head:64 * head + 64, kc * 128:(kc + 1) * 128],
                            qt.ap[64 * head:64 * head + 64, :], start=True, stop=True),
                         r=[kt, qt], w=[bk], dbg=f"qk{hp}_{qc}_{j}_{i}")
                if j > 0:
                    emit_pv(j - 1)
                n = len(us) * 512
                p = pt[j % 2]
                P.op("s", lambda e, sb0=sb0, n=n, p=p: e.activation(
                        p.ap[:, 0:n], psall.ap[:, sb0 * 512:sb0 * 512 + n], AF.Exp,
                        bias=negBB.ap[:, 0:1]),
                     r=bks_used + [negBB], w=[p], dbg=f"exp{hp}_{qc}_{j}")
            emit_pv(NB - 1)
            ao = aostage[qc % 2]
            osb = osbuf[qc % 2]
            for head, ob in ((0, obank[0]), (1, obank[1])):
                P.op("v", lambda e, ob=ob, osb=osb, head=head: e.tensor_copy(
                        osb.ap[:, head * 512:(head + 1) * 512], ob.ap[0:65, :]),
                     r=[ob], w=[osb], dbg=f"oev{hp}_{qc}_{head}")
            for head in (0, 1):
                P.op("v", lambda e, osb=osb, head=head: e.reciprocal(
                        rdsb.ap[:, head * 512:(head + 1) * 512],
                        osb.ap[64:65, head * 512:(head + 1) * 512]),
                     r=[osb], w=[rdsb], dbg=f"rd{hp}_{qc}_{head}")
                rb = obank[head]
                P.op("t", lambda e, rb=rb, head=head: e.matmul(
                        rb.ap[0:64, :], X["onesf"].ap[0:1, 0:64],
                        rdsb.ap[:, head * 512:(head + 1) * 512], start=True, stop=True),
                     r=[rdsb, X["onesf"]], w=[rb], dbg=f"rdbc{hp}_{qc}_{head}")
                P.op("v", lambda e, rb=rb, osb=osb, ao=ao, head=head: e.tensor_tensor(
                        ao.ap[64 * head:64 * head + 64, :], osb.ap[0:64, 512 * head:512 * head + 512],
                        rb.ap[0:64, :], ALU.mult),
                     r=[osb, rb], w=[ao], dbg=f"aonorm{hp}_{qc}_{head}")
            P.dma(C["spillout"], [lambda e, ao=ao, hp=hp, qc=qc: e.dma_start(
                    out=aospill[hp * 128:(hp + 1) * 128, qc * 512:(qc + 1) * 512], in_=ao.ap)],
                  r=[ao], w=[bas], dbg=f"aoout{hp}_{qc}")
    A.release()
    A.release()

    ob = emit_outproj_ln(P, A, io, C, X, banks,
                          ("w_out" if layer == 0 else "wfcT"), xin, xout, aospill, bas, bxout)
    A.release()
    return ob




C1R = float(3 * 2 ** 21)
C2R = float(3 * 2 ** 22)
C3R = float(3 * 2 ** 23)

def emit_fp4_quant(P, A, src_buf, dst_buf, rB_buf, sB_buf, tmp, ncols, pcols, src_off=0):
    v, w1, a05, a1 = tmp["v"], tmp["q"], tmp["a"], tmp["b"]
    m2, m4 = tmp["m2"], tmp["m4"]
    for lo in range(0, ncols, pcols):
        hi = min(lo + pcols, ncols)
        n = hi - lo
        s0, s1 = src_off + lo, src_off + hi
        P.op("v", lambda e, s0=s0, s1=s1, lo=lo, hi=hi, n=n: e.tensor_tensor(
                v.ap[:, 0:n], src_buf.ap[:, s0:s1], rB_buf.ap[:, lo:hi], ALU.mult),
             r=[src_buf, rB_buf], w=[v], dbg="q_v")
        P.op("s", lambda e, n=n: e.activation(w1.ap[:, 0:n], v.ap[:, 0:n], AF.Square),
             r=[v], w=[w1], dbg="q_vsq")
        P.op("v", lambda e, n=n: e.tensor_scalar(m2.ap[:, 0:n], w1.ap[:, 0:n], 4.0, None, ALU.is_gt),
             r=[w1], w=[m2], dbg="q_m2")
        P.op("v", lambda e, n=n: e.tensor_scalar(m4.ap[:, 0:n], w1.ap[:, 0:n], 16.0, None, ALU.is_gt),
             r=[w1], w=[m4], dbg="q_m4")
        P.op("v", lambda e, n=n: e.tensor_scalar(a05.ap[:, 0:n], v.ap[:, 0:n], C1R, C1R,
                                                 ALU.add, ALU.subtract),
             r=[v], w=[a05], dbg="q_a05")
        P.op("v", lambda e, n=n: e.tensor_scalar(a1.ap[:, 0:n], v.ap[:, 0:n], C2R, C2R,
                                                 ALU.add, ALU.subtract),
             r=[v], w=[a1], dbg="q_a1")
        P.op("v", lambda e, n=n: e.tensor_scalar(w1.ap[:, 0:n], v.ap[:, 0:n], C3R, C3R,
                                                 ALU.add, ALU.subtract),
             r=[v], w=[w1], dbg="q_a2")
        P.op("v", lambda e, n=n: e.copy_predicated(a05.ap[:, 0:n], m2.ap[:, 0:n], a1.ap[:, 0:n]),
             r=[m2, a1, a05], w=[a05], dbg="q_s1")
        P.op("v", lambda e, n=n: e.copy_predicated(a05.ap[:, 0:n], m4.ap[:, 0:n], w1.ap[:, 0:n]),
             r=[m4, w1, a05], w=[a05], dbg="q_s2")
        P.op("v", lambda e, s0=s0, s1=s1, n=n, lo=lo, hi=hi: e.tensor_tensor(
                dst_buf.ap[:, s0:s1], a05.ap[:, 0:n], sB_buf.ap[:, lo:hi], ALU.mult),
             r=[a05, sB_buf], w=[dst_buf], dbg="q_deq")


def gen_wquant_nat(P, A, io, C, X, banks, wname, wq_dram, nrows, kdim, bwq, tbpair):
    nch = nrows // 128
    kch = kdim // 128
    nblk = kdim // 16
    wnat = [A.alloc(f"wn{wname}{i}", 128, kdim, F32) for i in range(2)]
    sw = [A.alloc(f"sw{wname}{i}", 128, nblk, F32) for i in range(2)]
    rw = [A.alloc(f"rw{wname}{i}", 128, nblk, F32) for i in range(2)]
    swB = [A.alloc(f"swB{wname}{i}", 128, kdim, F32) for i in range(2)]
    rwB = [A.alloc(f"rwB{wname}{i}", 128, kdim, F32) for i in range(2)]
    tmp = [dict([(k, A.alloc(f"wt{wname}{k}{i}", 128, kdim, F32)) for k in ("v", "q", "a", "b")]
                + [("m2", A.alloc(f"wm2{wname}{i}", 128, kdim, U8)),
                   ("m4", A.alloc(f"wm4{wname}{i}", 128, kdim, U8))]) for i in range(2)]
    wqd = [A.alloc(f"wqd{wname}{i}", 128, kdim, BF16) for i in range(2)]
    stg = [A.alloc(f"wstg{wname}{i}", 128, kdim, BF16) for i in range(2)]
    tb0, tb1 = tbpair
    k16 = lambda b: b.ap.rearrange("p (b k) -> p b k", k=16)
    for a in range(nch):
        i = a % 2
        wn = wnat[i]
        P.dma(C["xin"], [lambda e, wn=wn, a=a: e.dma_start(
                out=wn.ap, in_=io[wname][a * 128:(a + 1) * 128, :])],
              w=[wn], dbg=f"wl{wname}{a}")
        P.op("v", lambda e, wn=wn, i=i: e.tensor_reduce(
                sw[i].ap, wn.ap.rearrange("p (b k) -> p b k", k=16), AX.X, ALU.max,
                apply_absolute_value=True),
             r=[wn], w=[sw[i]], dbg=f"ws{wname}{a}")
        P.op("v", lambda e, i=i: e.tensor_scalar(sw[i].ap, sw[i].ap, 1.0 / 6.0, 1e-12,
                                                 ALU.mult, ALU.max),
             r=[sw[i]], w=[sw[i]], dbg=f"ws2{wname}{a}")
        P.op("v", lambda e, i=i: e.reciprocal(rw[i].ap, sw[i].ap),
             r=[sw[i]], w=[rw[i]], dbg=f"wr{wname}{a}")
        P.op("s", lambda e, i=i: e.activation(
                k16(swB[i]), sw[i].ap.unsqueeze(2).broadcast_to([128, nblk, 16]), AF.Copy),
             r=[sw[i]], w=[swB[i]], dbg=f"wsb{wname}{a}")
        P.op("s", lambda e, i=i: e.activation(
                k16(rwB[i]), rw[i].ap.unsqueeze(2).broadcast_to([128, nblk, 16]), AF.Copy),
             r=[rw[i]], w=[rwB[i]], dbg=f"wrb{wname}{a}")
        emit_fp4_quant(P, A, wn, wqd[i], rwB[i], swB[i], tmp[i], kdim, kdim)
        for c in range(kch):
            tb = tb0 if c % 2 == 0 else tb1
            P.op("t", lambda e, c=c, tb=tb, i=i: e.transpose(
                    tb.ap.bitcast(BF16)[:, 0:128], wqd[i].ap[:, c * 128:(c + 1) * 128],
                    X["ident"].ap),
                 r=[wqd[i], X["ident"]], w=[tb], dbg=f"wT{wname}{a}_{c}")
            P.op("s", lambda e, c=c, tb=tb, i=i: e.activation(
                    stg[i].ap[:, c * 128:(c + 1) * 128], tb.ap.bitcast(BF16)[:, 0:128], AF.Copy),
                 r=[tb], w=[stg[i]], dbg=f"wTe{wname}{a}_{c}")
        P.dma(C["spillout"], [lambda e, a=a, i=i: e.dma_start(
                out=wq_dram.rearrange("(c p) n -> p c n", p=128)[:, :, a * 128:(a + 1) * 128],
                in_=stg[i].ap.rearrange("p (c o) -> p c o", o=128))],
              r=[stg[i]], w=[bwq], dbg=f"wst{wname}{a}")
        yield


def pump(main, aux, naux):
    credit = 0.0
    for _ in main:
        credit += naux
        while credit >= 1.0:
            next(aux, None)
            credit -= 1.0
    for _ in aux:
        pass


def build_l0ffn(P, A, io, C, X, banks):
    xin, xout = io["xin"], io["xout"]
    bxout = Buf("xout")
    bwgq, bwuq, bwdq = Buf("wgq"), Buf("wuq"), Buf("wdq")
    bscxs, bscxr = Buf("scxs"), Buf("scxr")
    bschs = [Buf("schs0"), Buf("schs1")]
    bschr = [Buf("schr0"), Buf("schr1")]
    FFC = FF // 128

    A.mark()
    g1 = gen_wquant_nat(P, A, io, C, X, banks, "wg", io["wgq"], FF, D, bwgq,
                        (banks[2], banks[3]))
    g2 = gen_wquant_nat(P, A, io, C, X, banks, "wu", io["wuq"], FF, D, bwuq,
                        (banks[5], banks[6]))
    for _ in g1:
        next(g2, None)
    for _ in g2:
        pass
    A.release()
    A.mark()
    for _ in gen_wquant_nat(P, A, io, C, X, banks, "wd", io["wdq"], D, FF, bwdq,
                            (banks[2], banks[3])):
        pass
    A.release()

    A.mark()
    xr_in = [A.alloc(f"xr{i}", 128, NFC * 512, F32) for i in range(2)]
    xq = [A.alloc(f"xq{i}", 128, NFC * 512, BF16) for i in range(2)]
    sxT = A.alloc("sxT", 128, 4 * 64, F32)
    rxT = A.alloc("rxT", 128, 4 * 64, F32)
    sx_fm = A.alloc("sxfm", 64, 512, F32)
    rx_fm = A.alloc("rxfm", 64, 512, F32)
    rxB = A.alloc("rxB", 128, 4 * 512, F32)
    sxB = A.alloc("sxB", 128, 4 * 512, F32)
    tmpx = {k: A.alloc(f"t{k}", 128, 2 * 512, F32) for k in ("v", "q", "a", "b")}
    tmpx["m2"] = A.alloc("tm2", 128, 2 * 512, U8)
    tmpx["m4"] = A.alloc("tm4", 128, 2 * 512, U8)
    hso = A.alloc("hso", 128, 512, BF16)
    hbuf = [A.alloc(f"hbuf{i}", 128, FFC * 512, BF16) for i in range(2)]
    shT = A.alloc("shT", 128, 4 * 176, F32)
    rhT = A.alloc("rhT", 128, 4 * 176, F32)
    sh_fm1 = A.alloc("shfm1", 128, 512, F32)
    sh_fm2 = A.alloc("shfm2", 48, 512, F32)
    rh_fm1 = A.alloc("rhfm1", 128, 512, F32)
    rh_fm2 = A.alloc("rhfm2", 48, 512, F32)
    rhB = A.alloc("rhB", 128, 4 * 512, F32)
    shB = A.alloc("shB", 128, 4 * 512, F32)
    wgt = [A.alloc(f"wgt{i}", 128, NFC * 128, BF16) for i in range(2)]
    wut = [A.alloc(f"wut{i}", 128, NFC * 128, BF16) for i in range(2)]
    wdt = [A.alloc("wdt0", 128, FFC * 128, BF16)]
    zt = A.alloc("zt2", 128, NFC * 512, F32)
    zsq = A.alloc("zsq2", 128, 512, BF16)
    mu = A.alloc("mu2", 1, 512, F32)
    ms = A.alloc("ms2", 1, 512, F32)
    aa = A.alloc("aa2", 1, 512, F32)
    bb2 = A.alloc("bb22", 1, 512, F32)
    pg, pu = banks[0], banks[1]
    pdb = [banks[2], banks[3]]
    tbb = [banks[4], banks[7]]
    stA, stB = banks[6], banks[5]
    abank, bbank = banks[5], banks[6]

    def gen_front(tc):
        sl = tc % 2
        xr = xr_in[sl]
        hb = hbuf[sl]
        src2 = xin[:, tc * 512:(tc + 1) * 512].rearrange("(c p) t -> p c t", p=128)
        P.dma(C["xin"], [lambda e, xr=xr, src2=src2: e.dma_start(
                out=xr.ap.rearrange("p (c t) -> p c t", c=NFC), in_=src2)],
              w=[xr], dbg=f"xr{tc}")
        for c in range(NFC):
            for tb in range(4):
                bk = tbb[(c * 4 + tb) % 2]
                P.op("t", lambda e, c=c, tb=tb, bk=bk, xr=xr: e.transpose(
                        bk.ap[:, 0:128],
                        xr.ap[:, c * 512 + tb * 128: c * 512 + (tb + 1) * 128], X["identf"].ap),
                     r=[xr, X["identf"]], w=[bk], dbg=f"sxT{tc}_{c}_{tb}")
                P.op("v", lambda e, c=c, tb=tb, bk=bk: e.tensor_reduce(
                        sxT.ap[:, tb * 64 + c * 8: tb * 64 + (c + 1) * 8],
                        bk.ap[:, 0:128].rearrange("p (b k) -> p b k", k=16),
                        AX.X, ALU.max, apply_absolute_value=True),
                     r=[bk], w=[sxT], dbg=f"sxR{tc}_{c}_{tb}")
        P.op("v", lambda e: e.tensor_scalar(sxT.ap, sxT.ap, 1.0 / 6.0, 1e-12, ALU.mult, ALU.max),
             r=[sxT], w=[sxT])
        P.op("v", lambda e: e.reciprocal(rxT.ap, sxT.ap), r=[sxT], w=[rxT])
        for tb in range(4):
            P.op("t", lambda e, tb=tb: e.transpose(
                    tbb[0].ap[:, 0:128][0:64, :], sxT.ap[:, tb * 64:(tb + 1) * 64], X["identf"].ap),
                 r=[sxT, X["identf"]], w=[tbb[0]], dbg=f"sxTb{tc}_{tb}")
            P.op("v", lambda e, tb=tb: e.tensor_copy(sx_fm.ap[:, tb * 128:(tb + 1) * 128],
                                                     tbb[0].ap[0:64, 0:128]),
                 r=[tbb[0]], w=[sx_fm])
            P.op("t", lambda e, tb=tb: e.transpose(
                    tbb[1].ap[:, 0:128][0:64, :], rxT.ap[:, tb * 64:(tb + 1) * 64], X["identf"].ap),
                 r=[rxT, X["identf"]], w=[tbb[1]], dbg=f"rxTb{tc}_{tb}")
            P.op("v", lambda e, tb=tb: e.tensor_copy(rx_fm.ap[:, tb * 128:(tb + 1) * 128],
                                                     tbb[1].ap[0:64, 0:128]),
                 r=[tbb[1]], w=[rx_fm])
        P.dma(C["spillout"], [
            lambda e: e.dma_start(out=io["scx_s"].rearrange("(b c) t -> c b t", b=8),
                                  in_=sx_fm.ap),
            lambda e: e.dma_start(out=io["scx_r"].rearrange("(b c) t -> c b t", b=8),
                                  in_=rx_fm.ap),
        ], r=[sx_fm, rx_fm], w=[bscxs, bscxr], dbg=f"scxout{tc}")
        yield
        for plo in range(0, NFC, 4):
            phi = plo + 4
            P.dma(C["sc"], [
                lambda e, plo=plo, phi=phi: e.dma_start(
                    out=sxB.ap,
                    in_=io["scx_s"].rearrange("(b c) t -> b (c t)", b=8)[:, plo * 512:phi * 512]
                    .unsqueeze(1).broadcast_to([8, 16, (phi - plo) * 512])),
                lambda e, plo=plo, phi=phi: e.dma_start(
                    out=rxB.ap,
                    in_=io["scx_r"].rearrange("(b c) t -> b (c t)", b=8)[:, plo * 512:phi * 512]
                    .unsqueeze(1).broadcast_to([8, 16, (phi - plo) * 512])),
            ], r=[bscxs, bscxr], w=[sxB, rxB], dbg=f"scxrep{tc}_{plo}")
            emit_fp4_quant(P, A, xr, xq[sl], rxB, sxB, tmpx, 4 * 512, 2 * 512, src_off=plo * 512)
            yield
        for j in range(FFC):
            wg_ = wgt[j % 2]
            wu_ = wut[j % 2]
            P.dma(C["w"], [
                lambda e, wg_=wg_, j=j: e.dma_start(
                    out=wg_.ap.rearrange("p (c o) -> p c o", c=NFC),
                    in_=io["wgq"].rearrange("(c p) n -> p c n", p=128)[:, :, j * 128:(j + 1) * 128]),
                lambda e, wu_=wu_, j=j: e.dma_start(
                    out=wu_.ap.rearrange("p (c o) -> p c o", c=NFC),
                    in_=io["wuq"].rearrange("(c p) n -> p c n", p=128)[:, :, j * 128:(j + 1) * 128]),
            ], r=[bwgq, bwuq], w=[wg_, wu_], dbg=f"wgl{tc}_{j}")
            for c in range(NFC):
                P.op("t", lambda e, c=c, wg_=wg_, sl=sl: e.matmul(
                        pg.ap, wg_.ap[:, c * 128:(c + 1) * 128], xq[sl].ap[:, c * 512:(c + 1) * 512],
                        start=(c == 0), stop=(c == NFC - 1)),
                     r=[wg_, xq[sl]], w=[pg], dbg=f"mg{tc}_{j}_{c}")
            for c in range(NFC):
                P.op("t", lambda e, c=c, wu_=wu_, sl=sl: e.matmul(
                        pu.ap, wu_.ap[:, c * 128:(c + 1) * 128], xq[sl].ap[:, c * 512:(c + 1) * 512],
                        start=(c == 0), stop=(c == NFC - 1)),
                     r=[wu_, xq[sl]], w=[pu], dbg=f"mu{tc}_{j}_{c}")
            P.op("s", lambda e: e.activation(hso.ap, pg.ap, AF.Silu), r=[pg], w=[hso], dbg=f"silu{tc}_{j}")
            P.op("v", lambda e, j=j, hb=hb: e.tensor_tensor(
                    hb.ap[:, j * 512:(j + 1) * 512], hso.ap, pu.ap, ALU.mult),
                 r=[hso, pu], w=[hb], dbg=f"hmul{tc}_{j}")
            if j % 2 == 1:
                yield
        for c in range(FFC):
            for tb in range(4):
                bk = tbb[(c * 4 + tb) % 2]
                P.op("t", lambda e, c=c, tb=tb, bk=bk, hb=hb: e.transpose(
                        bk.ap.bitcast(BF16)[:, 0:128],
                        hb.ap[:, c * 512 + tb * 128: c * 512 + (tb + 1) * 128], X["ident"].ap),
                     r=[hb, X["ident"]], w=[bk], dbg=f"shT{tc}_{c}_{tb}")
                P.op("v", lambda e, c=c, tb=tb, bk=bk: e.tensor_reduce(
                        shT.ap[:, tb * 176 + c * 8: tb * 176 + (c + 1) * 8],
                        bk.ap.bitcast(BF16)[:, 0:128].rearrange("p (b k) -> p b k", k=16),
                        AX.X, ALU.max, apply_absolute_value=True),
                     r=[bk], w=[shT], dbg=f"shR{tc}_{c}_{tb}")
            if c % 4 == 3:
                yield
        P.op("v", lambda e: e.tensor_scalar(shT.ap, shT.ap, 1.0 / 6.0, 1e-12, ALU.mult, ALU.max),
             r=[shT], w=[shT])
        P.op("v", lambda e: e.reciprocal(rhT.ap, shT.ap), r=[shT], w=[rhT])
        for tb in range(4):
            P.op("t", lambda e, tb=tb: e.transpose(
                    tbb[0].ap[:, 0:128], shT.ap[:, tb * 176: tb * 176 + 128], X["identf"].ap),
                 r=[shT, X["identf"]], w=[tbb[0]])
            P.op("v", lambda e, tb=tb: e.tensor_copy(sh_fm1.ap[:, tb * 128:(tb + 1) * 128],
                                                     tbb[0].ap[:, 0:128]),
                 r=[tbb[0]], w=[sh_fm1])
            P.op("t", lambda e, tb=tb: e.transpose(
                    tbb[1].ap[:, 0:128][0:48, :], shT.ap[:, tb * 176 + 128: tb * 176 + 176],
                    X["identf"].ap),
                 r=[shT, X["identf"]], w=[tbb[1]])
            P.op("v", lambda e, tb=tb: e.tensor_copy(sh_fm2.ap[:, tb * 128:(tb + 1) * 128],
                                                     tbb[1].ap[0:48, 0:128]),
                 r=[tbb[1]], w=[sh_fm2])
            P.op("t", lambda e, tb=tb: e.transpose(
                    tbb[0].ap[:, 0:128], rhT.ap[:, tb * 176: tb * 176 + 128], X["identf"].ap),
                 r=[rhT, X["identf"]], w=[tbb[0]])
            P.op("v", lambda e, tb=tb: e.tensor_copy(rh_fm1.ap[:, tb * 128:(tb + 1) * 128],
                                                     tbb[0].ap[:, 0:128]),
                 r=[tbb[0]], w=[rh_fm1])
            P.op("t", lambda e, tb=tb: e.transpose(
                    tbb[1].ap[:, 0:128][0:48, :], rhT.ap[:, tb * 176 + 128: tb * 176 + 176],
                    X["identf"].ap),
                 r=[rhT, X["identf"]], w=[tbb[1]])
            P.op("v", lambda e, tb=tb: e.tensor_copy(rh_fm2.ap[:, tb * 128:(tb + 1) * 128],
                                                     tbb[1].ap[0:48, 0:128]),
                 r=[tbb[1]], w=[rh_fm2])
        schs_v = io[f"sch_s{sl}"].rearrange("(b c) t -> c b t", b=8)
        schr_v = io[f"sch_r{sl}"].rearrange("(b c) t -> c b t", b=8)
        P.dma(C["spillout"], [
            lambda e, schs_v=schs_v: e.dma_start(out=schs_v[0:16], in_=sh_fm1.ap),
            lambda e, schs_v=schs_v: e.dma_start(out=schs_v[16:22], in_=sh_fm2.ap),
            lambda e, schr_v=schr_v: e.dma_start(out=schr_v[0:16], in_=rh_fm1.ap),
            lambda e, schr_v=schr_v: e.dma_start(out=schr_v[16:22], in_=rh_fm2.ap),
        ], r=[sh_fm1, sh_fm2, rh_fm1, rh_fm2], w=[bschs[sl], bschr[sl]], dbg=f"schout{tc}")
        yield

    def gen_hq(tc):
        sl = tc % 2
        hb = hbuf[sl]
        for plo in range(0, FFC, 4):
            phi = min(plo + 4, FFC)
            P.dma(C["sc"], [
                lambda e, plo=plo, phi=phi, sl=sl: e.dma_start(
                    out=shB.ap[:, 0:(phi - plo) * 512],
                    in_=io[f"sch_s{sl}"].rearrange("(b c) t -> b (c t)", b=8)[:, plo * 512:phi * 512]
                    .unsqueeze(1).broadcast_to([8, 16, (phi - plo) * 512])),
                lambda e, plo=plo, phi=phi, sl=sl: e.dma_start(
                    out=rhB.ap[:, 0:(phi - plo) * 512],
                    in_=io[f"sch_r{sl}"].rearrange("(b c) t -> b (c t)", b=8)[:, plo * 512:phi * 512]
                    .unsqueeze(1).broadcast_to([8, 16, (phi - plo) * 512])),
            ], r=[bschs[sl], bschr[sl]], w=[shB, rhB], dbg=f"schrep{tc}_{plo}")
            emit_fp4_quant(P, A, hb, hb, rhB, shB, tmpx, (phi - plo) * 512, 2 * 512,
                           src_off=plo * 512)
            yield

    def emit_down(tc):
        sl = tc % 2
        xr = xr_in[sl]
        hb = hbuf[sl]
        for j in range(NFC):
            wd_ = wdt[0]
            P.dma(C["gw"], [lambda e, wd_=wd_, j=j: e.dma_start(
                    out=wd_.ap.rearrange("p (c o) -> p c o", c=FFC),
                    in_=io["wdq"].rearrange("(c p) n -> p c n", p=128)[:, :, j * 128:(j + 1) * 128])],
                  r=[bwdq], w=[wd_], dbg=f"wdl{tc}_{j}")
            pd = pdb[j % 2]
            for c in range(FFC):
                P.op("t", lambda e, c=c, wd_=wd_, pd=pd, hb=hb: e.matmul(
                        pd.ap, wd_.ap[:, c * 128:(c + 1) * 128], hb.ap[:, c * 512:(c + 1) * 512],
                        start=(c == 0), stop=(c == FFC - 1)),
                     r=[wd_, hb], w=[pd], dbg=f"md{tc}_{j}_{c}")
            P.op("v", lambda e, pd=pd, xr=xr, j=j: e.tensor_tensor(
                    zt.ap[:, j * 512:(j + 1) * 512], pd.ap, xr.ap[:, j * 512:(j + 1) * 512], ALU.add),
                 r=[pd, xr], w=[zt], dbg=f"z2{tc}_{j}")
            P.op("s", lambda e, j=j: e.activation(zsq.ap, zt.ap[:, j * 512:(j + 1) * 512], AF.Square),
                 r=[zt], w=[zsq], dbg=f"z2sq{tc}_{j}")
            P.op("t", lambda e, j=j: e.matmul(stB.ap[0:1, :], X["onesb"].ap[:, 0:1], zsq.ap,
                                              start=(j == 0), stop=(j == NFC - 1)),
                 r=[zsq, X["onesb"]], w=[stB], dbg=f"st2b{tc}_{j}")
            P.op("t", lambda e, j=j: e.matmul(stA.ap[0:1, :], X["onesfc"].ap,
                                              zt.ap[:, j * 512:(j + 1) * 512],
                                              start=(j == 0), stop=(j == NFC - 1)),
                 r=[zt, X["onesfc"]], w=[stA], dbg=f"st2a{tc}_{j}")
        P.op("v", lambda e: e.tensor_scalar_mul(mu.ap, stA.ap[0:1, :], 1.0 / D), r=[stA], w=[mu])
        P.op("v", lambda e: e.tensor_scalar_mul(ms.ap, stB.ap[0:1, :], 1.0 / D), r=[stB], w=[ms])
        P.op("v", lambda e: e.tensor_tensor(bb2.ap, mu.ap, mu.ap, ALU.mult), r=[mu], w=[bb2])
        P.op("v", lambda e: e.tensor_tensor(ms.ap, ms.ap, bb2.ap, ALU.subtract), r=[ms, bb2], w=[ms])
        P.op("s", lambda e: e.activation(ms.ap, ms.ap, AF.Sqrt, bias=X["eps"].ap[0:1, :]),
             r=[ms, X["eps"]], w=[ms])
        P.op("v", lambda e: e.reciprocal(aa.ap, ms.ap), r=[ms], w=[aa])
        P.op("v", lambda e: e.tensor_tensor(bb2.ap, mu.ap, aa.ap, ALU.mult), r=[mu, aa], w=[bb2])
        P.op("v", lambda e: e.tensor_scalar_mul(bb2.ap, bb2.ap, -1.0), r=[bb2], w=[bb2])
        P.op("t", lambda e: e.matmul(abank.ap, X["onesf"].ap[0:1, :], aa.ap, start=True, stop=True),
             r=[aa, X["onesf"]], w=[abank], dbg=f"ab2{tc}")
        P.op("t", lambda e: e.matmul(bbank.ap, X["onesf"].ap[0:1, :], bb2.ap, start=True, stop=True),
             r=[bb2, X["onesf"]], w=[bbank], dbg=f"bb2{tc}")
        for j in range(NFC):
            P.op("v", lambda e, j=j: e.tensor_tensor(
                    zt.ap[:, j * 512:(j + 1) * 512], zt.ap[:, j * 512:(j + 1) * 512], abank.ap, ALU.mult),
                 r=[zt, abank], w=[zt], dbg=f"ln2m{tc}_{j}")
            P.op("v", lambda e, j=j: e.tensor_tensor(
                    zt.ap[:, j * 512:(j + 1) * 512], zt.ap[:, j * 512:(j + 1) * 512], bbank.ap, ALU.add),
                 r=[zt, bbank], w=[zt], dbg=f"ln2a{tc}_{j}")
        P.dma(C["xout"], [lambda e, tc=tc: e.dma_start(
                out=xout[:, tc * 512:(tc + 1) * 512].rearrange("(c p) t -> p c t", p=128),
                in_=zt.ap.rearrange("p (c t) -> p c t", c=NFC))],
              r=[zt], w=[bxout], dbg=f"xout2{tc}")

    for tc in range(NTC + 1):
        if 1 <= tc < NTC:
            pump(gen_front(tc), gen_hq(tc - 1), 0.5)
        elif tc < NTC:
            for _ in gen_front(tc):
                pass
        else:
            for _ in gen_hq(tc - 1):
                pass
        if tc >= 1:
            emit_down(tc - 1)
    A.release()
    return bxout


def build_l1attn(P, A, io, C, X, banks):
    xin, xout = io["xin"], io["xout"]
    bxout = Buf("xout3")
    bks, bvs, bas = Buf("kspill3"), Buf("vspill3"), Buf("aospill3")
    bsch = [Buf("scdram0"), Buf("scdram1")]
    badh = [Buf("adram0"), Buf("adram1")]

    A.mark()
    A.mark_top()
    qfm = [A.alloc_top(f"qfm{c}", 128, T, BF16) for c in range(NFC)]
    A.mark()
    xnb = [A.alloc(f"x3n{c}", 128, T, BF16) for c in range(NFC)]
    wqb = [A.alloc(f"w3{i}", 128, NFC * 128, BF16) for i in range(2)]
    kvstage = [A.alloc(f"kvst{i}", 128, 512, BF16) for i in range(2)]
    xin_t = [A.alloc(f"x3t{i}", 128, NFC * 512, F32) for i in range(2)]
    for tc in range(NTC):
        xt = xin_t[tc % 2]
        src = xin[:, tc * 512:(tc + 1) * 512].rearrange("(c p) t -> p c t", p=128)
        P.dma(C["xin"], [lambda e, xt=xt, src=src: e.dma_start(
                out=xt.ap.rearrange("p (c t) -> p c t", c=NFC), in_=src)],
              w=[xt], dbg=f"x3in{tc}")
        for c in range(NFC):
            P.op("v", lambda e, xt=xt, c=c, tc=tc: e.tensor_copy(
                    xnb[c].ap[:, tc * 512:(tc + 1) * 512], xt.ap[:, c * 512:(c + 1) * 512]),
                 r=[xt], w=[xnb[c]], dbg=f"x3c{tc}_{c}")
    for j in range(3 * NFC):
        kind, jj = divmod(j, NFC)
        wq = wqb[j % 2]
        wsrc = io[["wqT", "wkT", "wvT"][kind]].rearrange("(c p) o -> p c o", p=128)[:, :, jj * 128:(jj + 1) * 128]
        P.dma(C["w"], [lambda e, wq=wq, wsrc=wsrc: e.dma_start(
                out=wq.ap.rearrange("p (c o) -> p c o", c=NFC), in_=wsrc)],
              w=[wq], dbg=f"w3{j}")
        if kind == 0:
            P.op("v", lambda e, wq=wq: e.tensor_scalar_mul(wq.ap, wq.ap, 0.125),
                 r=[wq], w=[wq], dbg=f"w3s{j}")
        for tc in range(NTC):
            pb = banks[4 + ((j * NTC + tc) % 4)]
            for c in range(NFC):
                P.op("t", lambda e, wq=wq, c=c, tc=tc, pb=pb: e.matmul(
                        pb.ap, wq.ap[:, c * 128:(c + 1) * 128],
                        xnb[c].ap[:, tc * 512:(tc + 1) * 512],
                        start=(c == 0), stop=(c == NFC - 1)),
                     r=[wq, xnb[c]], w=[pb], dbg=f"p3{j}_{tc}_{c}")
            if kind == 0:
                P.op("s", lambda e, pb=pb, jj=jj, tc=tc: e.activation(
                        qfm[jj].ap[:, tc * 512:(tc + 1) * 512], pb.ap, AF.Copy),
                     r=[pb], w=[qfm[jj]], dbg=f"qe3{j}_{tc}")
            else:
                st = kvstage[tc % 2]
                P.op("s", lambda e, pb=pb, st=st: e.activation(st.ap, pb.ap, AF.Copy),
                     r=[pb], w=[st], dbg=f"kve3{j}_{tc}")
                spill = io["kspill"] if kind == 1 else io["vspill"]
                bsp = bks if kind == 1 else bvs
                P.dma(C["spillout"], [lambda e, st=st, spill=spill, jj=jj, tc=tc: e.dma_start(
                        out=spill[jj * 128:(jj + 1) * 128, tc * 512:(tc + 1) * 512], in_=st.ap)],
                      r=[st], w=[bsp], dbg=f"kvs3{j}_{tc}")
    A.release()
    HT = T // 2
    NTQ = HT // 512
    A.mark()
    kb = [A.alloc(f"kb{i}", 128, HT, BF16) for i in range(2)]
    prod = [A.alloc(f"pr{i}", 128, HT, BF16) for i in range(2)]
    scstg = [A.alloc("scst0", 16, HT, F32)]
    scsb = A.alloc("scsb", 16, 8 * 512, F32)
    asbh = [A.alloc(f"asbh{i}", 16, 8 * 512, BF16) for i in range(2)]
    mx1 = A.alloc("mx1", 16, 2, F32)
    mx2 = A.alloc("mx2", 16, 2, F32)
    sm1 = A.alloc("sm1", 16, 512, F32)
    sm2 = A.alloc("sm2", 16, 512, F32)
    rd = A.alloc("rd", 16, 512, F32)
    rdf = A.alloc("rdf", 16, 512, F32)
    rdb16 = A.alloc("rdb16", 16, 512, BF16)
    mxM = A.alloc("mxM", 16, 2, F32)
    cf = A.alloc("cf", 16, 2, F32)
    scv = io["scdram"].rearrange("p (h t) -> p h t", h=NH)
    adv = io["adram"].rearrange("p (h t) -> p h t", h=NH)

    def gen_scores(h):
        t0 = h * HT
        for hk in range(NH):
            kbt = kb[hk % 2]
            P.dma(C["kin"], [lambda e, kbt=kbt, hk=hk, t0=t0: e.dma_start(
                    out=kbt.ap,
                    in_=io["kspill"][hk * 64:(hk + 1) * 64, t0:t0 + HT]
                    .unsqueeze(0).broadcast_to([2, 64, HT]))],
                  r=[bks], w=[kbt], dbg=f"kb{h}_{hk}")
            for c in range(NFC):
                pr = prod[c % 2]
                P.op("v", lambda e, pr=pr, c=c, kbt=kbt, t0=t0: e.tensor_tensor(
                        pr.ap, qfm[c].ap[:, t0:t0 + HT], kbt.ap, ALU.mult),
                     r=[qfm[c], kbt], w=[pr], dbg=f"pr{h}_{hk}_{c}")
                for tq in range(NTQ):
                    bk = banks[4 * h + tq]
                    P.op("t", lambda e, pr=pr, c=c, tq=tq, bk=bk: e.matmul(
                            bk.ap[0:16, :], X["e2"].ap[:, c * 16:(c + 1) * 16],
                            pr.ap[:, tq * 512:(tq + 1) * 512],
                            start=(c == 0), stop=(c == NFC - 1)),
                         r=[pr, X["e2"]], w=[bk], dbg=f"sc{h}_{hk}_{c}_{tq}")
            st = scstg[0]
            for tq in range(NTQ):
                bk = banks[4 * h + tq]
                P.op("s", lambda e, bk=bk, tq=tq, st=st: e.activation(
                        st.ap[:, tq * 512:(tq + 1) * 512], bk.ap[0:16, :], AF.Copy),
                     r=[bk], w=[st], dbg=f"sce{h}_{hk}_{tq}")
            P.dma(C["spillout"], [lambda e, hk=hk, st=st, t0=t0: e.dma_start(
                    out=scv[:, hk, t0:t0 + HT], in_=st.ap)],
                  r=[st], w=[bsch[h]], dbg=f"scout{h}_{hk}")
            yield

    def gen_softmax(h):
        for tl in range(NTQ):
            tcc = NTQ * h + tl
            tsl = slice(tcc * 512, (tcc + 1) * 512)
            for i, mxb, smb in ((0, mx1, sm1), (1, mx2, sm2)):
                P.dma(C["xin"], [lambda e, i=i, tsl=tsl: e.dma_start(
                        out=scsb.ap.rearrange("p (h t) -> p h t", h=8),
                        in_=scv[:, 8 * i:8 * i + 8, tsl])],
                      r=[bsch[h]], w=[scsb], dbg=f"smin{tcc}_{i}")
                P.op("v", lambda e, mxb=mxb: e.tensor_reduce(
                        mxb.ap[:, 0:1], scsb.ap, AX.X, ALU.max),
                     r=[scsb], w=[mxb], dbg=f"smmax{tcc}_{i}")
                P.op("v", lambda e, mxb=mxb: e.tensor_scalar_mul(
                        mxb.ap[:, 1:2], mxb.ap[:, 0:1], -1.0),
                     r=[mxb], w=[mxb], dbg=f"smneg{tcc}_{i}")
                P.op("s", lambda e, i=i, mxb=mxb: e.activation(
                        asbh[i].ap, scsb.ap, AF.Exp, bias=mxb.ap[:, 1:2]),
                     r=[scsb, mxb], w=[asbh[i]], dbg=f"smexp{tcc}_{i}")
                P.op("v", lambda e, i=i, smb=smb: e.tensor_reduce(
                        smb.ap, asbh[i].ap.rearrange("p (h t) -> p t h", h=8), AX.X, ALU.add),
                     r=[asbh[i]], w=[smb], dbg=f"smsum{tcc}_{i}")
                yield
            P.op("v", lambda e: e.tensor_tensor(
                    mxM.ap[:, 0:1], mx1.ap[:, 0:1], mx2.ap[:, 0:1], ALU.max),
                 r=[mx1, mx2], w=[mxM], dbg=f"smM{tcc}")
            P.op("v", lambda e: e.tensor_scalar_mul(mxM.ap[:, 1:2], mxM.ap[:, 0:1], -1.0),
                 r=[mxM], w=[mxM], dbg=f"smMn{tcc}")
            for i, mxb in ((0, mx1), (1, mx2)):
                P.op("s", lambda e, i=i, mxb=mxb: e.activation(
                        cf.ap[:, i:i + 1], mxb.ap[:, 0:1], AF.Exp, bias=mxM.ap[:, 1:2]),
                     r=[mxb, mxM], w=[cf], dbg=f"smcf{tcc}_{i}")
            for i, smb in ((0, sm1), (1, sm2)):
                P.op("v", lambda e, i=i, smb=smb: e.tensor_scalar_mul(
                        smb.ap, smb.ap, cf.ap[:, i:i + 1]),
                     r=[smb, cf], w=[smb], dbg=f"smcs{tcc}_{i}")
            P.op("v", lambda e: e.tensor_tensor(sm1.ap, sm1.ap, sm2.ap, ALU.add),
                 r=[sm1, sm2], w=[sm1], dbg=f"smsumc{tcc}")
            P.op("v", lambda e: e.reciprocal(rd.ap, sm1.ap), r=[sm1], w=[rd], dbg=f"smrd{tcc}")
            yield
            for i in (0, 1):
                P.op("v", lambda e, i=i: e.tensor_scalar_mul(rdf.ap, rd.ap, cf.ap[:, i:i + 1]),
                     r=[rd, cf], w=[rdf], dbg=f"smri{tcc}_{i}")
                P.op("v", lambda e: e.tensor_copy(rdb16.ap, rdf.ap),
                     r=[rdf], w=[rdb16], dbg=f"smrb{tcc}_{i}")
                P.op("v", lambda e, i=i: e.tensor_tensor(
                        asbh[i].ap.rearrange("p (h t) -> p h t", h=8),
                        asbh[i].ap.rearrange("p (h t) -> p h t", h=8),
                        rdb16.ap.unsqueeze(1).broadcast_to([16, 8, 512]), ALU.mult),
                     r=[asbh[i], rdb16], w=[asbh[i]], dbg=f"smdiv{tcc}_{i}")
                P.dma(C["spillout"], [lambda e, i=i, tsl=tsl: e.dma_start(
                        out=adv[:, 8 * i:8 * i + 8, tsl],
                        in_=asbh[i].ap.rearrange("p (h t) -> p h t", h=8))],
                      r=[asbh[i]], w=[badh[h]], dbg=f"smout{tcc}_{i}")
                yield

    for _ in gen_scores(0):
        pass
    sm0 = gen_softmax(0)
    pump(gen_scores(1), sm0, 1)
    A.release_top()
    A.mark_top()
    vfm2 = [[A.alloc_top(f"vf{h}_{c}", 128, HT, BF16) for c in range(NFC)] for h in range(2)]
    arep = [A.alloc_top(f"ar{i}", 128, HT, BF16) for i in range(6)]
    prod2 = [A.alloc_top(f"p23{i}", 128, HT, BF16) for i in range(5)]
    aostg = [A.alloc_top(f"aos{i}", 64, HT, BF16) for i in range(2)]
    avchan = [C["qin"], C["gw"], C["sdma"]]
    def gen_av(h):
        t0 = h * HT
        vfm = vfm2[h]
        P.dma(C["aoin"],
              [(lambda e, c=c, t0=t0: e.dma_start(
                    out=vfm[c].ap, in_=io["vspill"][c * 128:(c + 1) * 128, t0:t0 + HT]))
               for c in range(NFC)],
              r=[bvs], w=vfm, dbg=f"vres{h}")
        for hq in range(NH):
            for c2 in range(NFC):
                k = hq * NFC + c2
                ar = arep[k % 6]
                P.dma(avchan[k % 3], [lambda e, ar=ar, hq=hq, c2=c2, t0=t0: e.dma_start(
                        out=ar.ap,
                        in_=adv[hq, 2 * c2:2 * c2 + 2, t0:t0 + HT]
                        .unsqueeze(1).broadcast_to([2, 64, HT]))],
                      r=[badh[h]], w=[ar], dbg=f"ar{h}_{hq}_{c2}")
                pr = prod2[k % 5]
                P.op("v", lambda e, pr=pr, ar=ar, c2=c2: e.tensor_tensor(
                        pr.ap, ar.ap, vfm[c2].ap, ALU.mult),
                     r=[ar, vfm[c2]], w=[pr], dbg=f"p2{h}_{hq}_{c2}")
                for tq in range(NTQ):
                    bk = banks[4 * h + tq]
                    P.op("t", lambda e, pr=pr, tq=tq, bk=bk, c2=c2: e.matmul(
                            bk.ap[0:64, :], X["e3"].ap, pr.ap[:, tq * 512:(tq + 1) * 512],
                            start=(c2 == 0), stop=(c2 == NFC - 1)),
                         r=[pr, X["e3"]], w=[bk], dbg=f"av{h}_{hq}_{c2}_{tq}")
            ast = aostg[hq % 2]
            for tq in range(NTQ):
                bk = banks[4 * h + tq]
                P.op("s", lambda e, bk=bk, tq=tq, ast=ast: e.activation(
                        ast.ap[:, tq * 512:(tq + 1) * 512], bk.ap[0:64, :], AF.Copy),
                     r=[bk], w=[ast], dbg=f"aoe{h}_{hq}_{tq}")
            P.dma(C["spillout"], [lambda e, hq=hq, ast=ast, t0=t0: e.dma_start(
                    out=io["aospill"][hq * 64:(hq + 1) * 64, t0:t0 + HT], in_=ast.ap)],
                  r=[ast], w=[bas], dbg=f"ao3{h}_{hq}")
            yield

    pump(gen_av(0), gen_softmax(1), 1)
    for _ in gen_av(1):
        pass
    A.release()
    A.release_top()
    ob = emit_outproj_ln(P, A, io, C, X, banks, "wfcT", xin, xout, io["aospill"], bas, bxout)
    A.release()
    return ob


def build_l1ffn(P, A, io, C, X, banks):
    xin, xout = io["xin"], io["xout"]
    bxout = Buf("xout4")
    bxg = Buf("xgdram")
    bsw1, bsw3, bsw2 = Buf("sw1b"), Buf("sw3b"), Buf("sw2b")
    HSC = HS // 128

    A.mark()
    sb1 = A.alloc("sb1", 128, HSC, F32)
    sb3 = A.alloc("sb3", 128, HSC, F32)
    b1c = A.alloc("b1c", 128, NFC * NE, F32)
    bcomb = A.alloc("bcomb", 128, NFC * NE, F32)
    sb2h = A.alloc("sb2h", 128, NFC, F32)
    P.dma(C["const"], [
        lambda e: e.dma_start(out=sb1.ap, in_=io["sb1r"]),
        lambda e: e.dma_start(out=sb3.ap, in_=io["sb3r"]),
        lambda e: e.dma_start(out=b1c.ap.rearrange("p (h e) -> p h e", e=NE), in_=io["b1r"]),
        lambda e: e.dma_start(out=bcomb.ap.rearrange("p (h e) -> p h e", e=NE), in_=io["b2r"]),
        lambda e: e.dma_start(out=sb2h.ap, in_=io["sb2r"]),
    ], w=[sb1, sb3, b1c, bcomb, sb2h], dbg="biases")
    bcn = A.alloc("bcn", 128, NFC * NE, F32)
    P.op("v", lambda e: e.tensor_scalar_mul(sb2h.ap, sb2h.ap, 0.5), r=[sb2h], w=[sb2h])
    for r in range(NE):
        e0, e1 = (3 * r) % NE, (3 * r + 7) % NE
        for oc in range(NFC):
            P.op("v", lambda e, oc=oc, r=r, e0=e0, e1=e1: e.tensor_tensor(
                    bcn.ap[:, oc * NE + r: oc * NE + r + 1],
                    bcomb.ap[:, oc * NE + e0: oc * NE + e0 + 1],
                    bcomb.ap[:, oc * NE + e1: oc * NE + e1 + 1], ALU.add),
                 r=[bcomb], w=[bcn], dbg=f"bc{r}_{oc}")
    P.op("v", lambda e: e.tensor_scalar_mul(bcn.ap, bcn.ap, 0.25), r=[bcn], w=[bcn])
    for oc in range(NFC):
        P.op("v", lambda e, oc=oc: e.tensor_tensor(
                bcn.ap[:, oc * NE:(oc + 1) * NE],
                bcn.ap[:, oc * NE:(oc + 1) * NE],
                sb2h.ap[:, oc:oc + 1].broadcast_to([128, NE]), ALU.add),
             r=[bcn, sb2h], w=[bcn], dbg=f"bcs{oc}")

    A.mark()
    swst = A.alloc("swst", 128, HS, BF16)
    for (wname, dramb, bflag, fold) in (("sw1", "sw1b", bsw1, 1.0), ("sw3", "sw3b", bsw3, 1.0),
                                        ("sw2", "sw2b", bsw2, 0.5)):
        nr = io[wname].shape[0]
        ncols = io[wname].shape[1]
        for a in range(nr // 128):
            P.dma(C["w"], [lambda e, a=a, wname=wname, ncols=ncols: e.dma_start(
                    out=swst.ap[:, 0:ncols], in_=io[wname][a * 128:(a + 1) * 128, :])],
                  w=[swst], dbg=f"swl{wname}{a}")
            if fold != 1.0:
                P.op("v", lambda e, ncols=ncols, fold=fold: e.tensor_scalar_mul(
                        swst.ap[:, 0:ncols], swst.ap[:, 0:ncols], fold),
                     r=[swst], w=[swst], dbg=f"swf{wname}{a}")
            P.dma(C["spillout"], [lambda e, a=a, dramb=dramb, ncols=ncols: e.dma_start(
                    out=io[dramb][a * 128:(a + 1) * 128, :], in_=swst.ap[:, 0:ncols])],
                  r=[swst], w=[bflag], dbg=f"sws{wname}{a}")
    A.release()

    A.mark()
    xin_t = [A.alloc(f"x4t{i}", 128, NFC * 512, F32) for i in range(2)]
    xgs = [A.alloc(f"xgs{i}", 128, NFC * 512, F32) for i in range(2)]
    for tc in range(NTC):
        xt = xin_t[tc % 2]
        xg_ = xgs[tc % 2]
        src = xin[:, tc * 512:(tc + 1) * 512].rearrange("(c p) t -> p c t", p=128)
        P.dma(C["xin"], [lambda e, xt=xt, src=src: e.dma_start(
                out=xt.ap.rearrange("p (c t) -> p c t", c=NFC), in_=src)],
              w=[xt], dbg=f"x4in{tc}")
        for c in range(NFC):
            P.op("v", lambda e, xt=xt, xg_=xg_, c=c: e.tensor_copy(
                    xg_.ap[:, c * 512:(c + 1) * 512].rearrange("p (r m) -> p r m", r=8),
                    xt.ap[:, c * 512:(c + 1) * 512].rearrange("p (m r) -> p r m", r=8)),
                 r=[xt], w=[xg_], dbg=f"cmp{tc}_{c}")
        for c in range(NFC):
            P.dma(C["spillout"], [lambda e, xg_=xg_, c=c, tc=tc: e.dma_start(
                    out=io["xgdram"][c * 128:(c + 1) * 128, :]
                    .rearrange("p (r m) -> p r m", m=CW)[:, :, tc * 64:(tc + 1) * 64],
                    in_=xg_.ap[:, c * 512:(c + 1) * 512].rearrange("p (r m) -> p r m", r=8))],
                  r=[xg_], w=[bxg], dbg=f"xgo{tc}_{c}")
    A.release()
    xg = [A.alloc(f"xg{c}", 128, T, BF16) for c in range(NFC)]
    es = [A.alloc(f"es{c}", 128, T, BF16) for c in range(NFC)]
    P.dma(C["gw"],
          [(lambda e, c=c: e.dma_start(out=xg[c].ap, in_=io["xgdram"][c * 128:(c + 1) * 128, :]))
           for c in range(NFC)],
          r=[bxg], w=xg, dbg="xgcast")

    A.mark()
    A.mark_top()
    w1b = [A.alloc_top(f"w1b{i}", 128, NFC * D, BF16) for i in range(2)]
    w2b = [A.alloc_top(f"w2b{i}", 128, NFC * D, BF16) for i in range(2)]
    hE = [A.alloc(f"hE{i}", 128, CW, BF16) for i in range(NFC)]
    phb = [banks[0], banks[2]]
    pob = [banks[1], banks[3]]
    written = set()
    for ex in range(NE):
        w1 = w1b[ex % 2]
        w2 = w2b[ex % 2]
        P.dma(C["gw"], [
            lambda e, ex=ex, w1=w1: e.dma_start(
                out=w1.ap.rearrange("p (c o) -> p c o", c=NFC),
                in_=io["w1e"][ex].rearrange("(c p) o -> p c o", p=128)),
        ], w=[w1], dbg=f"we1_{ex}")
        P.dma(C["w"], [
            lambda e, ex=ex, w2=w2: e.dma_start(
                out=w2.ap.rearrange("p (c o) -> p c o", c=NFC),
                in_=io["w2e"][ex].rearrange("(c p) o -> p c o", p=128)),
        ], w=[w2], dbg=f"we2_{ex}")
        P.op("v", lambda e, w2=w2: e.tensor_scalar_mul(w2.ap, w2.ap, 0.25), r=[w2], w=[w2],
             dbg=f"wf{ex}")
        rs = [r for r in range(NE) if (3 * r) % NE == ex or (3 * r + 7) % NE == ex]
        for r in rs:
            for hc in range(NFC):
                ph = phb[hc % 2]
                for c in range(NFC):
                    P.op("t", lambda e, hc=hc, c=c, r=r, w1=w1, ph=ph: e.matmul(
                            ph.ap[:, 0:CW], w1.ap[:, (c * NFC + hc) * 128:(c * NFC + hc + 1) * 128],
                            xg[c].ap[:, r * CW:(r + 1) * CW],
                            start=(c == 0), stop=(c == NFC - 1)),
                         r=[w1, xg[c]], w=[ph], dbg=f"e1_{ex}_{r}_{hc}_{c}")
                P.op("s", lambda e, hc=hc, ex=ex, ph=ph: e.activation(
                        hE[hc].ap, ph.ap[:, 0:CW], AF.Gelu, bias=b1c.ap[:, hc * NE + ex: hc * NE + ex + 1]),
                     r=[ph, b1c], w=[hE[hc]], dbg=f"gel{ex}_{r}_{hc}")
            for oc in range(NFC):
                po = pob[oc % 2]
                for hc in range(NFC):
                    P.op("t", lambda e, oc=oc, hc=hc, w2=w2, po=po: e.matmul(
                            po.ap[:, 0:CW], w2.ap[:, (hc * NFC + oc) * 128:(hc * NFC + oc + 1) * 128],
                            hE[hc].ap, start=(hc == 0), stop=(hc == NFC - 1)),
                         r=[w2, hE[hc]], w=[po], dbg=f"e2_{ex}_{r}_{oc}_{hc}")
                if r not in written:
                    P.op("v", lambda e, oc=oc, r=r, po=po: e.tensor_copy(
                            es[oc].ap[:, r * CW:(r + 1) * CW], po.ap[:, 0:CW]),
                         r=[po], w=[es[oc]], dbg=f"es0_{ex}_{r}_{oc}")
                else:
                    P.op("v", lambda e, oc=oc, r=r, po=po: e.tensor_tensor(
                            es[oc].ap[:, r * CW:(r + 1) * CW], po.ap[:, 0:CW],
                            es[oc].ap[:, r * CW:(r + 1) * CW], ALU.add),
                         r=[po, es[oc]], w=[es[oc]], dbg=f"es1_{ex}_{r}_{oc}")
            written.add(r)
    A.release()
    A.release_top()

    A.mark()
    sw1t = [A.alloc(f"s1t{i}", 128, NFC * 128, BF16) for i in range(2)]
    sw3t = [A.alloc(f"s3t{i}", 128, NFC * 128, BF16) for i in range(2)]
    sw2t = [A.alloc("s2t0", 128, HSC * 128, BF16)]
    mbf = [A.alloc(f"m{j}", 128, CW, BF16) for j in range(HSC)]
    abf = A.alloc("abf", 128, CW, BF16)
    xr4 = A.alloc("xr4", 128, NFC * CW, F32)
    zt = A.alloc("zt4", 128, NFC * CW, F32)
    zsq = A.alloc("zsq4", 128, CW, BF16)
    mu = A.alloc("mu4", 1, CW, F32)
    ms = A.alloc("ms4", 1, CW, F32)
    aa = A.alloc("aa4", 1, CW, F32)
    bb2 = A.alloc("bb24", 1, CW, F32)
    p1b, p3b = banks[0], banks[1]
    stA, stB = banks[6], banks[5]
    abank, bbank = banks[7], banks[4]
    for r in range(NE):
        P.dma(C["xin"], [lambda e, r=r: e.dma_start(
                out=xr4.ap.rearrange("p (c t) -> p c t", c=NFC),
                in_=io["xgdram"][:, r * CW:(r + 1) * CW].rearrange("(c p) t -> p c t", p=128))],
              r=[bxg], w=[xr4], dbg=f"xr4_{r}")
        for j in range(HSC):
            s1 = sw1t[j % 2]
            s3 = sw3t[j % 2]
            P.dma(C["w"], [
                lambda e, s1=s1, j=j: e.dma_start(
                    out=s1.ap.rearrange("p (c o) -> p c o", c=NFC),
                    in_=io["sw1b"].rearrange("(c p) o -> p c o", p=128)[:, :, j * 128:(j + 1) * 128]),
                lambda e, s3=s3, j=j: e.dma_start(
                    out=s3.ap.rearrange("p (c o) -> p c o", c=NFC),
                    in_=io["sw3b"].rearrange("(c p) o -> p c o", p=128)[:, :, j * 128:(j + 1) * 128]),
            ], r=[bsw1, bsw3], w=[s1, s3], dbg=f"swt{r}_{j}")
            for c in range(NFC):
                P.op("t", lambda e, s1=s1, c=c, r=r: e.matmul(
                        p1b.ap[:, 0:CW], s1.ap[:, c * 128:(c + 1) * 128], xg[c].ap[:, r * CW:(r + 1) * CW],
                        start=(c == 0), stop=(c == NFC - 1)),
                     r=[s1, xg[c]], w=[p1b], dbg=f"m1_{r}_{j}_{c}")
            for c in range(NFC):
                P.op("t", lambda e, s3=s3, c=c, r=r: e.matmul(
                        p3b.ap[:, 0:CW], s3.ap[:, c * 128:(c + 1) * 128], xg[c].ap[:, r * CW:(r + 1) * CW],
                        start=(c == 0), stop=(c == NFC - 1)),
                     r=[s3, xg[c]], w=[p3b], dbg=f"m3_{r}_{j}_{c}")
            P.op("s", lambda e, j=j: e.activation(abf.ap, p1b.ap[:, 0:CW], AF.Silu,
                                                  bias=sb1.ap[:, j:j + 1]),
                 r=[p1b, sb1], w=[abf], dbg=f"sil{r}_{j}")
            P.op("v", lambda e, j=j: e.tensor_scalar_add(mbf[j].ap, p3b.ap[:, 0:CW],
                                                         sb3.ap[:, j:j + 1]),
                 r=[p3b, sb3], w=[mbf[j]], dbg=f"u3{r}_{j}")
            P.op("v", lambda e, j=j: e.tensor_tensor(mbf[j].ap, abf.ap, mbf[j].ap, ALU.mult),
                 r=[abf, mbf[j]], w=[mbf[j]], dbg=f"mm{r}_{j}")
        for oc in range(NFC):
            s2 = sw2t[0]
            P.dma(C["gw"], [lambda e, s2=s2, oc=oc: e.dma_start(
                    out=s2.ap.rearrange("p (c o) -> p c o", c=HSC),
                    in_=io["sw2b"].rearrange("(c p) o -> p c o", p=128)[:, :, oc * 128:(oc + 1) * 128])],
                  r=[bsw2], w=[s2], dbg=f"sw2t{r}_{oc}")
            pb = banks[2 + (oc % 2)]
            for j in range(HSC):
                P.op("t", lambda e, s2=s2, j=j, pb=pb: e.matmul(
                        pb.ap[:, 0:CW], s2.ap[:, j * 128:(j + 1) * 128], mbf[j].ap,
                        start=(j == 0), stop=(j == HSC - 1)),
                     r=[s2, mbf[j]], w=[pb], dbg=f"m2_{r}_{oc}_{j}")
            P.op("v", lambda e, pb=pb, oc=oc, r=r: e.scalar_tensor_tensor(
                    zt.ap[:, oc * CW:(oc + 1) * CW], pb.ap[:, 0:CW],
                    bcn.ap[:, oc * NE + r: oc * NE + r + 1],
                    es[oc].ap[:, r * CW:(r + 1) * CW], ALU.add, ALU.add),
                 r=[pb, bcn, es[oc]], w=[zt], dbg=f"comb{r}_{oc}")
            P.op("v", lambda e, oc=oc: e.tensor_tensor(
                    zt.ap[:, oc * CW:(oc + 1) * CW], zt.ap[:, oc * CW:(oc + 1) * CW],
                    xr4.ap[:, oc * CW:(oc + 1) * CW], ALU.add),
                 r=[zt, xr4], w=[zt], dbg=f"zadd4_{r}_{oc}")
            P.op("s", lambda e, oc=oc: e.activation(zsq.ap, zt.ap[:, oc * CW:(oc + 1) * CW],
                                                    AF.Square),
                 r=[zt], w=[zsq], dbg=f"zsq4_{r}_{oc}")
            P.op("t", lambda e, oc=oc: e.matmul(stB.ap[0:1, 0:CW], X["onesb"].ap[:, 0:1], zsq.ap,
                                                start=(oc == 0), stop=(oc == NFC - 1)),
                 r=[zsq, X["onesb"]], w=[stB], dbg=f"s2st4_{r}_{oc}")
            P.op("t", lambda e, oc=oc: e.matmul(stA.ap[0:1, 0:CW], X["onesfc"].ap,
                                                zt.ap[:, oc * CW:(oc + 1) * CW],
                                                start=(oc == 0), stop=(oc == NFC - 1)),
                 r=[zt, X["onesfc"]], w=[stA], dbg=f"s1st4_{r}_{oc}")
        P.op("v", lambda e: e.tensor_scalar_mul(mu.ap, stA.ap[0:1, 0:CW], 1.0 / D), r=[stA], w=[mu])
        P.op("v", lambda e: e.tensor_scalar_mul(ms.ap, stB.ap[0:1, 0:CW], 1.0 / D), r=[stB], w=[ms])
        P.op("v", lambda e: e.tensor_tensor(bb2.ap, mu.ap, mu.ap, ALU.mult), r=[mu], w=[bb2])
        P.op("v", lambda e: e.tensor_tensor(ms.ap, ms.ap, bb2.ap, ALU.subtract), r=[ms, bb2], w=[ms])
        P.op("s", lambda e: e.activation(ms.ap, ms.ap, AF.Sqrt, bias=X["eps"].ap[0:1, :]),
             r=[ms, X["eps"]], w=[ms])
        P.op("v", lambda e: e.reciprocal(aa.ap, ms.ap), r=[ms], w=[aa])
        P.op("v", lambda e: e.tensor_tensor(bb2.ap, mu.ap, aa.ap, ALU.mult), r=[mu, aa], w=[bb2])
        P.op("v", lambda e: e.tensor_scalar_mul(bb2.ap, bb2.ap, -1.0), r=[bb2], w=[bb2])
        P.op("t", lambda e: e.matmul(abank.ap[:, 0:CW], X["onesf"].ap[0:1, :], aa.ap, start=True, stop=True),
             r=[aa, X["onesf"]], w=[abank], dbg=f"ab4{r}")
        P.op("t", lambda e: e.matmul(bbank.ap[:, 0:CW], X["onesf"].ap[0:1, :], bb2.ap, start=True, stop=True),
             r=[bb2, X["onesf"]], w=[bbank], dbg=f"bb4{r}")
        for oc in range(NFC):
            P.op("v", lambda e, oc=oc: e.tensor_tensor(
                    zt.ap[:, oc * CW:(oc + 1) * CW], zt.ap[:, oc * CW:(oc + 1) * CW],
                    abank.ap[:, 0:CW], ALU.mult),
                 r=[zt, abank], w=[zt], dbg=f"ln4m{r}_{oc}")
            P.op("v", lambda e, oc=oc: e.tensor_tensor(
                    zt.ap[:, oc * CW:(oc + 1) * CW], zt.ap[:, oc * CW:(oc + 1) * CW],
                    bbank.ap[:, 0:CW], ALU.add),
                 r=[zt, bbank], w=[zt], dbg=f"ln4a{r}_{oc}")
        P.dma(C["xout"], [lambda e, r=r: e.dma_start(
                out=xout[:, r * CW:(r + 1) * CW].rearrange("(c p) t -> p c t", p=128),
                in_=zt.ap.rearrange("p (c t) -> p c t", c=NFC))],
              r=[zt], w=[bxout], dbg=f"xout4{r}")
    A.release()
    A.release()
    return bxout


def make_chans(P):
    return {
        "const": P.chan("const", "g", 2),
        "w": P.chan("w", "g", 2),
        "xin": P.chan("xin", "q", 2),
        "spillout": P.chan("spout", "q", 2),
        "qin": P.chan("qin", "q", 4),
        "kin": P.chan("kin", "q", 2),
        "aoin": P.chan("aoin", "q", 2),
        "xout": P.chan("xout", "q", 2),
        "gw": P.chan("gw", "g", 4),
        "sdma": P.chan("sdma", "s", 4),
        "sc": P.chan("sc", "g", 2),
    }

CONST_SPECS = {
    "c_ident": ((128, 128), BF16),
    "c_onesb": ((128, 128), BF16),
    "c_onesf": ((1, 128), F32),
    "c_onesfc": ((128, 1), F32),
    "c_eps": ((128, 1), F32),
    "c_identf": ((128, 128), F32),
    "c_e2": ((128, 128), BF16),
    "c_e3": ((128, 64), BF16),
}

def const_arrays():
    return {
        "c_ident": np.eye(128, dtype=np.float32).astype(BF),
        "c_onesb": np.ones((128, 128), dtype=np.float32).astype(BF),
        "c_onesf": np.ones((1, 128), dtype=np.float32),
        "c_onesfc": np.ones((128, 1), dtype=np.float32),
        "c_eps": np.full((128, 1), 1e-5, dtype=np.float32),
        "c_identf": np.eye(128, dtype=np.float32),
        "c_e2": _mk_e2(),
        "c_e3": _mk_e3(),
    }

def _mk_e2():
    e2 = np.zeros((128, 8, 16), np.float32)
    for c in range(8):
        for p in range(128):
            e2[p, c, 2 * c + p // 64] = 1.0
    return e2.reshape(128, 128).astype(BF)

def _mk_e3():
    e3 = np.zeros((128, 64), np.float32)
    for p in range(128):
        e3[p, p % 64] = 1.0
    return e3.astype(BF)


def build_program(io_spec, phase_fns):
    nc = bass.Bass("TRN2", target_bir_lowering=False)
    io = {}
    for name, (shape, dt, kind) in io_spec.items():
        io[name] = nc.dram_tensor(name, list(shape), dt, kind=kind).ap()
    ctxs = []
    sbcm = nc.sbuf_tensor("arena", [128, SB_BYTES // 4], F32)
    sbh = sbcm.__enter__(); ctxs.append(sbcm)
    cmp_ = nc.psum_tensor("psarena", [128, 4096], F32)
    ph_ = cmp_.__enter__(); ctxs.append(cmp_)
    psall = ph_[:]
    banks = [Buf(f"bank{i}", psall[:, i * 512:(i + 1) * 512]) for i in range(8)]
    banks.append(Buf("psall", psall))

    P = Prog(nc)
    A = Arena(sbh[:])
    C = make_chans(P)
    X = setup_consts(P, A, io, C)
    outs = []
    for fn in phase_fns:
        ob = fn(P, A, io, C, X, banks)
        if ob is not None:
            outs.append(ob)
    P.op("q", lambda e: e.nop(), r=outs, dbg="final_wait")
    P.finalize()
    P.replay()
    for h in reversed(ctxs):
        h.__exit__(None, None, None)
    P.close()
    return nc

def prep_host(inputs):
    c = const_arrays()
    w = {}
    asc = np.ascontiguousarray
    w["w_qkv"] = inputs["L0_attn_w_qkv"]
    w["w_out"] = inputs["L0_attn_w_out"]
    w["gamma"] = asc(inputs["L0_attn_gamma"].reshape(NFC, 128).T)
    w["wqT"] = asc(inputs["L1_attn_Wq"].T)
    w["wkT"] = asc(inputs["L1_attn_Wk"].T)
    w["wvT"] = asc(inputs["L1_attn_Wv"].T)
    w["wfcT"] = asc(inputs["L1_attn_Wfc"].T)
    w["wg"] = inputs["L0_ffn_w_gate"]
    w["wu"] = inputs["L0_ffn_w_up"]
    w["wd"] = inputs["L0_ffn_w_down"]
    w["sw1"] = inputs["L1_ffn_sw1"]
    w["sw3"] = inputs["L1_ffn_sw3"]
    w["sw2"] = inputs["L1_ffn_sw2"]
    w["sb1r"] = asc(inputs["L1_ffn_sb1"].reshape(HS // 128, 128).T)
    w["sb3r"] = asc(inputs["L1_ffn_sb3"].reshape(HS // 128, 128).T)
    w["sb2r"] = asc(inputs["L1_ffn_sb2"].reshape(NFC, 128).T)
    w["w1e"] = inputs["L1_ffn_w1"]
    w["w2e"] = inputs["L1_ffn_w2"]
    w["b1r"] = asc(inputs["L1_ffn_b1"].T.reshape(NFC, 128, NE).transpose(1, 0, 2))
    w["b2r"] = asc(inputs["L1_ffn_b2"].T.reshape(NFC, 128, NE).transpose(1, 0, 2))
    _ = inputs["L1_ffn_t_emb"].shape
    xs = [asc(inputs["L0_attn_x"][b].T) for b in range(8)]
    return c, w, xs

def unpermute_tokens(yT):
    yt = yT.T
    return np.ascontiguousarray(yt.reshape(NE, T // NE, D).transpose(1, 0, 2).reshape(T, D))



def _spec_p1():
    s = {
        "xin": ((D, T), F32, "ExternalInput"),
        "xout": ((D, T), F32, "ExternalOutput"),
        "w_qkv": ((D, 3 * D), F32, "ExternalInput"),
        "w_out": ((D, D), F32, "ExternalInput"),
        "gamma": ((128, NFC), F32, "ExternalInput"),
        "qspill": ((D, T), BF16, "Internal"),
        "kspill": ((D, T), BF16, "Internal"),
        "aospill": ((D, T), BF16, "Internal"),
    }
    return s, ["w_qkv", "w_out", "gamma"]

def _spec_p2():
    s = {
        "xin": ((D, T), F32, "ExternalInput"),
        "xout": ((D, T), F32, "ExternalOutput"),
        "wg": ((FF, D), F32, "ExternalInput"),
        "wu": ((FF, D), F32, "ExternalInput"),
        "wd": ((D, FF), F32, "ExternalInput"),
        "wgq": ((D, FF), BF16, "Internal"),
        "wuq": ((D, FF), BF16, "Internal"),
        "wdq": ((FF, D), BF16, "Internal"),
        "scx_s": ((64, 512), F32, "Internal"),
        "scx_r": ((64, 512), F32, "Internal"),
        "sch_s0": ((176, 512), F32, "Internal"),
        "sch_r0": ((176, 512), F32, "Internal"),
        "sch_s1": ((176, 512), F32, "Internal"),
        "sch_r1": ((176, 512), F32, "Internal"),
    }
    return s, ["wg", "wu", "wd"]

def _spec_p3():
    s = {
        "xin": ((D, T), F32, "ExternalInput"),
        "xout": ((D, T), F32, "ExternalOutput"),
        "wqT": ((D, D), F32, "ExternalInput"),
        "wkT": ((D, D), F32, "ExternalInput"),
        "wvT": ((D, D), F32, "ExternalInput"),
        "wfcT": ((D, D), F32, "ExternalInput"),
        "kspill": ((D, T), BF16, "Internal"),
        "vspill": ((D, T), BF16, "Internal"),
        "aospill": ((D, T), BF16, "Internal"),
        "scdram": ((16, NH * T), F32, "Internal"),
        "adram": ((16, NH * T), BF16, "Internal"),
    }
    return s, ["wqT", "wkT", "wvT", "wfcT"]

def _spec_p4():
    s = {
        "xin": ((D, T), F32, "ExternalInput"),
        "xout": ((D, T), F32, "ExternalOutput"),
        "sw1": ((D, HS), F32, "ExternalInput"),
        "sw3": ((D, HS), F32, "ExternalInput"),
        "sw2": ((HS, D), F32, "ExternalInput"),
        "w1e": ((NE, D, D), F32, "ExternalInput"),
        "w2e": ((NE, D, D), F32, "ExternalInput"),
        "sb1r": ((128, HS // 128), F32, "ExternalInput"),
        "sb3r": ((128, HS // 128), F32, "ExternalInput"),
        "sb2r": ((128, NFC), F32, "ExternalInput"),
        "b1r": ((128, NFC, NE), F32, "ExternalInput"),
        "b2r": ((128, NFC, NE), F32, "ExternalInput"),
        "xgdram": ((D, T), F32, "Internal"),
        "sw1b": ((D, HS), BF16, "Internal"),
        "sw3b": ((D, HS), BF16, "Internal"),
        "sw2b": ((HS, D), BF16, "Internal"),
    }
    return s, ["sw1", "sw3", "sw2", "w1e", "w2e", "sb1r", "sb3r", "sb2r", "b1r", "b2r"]

_PHASES = [
    (_spec_p1, lambda P, A, io, C, X, banks: build_attn(P, A, io, C, X, banks, 0)),
    (_spec_p2, build_l0ffn),
    (_spec_p3, build_l1attn),
    (_spec_p4, build_l1ffn),
]

_prog_cache = {}

def _spec_fused():
    s = {
        "xin": ((D, T), F32, "ExternalInput"),
        "x1": ((D, T), F32, "Internal"),
        "x2": ((D, T), F32, "Internal"),
        "x3": ((D, T), F32, "Internal"),
        "xout": ((D, T), F32, "ExternalOutput"),
    }
    wall = []
    for i, (spec_fn, _) in enumerate(_PHASES):
        sp, wk = spec_fn()
        wall += wk
        for name, v in sp.items():
            if name in ("xin", "xout"):
                continue
            key = name if name not in s else f"{name}_p{i}"
            assert key not in s or s[key] == v, key
            s[key] = v
    for nm in ("kspill", "aospill"):
        s[nm + "3"] = ((D, T), BF16, "Internal")
    return s, wall

def _phase_io(io, i):
    xs = ["xin", "x1", "x2", "x3", "xout"]
    m = dict(io)
    m["xin"] = io[xs[i]]
    m["xout"] = io[xs[i + 1]]
    if i == 2:
        m["kspill"] = io["kspill3"]
        m["aospill"] = io["aospill3"]
    return m

def build_fused():
    spec, _ = _spec_fused()
    for n, (s, dt) in CONST_SPECS.items():
        spec[n] = (s, dt, "ExternalInput")
    def mk(i):
        def fn(P, A, io, C, X, banks):
            return _PHASES[i][1](P, A, _phase_io(io, i), C, X, banks)
        return fn
    return build_program(spec, [mk(i) for i in range(4)])

def _get_prog(i):
    if i not in _prog_cache:
        spec_fn, build_fn = _PHASES[i]
        spec, _ = spec_fn()
        for n, (s, dt) in CONST_SPECS.items():
            spec[n] = (s, dt, "ExternalInput")
        _prog_cache[i] = build_program(spec, [build_fn])
    return _prog_cache[i]

FUSED = True

def kernel(**inputs):
    consts, w, xs = prep_host(inputs)
    if FUSED:
        if "fused" not in _prog_cache:
            _prog_cache["fused"] = build_fused()
        nc = _prog_cache["fused"]
        _, wkeys = _spec_fused()
        in_maps = []
        for b in range(8):
            m = dict(consts)
            m["xin"] = xs[b]
            for k in wkeys:
                m[k] = w[k]
            in_maps.append(m)
        res = run_bass_kernel_spmd(nc, in_maps, core_ids=list(range(8)))
        cur = [np.ascontiguousarray(res.results[b]["xout"]) for b in range(8)]
    else:
        cur = xs
        for i in range(4):
            nc = _get_prog(i)
            _, wkeys = _PHASES[i][0]()
            in_maps = []
            for b in range(8):
                m = dict(consts)
                m["xin"] = cur[b]
                for k in wkeys:
                    m[k] = w[k]
                in_maps.append(m)
            res = run_bass_kernel_spmd(nc, in_maps, core_ids=list(range(8)))
            cur = [np.ascontiguousarray(res.results[b]["xout"]) for b in range(8)]
    out = np.stack([unpermute_tokens(cur[b]) for b in range(8)], axis=0)
    return out.astype(np.float32)


# version 22
# speedup vs baseline: 1.1366x; 1.1366x over previous
import numpy as np
import ml_dtypes
import concourse.bass as bass
import concourse.mybir as mybir
from concourse.bass_utils import run_bass_kernel_spmd

F32 = mybir.dt.float32
BF16 = mybir.dt.bfloat16
U8 = mybir.dt.uint8
AF = mybir.ActivationFunctionType
ALU = mybir.AluOpType
AX = mybir.AxisListType
BF = ml_dtypes.bfloat16

T = 4096
D = 1024
NH = 16
HD = 64
FF = 2816
HS = 2048
NE = 8
NFC = D // 128
NTC = T // 512
KC = T // 128
CW = T // NE

def set_T(t):
    global T, NTC, KC, CW
    T = t
    NTC = T // 512
    KC = T // 128
    CW = T // NE
SB_BYTES = 203 * 1024

ENGS = ("t", "v", "s", "g", "q")
ENGMAP = {"t": "tensor", "v": "vector", "s": "scalar", "g": "gpsimd", "q": "sync"}


class Buf:
    __slots__ = ("name", "ap", "aliases", "rng")
    def __init__(self, name, ap=None):
        self.name = name
        self.ap = ap
        self.aliases = []
        self.rng = None

class Op:
    __slots__ = ("eng", "fn", "seq", "waits", "inc_needed", "num", "sem", "incby",
                 "chan", "slot", "dbg", "blast")
    def __init__(self, eng, fn, seq, dbg=""):
        self.eng = eng
        self.fn = fn
        self.seq = seq
        self.waits = []
        self.inc_needed = False
        self.num = None
        self.sem = None
        self.incby = 1
        self.chan = None
        self.slot = None
        self.dbg = dbg
        self.blast = None

class Chan:
    def __init__(self, prog, name, queue, nslots=2):
        self.p = prog
        self.name = name
        self.queue = queue
        self.nslots = nslots
        self.sems = [prog.new_sem(f"{name}{i}") for i in range(nslots)]
        self.count = [0] * nslots
        self.batch = 0
        self.waiters = [[] for _ in range(nslots)]

class Arena:
    def __init__(self, base_ap_f32):
        self.base = base_ap_f32
        self.off = 0
        self.marks = []
        self.top = SB_BYTES
        self.tmarks = []
        self.hi = 0
        self.live = []
        self.freed = []
    def alloc(self, name, rows, cols, dt):
        esz = 4 if dt == F32 else (1 if dt == U8 else 2)
        off = (self.off + 63) & ~63
        nbytes = cols * esz
        assert off + nbytes <= self.top, f"SBUF arena overflow at {name}: {(off+nbytes)/1024:.1f}KB (top {self.top/1024:.1f}KB)"
        self.off = off + nbytes
        self.hi = max(self.hi, self.off)
        ap = self.base.bitcast(dt)[0:rows, off // esz: off // esz + cols]
        b = Buf(name, ap)
        b.rng = (off, off + nbytes)
        b.aliases = [f for f in self.freed if f.rng[0] < b.rng[1] and b.rng[0] < f.rng[1]]
        self.live.append(b)
        return b
    def alloc_top(self, name, rows, cols, dt):
        esz = 4 if dt == F32 else (1 if dt == U8 else 2)
        nbytes = cols * esz
        top = (self.top - nbytes) & ~63
        assert top >= self.off, f"SBUF arena overflow (top) at {name}"
        self.top = top
        ap = self.base.bitcast(dt)[0:rows, top // esz: top // esz + cols]
        b = Buf(name, ap)
        b.rng = (top, top + nbytes)
        b.aliases = [f for f in self.freed if f.rng[0] < b.rng[1] and b.rng[0] < f.rng[1]]
        self.live.append(b)
        return b
    def mark_top(self):
        self.tmarks.append(self.top)
    def release_top(self):
        hi = self.tmarks.pop()
        still = []
        for b in self.live:
            if b.rng[0] >= self.top and b.rng[1] <= hi:
                self.freed.append(b)
            else:
                still.append(b)
        self.live = still
        self.top = hi
    def mark(self):
        self.marks.append(self.off)
    def release(self):
        lo = self.marks.pop()
        still = []
        for b in self.live:
            if b.rng[0] >= lo:
                self.freed.append(b)
            else:
                still.append(b)
        self.live = still
        self.freed = [f for f in self.freed if f.rng[1] > 0]
        self.off = lo

class Prog:
    def __init__(self, nc):
        self.nc = nc
        self.E = {e: [] for e in ENGS}
        self.seq = 0
        self.state = {}
        self.sem_ctx = []
        self.esem = {}
        for e in ENGS:
            self.esem[e] = self.new_sem(f"eng_{e}")
    def new_sem(self, name):
        h = self.nc.semaphore(name=f"sm_{name}")
        sem = h.__enter__()
        self.sem_ctx.append(h)
        return sem
    def chan(self, name, queue, nslots=2):
        return Chan(self, name, queue, nslots)
    def _dep(self, o, producers):
        for p in producers:
            if p is None or p is o:
                continue
            if p.chan is not None and p.blast is not None and p.blast is not p:
                p = p.blast
                if p is o:
                    continue
            if p.eng == o.eng and p.eng == "t":
                continue
            if p not in o.waits:
                o.waits.append(p)
                p.inc_needed = True
                if p.chan is not None:
                    p.chan.waiters[p.slot].append(o)
    def _track(self, o, r, w):
        for b in w:
            if b.aliases and id(b) not in self.state:
                for ab in b.aliases:
                    st = self.state.get(id(ab))
                    if st:
                        self._dep(o, [st["w"]])
                        self._dep(o, st["r"])
                b.aliases = []
        for b in r:
            st = self.state.get(id(b))
            if st:
                self._dep(o, [st["w"]])
        for b in w:
            st = self.state.get(id(b))
            if st:
                self._dep(o, [st["w"]])
                self._dep(o, st["r"])
        for b in r:
            self.state.setdefault(id(b), {"w": None, "r": []})["r"].append(o)
        for b in w:
            self.state[id(b)] = {"w": o, "r": []}
    def op(self, eng, fn, r=(), w=(), dbg=""):
        o = Op(eng, fn, self.seq, dbg); self.seq += 1
        self._track(o, r, w)
        self.E[eng].append(o)
        return o
    def dma(self, chan, fns, r=(), w=(), dbg=""):
        s = chan.batch % chan.nslots
        gate = chan.waiters[s]
        chan.waiters[s] = []
        chan.batch += 1
        sem = chan.sems[s]
        ops = []
        for i, fn in enumerate(fns):
            o = Op(chan.queue, fn, self.seq, dbg); self.seq += 1
            o.sem = sem
            o.incby = 16
            chan.count[s] += 1
            o.num = chan.count[s]
            o.chan = chan
            o.slot = s
            if i == 0:
                for g0 in gate:
                    self._dep(o, [g0])
            ops.append(o)
            self.E[chan.queue].append(o)
        for o in ops:
            o.blast = ops[-1]
        o0, oL = ops[0], ops[-1]
        for b in w:
            if b.aliases and id(b) not in self.state:
                for ab in b.aliases:
                    st = self.state.get(id(ab))
                    if st:
                        self._dep(o0, [st["w"]])
                        self._dep(o0, st["r"])
                b.aliases = []
        for b in r:
            st = self.state.get(id(b))
            if st:
                self._dep(o0, [st["w"]])
        for b in w:
            st = self.state.get(id(b))
            if st:
                self._dep(o0, [st["w"]])
                self._dep(o0, st["r"])
        for b in r:
            self.state.setdefault(id(b), {"w": None, "r": []})["r"].append(oL)
        for b in w:
            self.state[id(b)] = {"w": oL, "r": []}
        return oL
    def finalize(self):
        for e in ("t", "v", "s", "g", "q"):
            n = 0
            for o in self.E[e]:
                if o.sem is None and o.inc_needed:
                    n += 1
                    o.num = n
                    o.sem = self.esem[e]
                    o.incby = 1
    def replay(self):
        nc = self.nc
        with nc.Block() as blk:
            for e in ENGS:
                ops = self.E[e]
                def body(eng, ops=ops):
                    waited = {}
                    for o in ops:
                        for p in o.waits:
                            tgt = p.num * p.incby
                            key = id(p.sem)
                            if waited.get(key, -1) >= tgt:
                                continue
                            waited[key] = tgt
                            eng.wait_ge(p.sem, tgt)
                        inst = o.fn(eng)
                        if o.sem is not None and (o.incby == 16 or o.inc_needed):
                            inst.then_inc(o.sem, o.incby)
                getattr(blk, ENGMAP[e])(body)
    def close(self):
        for h in reversed(self.sem_ctx):
            h.__exit__(None, None, None)



def setup_consts(P, A, io, C):
    X = {}
    X["ident"] = A.alloc("ident", 128, 128, BF16)
    X["onesb"] = A.alloc("onesb", 128, 128, BF16)
    X["onesf"] = A.alloc("onesf", 1, 128, F32)
    X["onesfc"] = A.alloc("onesfc", 128, 1, F32)
    X["eps"] = A.alloc("eps", 128, 1, F32)
    X["identf"] = A.alloc("identf", 128, 128, F32)
    X["e2"] = A.alloc("e2", 128, 128, BF16)
    X["e3"] = A.alloc("e3", 128, 64, BF16)
    ch = C["const"]
    P.dma(ch, [
        lambda e: e.dma_start(out=X["ident"].ap, in_=io["c_ident"]),
        lambda e: e.dma_start(out=X["onesb"].ap, in_=io["c_onesb"]),
        lambda e: e.dma_start(out=X["onesf"].ap, in_=io["c_onesf"]),
        lambda e: e.dma_start(out=X["onesfc"].ap, in_=io["c_onesfc"]),
        lambda e: e.dma_start(out=X["eps"].ap, in_=io["c_eps"]),
        lambda e: e.dma_start(out=X["identf"].ap, in_=io["c_identf"]),
        lambda e: e.dma_start(out=X["e2"].ap, in_=io["c_e2"]),
        lambda e: e.dma_start(out=X["e3"].ap, in_=io["c_e3"]),
    ], w=[X["ident"], X["onesb"], X["onesf"], X["onesfc"], X["eps"], X["identf"], X["e2"], X["e3"]], dbg="consts")
    return X


def emit_outproj_ln(P, A, io, C, X, banks, wname, xin, xout, aospill, bas, bxout):
    A.mark()
    wo = A.alloc("wo", 128, NFC * NFC * 128, BF16)
    P.dma(C["w"], [lambda e: e.dma_start(
            out=wo.ap.rearrange("p (c j o) -> p c j o", c=NFC, j=NFC),
            in_=io[wname].rearrange("(c p) (j o) -> p c j o", p=128, j=NFC))],
          w=[wo], dbg="wo")
    ao_in = [A.alloc(f"aoin{i}", 128, NFC * 512, BF16) for i in range(2)]
    xr_in = [A.alloc(f"xrin{i}", 128, NFC * 512, F32) for i in range(2)]
    zt = A.alloc("zt", 128, NFC * 512, F32)
    zsq = A.alloc("zsq", 128, 512, BF16)
    mu = A.alloc("mu", 1, 512, F32)
    ms = A.alloc("ms", 1, 512, F32)
    aa = A.alloc("aa", 1, 512, F32)
    bb2 = A.alloc("bb2", 1, 512, F32)
    yst = [A.alloc(f"yst{i}", 128, NFC * 512, F32) for i in range(2)]
    stA, stB = banks[6], banks[5]
    abank, bbank = banks[7], banks[3]

    for tc in range(NTC):
        sl = tc % 2
        ai = ao_in[sl]
        xr = xr_in[sl]
        P.dma(C["aoin"], [lambda e, ai=ai, tc=tc: e.dma_start(
                out=ai.ap.rearrange("p (c t) -> p c t", c=NFC),
                in_=aospill[:, tc * 512:(tc + 1) * 512].rearrange("(c p) t -> p c t", p=128))],
              r=[bas], w=[ai], dbg=f"aoin{tc}")
        P.dma(C["xin"], [lambda e, xr=xr, tc=tc: e.dma_start(
                out=xr.ap.rearrange("p (c t) -> p c t", c=NFC),
                in_=xin[:, tc * 512:(tc + 1) * 512].rearrange("(c p) t -> p c t", p=128))],
              w=[xr], dbg=f"xrin{tc}")
        for j in range(NFC):
            pb = banks[(j % 2)]
            for c in range(NFC):
                P.op("t", lambda e, j=j, c=c, ai=ai, pb=pb: e.matmul(
                        pb.ap, wo.ap[:, (c * NFC + j) * 128:(c * NFC + j + 1) * 128],
                        ai.ap[:, c * 512:(c + 1) * 512], start=(c == 0), stop=(c == NFC - 1)),
                     r=[wo, ai], w=[pb], dbg=f"wout{tc}_{j}_{c}")
            P.op("v", lambda e, pb=pb, xr=xr, j=j: e.tensor_tensor(
                    zt.ap[:, j * 512:(j + 1) * 512], pb.ap, xr.ap[:, j * 512:(j + 1) * 512], ALU.add),
                 r=[pb, xr], w=[zt], dbg=f"zadd{tc}_{j}")
            P.op("s", lambda e, j=j: e.activation(zsq.ap, zt.ap[:, j * 512:(j + 1) * 512], AF.Square),
                 r=[zt], w=[zsq], dbg=f"zsq{tc}_{j}")
            P.op("t", lambda e, j=j: e.matmul(stB.ap[0:1, :], X["onesb"].ap[:, 0:1], zsq.ap,
                                              start=(j == 0), stop=(j == NFC - 1)),
                 r=[zsq, X["onesb"]], w=[stB], dbg=f"s2st{tc}_{j}")
            P.op("t", lambda e, j=j: e.matmul(stA.ap[0:1, :], X["onesfc"].ap,
                                              zt.ap[:, j * 512:(j + 1) * 512],
                                              start=(j == 0), stop=(j == NFC - 1)),
                 r=[zt, X["onesfc"]], w=[stA], dbg=f"s1st{tc}_{j}")
        P.op("v", lambda e: e.tensor_scalar_mul(mu.ap, stA.ap[0:1, :], 1.0 / D), r=[stA], w=[mu])
        P.op("v", lambda e: e.tensor_scalar_mul(ms.ap, stB.ap[0:1, :], 1.0 / D), r=[stB], w=[ms])
        P.op("v", lambda e: e.tensor_tensor(bb2.ap, mu.ap, mu.ap, ALU.mult), r=[mu], w=[bb2])
        P.op("v", lambda e: e.tensor_tensor(ms.ap, ms.ap, bb2.ap, ALU.subtract),
             r=[ms, bb2], w=[ms], dbg="var")
        P.op("s", lambda e: e.activation(ms.ap, ms.ap, AF.Sqrt, bias=X["eps"].ap[0:1, :]), r=[ms, X["eps"]], w=[ms])
        P.op("v", lambda e: e.reciprocal(aa.ap, ms.ap), r=[ms], w=[aa])
        P.op("v", lambda e: e.tensor_tensor(bb2.ap, mu.ap, aa.ap, ALU.mult), r=[mu, aa], w=[bb2])
        P.op("v", lambda e: e.tensor_scalar_mul(bb2.ap, bb2.ap, -1.0), r=[bb2], w=[bb2])
        P.op("t", lambda e: e.matmul(abank.ap, X["onesf"].ap[0:1, :], aa.ap, start=True, stop=True),
             r=[aa, X["onesf"]], w=[abank], dbg=f"abc{tc}")
        P.op("t", lambda e: e.matmul(bbank.ap, X["onesf"].ap[0:1, :], bb2.ap, start=True, stop=True),
             r=[bb2, X["onesf"]], w=[bbank], dbg=f"bbc{tc}")
        ys = yst[sl]
        for j in range(NFC):
            P.op("v", lambda e, j=j, ys=ys: e.tensor_tensor(
                    ys.ap[:, j * 512:(j + 1) * 512], zt.ap[:, j * 512:(j + 1) * 512],
                    abank.ap, ALU.mult),
                 r=[zt, abank], w=[ys], dbg=f"lnm{tc}_{j}")
            P.op("v", lambda e, j=j, ys=ys: e.tensor_tensor(
                    ys.ap[:, j * 512:(j + 1) * 512], ys.ap[:, j * 512:(j + 1) * 512],
                    bbank.ap, ALU.add),
                 r=[ys, bbank], w=[ys], dbg=f"lna{tc}_{j}")
        P.dma(C["xout"], [lambda e, ys=ys, tc=tc: e.dma_start(
                out=xout[:, tc * 512:(tc + 1) * 512].rearrange("(c p) t -> p c t", p=128),
                in_=ys.ap.rearrange("p (c t) -> p c t", c=NFC))],
              r=[ys], w=[bxout], dbg=f"xout{tc}")
    A.release()
    return bxout


def build_attn(P, A, io, C, X, banks, layer):
    nc = P.nc
    xin, xout = io["xin"], io["xout"]
    qspill, kspill, aospill = io["qspill"], io["kspill"], io["aospill"]
    bqs, bks, bas = Buf("qspill"), Buf("kspill"), Buf("aospill")
    bxout = Buf("xout")

    A.mark()
    negBB = A.alloc("negBB", 128, NH * NTC, F32)
    g32 = A.alloc("g32", 128, NFC, F32)
    gsb = A.alloc("gsb", 128, NFC, F32)
    qn2m = A.alloc("qn2m", 1, NH * NTC, F32)
    kn2m = A.alloc("kn2m", 1, NH * NTC, F32)
    kn2h = A.alloc("kn2h", 1, NH, F32)
    negB = A.alloc("negB", 1, NH * NTC, F32)
    A.mark()
    vP = A.alloc("vP", 128, NH * KC * 65, BF16)
    A.mark()
    xnb = [A.alloc(f"xn{c}", 128, T, BF16) for c in range(NFC)]

    P.op("v", lambda e: e.memset(vP.ap, 1.0), w=[vP], dbg="vp_ones")
    if layer == 0:
        P.dma(C["const"], [lambda e: e.dma_start(out=gsb.ap, in_=io["gamma"])], w=[gsb])
        P.op("v", lambda e: e.tensor_scalar(g32.ap, gsb.ap, 1.0, 32.0, ALU.add, ALU.mult),
             r=[gsb], w=[g32], dbg="g32")

    A.mark()
    xin_t = [A.alloc(f"xint{i}", 128, NFC * 512, F32) for i in range(2)]
    xsq = [A.alloc(f"xsq{i}", 128, NFC * 512, BF16) for i in range(2)]
    snt = A.alloc("snt", 1, 512, F32)
    stt = A.alloc("stt", 1, 512, F32)
    stat_bank = banks[6]
    bc_bank = banks[7]
    for tc in range(NTC):
        sl = tc % 2
        xt = xin_t[sl]
        src = xin[:, tc * 512:(tc + 1) * 512].rearrange("(c p) t -> p c t", p=128)
        P.dma(C["xin"], [lambda e, xt=xt, src=src: e.dma_start(
                out=xt.ap.rearrange("p (c t) -> p c t", c=NFC), in_=src)],
              w=[xt], dbg=f"xin{tc}")
        if layer == 0:
            xq = xsq[sl]
            P.op("s", lambda e, xt=xt, xq=xq: e.activation(xq.ap, xt.ap, AF.Square),
                 r=[xt], w=[xq], dbg=f"xsq{tc}")
            for c in range(NFC):
                P.op("t", lambda e, xq=xq, c=c: e.matmul(
                        stat_bank.ap[0:1, :], X["onesb"].ap[:, 0:1],
                        xq.ap[:, c * 512:(c + 1) * 512], start=(c == 0), stop=(c == NFC - 1)),
                     r=[xq, X["onesb"]], w=[stat_bank], dbg=f"stat{tc}_{c}")
            P.op("v", lambda e: e.tensor_copy(snt.ap, stat_bank.ap[0:1, :]),
                 r=[stat_bank], w=[snt], dbg=f"snev{tc}")
            P.op("s", lambda e: e.activation(snt.ap, snt.ap, AF.Sqrt),
                 r=[snt], w=[snt], dbg=f"sqrt{tc}")
            P.op("v", lambda e: e.tensor_scalar_max(snt.ap, snt.ap, 1e-12),
                 r=[snt], w=[snt])
            P.op("v", lambda e: e.reciprocal(stt.ap, snt.ap),
                 r=[snt], w=[stt], dbg=f"recip{tc}")
            P.op("t", lambda e: e.matmul(bc_bank.ap, X["onesf"].ap[0:1, :],
                                         stt.ap, start=True, stop=True),
                 r=[stt, X["onesf"]], w=[bc_bank], dbg=f"stokB{tc}")
            for c in range(NFC):
                P.op("v", lambda e, xt=xt, c=c, tc=tc: e.tensor_tensor(
                        xnb[c].ap[:, tc * 512:(tc + 1) * 512],
                        xt.ap[:, c * 512:(c + 1) * 512], bc_bank.ap, ALU.mult),
                     r=[xt, bc_bank], w=[xnb[c]], dbg=f"xn{tc}_{c}")
        else:
            for c in range(NFC):
                P.op("v", lambda e, xt=xt, c=c, tc=tc: e.tensor_copy(
                        xnb[c].ap[:, tc * 512:(tc + 1) * 512], xt.ap[:, c * 512:(c + 1) * 512]),
                     r=[xt], w=[xnb[c]], dbg=f"xcast{tc}_{c}")
    A.release()

    A.mark()
    wqb = [A.alloc(f"wq{i}", 128, NFC * 128, BF16) for i in range(2)]
    qstage = [A.alloc(f"qst{i}", 128, 512, BF16) for i in range(2)]
    sqstage = [A.alloc(f"sqst{i}", 128, 512, BF16) for i in range(2)]
    vstage = [A.alloc(f"vst{i}", 128, T, BF16) for i in range(2)]
    n2bank = banks[5]

    def w_src(j):
        if layer == 0:
            return io["w_qkv"].rearrange("(c p) o -> p c o", p=128)[:, :, j * 128:(j + 1) * 128]
        kind, jj = divmod(j, NFC)
        wsrc = io[["wqT", "wkT", "wvT"][kind]]
        return wsrc.rearrange("(c p) o -> p c o", p=128)[:, :, jj * 128:(jj + 1) * 128]

    for j in range(3 * NFC):
        sl = j % 2
        wq = wqb[sl]
        P.dma(C["w"], [lambda e, wq=wq, j=j: e.dma_start(
                out=wq.ap.rearrange("p (c o) -> p c o", c=NFC), in_=w_src(j))],
              w=[wq], dbg=f"wq{j}")
        kind, jj = divmod(j, NFC)
        if layer == 0:
            isq = (j < NFC)
            for c in range(NFC):
                if isq:
                    P.op("v", lambda e, wq=wq, c=c: e.tensor_scalar(
                            wq.ap[:, c * 128:(c + 1) * 128], wq.ap[:, c * 128:(c + 1) * 128],
                            g32.ap[:, c:c + 1], 0.125, ALU.mult, ALU.mult),
                         r=[wq, g32], w=[wq], dbg=f"wfix{j}_{c}")
                else:
                    P.op("v", lambda e, wq=wq, c=c: e.tensor_scalar_mul(
                            wq.ap[:, c * 128:(c + 1) * 128], wq.ap[:, c * 128:(c + 1) * 128],
                            g32.ap[:, c:c + 1]),
                         r=[wq, g32], w=[wq], dbg=f"wfix{j}_{c}")
        elif j < NFC:
            P.op("v", lambda e, wq=wq: e.tensor_scalar_mul(wq.ap, wq.ap, 0.125),
                 r=[wq], w=[wq], dbg=f"wq_scale{j}")
        for tc in range(NTC):
            pb = banks[(j * NTC + tc) % 2]
            for c in range(NFC):
                P.op("t", lambda e, wq=wq, c=c, tc=tc, pb=pb: e.matmul(
                        pb.ap, wq.ap[:, c * 128:(c + 1) * 128],
                        xnb[c].ap[:, tc * 512:(tc + 1) * 512],
                        start=(c == 0), stop=(c == NFC - 1)),
                     r=[wq, xnb[c]], w=[pb], dbg=f"qkv{j}_{tc}_{c}")
            if kind < 2:
                st = qstage[tc % 2]
                sq = sqstage[tc % 2]
                P.op("s", lambda e, pb=pb, st=st: e.activation(st.ap, pb.ap, AF.Copy),
                     r=[pb], w=[st], dbg=f"qkev{j}_{tc}")
                P.op("s", lambda e, st=st, sq=sq: e.activation(sq.ap, st.ap, AF.Square),
                     r=[st], w=[sq], dbg=f"qsq{j}_{tc}")
                for half in range(2):
                    h = jj * 2 + half
                    P.op("t", lambda e, sq=sq, half=half: e.matmul(
                            n2bank.ap[0:1, :], X["onesb"].ap[64 * half:64 * half + 64, 0:1],
                            sq.ap[64 * half:64 * half + 64, :], start=True, stop=True),
                         r=[sq, X["onesb"]], w=[n2bank], dbg=f"n2{j}_{tc}_{half}")
                    dst = (qn2m if kind == 0 else kn2m)
                    P.op("v", lambda e, dst=dst, h=h, tc=tc: e.reduce_max(
                            dst.ap[:, h * NTC + tc: h * NTC + tc + 1], n2bank.ap[0:1, :], AX.X),
                         r=[n2bank], w=[dst], dbg=f"n2m{j}_{tc}_{half}")
                spill = qspill if kind == 0 else kspill
                bsp = bqs if kind == 0 else bks
                P.dma(C["spillout"], [lambda e, st=st, spill=spill, jj=jj, tc=tc: e.dma_start(
                        out=spill[jj * 128:(jj + 1) * 128, tc * 512:(tc + 1) * 512], in_=st.ap)],
                      r=[st], w=[bsp], dbg=f"spill{j}_{tc}")
            else:
                vs = vstage[jj % 2]
                P.op("v", lambda e, pb=pb, vs=vs, tc=tc: e.tensor_copy(
                        vs.ap[:, tc * 512:(tc + 1) * 512], pb.ap),
                     r=[pb], w=[vs], dbg=f"vev{j}_{tc}")
        if kind == 2:
            vs = vstage[jj % 2]
            for tb in range(KC):
                tb_bank = banks[2 + (tb % 2)]
                P.op("t", lambda e, vs=vs, tb=tb, tb_bank=tb_bank: e.transpose(
                        tb_bank.ap.bitcast(BF16)[:, 0:128], vs.ap[:, tb * 128:(tb + 1) * 128],
                        X["ident"].ap),
                     r=[vs, X["ident"]], w=[tb_bank], dbg=f"vT{jj}_{tb}")
                def cp(e, tb_bank=tb_bank, jj=jj, tb=tb):
                    srcv = tb_bank.ap.bitcast(BF16)[:, 0:128].rearrange("p (h d) -> p h d", h=2)
                    dstv = vP.ap.rearrange("p (h k o) -> p h k o", h=NH, k=KC)[:, 2 * jj:2 * jj + 2, tb, 0:64]
                    return e.tensor_copy(dstv, srcv)
                P.op("v", cp, r=[tb_bank], w=[vP], dbg=f"vPcp{jj}_{tb}")
    A.release()
    A.release()

    bc_bank = banks[7]
    P.op("v", lambda e: e.tensor_reduce(kn2h.ap[:, 0:1], kn2m.ap, AX.X, ALU.max),
         r=[kn2m], w=[kn2h], dbg="kn2h")
    P.op("v", lambda e: e.tensor_reduce(negB.ap[:, 0:1], qn2m.ap, AX.X, ALU.max),
         r=[qn2m], w=[negB], dbg="qn2g")
    P.op("s", lambda e: e.activation(negB.ap[:, 0:1], negB.ap[:, 0:1], AF.Sqrt), r=[negB], w=[negB])
    P.op("s", lambda e: e.activation(kn2h.ap[:, 0:1], kn2h.ap[:, 0:1], AF.Sqrt), r=[kn2h], w=[kn2h])
    P.op("v", lambda e: e.tensor_tensor(negB.ap[:, 0:1], negB.ap[:, 0:1], kn2h.ap[:, 0:1], ALU.mult),
         r=[negB, kn2h], w=[negB], dbg="negB")
    P.op("v", lambda e: e.tensor_scalar_mul(negB.ap[:, 0:1], negB.ap[:, 0:1], -1.0), r=[negB], w=[negB])
    P.op("t", lambda e: e.matmul(bc_bank.ap[:, 0:1], X["onesf"].ap[0:1, :], negB.ap[:, 0:1],
                                 start=True, stop=True),
         r=[negB, X["onesf"]], w=[bc_bank], dbg="negBbc")
    P.op("v", lambda e: e.tensor_copy(negBB.ap[:, 0:1], bc_bank.ap[:, 0:1]), r=[bc_bank], w=[negBB])

    A.mark()
    psall = banks[8]
    ktile = [A.alloc(f"kt{i}", 128, T, BF16) for i in range(2)]
    qtile = [A.alloc(f"qt{i}", 128, 512, BF16) for i in range(4)]
    pt = [A.alloc(f"pt{i}", 128, 1536, BF16) for i in range(3)]
    rdsb = A.alloc("rdsb", 1, 1024, F32)
    rdbb = A.alloc("rdbb", 64, 512, F32)
    aostage = [A.alloc(f"aost{i}", 128, 512, BF16) for i in range(2)]
    osbuf = [A.alloc(f"osb{i}", 65, 1024, F32) for i in range(2)]
    obank = {0: banks[6], 1: banks[7]}

    units = [(kc, head) for kc in range(KC) for head in (0, 1)]
    NB = (len(units) + 2) // 3
    for hp in range(NFC):
        kt = ktile[hp % 2]
        P.dma(C["kin"], [lambda e, kt=kt, hp=hp: e.dma_start(
                out=kt.ap, in_=kspill[hp * 128:(hp + 1) * 128, :])],
              r=[bks], w=[kt], dbg=f"kin{hp}")
        for qc in range(NTC):
            qt = qtile[qc % 4]
            P.dma(C["qin"], [lambda e, qt=qt, hp=hp, qc=qc: e.dma_start(
                    out=qt.ap, in_=qspill[hp * 128:(hp + 1) * 128, qc * 512:(qc + 1) * 512])],
                  r=[bqs], w=[qt], dbg=f"qin{hp}_{qc}")

            def emit_pv(j):
                us = units[3 * j:3 * j + 3]
                p = pt[j % 3]
                for i, (kc, head) in enumerate(us):
                    ob = obank[head]
                    def pv(e, p=p, i=i, kc=kc, head=head, ob=ob, hp=hp):
                        lhs = vP.ap.rearrange("p (h k o) -> p h k o", h=NH, k=KC)[:, 2 * hp + head, kc, :]
                        return e.matmul(ob.ap[0:65, :], lhs, p.ap[:, i * 512:(i + 1) * 512],
                                        start=(kc == 0), stop=(kc == KC - 1))
                    P.op("t", pv, r=[p, vP], w=[ob], dbg=f"pv{hp}_{qc}_{j}_{i}")

            for j in range(NB):
                us = units[3 * j:3 * j + 3]
                sb0 = (3 * j) % 6
                bks_used = [banks[sb0 + i] for i in range(len(us))]
                for i, (kc, head) in enumerate(us):
                    bk = bks_used[i]
                    P.op("t", lambda e, kt=kt, qt=qt, kc=kc, head=head, bk=bk: e.matmul(
                            bk.ap, kt.ap[64 * head:64 * head + 64, kc * 128:(kc + 1) * 128],
                            qt.ap[64 * head:64 * head + 64, :], start=True, stop=True),
                         r=[kt, qt], w=[bk], dbg=f"qk{hp}_{qc}_{j}_{i}")
                if j > 0:
                    emit_pv(j - 1)
                n = len(us) * 512
                p = pt[j % 3]
                P.op("s", lambda e, sb0=sb0, n=n, p=p: e.activation(
                        p.ap[:, 0:n], psall.ap[:, sb0 * 512:sb0 * 512 + n], AF.Exp,
                        bias=negBB.ap[:, 0:1]),
                     r=bks_used + [negBB], w=[p], dbg=f"exp{hp}_{qc}_{j}")
            emit_pv(NB - 1)
            ao = aostage[qc % 2]
            osb = osbuf[qc % 2]
            for head, ob in ((0, obank[0]), (1, obank[1])):
                P.op("v", lambda e, ob=ob, osb=osb, head=head: e.tensor_copy(
                        osb.ap[:, head * 512:(head + 1) * 512], ob.ap[0:65, :]),
                     r=[ob], w=[osb], dbg=f"oev{hp}_{qc}_{head}")
            for head in (0, 1):
                P.op("v", lambda e, osb=osb, head=head: e.reciprocal(
                        rdsb.ap[:, head * 512:(head + 1) * 512],
                        osb.ap[64:65, head * 512:(head + 1) * 512]),
                     r=[osb], w=[rdsb], dbg=f"rd{hp}_{qc}_{head}")
                rb = obank[head]
                P.op("t", lambda e, rb=rb, head=head: e.matmul(
                        rb.ap[0:64, :], X["onesf"].ap[0:1, 0:64],
                        rdsb.ap[:, head * 512:(head + 1) * 512], start=True, stop=True),
                     r=[rdsb, X["onesf"]], w=[rb], dbg=f"rdbc{hp}_{qc}_{head}")
                P.op("v", lambda e, rb=rb, osb=osb, ao=ao, head=head: e.tensor_tensor(
                        ao.ap[64 * head:64 * head + 64, :], osb.ap[0:64, 512 * head:512 * head + 512],
                        rb.ap[0:64, :], ALU.mult),
                     r=[osb, rb], w=[ao], dbg=f"aonorm{hp}_{qc}_{head}")
            P.dma(C["spillout"], [lambda e, ao=ao, hp=hp, qc=qc: e.dma_start(
                    out=aospill[hp * 128:(hp + 1) * 128, qc * 512:(qc + 1) * 512], in_=ao.ap)],
                  r=[ao], w=[bas], dbg=f"aoout{hp}_{qc}")
    A.release()
    A.release()

    ob = emit_outproj_ln(P, A, io, C, X, banks,
                          ("w_out" if layer == 0 else "wfcT"), xin, xout, aospill, bas, bxout)
    A.release()
    return ob




C1R = float(3 * 2 ** 21)
C2R = float(3 * 2 ** 22)
C3R = float(3 * 2 ** 23)

def emit_fp4_quant(P, A, src_buf, dst_buf, rB_buf, sB_buf, tmp, ncols, pcols, src_off=0):
    v, w1, a05, a1 = tmp["v"], tmp["q"], tmp["a"], tmp["b"]
    m2, m4 = tmp["m2"], tmp["m4"]
    for lo in range(0, ncols, pcols):
        hi = min(lo + pcols, ncols)
        n = hi - lo
        s0, s1 = src_off + lo, src_off + hi
        P.op("v", lambda e, s0=s0, s1=s1, lo=lo, hi=hi, n=n: e.tensor_tensor(
                v.ap[:, 0:n], src_buf.ap[:, s0:s1], rB_buf.ap[:, lo:hi], ALU.mult),
             r=[src_buf, rB_buf], w=[v], dbg="q_v")
        P.op("s", lambda e, n=n: e.activation(w1.ap[:, 0:n], v.ap[:, 0:n], AF.Square),
             r=[v], w=[w1], dbg="q_vsq")
        P.op("v", lambda e, n=n: e.tensor_scalar(m2.ap[:, 0:n], w1.ap[:, 0:n], 4.0, None, ALU.is_gt),
             r=[w1], w=[m2], dbg="q_m2")
        P.op("v", lambda e, n=n: e.tensor_scalar(m4.ap[:, 0:n], w1.ap[:, 0:n], 16.0, None, ALU.is_gt),
             r=[w1], w=[m4], dbg="q_m4")
        P.op("v", lambda e, n=n: e.tensor_scalar(a05.ap[:, 0:n], v.ap[:, 0:n], C1R, C1R,
                                                 ALU.add, ALU.subtract),
             r=[v], w=[a05], dbg="q_a05")
        P.op("v", lambda e, n=n: e.tensor_scalar(a1.ap[:, 0:n], v.ap[:, 0:n], C2R, C2R,
                                                 ALU.add, ALU.subtract),
             r=[v], w=[a1], dbg="q_a1")
        P.op("v", lambda e, n=n: e.tensor_scalar(w1.ap[:, 0:n], v.ap[:, 0:n], C3R, C3R,
                                                 ALU.add, ALU.subtract),
             r=[v], w=[w1], dbg="q_a2")
        P.op("v", lambda e, n=n: e.copy_predicated(a05.ap[:, 0:n], m2.ap[:, 0:n], a1.ap[:, 0:n]),
             r=[m2, a1, a05], w=[a05], dbg="q_s1")
        P.op("v", lambda e, n=n: e.copy_predicated(a05.ap[:, 0:n], m4.ap[:, 0:n], w1.ap[:, 0:n]),
             r=[m4, w1, a05], w=[a05], dbg="q_s2")
        P.op("v", lambda e, s0=s0, s1=s1, n=n, lo=lo, hi=hi: e.tensor_tensor(
                dst_buf.ap[:, s0:s1], a05.ap[:, 0:n], sB_buf.ap[:, lo:hi], ALU.mult),
             r=[a05, sB_buf], w=[dst_buf], dbg="q_deq")


def gen_wquant_nat(P, A, io, C, X, banks, wname, wq_dram, nrows, kdim, bwq, tbpair):
    nch = nrows // 128
    kch = kdim // 128
    nblk = kdim // 16
    wnat = [A.alloc(f"wn{wname}{i}", 128, kdim, F32) for i in range(2)]
    sw = [A.alloc(f"sw{wname}{i}", 128, nblk, F32) for i in range(2)]
    rw = [A.alloc(f"rw{wname}{i}", 128, nblk, F32) for i in range(2)]
    swB = [A.alloc(f"swB{wname}{i}", 128, kdim, F32) for i in range(2)]
    rwB = [A.alloc(f"rwB{wname}{i}", 128, kdim, F32) for i in range(2)]
    tmp = [dict([(k, A.alloc(f"wt{wname}{k}{i}", 128, kdim, F32)) for k in ("v", "q", "a", "b")]
                + [("m2", A.alloc(f"wm2{wname}{i}", 128, kdim, U8)),
                   ("m4", A.alloc(f"wm4{wname}{i}", 128, kdim, U8))]) for i in range(2)]
    wqd = [A.alloc(f"wqd{wname}{i}", 128, kdim, BF16) for i in range(2)]
    stg = [A.alloc(f"wstg{wname}{i}", 128, kdim, BF16) for i in range(2)]
    tb0, tb1 = tbpair
    k16 = lambda b: b.ap.rearrange("p (b k) -> p b k", k=16)
    for a in range(nch):
        i = a % 2
        wn = wnat[i]
        P.dma(C["xin"], [lambda e, wn=wn, a=a: e.dma_start(
                out=wn.ap, in_=io[wname][a * 128:(a + 1) * 128, :])],
              w=[wn], dbg=f"wl{wname}{a}")
        P.op("v", lambda e, wn=wn, i=i: e.tensor_reduce(
                sw[i].ap, wn.ap.rearrange("p (b k) -> p b k", k=16), AX.X, ALU.max,
                apply_absolute_value=True),
             r=[wn], w=[sw[i]], dbg=f"ws{wname}{a}")
        P.op("v", lambda e, i=i: e.tensor_scalar(sw[i].ap, sw[i].ap, 1.0 / 6.0, 1e-12,
                                                 ALU.mult, ALU.max),
             r=[sw[i]], w=[sw[i]], dbg=f"ws2{wname}{a}")
        P.op("v", lambda e, i=i: e.reciprocal(rw[i].ap, sw[i].ap),
             r=[sw[i]], w=[rw[i]], dbg=f"wr{wname}{a}")
        P.op("s", lambda e, i=i: e.activation(
                k16(swB[i]), sw[i].ap.unsqueeze(2).broadcast_to([128, nblk, 16]), AF.Copy),
             r=[sw[i]], w=[swB[i]], dbg=f"wsb{wname}{a}")
        P.op("s", lambda e, i=i: e.activation(
                k16(rwB[i]), rw[i].ap.unsqueeze(2).broadcast_to([128, nblk, 16]), AF.Copy),
             r=[rw[i]], w=[rwB[i]], dbg=f"wrb{wname}{a}")
        emit_fp4_quant(P, A, wn, wqd[i], rwB[i], swB[i], tmp[i], kdim, kdim)
        for c in range(kch):
            tb = tb0 if c % 2 == 0 else tb1
            P.op("t", lambda e, c=c, tb=tb, i=i: e.transpose(
                    tb.ap.bitcast(BF16)[:, 0:128], wqd[i].ap[:, c * 128:(c + 1) * 128],
                    X["ident"].ap),
                 r=[wqd[i], X["ident"]], w=[tb], dbg=f"wT{wname}{a}_{c}")
            P.op("s", lambda e, c=c, tb=tb, i=i: e.activation(
                    stg[i].ap[:, c * 128:(c + 1) * 128], tb.ap.bitcast(BF16)[:, 0:128], AF.Copy),
                 r=[tb], w=[stg[i]], dbg=f"wTe{wname}{a}_{c}")
        P.dma(C["spillout"], [lambda e, a=a, i=i: e.dma_start(
                out=wq_dram.rearrange("(c p) n -> p c n", p=128)[:, :, a * 128:(a + 1) * 128],
                in_=stg[i].ap.rearrange("p (c o) -> p c o", o=128))],
              r=[stg[i]], w=[bwq], dbg=f"wst{wname}{a}")
        yield


def pump(main, aux, naux):
    credit = 0.0
    for _ in main:
        credit += naux
        while credit >= 1.0:
            next(aux, None)
            credit -= 1.0
    for _ in aux:
        pass


def build_l0ffn(P, A, io, C, X, banks):
    xin, xout = io["xin"], io["xout"]
    bxout = Buf("xout")
    bwgq, bwuq, bwdq = Buf("wgq"), Buf("wuq"), Buf("wdq")
    bscxs, bscxr = Buf("scxs"), Buf("scxr")
    bschs = [Buf("schs0"), Buf("schs1")]
    bschr = [Buf("schr0"), Buf("schr1")]
    FFC = FF // 128

    A.mark()
    g1 = gen_wquant_nat(P, A, io, C, X, banks, "wg", io["wgq"], FF, D, bwgq,
                        (banks[2], banks[3]))
    g2 = gen_wquant_nat(P, A, io, C, X, banks, "wu", io["wuq"], FF, D, bwuq,
                        (banks[5], banks[6]))
    for _ in g1:
        next(g2, None)
    for _ in g2:
        pass
    A.release()
    A.mark()
    for _ in gen_wquant_nat(P, A, io, C, X, banks, "wd", io["wdq"], D, FF, bwdq,
                            (banks[2], banks[3])):
        pass
    A.release()

    A.mark()
    xr_in = [A.alloc(f"xr{i}", 128, NFC * 512, F32) for i in range(2)]
    xq = [A.alloc(f"xq{i}", 128, NFC * 512, BF16) for i in range(2)]
    sxT = A.alloc("sxT", 128, 4 * 64, F32)
    rxT = A.alloc("rxT", 128, 4 * 64, F32)
    sx_fm = A.alloc("sxfm", 64, 512, F32)
    rx_fm = A.alloc("rxfm", 64, 512, F32)
    rxB = A.alloc("rxB", 128, 4 * 512, F32)
    sxB = A.alloc("sxB", 128, 4 * 512, F32)
    tmpx = {k: A.alloc(f"t{k}", 128, 2 * 512, F32) for k in ("v", "q", "a", "b")}
    tmpx["m2"] = A.alloc("tm2", 128, 2 * 512, U8)
    tmpx["m4"] = A.alloc("tm4", 128, 2 * 512, U8)
    hso = A.alloc("hso", 128, 512, BF16)
    hbuf = [A.alloc(f"hbuf{i}", 128, FFC * 512, BF16) for i in range(2)]
    shT = A.alloc("shT", 128, 4 * 176, F32)
    rhT = A.alloc("rhT", 128, 4 * 176, F32)
    sh_fm1 = A.alloc("shfm1", 128, 512, F32)
    sh_fm2 = A.alloc("shfm2", 48, 512, F32)
    rh_fm1 = A.alloc("rhfm1", 128, 512, F32)
    rh_fm2 = A.alloc("rhfm2", 48, 512, F32)
    rhB = A.alloc("rhB", 128, 4 * 512, F32)
    shB = A.alloc("shB", 128, 4 * 512, F32)
    wgt = [A.alloc(f"wgt{i}", 128, NFC * 128, BF16) for i in range(2)]
    wut = [A.alloc(f"wut{i}", 128, NFC * 128, BF16) for i in range(2)]
    wdt = [A.alloc("wdt0", 128, FFC * 128, BF16)]
    zt = A.alloc("zt2", 128, NFC * 512, F32)
    zsq = A.alloc("zsq2", 128, 512, BF16)
    mu = A.alloc("mu2", 1, 512, F32)
    ms = A.alloc("ms2", 1, 512, F32)
    aa = A.alloc("aa2", 1, 512, F32)
    bb2 = A.alloc("bb22", 1, 512, F32)
    pg, pu = banks[0], banks[1]
    pdb = [banks[2], banks[3]]
    tbb = [banks[4], banks[7]]
    stA, stB = banks[6], banks[5]
    abank, bbank = banks[5], banks[6]

    def gen_front(tc):
        sl = tc % 2
        xr = xr_in[sl]
        hb = hbuf[sl]
        src2 = xin[:, tc * 512:(tc + 1) * 512].rearrange("(c p) t -> p c t", p=128)
        P.dma(C["xin"], [lambda e, xr=xr, src2=src2: e.dma_start(
                out=xr.ap.rearrange("p (c t) -> p c t", c=NFC), in_=src2)],
              w=[xr], dbg=f"xr{tc}")
        for c in range(NFC):
            for tb in range(4):
                bk = tbb[(c * 4 + tb) % 2]
                P.op("t", lambda e, c=c, tb=tb, bk=bk, xr=xr: e.transpose(
                        bk.ap[:, 0:128],
                        xr.ap[:, c * 512 + tb * 128: c * 512 + (tb + 1) * 128], X["identf"].ap),
                     r=[xr, X["identf"]], w=[bk], dbg=f"sxT{tc}_{c}_{tb}")
                P.op("v", lambda e, c=c, tb=tb, bk=bk: e.tensor_reduce(
                        sxT.ap[:, tb * 64 + c * 8: tb * 64 + (c + 1) * 8],
                        bk.ap[:, 0:128].rearrange("p (b k) -> p b k", k=16),
                        AX.X, ALU.max, apply_absolute_value=True),
                     r=[bk], w=[sxT], dbg=f"sxR{tc}_{c}_{tb}")
        P.op("v", lambda e: e.tensor_scalar(sxT.ap, sxT.ap, 1.0 / 6.0, 1e-12, ALU.mult, ALU.max),
             r=[sxT], w=[sxT])
        P.op("v", lambda e: e.reciprocal(rxT.ap, sxT.ap), r=[sxT], w=[rxT])
        for tb in range(4):
            P.op("t", lambda e, tb=tb: e.transpose(
                    tbb[0].ap[:, 0:128][0:64, :], sxT.ap[:, tb * 64:(tb + 1) * 64], X["identf"].ap),
                 r=[sxT, X["identf"]], w=[tbb[0]], dbg=f"sxTb{tc}_{tb}")
            P.op("v", lambda e, tb=tb: e.tensor_copy(sx_fm.ap[:, tb * 128:(tb + 1) * 128],
                                                     tbb[0].ap[0:64, 0:128]),
                 r=[tbb[0]], w=[sx_fm])
            P.op("t", lambda e, tb=tb: e.transpose(
                    tbb[1].ap[:, 0:128][0:64, :], rxT.ap[:, tb * 64:(tb + 1) * 64], X["identf"].ap),
                 r=[rxT, X["identf"]], w=[tbb[1]], dbg=f"rxTb{tc}_{tb}")
            P.op("v", lambda e, tb=tb: e.tensor_copy(rx_fm.ap[:, tb * 128:(tb + 1) * 128],
                                                     tbb[1].ap[0:64, 0:128]),
                 r=[tbb[1]], w=[rx_fm])
        P.dma(C["spillout"], [
            lambda e: e.dma_start(out=io["scx_s"].rearrange("(b c) t -> c b t", b=8),
                                  in_=sx_fm.ap),
            lambda e: e.dma_start(out=io["scx_r"].rearrange("(b c) t -> c b t", b=8),
                                  in_=rx_fm.ap),
        ], r=[sx_fm, rx_fm], w=[bscxs, bscxr], dbg=f"scxout{tc}")
        yield
        for plo in range(0, NFC, 4):
            phi = plo + 4
            P.dma(C["sc"], [
                lambda e, plo=plo, phi=phi: e.dma_start(
                    out=sxB.ap,
                    in_=io["scx_s"].rearrange("(b c) t -> b (c t)", b=8)[:, plo * 512:phi * 512]
                    .unsqueeze(1).broadcast_to([8, 16, (phi - plo) * 512])),
                lambda e, plo=plo, phi=phi: e.dma_start(
                    out=rxB.ap,
                    in_=io["scx_r"].rearrange("(b c) t -> b (c t)", b=8)[:, plo * 512:phi * 512]
                    .unsqueeze(1).broadcast_to([8, 16, (phi - plo) * 512])),
            ], r=[bscxs, bscxr], w=[sxB, rxB], dbg=f"scxrep{tc}_{plo}")
            emit_fp4_quant(P, A, xr, xq[sl], rxB, sxB, tmpx, 4 * 512, 2 * 512, src_off=plo * 512)
            yield
        for j in range(FFC):
            wg_ = wgt[j % 2]
            wu_ = wut[j % 2]
            P.dma(C["w"], [
                lambda e, wg_=wg_, j=j: e.dma_start(
                    out=wg_.ap.rearrange("p (c o) -> p c o", c=NFC),
                    in_=io["wgq"].rearrange("(c p) n -> p c n", p=128)[:, :, j * 128:(j + 1) * 128]),
                lambda e, wu_=wu_, j=j: e.dma_start(
                    out=wu_.ap.rearrange("p (c o) -> p c o", c=NFC),
                    in_=io["wuq"].rearrange("(c p) n -> p c n", p=128)[:, :, j * 128:(j + 1) * 128]),
            ], r=[bwgq, bwuq], w=[wg_, wu_], dbg=f"wgl{tc}_{j}")
            for c in range(NFC):
                P.op("t", lambda e, c=c, wg_=wg_, sl=sl: e.matmul(
                        pg.ap, wg_.ap[:, c * 128:(c + 1) * 128], xq[sl].ap[:, c * 512:(c + 1) * 512],
                        start=(c == 0), stop=(c == NFC - 1)),
                     r=[wg_, xq[sl]], w=[pg], dbg=f"mg{tc}_{j}_{c}")
            for c in range(NFC):
                P.op("t", lambda e, c=c, wu_=wu_, sl=sl: e.matmul(
                        pu.ap, wu_.ap[:, c * 128:(c + 1) * 128], xq[sl].ap[:, c * 512:(c + 1) * 512],
                        start=(c == 0), stop=(c == NFC - 1)),
                     r=[wu_, xq[sl]], w=[pu], dbg=f"mu{tc}_{j}_{c}")
            P.op("s", lambda e: e.activation(hso.ap, pg.ap, AF.Silu), r=[pg], w=[hso], dbg=f"silu{tc}_{j}")
            P.op("v", lambda e, j=j, hb=hb: e.tensor_tensor(
                    hb.ap[:, j * 512:(j + 1) * 512], hso.ap, pu.ap, ALU.mult),
                 r=[hso, pu], w=[hb], dbg=f"hmul{tc}_{j}")
            if j % 2 == 1:
                yield
        for c in range(FFC):
            for tb in range(4):
                bk = tbb[(c * 4 + tb) % 2]
                P.op("t", lambda e, c=c, tb=tb, bk=bk, hb=hb: e.transpose(
                        bk.ap.bitcast(BF16)[:, 0:128],
                        hb.ap[:, c * 512 + tb * 128: c * 512 + (tb + 1) * 128], X["ident"].ap),
                     r=[hb, X["ident"]], w=[bk], dbg=f"shT{tc}_{c}_{tb}")
                P.op("v", lambda e, c=c, tb=tb, bk=bk: e.tensor_reduce(
                        shT.ap[:, tb * 176 + c * 8: tb * 176 + (c + 1) * 8],
                        bk.ap.bitcast(BF16)[:, 0:128].rearrange("p (b k) -> p b k", k=16),
                        AX.X, ALU.max, apply_absolute_value=True),
                     r=[bk], w=[shT], dbg=f"shR{tc}_{c}_{tb}")
            if c % 4 == 3:
                yield
        P.op("v", lambda e: e.tensor_scalar(shT.ap, shT.ap, 1.0 / 6.0, 1e-12, ALU.mult, ALU.max),
             r=[shT], w=[shT])
        P.op("v", lambda e: e.reciprocal(rhT.ap, shT.ap), r=[shT], w=[rhT])
        for tb in range(4):
            P.op("t", lambda e, tb=tb: e.transpose(
                    tbb[0].ap[:, 0:128], shT.ap[:, tb * 176: tb * 176 + 128], X["identf"].ap),
                 r=[shT, X["identf"]], w=[tbb[0]])
            P.op("v", lambda e, tb=tb: e.tensor_copy(sh_fm1.ap[:, tb * 128:(tb + 1) * 128],
                                                     tbb[0].ap[:, 0:128]),
                 r=[tbb[0]], w=[sh_fm1])
            P.op("t", lambda e, tb=tb: e.transpose(
                    tbb[1].ap[:, 0:128][0:48, :], shT.ap[:, tb * 176 + 128: tb * 176 + 176],
                    X["identf"].ap),
                 r=[shT, X["identf"]], w=[tbb[1]])
            P.op("v", lambda e, tb=tb: e.tensor_copy(sh_fm2.ap[:, tb * 128:(tb + 1) * 128],
                                                     tbb[1].ap[0:48, 0:128]),
                 r=[tbb[1]], w=[sh_fm2])
            P.op("t", lambda e, tb=tb: e.transpose(
                    tbb[0].ap[:, 0:128], rhT.ap[:, tb * 176: tb * 176 + 128], X["identf"].ap),
                 r=[rhT, X["identf"]], w=[tbb[0]])
            P.op("v", lambda e, tb=tb: e.tensor_copy(rh_fm1.ap[:, tb * 128:(tb + 1) * 128],
                                                     tbb[0].ap[:, 0:128]),
                 r=[tbb[0]], w=[rh_fm1])
            P.op("t", lambda e, tb=tb: e.transpose(
                    tbb[1].ap[:, 0:128][0:48, :], rhT.ap[:, tb * 176 + 128: tb * 176 + 176],
                    X["identf"].ap),
                 r=[rhT, X["identf"]], w=[tbb[1]])
            P.op("v", lambda e, tb=tb: e.tensor_copy(rh_fm2.ap[:, tb * 128:(tb + 1) * 128],
                                                     tbb[1].ap[0:48, 0:128]),
                 r=[tbb[1]], w=[rh_fm2])
        schs_v = io[f"sch_s{sl}"].rearrange("(b c) t -> c b t", b=8)
        schr_v = io[f"sch_r{sl}"].rearrange("(b c) t -> c b t", b=8)
        P.dma(C["spillout"], [
            lambda e, schs_v=schs_v: e.dma_start(out=schs_v[0:16], in_=sh_fm1.ap),
            lambda e, schs_v=schs_v: e.dma_start(out=schs_v[16:22], in_=sh_fm2.ap),
            lambda e, schr_v=schr_v: e.dma_start(out=schr_v[0:16], in_=rh_fm1.ap),
            lambda e, schr_v=schr_v: e.dma_start(out=schr_v[16:22], in_=rh_fm2.ap),
        ], r=[sh_fm1, sh_fm2, rh_fm1, rh_fm2], w=[bschs[sl], bschr[sl]], dbg=f"schout{tc}")
        yield

    def gen_hq(tc):
        sl = tc % 2
        hb = hbuf[sl]
        for plo in range(0, FFC, 4):
            phi = min(plo + 4, FFC)
            P.dma(C["sc"], [
                lambda e, plo=plo, phi=phi, sl=sl: e.dma_start(
                    out=shB.ap[:, 0:(phi - plo) * 512],
                    in_=io[f"sch_s{sl}"].rearrange("(b c) t -> b (c t)", b=8)[:, plo * 512:phi * 512]
                    .unsqueeze(1).broadcast_to([8, 16, (phi - plo) * 512])),
                lambda e, plo=plo, phi=phi, sl=sl: e.dma_start(
                    out=rhB.ap[:, 0:(phi - plo) * 512],
                    in_=io[f"sch_r{sl}"].rearrange("(b c) t -> b (c t)", b=8)[:, plo * 512:phi * 512]
                    .unsqueeze(1).broadcast_to([8, 16, (phi - plo) * 512])),
            ], r=[bschs[sl], bschr[sl]], w=[shB, rhB], dbg=f"schrep{tc}_{plo}")
            emit_fp4_quant(P, A, hb, hb, rhB, shB, tmpx, (phi - plo) * 512, 2 * 512,
                           src_off=plo * 512)
            yield

    def emit_down(tc):
        sl = tc % 2
        xr = xr_in[sl]
        hb = hbuf[sl]
        for j in range(NFC):
            wd_ = wdt[0]
            P.dma(C["gw"], [lambda e, wd_=wd_, j=j: e.dma_start(
                    out=wd_.ap.rearrange("p (c o) -> p c o", c=FFC),
                    in_=io["wdq"].rearrange("(c p) n -> p c n", p=128)[:, :, j * 128:(j + 1) * 128])],
                  r=[bwdq], w=[wd_], dbg=f"wdl{tc}_{j}")
            pd = pdb[j % 2]
            for c in range(FFC):
                P.op("t", lambda e, c=c, wd_=wd_, pd=pd, hb=hb: e.matmul(
                        pd.ap, wd_.ap[:, c * 128:(c + 1) * 128], hb.ap[:, c * 512:(c + 1) * 512],
                        start=(c == 0), stop=(c == FFC - 1)),
                     r=[wd_, hb], w=[pd], dbg=f"md{tc}_{j}_{c}")
            P.op("v", lambda e, pd=pd, xr=xr, j=j: e.tensor_tensor(
                    zt.ap[:, j * 512:(j + 1) * 512], pd.ap, xr.ap[:, j * 512:(j + 1) * 512], ALU.add),
                 r=[pd, xr], w=[zt], dbg=f"z2{tc}_{j}")
            P.op("s", lambda e, j=j: e.activation(zsq.ap, zt.ap[:, j * 512:(j + 1) * 512], AF.Square),
                 r=[zt], w=[zsq], dbg=f"z2sq{tc}_{j}")
            P.op("t", lambda e, j=j: e.matmul(stB.ap[0:1, :], X["onesb"].ap[:, 0:1], zsq.ap,
                                              start=(j == 0), stop=(j == NFC - 1)),
                 r=[zsq, X["onesb"]], w=[stB], dbg=f"st2b{tc}_{j}")
            P.op("t", lambda e, j=j: e.matmul(stA.ap[0:1, :], X["onesfc"].ap,
                                              zt.ap[:, j * 512:(j + 1) * 512],
                                              start=(j == 0), stop=(j == NFC - 1)),
                 r=[zt, X["onesfc"]], w=[stA], dbg=f"st2a{tc}_{j}")
        P.op("v", lambda e: e.tensor_scalar_mul(mu.ap, stA.ap[0:1, :], 1.0 / D), r=[stA], w=[mu])
        P.op("v", lambda e: e.tensor_scalar_mul(ms.ap, stB.ap[0:1, :], 1.0 / D), r=[stB], w=[ms])
        P.op("v", lambda e: e.tensor_tensor(bb2.ap, mu.ap, mu.ap, ALU.mult), r=[mu], w=[bb2])
        P.op("v", lambda e: e.tensor_tensor(ms.ap, ms.ap, bb2.ap, ALU.subtract), r=[ms, bb2], w=[ms])
        P.op("s", lambda e: e.activation(ms.ap, ms.ap, AF.Sqrt, bias=X["eps"].ap[0:1, :]),
             r=[ms, X["eps"]], w=[ms])
        P.op("v", lambda e: e.reciprocal(aa.ap, ms.ap), r=[ms], w=[aa])
        P.op("v", lambda e: e.tensor_tensor(bb2.ap, mu.ap, aa.ap, ALU.mult), r=[mu, aa], w=[bb2])
        P.op("v", lambda e: e.tensor_scalar_mul(bb2.ap, bb2.ap, -1.0), r=[bb2], w=[bb2])
        P.op("t", lambda e: e.matmul(abank.ap, X["onesf"].ap[0:1, :], aa.ap, start=True, stop=True),
             r=[aa, X["onesf"]], w=[abank], dbg=f"ab2{tc}")
        P.op("t", lambda e: e.matmul(bbank.ap, X["onesf"].ap[0:1, :], bb2.ap, start=True, stop=True),
             r=[bb2, X["onesf"]], w=[bbank], dbg=f"bb2{tc}")
        for j in range(NFC):
            P.op("v", lambda e, j=j: e.tensor_tensor(
                    zt.ap[:, j * 512:(j + 1) * 512], zt.ap[:, j * 512:(j + 1) * 512], abank.ap, ALU.mult),
                 r=[zt, abank], w=[zt], dbg=f"ln2m{tc}_{j}")
            P.op("v", lambda e, j=j: e.tensor_tensor(
                    zt.ap[:, j * 512:(j + 1) * 512], zt.ap[:, j * 512:(j + 1) * 512], bbank.ap, ALU.add),
                 r=[zt, bbank], w=[zt], dbg=f"ln2a{tc}_{j}")
        P.dma(C["xout"], [lambda e, tc=tc: e.dma_start(
                out=xout[:, tc * 512:(tc + 1) * 512].rearrange("(c p) t -> p c t", p=128),
                in_=zt.ap.rearrange("p (c t) -> p c t", c=NFC))],
              r=[zt], w=[bxout], dbg=f"xout2{tc}")

    for tc in range(NTC + 1):
        if 1 <= tc < NTC:
            pump(gen_front(tc), gen_hq(tc - 1), 0.5)
        elif tc < NTC:
            for _ in gen_front(tc):
                pass
        else:
            for _ in gen_hq(tc - 1):
                pass
        if tc >= 1:
            emit_down(tc - 1)
    A.release()
    return bxout


def build_l1attn(P, A, io, C, X, banks):
    xin, xout = io["xin"], io["xout"]
    bxout = Buf("xout3")
    bks, bvs, bas = Buf("kspill3"), Buf("vspill3"), Buf("aospill3")
    bsch = [Buf("scdram0"), Buf("scdram1")]
    badh = [Buf("adram0"), Buf("adram1")]

    A.mark()
    A.mark_top()
    qfm = [A.alloc_top(f"qfm{c}", 128, T, BF16) for c in range(NFC)]
    A.mark()
    xnb = [A.alloc(f"x3n{c}", 128, T, BF16) for c in range(NFC)]
    wqb = [A.alloc(f"w3{i}", 128, NFC * 128, BF16) for i in range(2)]
    kvstage = [A.alloc(f"kvst{i}", 128, 512, BF16) for i in range(2)]
    xin_t = [A.alloc(f"x3t{i}", 128, NFC * 512, F32) for i in range(2)]
    for tc in range(NTC):
        xt = xin_t[tc % 2]
        src = xin[:, tc * 512:(tc + 1) * 512].rearrange("(c p) t -> p c t", p=128)
        P.dma(C["xin"], [lambda e, xt=xt, src=src: e.dma_start(
                out=xt.ap.rearrange("p (c t) -> p c t", c=NFC), in_=src)],
              w=[xt], dbg=f"x3in{tc}")
        for c in range(NFC):
            P.op("v", lambda e, xt=xt, c=c, tc=tc: e.tensor_copy(
                    xnb[c].ap[:, tc * 512:(tc + 1) * 512], xt.ap[:, c * 512:(c + 1) * 512]),
                 r=[xt], w=[xnb[c]], dbg=f"x3c{tc}_{c}")
    for j in range(3 * NFC):
        kind, jj = divmod(j, NFC)
        wq = wqb[j % 2]
        wsrc = io[["wqT", "wkT", "wvT"][kind]].rearrange("(c p) o -> p c o", p=128)[:, :, jj * 128:(jj + 1) * 128]
        P.dma(C["w"], [lambda e, wq=wq, wsrc=wsrc: e.dma_start(
                out=wq.ap.rearrange("p (c o) -> p c o", c=NFC), in_=wsrc)],
              w=[wq], dbg=f"w3{j}")
        if kind == 0:
            P.op("v", lambda e, wq=wq: e.tensor_scalar_mul(wq.ap, wq.ap, 0.125),
                 r=[wq], w=[wq], dbg=f"w3s{j}")
        for tc in range(NTC):
            pb = banks[4 + ((j * NTC + tc) % 4)]
            for c in range(NFC):
                P.op("t", lambda e, wq=wq, c=c, tc=tc, pb=pb: e.matmul(
                        pb.ap, wq.ap[:, c * 128:(c + 1) * 128],
                        xnb[c].ap[:, tc * 512:(tc + 1) * 512],
                        start=(c == 0), stop=(c == NFC - 1)),
                     r=[wq, xnb[c]], w=[pb], dbg=f"p3{j}_{tc}_{c}")
            if kind == 0:
                P.op("s", lambda e, pb=pb, jj=jj, tc=tc: e.activation(
                        qfm[jj].ap[:, tc * 512:(tc + 1) * 512], pb.ap, AF.Copy),
                     r=[pb], w=[qfm[jj]], dbg=f"qe3{j}_{tc}")
            else:
                st = kvstage[tc % 2]
                P.op("s", lambda e, pb=pb, st=st: e.activation(st.ap, pb.ap, AF.Copy),
                     r=[pb], w=[st], dbg=f"kve3{j}_{tc}")
                spill = io["kspill"] if kind == 1 else io["vspill"]
                bsp = bks if kind == 1 else bvs
                P.dma(C["spillout"], [lambda e, st=st, spill=spill, jj=jj, tc=tc: e.dma_start(
                        out=spill[jj * 128:(jj + 1) * 128, tc * 512:(tc + 1) * 512], in_=st.ap)],
                      r=[st], w=[bsp], dbg=f"kvs3{j}_{tc}")
    A.release()
    HT = T // 2
    NTQ = HT // 512
    A.mark()
    kb = [A.alloc(f"kb{i}", 128, HT, BF16) for i in range(2)]
    prod = [A.alloc(f"pr{i}", 128, HT, BF16) for i in range(2)]
    scstg = [A.alloc("scst0", 16, HT, F32)]
    scsb = A.alloc("scsb", 16, 8 * 512, F32)
    asbh = [A.alloc(f"asbh{i}", 16, 8 * 512, BF16) for i in range(2)]
    mx1 = A.alloc("mx1", 16, 2, F32)
    mx2 = A.alloc("mx2", 16, 2, F32)
    sm1 = A.alloc("sm1", 16, 512, F32)
    sm2 = A.alloc("sm2", 16, 512, F32)
    rd = A.alloc("rd", 16, 512, F32)
    rdf = A.alloc("rdf", 16, 512, F32)
    rdb16 = A.alloc("rdb16", 16, 512, BF16)
    mxM = A.alloc("mxM", 16, 2, F32)
    cf = A.alloc("cf", 16, 2, F32)
    scv = io["scdram"].rearrange("p (h t) -> p h t", h=NH)
    adv = io["adram"].rearrange("p (h t) -> p h t", h=NH)

    def gen_scores(h):
        t0 = h * HT
        for hk in range(NH):
            kbt = kb[hk % 2]
            P.dma(C["kin"], [lambda e, kbt=kbt, hk=hk, t0=t0: e.dma_start(
                    out=kbt.ap,
                    in_=io["kspill"][hk * 64:(hk + 1) * 64, t0:t0 + HT]
                    .unsqueeze(0).broadcast_to([2, 64, HT]))],
                  r=[bks], w=[kbt], dbg=f"kb{h}_{hk}")
            for c in range(NFC):
                pr = prod[c % 2]
                P.op("v", lambda e, pr=pr, c=c, kbt=kbt, t0=t0: e.tensor_tensor(
                        pr.ap, qfm[c].ap[:, t0:t0 + HT], kbt.ap, ALU.mult),
                     r=[qfm[c], kbt], w=[pr], dbg=f"pr{h}_{hk}_{c}")
                for tq in range(NTQ):
                    bk = banks[4 * h + tq]
                    P.op("t", lambda e, pr=pr, c=c, tq=tq, bk=bk: e.matmul(
                            bk.ap[0:16, :], X["e2"].ap[:, c * 16:(c + 1) * 16],
                            pr.ap[:, tq * 512:(tq + 1) * 512],
                            start=(c == 0), stop=(c == NFC - 1)),
                         r=[pr, X["e2"]], w=[bk], dbg=f"sc{h}_{hk}_{c}_{tq}")
            st = scstg[0]
            for tq in range(NTQ):
                bk = banks[4 * h + tq]
                P.op("s", lambda e, bk=bk, tq=tq, st=st: e.activation(
                        st.ap[:, tq * 512:(tq + 1) * 512], bk.ap[0:16, :], AF.Copy),
                     r=[bk], w=[st], dbg=f"sce{h}_{hk}_{tq}")
            P.dma(C["spillout"], [lambda e, hk=hk, st=st, t0=t0: e.dma_start(
                    out=scv[:, hk, t0:t0 + HT], in_=st.ap)],
                  r=[st], w=[bsch[h]], dbg=f"scout{h}_{hk}")
            yield

    def gen_softmax(h):
        for tl in range(NTQ):
            tcc = NTQ * h + tl
            tsl = slice(tcc * 512, (tcc + 1) * 512)
            for i, mxb, smb in ((0, mx1, sm1), (1, mx2, sm2)):
                P.dma(C["xin"], [lambda e, i=i, tsl=tsl: e.dma_start(
                        out=scsb.ap.rearrange("p (h t) -> p h t", h=8),
                        in_=scv[:, 8 * i:8 * i + 8, tsl])],
                      r=[bsch[h]], w=[scsb], dbg=f"smin{tcc}_{i}")
                P.op("v", lambda e, mxb=mxb: e.tensor_reduce(
                        mxb.ap[:, 0:1], scsb.ap, AX.X, ALU.max),
                     r=[scsb], w=[mxb], dbg=f"smmax{tcc}_{i}")
                P.op("v", lambda e, mxb=mxb: e.tensor_scalar_mul(
                        mxb.ap[:, 1:2], mxb.ap[:, 0:1], -1.0),
                     r=[mxb], w=[mxb], dbg=f"smneg{tcc}_{i}")
                P.op("s", lambda e, i=i, mxb=mxb: e.activation(
                        asbh[i].ap, scsb.ap, AF.Exp, bias=mxb.ap[:, 1:2]),
                     r=[scsb, mxb], w=[asbh[i]], dbg=f"smexp{tcc}_{i}")
                P.op("v", lambda e, i=i, smb=smb: e.tensor_reduce(
                        smb.ap, asbh[i].ap.rearrange("p (h t) -> p t h", h=8), AX.X, ALU.add),
                     r=[asbh[i]], w=[smb], dbg=f"smsum{tcc}_{i}")
                yield
            P.op("v", lambda e: e.tensor_tensor(
                    mxM.ap[:, 0:1], mx1.ap[:, 0:1], mx2.ap[:, 0:1], ALU.max),
                 r=[mx1, mx2], w=[mxM], dbg=f"smM{tcc}")
            P.op("v", lambda e: e.tensor_scalar_mul(mxM.ap[:, 1:2], mxM.ap[:, 0:1], -1.0),
                 r=[mxM], w=[mxM], dbg=f"smMn{tcc}")
            for i, mxb in ((0, mx1), (1, mx2)):
                P.op("s", lambda e, i=i, mxb=mxb: e.activation(
                        cf.ap[:, i:i + 1], mxb.ap[:, 0:1], AF.Exp, bias=mxM.ap[:, 1:2]),
                     r=[mxb, mxM], w=[cf], dbg=f"smcf{tcc}_{i}")
            for i, smb in ((0, sm1), (1, sm2)):
                P.op("v", lambda e, i=i, smb=smb: e.tensor_scalar_mul(
                        smb.ap, smb.ap, cf.ap[:, i:i + 1]),
                     r=[smb, cf], w=[smb], dbg=f"smcs{tcc}_{i}")
            P.op("v", lambda e: e.tensor_tensor(sm1.ap, sm1.ap, sm2.ap, ALU.add),
                 r=[sm1, sm2], w=[sm1], dbg=f"smsumc{tcc}")
            P.op("v", lambda e: e.reciprocal(rd.ap, sm1.ap), r=[sm1], w=[rd], dbg=f"smrd{tcc}")
            yield
            for i in (0, 1):
                P.op("v", lambda e, i=i: e.tensor_scalar_mul(rdf.ap, rd.ap, cf.ap[:, i:i + 1]),
                     r=[rd, cf], w=[rdf], dbg=f"smri{tcc}_{i}")
                P.op("v", lambda e: e.tensor_copy(rdb16.ap, rdf.ap),
                     r=[rdf], w=[rdb16], dbg=f"smrb{tcc}_{i}")
                P.op("v", lambda e, i=i: e.tensor_tensor(
                        asbh[i].ap.rearrange("p (h t) -> p h t", h=8),
                        asbh[i].ap.rearrange("p (h t) -> p h t", h=8),
                        rdb16.ap.unsqueeze(1).broadcast_to([16, 8, 512]), ALU.mult),
                     r=[asbh[i], rdb16], w=[asbh[i]], dbg=f"smdiv{tcc}_{i}")
                P.dma(C["spillout"], [lambda e, i=i, tsl=tsl: e.dma_start(
                        out=adv[:, 8 * i:8 * i + 8, tsl],
                        in_=asbh[i].ap.rearrange("p (h t) -> p h t", h=8))],
                      r=[asbh[i]], w=[badh[h]], dbg=f"smout{tcc}_{i}")
                yield

    for _ in gen_scores(0):
        pass
    sm0 = gen_softmax(0)
    pump(gen_scores(1), sm0, 1)
    A.release_top()
    A.mark_top()
    vfm2 = [[A.alloc_top(f"vf{h}_{c}", 128, HT, BF16) for c in range(NFC)] for h in range(2)]
    arep = [A.alloc_top(f"ar{i}", 128, HT, BF16) for i in range(6)]
    prod2 = [A.alloc_top(f"p23{i}", 128, HT, BF16) for i in range(5)]
    aostg = [A.alloc_top(f"aos{i}", 64, HT, BF16) for i in range(2)]
    avchan = [C["qin"], C["gw"], C["sdma"]]
    def gen_av(h):
        t0 = h * HT
        vfm = vfm2[h]
        P.dma(C["aoin"],
              [(lambda e, c=c, t0=t0: e.dma_start(
                    out=vfm[c].ap, in_=io["vspill"][c * 128:(c + 1) * 128, t0:t0 + HT]))
               for c in range(NFC)],
              r=[bvs], w=vfm, dbg=f"vres{h}")
        for hq in range(NH):
            for c2 in range(NFC):
                k = hq * NFC + c2
                ar = arep[k % 6]
                P.dma(avchan[k % 3], [lambda e, ar=ar, hq=hq, c2=c2, t0=t0: e.dma_start(
                        out=ar.ap,
                        in_=adv[hq, 2 * c2:2 * c2 + 2, t0:t0 + HT]
                        .unsqueeze(1).broadcast_to([2, 64, HT]))],
                      r=[badh[h]], w=[ar], dbg=f"ar{h}_{hq}_{c2}")
                pr = prod2[k % 5]
                P.op("v", lambda e, pr=pr, ar=ar, c2=c2: e.tensor_tensor(
                        pr.ap, ar.ap, vfm[c2].ap, ALU.mult),
                     r=[ar, vfm[c2]], w=[pr], dbg=f"p2{h}_{hq}_{c2}")
                for tq in range(NTQ):
                    bk = banks[4 * h + tq]
                    P.op("t", lambda e, pr=pr, tq=tq, bk=bk, c2=c2: e.matmul(
                            bk.ap[0:64, :], X["e3"].ap, pr.ap[:, tq * 512:(tq + 1) * 512],
                            start=(c2 == 0), stop=(c2 == NFC - 1)),
                         r=[pr, X["e3"]], w=[bk], dbg=f"av{h}_{hq}_{c2}_{tq}")
            ast = aostg[hq % 2]
            for tq in range(NTQ):
                bk = banks[4 * h + tq]
                P.op("s", lambda e, bk=bk, tq=tq, ast=ast: e.activation(
                        ast.ap[:, tq * 512:(tq + 1) * 512], bk.ap[0:64, :], AF.Copy),
                     r=[bk], w=[ast], dbg=f"aoe{h}_{hq}_{tq}")
            P.dma(C["spillout"], [lambda e, hq=hq, ast=ast, t0=t0: e.dma_start(
                    out=io["aospill"][hq * 64:(hq + 1) * 64, t0:t0 + HT], in_=ast.ap)],
                  r=[ast], w=[bas], dbg=f"ao3{h}_{hq}")
            yield

    pump(gen_av(0), gen_softmax(1), 1)
    for _ in gen_av(1):
        pass
    A.release()
    A.release_top()
    ob = emit_outproj_ln(P, A, io, C, X, banks, "wfcT", xin, xout, io["aospill"], bas, bxout)
    A.release()
    return ob


def build_l1ffn(P, A, io, C, X, banks):
    xin, xout = io["xin"], io["xout"]
    bxout = Buf("xout4")
    bxg = Buf("xgdram")
    bsw1, bsw3, bsw2 = Buf("sw1b"), Buf("sw3b"), Buf("sw2b")
    HSC = HS // 128

    A.mark()
    sb1 = A.alloc("sb1", 128, HSC, F32)
    sb3 = A.alloc("sb3", 128, HSC, F32)
    b1c = A.alloc("b1c", 128, NFC * NE, F32)
    bcomb = A.alloc("bcomb", 128, NFC * NE, F32)
    sb2h = A.alloc("sb2h", 128, NFC, F32)
    P.dma(C["const"], [
        lambda e: e.dma_start(out=sb1.ap, in_=io["sb1r"]),
        lambda e: e.dma_start(out=sb3.ap, in_=io["sb3r"]),
        lambda e: e.dma_start(out=b1c.ap.rearrange("p (h e) -> p h e", e=NE), in_=io["b1r"]),
        lambda e: e.dma_start(out=bcomb.ap.rearrange("p (h e) -> p h e", e=NE), in_=io["b2r"]),
        lambda e: e.dma_start(out=sb2h.ap, in_=io["sb2r"]),
    ], w=[sb1, sb3, b1c, bcomb, sb2h], dbg="biases")
    bcn = A.alloc("bcn", 128, NFC * NE, F32)
    P.op("v", lambda e: e.tensor_scalar_mul(sb2h.ap, sb2h.ap, 0.5), r=[sb2h], w=[sb2h])
    for r in range(NE):
        e0, e1 = (3 * r) % NE, (3 * r + 7) % NE
        for oc in range(NFC):
            P.op("v", lambda e, oc=oc, r=r, e0=e0, e1=e1: e.tensor_tensor(
                    bcn.ap[:, oc * NE + r: oc * NE + r + 1],
                    bcomb.ap[:, oc * NE + e0: oc * NE + e0 + 1],
                    bcomb.ap[:, oc * NE + e1: oc * NE + e1 + 1], ALU.add),
                 r=[bcomb], w=[bcn], dbg=f"bc{r}_{oc}")
    P.op("v", lambda e: e.tensor_scalar_mul(bcn.ap, bcn.ap, 0.25), r=[bcn], w=[bcn])
    for oc in range(NFC):
        P.op("v", lambda e, oc=oc: e.tensor_tensor(
                bcn.ap[:, oc * NE:(oc + 1) * NE],
                bcn.ap[:, oc * NE:(oc + 1) * NE],
                sb2h.ap[:, oc:oc + 1].broadcast_to([128, NE]), ALU.add),
             r=[bcn, sb2h], w=[bcn], dbg=f"bcs{oc}")

    A.mark()
    swst = A.alloc("swst", 128, HS, BF16)
    for (wname, dramb, bflag, fold) in (("sw1", "sw1b", bsw1, 1.0), ("sw3", "sw3b", bsw3, 1.0),
                                        ("sw2", "sw2b", bsw2, 0.5)):
        nr = io[wname].shape[0]
        ncols = io[wname].shape[1]
        for a in range(nr // 128):
            P.dma(C["w"], [lambda e, a=a, wname=wname, ncols=ncols: e.dma_start(
                    out=swst.ap[:, 0:ncols], in_=io[wname][a * 128:(a + 1) * 128, :])],
                  w=[swst], dbg=f"swl{wname}{a}")
            if fold != 1.0:
                P.op("v", lambda e, ncols=ncols, fold=fold: e.tensor_scalar_mul(
                        swst.ap[:, 0:ncols], swst.ap[:, 0:ncols], fold),
                     r=[swst], w=[swst], dbg=f"swf{wname}{a}")
            P.dma(C["spillout"], [lambda e, a=a, dramb=dramb, ncols=ncols: e.dma_start(
                    out=io[dramb][a * 128:(a + 1) * 128, :], in_=swst.ap[:, 0:ncols])],
                  r=[swst], w=[bflag], dbg=f"sws{wname}{a}")
    A.release()

    A.mark()
    xin_t = [A.alloc(f"x4t{i}", 128, NFC * 512, F32) for i in range(2)]
    xgs = [A.alloc(f"xgs{i}", 128, NFC * 512, F32) for i in range(2)]
    for tc in range(NTC):
        xt = xin_t[tc % 2]
        xg_ = xgs[tc % 2]
        src = xin[:, tc * 512:(tc + 1) * 512].rearrange("(c p) t -> p c t", p=128)
        P.dma(C["xin"], [lambda e, xt=xt, src=src: e.dma_start(
                out=xt.ap.rearrange("p (c t) -> p c t", c=NFC), in_=src)],
              w=[xt], dbg=f"x4in{tc}")
        for c in range(NFC):
            P.op("v", lambda e, xt=xt, xg_=xg_, c=c: e.tensor_copy(
                    xg_.ap[:, c * 512:(c + 1) * 512].rearrange("p (r m) -> p r m", r=8),
                    xt.ap[:, c * 512:(c + 1) * 512].rearrange("p (m r) -> p r m", r=8)),
                 r=[xt], w=[xg_], dbg=f"cmp{tc}_{c}")
        for c in range(NFC):
            P.dma(C["spillout"], [lambda e, xg_=xg_, c=c, tc=tc: e.dma_start(
                    out=io["xgdram"][c * 128:(c + 1) * 128, :]
                    .rearrange("p (r m) -> p r m", m=CW)[:, :, tc * 64:(tc + 1) * 64],
                    in_=xg_.ap[:, c * 512:(c + 1) * 512].rearrange("p (r m) -> p r m", r=8))],
                  r=[xg_], w=[bxg], dbg=f"xgo{tc}_{c}")
    A.release()
    xg = [A.alloc(f"xg{c}", 128, T, BF16) for c in range(NFC)]
    es = [A.alloc(f"es{c}", 128, T, BF16) for c in range(NFC)]
    P.dma(C["gw"],
          [(lambda e, c=c: e.dma_start(out=xg[c].ap, in_=io["xgdram"][c * 128:(c + 1) * 128, :]))
           for c in range(NFC)],
          r=[bxg], w=xg, dbg="xgcast")

    A.mark()
    A.mark_top()
    w1b = [A.alloc_top(f"w1b{i}", 128, NFC * D, BF16) for i in range(2)]
    w2b = [A.alloc_top(f"w2b{i}", 128, NFC * D, BF16) for i in range(2)]
    hE = [A.alloc(f"hE{i}", 128, CW, BF16) for i in range(NFC)]
    phb = [banks[0], banks[2]]
    pob = [banks[1], banks[3]]
    written = set()
    for ex in range(NE):
        w1 = w1b[ex % 2]
        w2 = w2b[ex % 2]
        P.dma(C["gw"], [
            lambda e, ex=ex, w1=w1: e.dma_start(
                out=w1.ap.rearrange("p (c o) -> p c o", c=NFC),
                in_=io["w1e"][ex].rearrange("(c p) o -> p c o", p=128)),
        ], w=[w1], dbg=f"we1_{ex}")
        P.dma(C["w"], [
            lambda e, ex=ex, w2=w2: e.dma_start(
                out=w2.ap.rearrange("p (c o) -> p c o", c=NFC),
                in_=io["w2e"][ex].rearrange("(c p) o -> p c o", p=128)),
        ], w=[w2], dbg=f"we2_{ex}")
        P.op("v", lambda e, w2=w2: e.tensor_scalar_mul(w2.ap, w2.ap, 0.25), r=[w2], w=[w2],
             dbg=f"wf{ex}")
        rs = [r for r in range(NE) if (3 * r) % NE == ex or (3 * r + 7) % NE == ex]
        for r in rs:
            for hc in range(NFC):
                ph = phb[hc % 2]
                for c in range(NFC):
                    P.op("t", lambda e, hc=hc, c=c, r=r, w1=w1, ph=ph: e.matmul(
                            ph.ap[:, 0:CW], w1.ap[:, (c * NFC + hc) * 128:(c * NFC + hc + 1) * 128],
                            xg[c].ap[:, r * CW:(r + 1) * CW],
                            start=(c == 0), stop=(c == NFC - 1)),
                         r=[w1, xg[c]], w=[ph], dbg=f"e1_{ex}_{r}_{hc}_{c}")
                P.op("s", lambda e, hc=hc, ex=ex, ph=ph: e.activation(
                        hE[hc].ap, ph.ap[:, 0:CW], AF.Gelu, bias=b1c.ap[:, hc * NE + ex: hc * NE + ex + 1]),
                     r=[ph, b1c], w=[hE[hc]], dbg=f"gel{ex}_{r}_{hc}")
            for oc in range(NFC):
                po = pob[oc % 2]
                for hc in range(NFC):
                    P.op("t", lambda e, oc=oc, hc=hc, w2=w2, po=po: e.matmul(
                            po.ap[:, 0:CW], w2.ap[:, (hc * NFC + oc) * 128:(hc * NFC + oc + 1) * 128],
                            hE[hc].ap, start=(hc == 0), stop=(hc == NFC - 1)),
                         r=[w2, hE[hc]], w=[po], dbg=f"e2_{ex}_{r}_{oc}_{hc}")
                if r not in written:
                    P.op("v", lambda e, oc=oc, r=r, po=po: e.tensor_copy(
                            es[oc].ap[:, r * CW:(r + 1) * CW], po.ap[:, 0:CW]),
                         r=[po], w=[es[oc]], dbg=f"es0_{ex}_{r}_{oc}")
                else:
                    P.op("v", lambda e, oc=oc, r=r, po=po: e.tensor_tensor(
                            es[oc].ap[:, r * CW:(r + 1) * CW], po.ap[:, 0:CW],
                            es[oc].ap[:, r * CW:(r + 1) * CW], ALU.add),
                         r=[po, es[oc]], w=[es[oc]], dbg=f"es1_{ex}_{r}_{oc}")
            written.add(r)
    A.release()
    A.release_top()

    A.mark()
    sw1t = [A.alloc(f"s1t{i}", 128, NFC * 128, BF16) for i in range(2)]
    sw3t = [A.alloc(f"s3t{i}", 128, NFC * 128, BF16) for i in range(2)]
    sw2t = [A.alloc("s2t0", 128, HSC * 128, BF16)]
    mbf = [A.alloc(f"m{j}", 128, CW, BF16) for j in range(HSC)]
    abf = A.alloc("abf", 128, CW, BF16)
    xr4 = A.alloc("xr4", 128, NFC * CW, F32)
    zt = A.alloc("zt4", 128, NFC * CW, F32)
    zsq = A.alloc("zsq4", 128, CW, BF16)
    mu = A.alloc("mu4", 1, CW, F32)
    ms = A.alloc("ms4", 1, CW, F32)
    aa = A.alloc("aa4", 1, CW, F32)
    bb2 = A.alloc("bb24", 1, CW, F32)
    p1b, p3b = banks[0], banks[1]
    stA, stB = banks[6], banks[5]
    abank, bbank = banks[7], banks[4]
    for r in range(NE):
        P.dma(C["xin"], [lambda e, r=r: e.dma_start(
                out=xr4.ap.rearrange("p (c t) -> p c t", c=NFC),
                in_=io["xgdram"][:, r * CW:(r + 1) * CW].rearrange("(c p) t -> p c t", p=128))],
              r=[bxg], w=[xr4], dbg=f"xr4_{r}")
        for j in range(HSC):
            s1 = sw1t[j % 2]
            s3 = sw3t[j % 2]
            P.dma(C["w"], [
                lambda e, s1=s1, j=j: e.dma_start(
                    out=s1.ap.rearrange("p (c o) -> p c o", c=NFC),
                    in_=io["sw1b"].rearrange("(c p) o -> p c o", p=128)[:, :, j * 128:(j + 1) * 128]),
                lambda e, s3=s3, j=j: e.dma_start(
                    out=s3.ap.rearrange("p (c o) -> p c o", c=NFC),
                    in_=io["sw3b"].rearrange("(c p) o -> p c o", p=128)[:, :, j * 128:(j + 1) * 128]),
            ], r=[bsw1, bsw3], w=[s1, s3], dbg=f"swt{r}_{j}")
            for c in range(NFC):
                P.op("t", lambda e, s1=s1, c=c, r=r: e.matmul(
                        p1b.ap[:, 0:CW], s1.ap[:, c * 128:(c + 1) * 128], xg[c].ap[:, r * CW:(r + 1) * CW],
                        start=(c == 0), stop=(c == NFC - 1)),
                     r=[s1, xg[c]], w=[p1b], dbg=f"m1_{r}_{j}_{c}")
            for c in range(NFC):
                P.op("t", lambda e, s3=s3, c=c, r=r: e.matmul(
                        p3b.ap[:, 0:CW], s3.ap[:, c * 128:(c + 1) * 128], xg[c].ap[:, r * CW:(r + 1) * CW],
                        start=(c == 0), stop=(c == NFC - 1)),
                     r=[s3, xg[c]], w=[p3b], dbg=f"m3_{r}_{j}_{c}")
            P.op("s", lambda e, j=j: e.activation(abf.ap, p1b.ap[:, 0:CW], AF.Silu,
                                                  bias=sb1.ap[:, j:j + 1]),
                 r=[p1b, sb1], w=[abf], dbg=f"sil{r}_{j}")
            P.op("v", lambda e, j=j: e.tensor_scalar_add(mbf[j].ap, p3b.ap[:, 0:CW],
                                                         sb3.ap[:, j:j + 1]),
                 r=[p3b, sb3], w=[mbf[j]], dbg=f"u3{r}_{j}")
            P.op("v", lambda e, j=j: e.tensor_tensor(mbf[j].ap, abf.ap, mbf[j].ap, ALU.mult),
                 r=[abf, mbf[j]], w=[mbf[j]], dbg=f"mm{r}_{j}")
        for oc in range(NFC):
            s2 = sw2t[0]
            P.dma(C["gw"], [lambda e, s2=s2, oc=oc: e.dma_start(
                    out=s2.ap.rearrange("p (c o) -> p c o", c=HSC),
                    in_=io["sw2b"].rearrange("(c p) o -> p c o", p=128)[:, :, oc * 128:(oc + 1) * 128])],
                  r=[bsw2], w=[s2], dbg=f"sw2t{r}_{oc}")
            pb = banks[2 + (oc % 2)]
            for j in range(HSC):
                P.op("t", lambda e, s2=s2, j=j, pb=pb: e.matmul(
                        pb.ap[:, 0:CW], s2.ap[:, j * 128:(j + 1) * 128], mbf[j].ap,
                        start=(j == 0), stop=(j == HSC - 1)),
                     r=[s2, mbf[j]], w=[pb], dbg=f"m2_{r}_{oc}_{j}")
            P.op("v", lambda e, pb=pb, oc=oc, r=r: e.scalar_tensor_tensor(
                    zt.ap[:, oc * CW:(oc + 1) * CW], pb.ap[:, 0:CW],
                    bcn.ap[:, oc * NE + r: oc * NE + r + 1],
                    es[oc].ap[:, r * CW:(r + 1) * CW], ALU.add, ALU.add),
                 r=[pb, bcn, es[oc]], w=[zt], dbg=f"comb{r}_{oc}")
            P.op("v", lambda e, oc=oc: e.tensor_tensor(
                    zt.ap[:, oc * CW:(oc + 1) * CW], zt.ap[:, oc * CW:(oc + 1) * CW],
                    xr4.ap[:, oc * CW:(oc + 1) * CW], ALU.add),
                 r=[zt, xr4], w=[zt], dbg=f"zadd4_{r}_{oc}")
            P.op("s", lambda e, oc=oc: e.activation(zsq.ap, zt.ap[:, oc * CW:(oc + 1) * CW],
                                                    AF.Square),
                 r=[zt], w=[zsq], dbg=f"zsq4_{r}_{oc}")
            P.op("t", lambda e, oc=oc: e.matmul(stB.ap[0:1, 0:CW], X["onesb"].ap[:, 0:1], zsq.ap,
                                                start=(oc == 0), stop=(oc == NFC - 1)),
                 r=[zsq, X["onesb"]], w=[stB], dbg=f"s2st4_{r}_{oc}")
            P.op("t", lambda e, oc=oc: e.matmul(stA.ap[0:1, 0:CW], X["onesfc"].ap,
                                                zt.ap[:, oc * CW:(oc + 1) * CW],
                                                start=(oc == 0), stop=(oc == NFC - 1)),
                 r=[zt, X["onesfc"]], w=[stA], dbg=f"s1st4_{r}_{oc}")
        P.op("v", lambda e: e.tensor_scalar_mul(mu.ap, stA.ap[0:1, 0:CW], 1.0 / D), r=[stA], w=[mu])
        P.op("v", lambda e: e.tensor_scalar_mul(ms.ap, stB.ap[0:1, 0:CW], 1.0 / D), r=[stB], w=[ms])
        P.op("v", lambda e: e.tensor_tensor(bb2.ap, mu.ap, mu.ap, ALU.mult), r=[mu], w=[bb2])
        P.op("v", lambda e: e.tensor_tensor(ms.ap, ms.ap, bb2.ap, ALU.subtract), r=[ms, bb2], w=[ms])
        P.op("s", lambda e: e.activation(ms.ap, ms.ap, AF.Sqrt, bias=X["eps"].ap[0:1, :]),
             r=[ms, X["eps"]], w=[ms])
        P.op("v", lambda e: e.reciprocal(aa.ap, ms.ap), r=[ms], w=[aa])
        P.op("v", lambda e: e.tensor_tensor(bb2.ap, mu.ap, aa.ap, ALU.mult), r=[mu, aa], w=[bb2])
        P.op("v", lambda e: e.tensor_scalar_mul(bb2.ap, bb2.ap, -1.0), r=[bb2], w=[bb2])
        P.op("t", lambda e: e.matmul(abank.ap[:, 0:CW], X["onesf"].ap[0:1, :], aa.ap, start=True, stop=True),
             r=[aa, X["onesf"]], w=[abank], dbg=f"ab4{r}")
        P.op("t", lambda e: e.matmul(bbank.ap[:, 0:CW], X["onesf"].ap[0:1, :], bb2.ap, start=True, stop=True),
             r=[bb2, X["onesf"]], w=[bbank], dbg=f"bb4{r}")
        for oc in range(NFC):
            P.op("v", lambda e, oc=oc: e.tensor_tensor(
                    zt.ap[:, oc * CW:(oc + 1) * CW], zt.ap[:, oc * CW:(oc + 1) * CW],
                    abank.ap[:, 0:CW], ALU.mult),
                 r=[zt, abank], w=[zt], dbg=f"ln4m{r}_{oc}")
            P.op("v", lambda e, oc=oc: e.tensor_tensor(
                    zt.ap[:, oc * CW:(oc + 1) * CW], zt.ap[:, oc * CW:(oc + 1) * CW],
                    bbank.ap[:, 0:CW], ALU.add),
                 r=[zt, bbank], w=[zt], dbg=f"ln4a{r}_{oc}")
        P.dma(C["xout"], [lambda e, r=r: e.dma_start(
                out=xout[:, r * CW:(r + 1) * CW].rearrange("(c p) t -> p c t", p=128),
                in_=zt.ap.rearrange("p (c t) -> p c t", c=NFC))],
              r=[zt], w=[bxout], dbg=f"xout4{r}")
    A.release()
    A.release()
    return bxout


def make_chans(P):
    return {
        "const": P.chan("const", "g", 2),
        "w": P.chan("w", "g", 2),
        "xin": P.chan("xin", "q", 2),
        "spillout": P.chan("spout", "q", 2),
        "qin": P.chan("qin", "q", 4),
        "kin": P.chan("kin", "q", 2),
        "aoin": P.chan("aoin", "q", 2),
        "xout": P.chan("xout", "q", 2),
        "gw": P.chan("gw", "g", 4),
        "sdma": P.chan("sdma", "s", 4),
        "sc": P.chan("sc", "g", 2),
    }

CONST_SPECS = {
    "c_ident": ((128, 128), BF16),
    "c_onesb": ((128, 128), BF16),
    "c_onesf": ((1, 128), F32),
    "c_onesfc": ((128, 1), F32),
    "c_eps": ((128, 1), F32),
    "c_identf": ((128, 128), F32),
    "c_e2": ((128, 128), BF16),
    "c_e3": ((128, 64), BF16),
}

def const_arrays():
    return {
        "c_ident": np.eye(128, dtype=np.float32).astype(BF),
        "c_onesb": np.ones((128, 128), dtype=np.float32).astype(BF),
        "c_onesf": np.ones((1, 128), dtype=np.float32),
        "c_onesfc": np.ones((128, 1), dtype=np.float32),
        "c_eps": np.full((128, 1), 1e-5, dtype=np.float32),
        "c_identf": np.eye(128, dtype=np.float32),
        "c_e2": _mk_e2(),
        "c_e3": _mk_e3(),
    }

def _mk_e2():
    e2 = np.zeros((128, 8, 16), np.float32)
    for c in range(8):
        for p in range(128):
            e2[p, c, 2 * c + p // 64] = 1.0
    return e2.reshape(128, 128).astype(BF)

def _mk_e3():
    e3 = np.zeros((128, 64), np.float32)
    for p in range(128):
        e3[p, p % 64] = 1.0
    return e3.astype(BF)


def build_program(io_spec, phase_fns):
    nc = bass.Bass("TRN2", target_bir_lowering=False)
    io = {}
    for name, (shape, dt, kind) in io_spec.items():
        io[name] = nc.dram_tensor(name, list(shape), dt, kind=kind).ap()
    ctxs = []
    sbcm = nc.sbuf_tensor("arena", [128, SB_BYTES // 4], F32)
    sbh = sbcm.__enter__(); ctxs.append(sbcm)
    cmp_ = nc.psum_tensor("psarena", [128, 4096], F32)
    ph_ = cmp_.__enter__(); ctxs.append(cmp_)
    psall = ph_[:]
    banks = [Buf(f"bank{i}", psall[:, i * 512:(i + 1) * 512]) for i in range(8)]
    banks.append(Buf("psall", psall))

    P = Prog(nc)
    A = Arena(sbh[:])
    C = make_chans(P)
    X = setup_consts(P, A, io, C)
    outs = []
    for fn in phase_fns:
        ob = fn(P, A, io, C, X, banks)
        if ob is not None:
            outs.append(ob)
    P.op("q", lambda e: e.nop(), r=outs, dbg="final_wait")
    P.finalize()
    P.replay()
    for h in reversed(ctxs):
        h.__exit__(None, None, None)
    P.close()
    return nc

def prep_host(inputs):
    c = const_arrays()
    w = {}
    asc = np.ascontiguousarray
    w["w_qkv"] = inputs["L0_attn_w_qkv"]
    w["w_out"] = inputs["L0_attn_w_out"]
    w["gamma"] = asc(inputs["L0_attn_gamma"].reshape(NFC, 128).T)
    w["wqT"] = asc(inputs["L1_attn_Wq"].T)
    w["wkT"] = asc(inputs["L1_attn_Wk"].T)
    w["wvT"] = asc(inputs["L1_attn_Wv"].T)
    w["wfcT"] = asc(inputs["L1_attn_Wfc"].T)
    w["wg"] = inputs["L0_ffn_w_gate"]
    w["wu"] = inputs["L0_ffn_w_up"]
    w["wd"] = inputs["L0_ffn_w_down"]
    w["sw1"] = inputs["L1_ffn_sw1"]
    w["sw3"] = inputs["L1_ffn_sw3"]
    w["sw2"] = inputs["L1_ffn_sw2"]
    w["sb1r"] = asc(inputs["L1_ffn_sb1"].reshape(HS // 128, 128).T)
    w["sb3r"] = asc(inputs["L1_ffn_sb3"].reshape(HS // 128, 128).T)
    w["sb2r"] = asc(inputs["L1_ffn_sb2"].reshape(NFC, 128).T)
    w["w1e"] = inputs["L1_ffn_w1"]
    w["w2e"] = inputs["L1_ffn_w2"]
    w["b1r"] = asc(inputs["L1_ffn_b1"].T.reshape(NFC, 128, NE).transpose(1, 0, 2))
    w["b2r"] = asc(inputs["L1_ffn_b2"].T.reshape(NFC, 128, NE).transpose(1, 0, 2))
    _ = inputs["L1_ffn_t_emb"].shape
    xs = [asc(inputs["L0_attn_x"][b].T) for b in range(8)]
    return c, w, xs

def unpermute_tokens(yT):
    yt = yT.T
    return np.ascontiguousarray(yt.reshape(NE, T // NE, D).transpose(1, 0, 2).reshape(T, D))



def _spec_p1():
    s = {
        "xin": ((D, T), F32, "ExternalInput"),
        "xout": ((D, T), F32, "ExternalOutput"),
        "w_qkv": ((D, 3 * D), F32, "ExternalInput"),
        "w_out": ((D, D), F32, "ExternalInput"),
        "gamma": ((128, NFC), F32, "ExternalInput"),
        "qspill": ((D, T), BF16, "Internal"),
        "kspill": ((D, T), BF16, "Internal"),
        "aospill": ((D, T), BF16, "Internal"),
    }
    return s, ["w_qkv", "w_out", "gamma"]

def _spec_p2():
    s = {
        "xin": ((D, T), F32, "ExternalInput"),
        "xout": ((D, T), F32, "ExternalOutput"),
        "wg": ((FF, D), F32, "ExternalInput"),
        "wu": ((FF, D), F32, "ExternalInput"),
        "wd": ((D, FF), F32, "ExternalInput"),
        "wgq": ((D, FF), BF16, "Internal"),
        "wuq": ((D, FF), BF16, "Internal"),
        "wdq": ((FF, D), BF16, "Internal"),
        "scx_s": ((64, 512), F32, "Internal"),
        "scx_r": ((64, 512), F32, "Internal"),
        "sch_s0": ((176, 512), F32, "Internal"),
        "sch_r0": ((176, 512), F32, "Internal"),
        "sch_s1": ((176, 512), F32, "Internal"),
        "sch_r1": ((176, 512), F32, "Internal"),
    }
    return s, ["wg", "wu", "wd"]

def _spec_p3():
    s = {
        "xin": ((D, T), F32, "ExternalInput"),
        "xout": ((D, T), F32, "ExternalOutput"),
        "wqT": ((D, D), F32, "ExternalInput"),
        "wkT": ((D, D), F32, "ExternalInput"),
        "wvT": ((D, D), F32, "ExternalInput"),
        "wfcT": ((D, D), F32, "ExternalInput"),
        "kspill": ((D, T), BF16, "Internal"),
        "vspill": ((D, T), BF16, "Internal"),
        "aospill": ((D, T), BF16, "Internal"),
        "scdram": ((16, NH * T), F32, "Internal"),
        "adram": ((16, NH * T), BF16, "Internal"),
    }
    return s, ["wqT", "wkT", "wvT", "wfcT"]

def _spec_p4():
    s = {
        "xin": ((D, T), F32, "ExternalInput"),
        "xout": ((D, T), F32, "ExternalOutput"),
        "sw1": ((D, HS), F32, "ExternalInput"),
        "sw3": ((D, HS), F32, "ExternalInput"),
        "sw2": ((HS, D), F32, "ExternalInput"),
        "w1e": ((NE, D, D), F32, "ExternalInput"),
        "w2e": ((NE, D, D), F32, "ExternalInput"),
        "sb1r": ((128, HS // 128), F32, "ExternalInput"),
        "sb3r": ((128, HS // 128), F32, "ExternalInput"),
        "sb2r": ((128, NFC), F32, "ExternalInput"),
        "b1r": ((128, NFC, NE), F32, "ExternalInput"),
        "b2r": ((128, NFC, NE), F32, "ExternalInput"),
        "xgdram": ((D, T), F32, "Internal"),
        "sw1b": ((D, HS), BF16, "Internal"),
        "sw3b": ((D, HS), BF16, "Internal"),
        "sw2b": ((HS, D), BF16, "Internal"),
    }
    return s, ["sw1", "sw3", "sw2", "w1e", "w2e", "sb1r", "sb3r", "sb2r", "b1r", "b2r"]

_PHASES = [
    (_spec_p1, lambda P, A, io, C, X, banks: build_attn(P, A, io, C, X, banks, 0)),
    (_spec_p2, build_l0ffn),
    (_spec_p3, build_l1attn),
    (_spec_p4, build_l1ffn),
]

_prog_cache = {}

def _spec_fused():
    s = {
        "xin": ((D, T), F32, "ExternalInput"),
        "x1": ((D, T), F32, "Internal"),
        "x2": ((D, T), F32, "Internal"),
        "x3": ((D, T), F32, "Internal"),
        "xout": ((D, T), F32, "ExternalOutput"),
    }
    wall = []
    for i, (spec_fn, _) in enumerate(_PHASES):
        sp, wk = spec_fn()
        wall += wk
        for name, v in sp.items():
            if name in ("xin", "xout"):
                continue
            key = name if name not in s else f"{name}_p{i}"
            assert key not in s or s[key] == v, key
            s[key] = v
    for nm in ("kspill", "aospill"):
        s[nm + "3"] = ((D, T), BF16, "Internal")
    return s, wall

def _phase_io(io, i):
    xs = ["xin", "x1", "x2", "x3", "xout"]
    m = dict(io)
    m["xin"] = io[xs[i]]
    m["xout"] = io[xs[i + 1]]
    if i == 2:
        m["kspill"] = io["kspill3"]
        m["aospill"] = io["aospill3"]
    return m

def build_fused():
    spec, _ = _spec_fused()
    for n, (s, dt) in CONST_SPECS.items():
        spec[n] = (s, dt, "ExternalInput")
    def mk(i):
        def fn(P, A, io, C, X, banks):
            return _PHASES[i][1](P, A, _phase_io(io, i), C, X, banks)
        return fn
    return build_program(spec, [mk(i) for i in range(4)])

def _get_prog(i):
    if i not in _prog_cache:
        spec_fn, build_fn = _PHASES[i]
        spec, _ = spec_fn()
        for n, (s, dt) in CONST_SPECS.items():
            spec[n] = (s, dt, "ExternalInput")
        _prog_cache[i] = build_program(spec, [build_fn])
    return _prog_cache[i]

FUSED = True

def kernel(**inputs):
    consts, w, xs = prep_host(inputs)
    if FUSED:
        if "fused" not in _prog_cache:
            _prog_cache["fused"] = build_fused()
        nc = _prog_cache["fused"]
        _, wkeys = _spec_fused()
        in_maps = []
        for b in range(8):
            m = dict(consts)
            m["xin"] = xs[b]
            for k in wkeys:
                m[k] = w[k]
            in_maps.append(m)
        res = run_bass_kernel_spmd(nc, in_maps, core_ids=list(range(8)))
        cur = [np.ascontiguousarray(res.results[b]["xout"]) for b in range(8)]
    else:
        cur = xs
        for i in range(4):
            nc = _get_prog(i)
            _, wkeys = _PHASES[i][0]()
            in_maps = []
            for b in range(8):
                m = dict(consts)
                m["xin"] = cur[b]
                for k in wkeys:
                    m[k] = w[k]
                in_maps.append(m)
            res = run_bass_kernel_spmd(nc, in_maps, core_ids=list(range(8)))
            cur = [np.ascontiguousarray(res.results[b]["xout"]) for b in range(8)]
    out = np.stack([unpermute_tokens(cur[b]) for b in range(8)], axis=0)
    return out.astype(np.float32)


# version 23
# speedup vs baseline: 1.1368x; 1.0001x over previous
import numpy as np
import ml_dtypes
import concourse.bass as bass
import concourse.mybir as mybir
from concourse.bass_utils import run_bass_kernel_spmd

F32 = mybir.dt.float32
BF16 = mybir.dt.bfloat16
U8 = mybir.dt.uint8
AF = mybir.ActivationFunctionType
ALU = mybir.AluOpType
AX = mybir.AxisListType
BF = ml_dtypes.bfloat16

T = 4096
D = 1024
NH = 16
HD = 64
FF = 2816
HS = 2048
NE = 8
NFC = D // 128
NTC = T // 512
KC = T // 128
CW = T // NE

def set_T(t):
    global T, NTC, KC, CW
    T = t
    NTC = T // 512
    KC = T // 128
    CW = T // NE
SB_BYTES = 203 * 1024

ENGS = ("t", "v", "s", "g", "q")
ENGMAP = {"t": "tensor", "v": "vector", "s": "scalar", "g": "gpsimd", "q": "sync"}


class Buf:
    __slots__ = ("name", "ap", "aliases", "rng")
    def __init__(self, name, ap=None):
        self.name = name
        self.ap = ap
        self.aliases = []
        self.rng = None

class Op:
    __slots__ = ("eng", "fn", "seq", "waits", "inc_needed", "num", "sem", "incby",
                 "chan", "slot", "dbg", "blast")
    def __init__(self, eng, fn, seq, dbg=""):
        self.eng = eng
        self.fn = fn
        self.seq = seq
        self.waits = []
        self.inc_needed = False
        self.num = None
        self.sem = None
        self.incby = 1
        self.chan = None
        self.slot = None
        self.dbg = dbg
        self.blast = None

class Chan:
    def __init__(self, prog, name, queue, nslots=2):
        self.p = prog
        self.name = name
        self.queue = queue
        self.nslots = nslots
        self.sems = [prog.new_sem(f"{name}{i}") for i in range(nslots)]
        self.count = [0] * nslots
        self.batch = 0
        self.waiters = [[] for _ in range(nslots)]

class Arena:
    def __init__(self, base_ap_f32):
        self.base = base_ap_f32
        self.off = 0
        self.marks = []
        self.top = SB_BYTES
        self.tmarks = []
        self.hi = 0
        self.live = []
        self.freed = []
    def alloc(self, name, rows, cols, dt):
        esz = 4 if dt == F32 else (1 if dt == U8 else 2)
        off = (self.off + 63) & ~63
        nbytes = cols * esz
        assert off + nbytes <= self.top, f"SBUF arena overflow at {name}: {(off+nbytes)/1024:.1f}KB (top {self.top/1024:.1f}KB)"
        self.off = off + nbytes
        self.hi = max(self.hi, self.off)
        ap = self.base.bitcast(dt)[0:rows, off // esz: off // esz + cols]
        b = Buf(name, ap)
        b.rng = (off, off + nbytes)
        b.aliases = [f for f in self.freed if f.rng[0] < b.rng[1] and b.rng[0] < f.rng[1]]
        self.live.append(b)
        return b
    def alloc_top(self, name, rows, cols, dt):
        esz = 4 if dt == F32 else (1 if dt == U8 else 2)
        nbytes = cols * esz
        top = (self.top - nbytes) & ~63
        assert top >= self.off, f"SBUF arena overflow (top) at {name}"
        self.top = top
        ap = self.base.bitcast(dt)[0:rows, top // esz: top // esz + cols]
        b = Buf(name, ap)
        b.rng = (top, top + nbytes)
        b.aliases = [f for f in self.freed if f.rng[0] < b.rng[1] and b.rng[0] < f.rng[1]]
        self.live.append(b)
        return b
    def mark_top(self):
        self.tmarks.append(self.top)
    def release_top(self):
        hi = self.tmarks.pop()
        still = []
        for b in self.live:
            if b.rng[0] >= self.top and b.rng[1] <= hi:
                self.freed.append(b)
            else:
                still.append(b)
        self.live = still
        self.top = hi
    def mark(self):
        self.marks.append(self.off)
    def release(self):
        lo = self.marks.pop()
        still = []
        for b in self.live:
            if b.rng[0] >= lo:
                self.freed.append(b)
            else:
                still.append(b)
        self.live = still
        self.freed = [f for f in self.freed if f.rng[1] > 0]
        self.off = lo

class Prog:
    def __init__(self, nc):
        self.nc = nc
        self.E = {e: [] for e in ENGS}
        self.seq = 0
        self.state = {}
        self.sem_ctx = []
        self.esem = {}
        for e in ENGS:
            self.esem[e] = self.new_sem(f"eng_{e}")
    def new_sem(self, name):
        h = self.nc.semaphore(name=f"sm_{name}")
        sem = h.__enter__()
        self.sem_ctx.append(h)
        return sem
    def chan(self, name, queue, nslots=2):
        return Chan(self, name, queue, nslots)
    def _dep(self, o, producers):
        for p in producers:
            if p is None or p is o:
                continue
            if p.chan is not None and p.blast is not None and p.blast is not p:
                p = p.blast
                if p is o:
                    continue
            if p.eng == o.eng and p.eng == "t":
                continue
            if p not in o.waits:
                o.waits.append(p)
                p.inc_needed = True
                if p.chan is not None:
                    p.chan.waiters[p.slot].append(o)
    def _track(self, o, r, w):
        for b in w:
            if b.aliases and id(b) not in self.state:
                for ab in b.aliases:
                    st = self.state.get(id(ab))
                    if st:
                        self._dep(o, [st["w"]])
                        self._dep(o, st["r"])
                b.aliases = []
        for b in r:
            st = self.state.get(id(b))
            if st:
                self._dep(o, [st["w"]])
        for b in w:
            st = self.state.get(id(b))
            if st:
                self._dep(o, [st["w"]])
                self._dep(o, st["r"])
        for b in r:
            self.state.setdefault(id(b), {"w": None, "r": []})["r"].append(o)
        for b in w:
            self.state[id(b)] = {"w": o, "r": []}
    def op(self, eng, fn, r=(), w=(), dbg=""):
        o = Op(eng, fn, self.seq, dbg); self.seq += 1
        self._track(o, r, w)
        self.E[eng].append(o)
        return o
    def dma(self, chan, fns, r=(), w=(), dbg=""):
        s = chan.batch % chan.nslots
        gate = chan.waiters[s]
        chan.waiters[s] = []
        chan.batch += 1
        sem = chan.sems[s]
        ops = []
        for i, fn in enumerate(fns):
            o = Op(chan.queue, fn, self.seq, dbg); self.seq += 1
            o.sem = sem
            o.incby = 16
            chan.count[s] += 1
            o.num = chan.count[s]
            o.chan = chan
            o.slot = s
            if i == 0:
                for g0 in gate:
                    self._dep(o, [g0])
            ops.append(o)
            self.E[chan.queue].append(o)
        for o in ops:
            o.blast = ops[-1]
        o0, oL = ops[0], ops[-1]
        for b in w:
            if b.aliases and id(b) not in self.state:
                for ab in b.aliases:
                    st = self.state.get(id(ab))
                    if st:
                        self._dep(o0, [st["w"]])
                        self._dep(o0, st["r"])
                b.aliases = []
        for b in r:
            st = self.state.get(id(b))
            if st:
                self._dep(o0, [st["w"]])
        for b in w:
            st = self.state.get(id(b))
            if st:
                self._dep(o0, [st["w"]])
                self._dep(o0, st["r"])
        for b in r:
            self.state.setdefault(id(b), {"w": None, "r": []})["r"].append(oL)
        for b in w:
            self.state[id(b)] = {"w": oL, "r": []}
        return oL
    def finalize(self):
        for e in ("t", "v", "s", "g", "q"):
            n = 0
            for o in self.E[e]:
                if o.sem is None and o.inc_needed:
                    n += 1
                    o.num = n
                    o.sem = self.esem[e]
                    o.incby = 1
    def replay(self):
        nc = self.nc
        with nc.Block() as blk:
            for e in ENGS:
                ops = self.E[e]
                def body(eng, ops=ops):
                    waited = {}
                    for o in ops:
                        for p in o.waits:
                            tgt = p.num * p.incby
                            key = id(p.sem)
                            if waited.get(key, -1) >= tgt:
                                continue
                            waited[key] = tgt
                            eng.wait_ge(p.sem, tgt)
                        inst = o.fn(eng)
                        if o.sem is not None and (o.incby == 16 or o.inc_needed):
                            inst.then_inc(o.sem, o.incby)
                getattr(blk, ENGMAP[e])(body)
    def close(self):
        for h in reversed(self.sem_ctx):
            h.__exit__(None, None, None)



def setup_consts(P, A, io, C):
    X = {}
    X["ident"] = A.alloc("ident", 128, 128, BF16)
    X["onesb"] = A.alloc("onesb", 128, 128, BF16)
    X["onesf"] = A.alloc("onesf", 1, 128, F32)
    X["onesfc"] = A.alloc("onesfc", 128, 1, F32)
    X["eps"] = A.alloc("eps", 128, 1, F32)
    X["identf"] = A.alloc("identf", 128, 128, F32)
    X["e2"] = A.alloc("e2", 128, 128, BF16)
    X["e3"] = A.alloc("e3", 128, 64, BF16)
    ch = C["const"]
    P.dma(ch, [
        lambda e: e.dma_start(out=X["ident"].ap, in_=io["c_ident"]),
        lambda e: e.dma_start(out=X["onesb"].ap, in_=io["c_onesb"]),
        lambda e: e.dma_start(out=X["onesf"].ap, in_=io["c_onesf"]),
        lambda e: e.dma_start(out=X["onesfc"].ap, in_=io["c_onesfc"]),
        lambda e: e.dma_start(out=X["eps"].ap, in_=io["c_eps"]),
        lambda e: e.dma_start(out=X["identf"].ap, in_=io["c_identf"]),
        lambda e: e.dma_start(out=X["e2"].ap, in_=io["c_e2"]),
        lambda e: e.dma_start(out=X["e3"].ap, in_=io["c_e3"]),
    ], w=[X["ident"], X["onesb"], X["onesf"], X["onesfc"], X["eps"], X["identf"], X["e2"], X["e3"]], dbg="consts")
    return X


def emit_outproj_ln(P, A, io, C, X, banks, wname, xin, xout, aospill, bas, bxout):
    A.mark()
    wo = A.alloc("wo", 128, NFC * NFC * 128, BF16)
    P.dma(C["w"], [lambda e: e.dma_start(
            out=wo.ap.rearrange("p (c j o) -> p c j o", c=NFC, j=NFC),
            in_=io[wname].rearrange("(c p) (j o) -> p c j o", p=128, j=NFC))],
          w=[wo], dbg="wo")
    ao_in = [A.alloc(f"aoin{i}", 128, NFC * 512, BF16) for i in range(2)]
    xr_in = [A.alloc(f"xrin{i}", 128, NFC * 512, F32) for i in range(2)]
    ztb = [A.alloc(f"zt{i}", 128, NFC * 512, F32) for i in range(2)]
    zsq = A.alloc("zsq", 128, 512, BF16)
    mu = A.alloc("mu", 1, 512, F32)
    ms = A.alloc("ms", 1, 512, F32)
    aa = A.alloc("aa", 1, 512, F32)
    bb2 = A.alloc("bb2", 1, 512, F32)
    yst = [A.alloc(f"yst{i}", 128, NFC * 512, F32) for i in range(2)]
    stA, stB = banks[6], banks[5]
    abank, bbank = banks[7], banks[3]

    for tc in range(NTC):
        sl = tc % 2
        ai = ao_in[sl]
        xr = xr_in[sl]
        zt = ztb[sl]
        P.dma(C["aoin"], [lambda e, ai=ai, tc=tc: e.dma_start(
                out=ai.ap.rearrange("p (c t) -> p c t", c=NFC),
                in_=aospill[:, tc * 512:(tc + 1) * 512].rearrange("(c p) t -> p c t", p=128))],
              r=[bas], w=[ai], dbg=f"aoin{tc}")
        P.dma(C["xin"], [lambda e, xr=xr, tc=tc: e.dma_start(
                out=xr.ap.rearrange("p (c t) -> p c t", c=NFC),
                in_=xin[:, tc * 512:(tc + 1) * 512].rearrange("(c p) t -> p c t", p=128))],
              w=[xr], dbg=f"xrin{tc}")
        for j in range(NFC):
            pb = banks[(j % 2)]
            for c in range(NFC):
                P.op("t", lambda e, j=j, c=c, ai=ai, pb=pb: e.matmul(
                        pb.ap, wo.ap[:, (c * NFC + j) * 128:(c * NFC + j + 1) * 128],
                        ai.ap[:, c * 512:(c + 1) * 512], start=(c == 0), stop=(c == NFC - 1)),
                     r=[wo, ai], w=[pb], dbg=f"wout{tc}_{j}_{c}")
            P.op("v", lambda e, pb=pb, xr=xr, j=j, zt=zt: e.tensor_tensor(
                    zt.ap[:, j * 512:(j + 1) * 512], pb.ap, xr.ap[:, j * 512:(j + 1) * 512], ALU.add),
                 r=[pb, xr], w=[zt], dbg=f"zadd{tc}_{j}")
            P.op("s", lambda e, j=j, zt=zt: e.activation(zsq.ap, zt.ap[:, j * 512:(j + 1) * 512], AF.Square),
                 r=[zt], w=[zsq], dbg=f"zsq{tc}_{j}")
            P.op("t", lambda e, j=j: e.matmul(stB.ap[0:1, :], X["onesb"].ap[:, 0:1], zsq.ap,
                                              start=(j == 0), stop=(j == NFC - 1)),
                 r=[zsq, X["onesb"]], w=[stB], dbg=f"s2st{tc}_{j}")
            P.op("t", lambda e, j=j, zt=zt: e.matmul(stA.ap[0:1, :], X["onesfc"].ap,
                                              zt.ap[:, j * 512:(j + 1) * 512],
                                              start=(j == 0), stop=(j == NFC - 1)),
                 r=[zt, X["onesfc"]], w=[stA], dbg=f"s1st{tc}_{j}")
        P.op("v", lambda e: e.tensor_scalar_mul(mu.ap, stA.ap[0:1, :], 1.0 / D), r=[stA], w=[mu])
        P.op("v", lambda e: e.tensor_scalar_mul(ms.ap, stB.ap[0:1, :], 1.0 / D), r=[stB], w=[ms])
        P.op("v", lambda e: e.tensor_tensor(bb2.ap, mu.ap, mu.ap, ALU.mult), r=[mu], w=[bb2])
        P.op("v", lambda e: e.tensor_tensor(ms.ap, ms.ap, bb2.ap, ALU.subtract),
             r=[ms, bb2], w=[ms], dbg="var")
        P.op("s", lambda e: e.activation(ms.ap, ms.ap, AF.Sqrt, bias=X["eps"].ap[0:1, :]), r=[ms, X["eps"]], w=[ms])
        P.op("v", lambda e: e.reciprocal(aa.ap, ms.ap), r=[ms], w=[aa])
        P.op("v", lambda e: e.tensor_tensor(bb2.ap, mu.ap, aa.ap, ALU.mult), r=[mu, aa], w=[bb2])
        P.op("v", lambda e: e.tensor_scalar_mul(bb2.ap, bb2.ap, -1.0), r=[bb2], w=[bb2])
        P.op("t", lambda e: e.matmul(abank.ap, X["onesf"].ap[0:1, :], aa.ap, start=True, stop=True),
             r=[aa, X["onesf"]], w=[abank], dbg=f"abc{tc}")
        P.op("t", lambda e: e.matmul(bbank.ap, X["onesf"].ap[0:1, :], bb2.ap, start=True, stop=True),
             r=[bb2, X["onesf"]], w=[bbank], dbg=f"bbc{tc}")
        ys = yst[sl]
        for j in range(NFC):
            P.op("v", lambda e, j=j, ys=ys, zt=zt: e.tensor_tensor(
                    ys.ap[:, j * 512:(j + 1) * 512], zt.ap[:, j * 512:(j + 1) * 512],
                    abank.ap, ALU.mult),
                 r=[zt, abank], w=[ys], dbg=f"lnm{tc}_{j}")
            P.op("v", lambda e, j=j, ys=ys: e.tensor_tensor(
                    ys.ap[:, j * 512:(j + 1) * 512], ys.ap[:, j * 512:(j + 1) * 512],
                    bbank.ap, ALU.add),
                 r=[ys, bbank], w=[ys], dbg=f"lna{tc}_{j}")
        P.dma(C["xout"], [lambda e, ys=ys, tc=tc: e.dma_start(
                out=xout[:, tc * 512:(tc + 1) * 512].rearrange("(c p) t -> p c t", p=128),
                in_=ys.ap.rearrange("p (c t) -> p c t", c=NFC))],
              r=[ys], w=[bxout], dbg=f"xout{tc}")
    A.release()
    return bxout


def build_attn(P, A, io, C, X, banks, layer):
    nc = P.nc
    xin, xout = io["xin"], io["xout"]
    qspill, kspill, aospill = io["qspill"], io["kspill"], io["aospill"]
    bqs, bks, bas = Buf("qspill"), Buf("kspill"), Buf("aospill")
    bxout = Buf("xout")

    A.mark()
    negBB = A.alloc("negBB", 128, NH * NTC, F32)
    g32 = A.alloc("g32", 128, NFC, F32)
    gsb = A.alloc("gsb", 128, NFC, F32)
    qn2m = A.alloc("qn2m", 1, NH * NTC, F32)
    kn2m = A.alloc("kn2m", 1, NH * NTC, F32)
    kn2h = A.alloc("kn2h", 1, NH, F32)
    negB = A.alloc("negB", 1, NH * NTC, F32)
    A.mark()
    vP = A.alloc("vP", 128, NH * KC * 65, BF16)
    A.mark()
    xnb = [A.alloc(f"xn{c}", 128, T, BF16) for c in range(NFC)]

    P.op("v", lambda e: e.memset(vP.ap, 1.0), w=[vP], dbg="vp_ones")
    if layer == 0:
        P.dma(C["const"], [lambda e: e.dma_start(out=gsb.ap, in_=io["gamma"])], w=[gsb])
        P.op("v", lambda e: e.tensor_scalar(g32.ap, gsb.ap, 1.0, 32.0, ALU.add, ALU.mult),
             r=[gsb], w=[g32], dbg="g32")

    A.mark()
    xin_t = [A.alloc(f"xint{i}", 128, NFC * 512, F32) for i in range(2)]
    xsq = [A.alloc(f"xsq{i}", 128, NFC * 512, BF16) for i in range(2)]
    snt = A.alloc("snt", 1, 512, F32)
    stt = A.alloc("stt", 1, 512, F32)
    stat_bank = banks[6]
    bc_bank = banks[7]
    for tc in range(NTC):
        sl = tc % 2
        xt = xin_t[sl]
        src = xin[:, tc * 512:(tc + 1) * 512].rearrange("(c p) t -> p c t", p=128)
        P.dma(C["xin"], [lambda e, xt=xt, src=src: e.dma_start(
                out=xt.ap.rearrange("p (c t) -> p c t", c=NFC), in_=src)],
              w=[xt], dbg=f"xin{tc}")
        if layer == 0:
            xq = xsq[sl]
            P.op("s", lambda e, xt=xt, xq=xq: e.activation(xq.ap, xt.ap, AF.Square),
                 r=[xt], w=[xq], dbg=f"xsq{tc}")
            for c in range(NFC):
                P.op("t", lambda e, xq=xq, c=c: e.matmul(
                        stat_bank.ap[0:1, :], X["onesb"].ap[:, 0:1],
                        xq.ap[:, c * 512:(c + 1) * 512], start=(c == 0), stop=(c == NFC - 1)),
                     r=[xq, X["onesb"]], w=[stat_bank], dbg=f"stat{tc}_{c}")
            P.op("v", lambda e: e.tensor_copy(snt.ap, stat_bank.ap[0:1, :]),
                 r=[stat_bank], w=[snt], dbg=f"snev{tc}")
            P.op("s", lambda e: e.activation(snt.ap, snt.ap, AF.Sqrt),
                 r=[snt], w=[snt], dbg=f"sqrt{tc}")
            P.op("v", lambda e: e.tensor_scalar_max(snt.ap, snt.ap, 1e-12),
                 r=[snt], w=[snt])
            P.op("v", lambda e: e.reciprocal(stt.ap, snt.ap),
                 r=[snt], w=[stt], dbg=f"recip{tc}")
            P.op("t", lambda e: e.matmul(bc_bank.ap, X["onesf"].ap[0:1, :],
                                         stt.ap, start=True, stop=True),
                 r=[stt, X["onesf"]], w=[bc_bank], dbg=f"stokB{tc}")
            for c in range(NFC):
                P.op("v", lambda e, xt=xt, c=c, tc=tc: e.tensor_tensor(
                        xnb[c].ap[:, tc * 512:(tc + 1) * 512],
                        xt.ap[:, c * 512:(c + 1) * 512], bc_bank.ap, ALU.mult),
                     r=[xt, bc_bank], w=[xnb[c]], dbg=f"xn{tc}_{c}")
        else:
            for c in range(NFC):
                P.op("v", lambda e, xt=xt, c=c, tc=tc: e.tensor_copy(
                        xnb[c].ap[:, tc * 512:(tc + 1) * 512], xt.ap[:, c * 512:(c + 1) * 512]),
                     r=[xt], w=[xnb[c]], dbg=f"xcast{tc}_{c}")
    A.release()

    A.mark()
    wqb = [A.alloc(f"wq{i}", 128, NFC * 128, BF16) for i in range(2)]
    qstage = [A.alloc(f"qst{i}", 128, 512, BF16) for i in range(2)]
    sqstage = [A.alloc(f"sqst{i}", 128, 512, BF16) for i in range(2)]
    vstage = [A.alloc(f"vst{i}", 128, T, BF16) for i in range(2)]
    n2bank = banks[5]

    def w_src(j):
        if layer == 0:
            return io["w_qkv"].rearrange("(c p) o -> p c o", p=128)[:, :, j * 128:(j + 1) * 128]
        kind, jj = divmod(j, NFC)
        wsrc = io[["wqT", "wkT", "wvT"][kind]]
        return wsrc.rearrange("(c p) o -> p c o", p=128)[:, :, jj * 128:(jj + 1) * 128]

    for j in range(3 * NFC):
        sl = j % 2
        wq = wqb[sl]
        P.dma(C["w"], [lambda e, wq=wq, j=j: e.dma_start(
                out=wq.ap.rearrange("p (c o) -> p c o", c=NFC), in_=w_src(j))],
              w=[wq], dbg=f"wq{j}")
        kind, jj = divmod(j, NFC)
        if layer == 0:
            isq = (j < NFC)
            for c in range(NFC):
                if isq:
                    P.op("v", lambda e, wq=wq, c=c: e.tensor_scalar(
                            wq.ap[:, c * 128:(c + 1) * 128], wq.ap[:, c * 128:(c + 1) * 128],
                            g32.ap[:, c:c + 1], 0.125, ALU.mult, ALU.mult),
                         r=[wq, g32], w=[wq], dbg=f"wfix{j}_{c}")
                else:
                    P.op("v", lambda e, wq=wq, c=c: e.tensor_scalar_mul(
                            wq.ap[:, c * 128:(c + 1) * 128], wq.ap[:, c * 128:(c + 1) * 128],
                            g32.ap[:, c:c + 1]),
                         r=[wq, g32], w=[wq], dbg=f"wfix{j}_{c}")
        elif j < NFC:
            P.op("v", lambda e, wq=wq: e.tensor_scalar_mul(wq.ap, wq.ap, 0.125),
                 r=[wq], w=[wq], dbg=f"wq_scale{j}")
        for tc in range(NTC):
            pb = banks[(j * NTC + tc) % 2]
            for c in range(NFC):
                P.op("t", lambda e, wq=wq, c=c, tc=tc, pb=pb: e.matmul(
                        pb.ap, wq.ap[:, c * 128:(c + 1) * 128],
                        xnb[c].ap[:, tc * 512:(tc + 1) * 512],
                        start=(c == 0), stop=(c == NFC - 1)),
                     r=[wq, xnb[c]], w=[pb], dbg=f"qkv{j}_{tc}_{c}")
            if kind < 2:
                st = qstage[tc % 2]
                sq = sqstage[tc % 2]
                P.op("s", lambda e, pb=pb, st=st: e.activation(st.ap, pb.ap, AF.Copy),
                     r=[pb], w=[st], dbg=f"qkev{j}_{tc}")
                P.op("s", lambda e, st=st, sq=sq: e.activation(sq.ap, st.ap, AF.Square),
                     r=[st], w=[sq], dbg=f"qsq{j}_{tc}")
                for half in range(2):
                    h = jj * 2 + half
                    P.op("t", lambda e, sq=sq, half=half: e.matmul(
                            n2bank.ap[0:1, :], X["onesb"].ap[64 * half:64 * half + 64, 0:1],
                            sq.ap[64 * half:64 * half + 64, :], start=True, stop=True),
                         r=[sq, X["onesb"]], w=[n2bank], dbg=f"n2{j}_{tc}_{half}")
                    dst = (qn2m if kind == 0 else kn2m)
                    P.op("v", lambda e, dst=dst, h=h, tc=tc: e.reduce_max(
                            dst.ap[:, h * NTC + tc: h * NTC + tc + 1], n2bank.ap[0:1, :], AX.X),
                         r=[n2bank], w=[dst], dbg=f"n2m{j}_{tc}_{half}")
                spill = qspill if kind == 0 else kspill
                bsp = bqs if kind == 0 else bks
                P.dma(C["spillout"], [lambda e, st=st, spill=spill, jj=jj, tc=tc: e.dma_start(
                        out=spill[jj * 128:(jj + 1) * 128, tc * 512:(tc + 1) * 512], in_=st.ap)],
                      r=[st], w=[bsp], dbg=f"spill{j}_{tc}")
            else:
                vs = vstage[jj % 2]
                P.op("v", lambda e, pb=pb, vs=vs, tc=tc: e.tensor_copy(
                        vs.ap[:, tc * 512:(tc + 1) * 512], pb.ap),
                     r=[pb], w=[vs], dbg=f"vev{j}_{tc}")
        if kind == 2:
            vs = vstage[jj % 2]
            for tb in range(KC):
                tb_bank = banks[2 + (tb % 2)]
                P.op("t", lambda e, vs=vs, tb=tb, tb_bank=tb_bank: e.transpose(
                        tb_bank.ap.bitcast(BF16)[:, 0:128], vs.ap[:, tb * 128:(tb + 1) * 128],
                        X["ident"].ap),
                     r=[vs, X["ident"]], w=[tb_bank], dbg=f"vT{jj}_{tb}")
                def cp(e, tb_bank=tb_bank, jj=jj, tb=tb):
                    srcv = tb_bank.ap.bitcast(BF16)[:, 0:128].rearrange("p (h d) -> p h d", h=2)
                    dstv = vP.ap.rearrange("p (h k o) -> p h k o", h=NH, k=KC)[:, 2 * jj:2 * jj + 2, tb, 0:64]
                    return e.tensor_copy(dstv, srcv)
                P.op("v", cp, r=[tb_bank], w=[vP], dbg=f"vPcp{jj}_{tb}")
    A.release()
    A.release()

    bc_bank = banks[7]
    P.op("v", lambda e: e.tensor_reduce(kn2h.ap[:, 0:1], kn2m.ap, AX.X, ALU.max),
         r=[kn2m], w=[kn2h], dbg="kn2h")
    P.op("v", lambda e: e.tensor_reduce(negB.ap[:, 0:1], qn2m.ap, AX.X, ALU.max),
         r=[qn2m], w=[negB], dbg="qn2g")
    P.op("s", lambda e: e.activation(negB.ap[:, 0:1], negB.ap[:, 0:1], AF.Sqrt), r=[negB], w=[negB])
    P.op("s", lambda e: e.activation(kn2h.ap[:, 0:1], kn2h.ap[:, 0:1], AF.Sqrt), r=[kn2h], w=[kn2h])
    P.op("v", lambda e: e.tensor_tensor(negB.ap[:, 0:1], negB.ap[:, 0:1], kn2h.ap[:, 0:1], ALU.mult),
         r=[negB, kn2h], w=[negB], dbg="negB")
    P.op("v", lambda e: e.tensor_scalar_mul(negB.ap[:, 0:1], negB.ap[:, 0:1], -1.0), r=[negB], w=[negB])
    P.op("t", lambda e: e.matmul(bc_bank.ap[:, 0:1], X["onesf"].ap[0:1, :], negB.ap[:, 0:1],
                                 start=True, stop=True),
         r=[negB, X["onesf"]], w=[bc_bank], dbg="negBbc")
    P.op("v", lambda e: e.tensor_copy(negBB.ap[:, 0:1], bc_bank.ap[:, 0:1]), r=[bc_bank], w=[negBB])

    A.mark()
    psall = banks[8]
    ktile = [A.alloc(f"kt{i}", 128, T, BF16) for i in range(2)]
    qtile = [A.alloc(f"qt{i}", 128, 512, BF16) for i in range(4)]
    pt = [A.alloc(f"pt{i}", 128, 1536, BF16) for i in range(3)]
    rdsb = A.alloc("rdsb", 1, 1024, F32)
    rdbb = A.alloc("rdbb", 64, 512, F32)
    aostage = [A.alloc(f"aost{i}", 128, 512, BF16) for i in range(2)]
    osbuf = [A.alloc(f"osb{i}", 65, 1024, F32) for i in range(2)]
    obank = {0: banks[6], 1: banks[7]}

    units = [(kc, head) for kc in range(KC) for head in (0, 1)]
    NB = (len(units) + 2) // 3
    for hp in range(NFC):
        kt = ktile[hp % 2]
        P.dma(C["kin"], [lambda e, kt=kt, hp=hp: e.dma_start(
                out=kt.ap, in_=kspill[hp * 128:(hp + 1) * 128, :])],
              r=[bks], w=[kt], dbg=f"kin{hp}")
        for qc in range(NTC):
            qt = qtile[qc % 4]
            P.dma(C["qin"], [lambda e, qt=qt, hp=hp, qc=qc: e.dma_start(
                    out=qt.ap, in_=qspill[hp * 128:(hp + 1) * 128, qc * 512:(qc + 1) * 512])],
                  r=[bqs], w=[qt], dbg=f"qin{hp}_{qc}")

            def emit_pv(j):
                us = units[3 * j:3 * j + 3]
                p = pt[j % 3]
                for i, (kc, head) in enumerate(us):
                    ob = obank[head]
                    def pv(e, p=p, i=i, kc=kc, head=head, ob=ob, hp=hp):
                        lhs = vP.ap.rearrange("p (h k o) -> p h k o", h=NH, k=KC)[:, 2 * hp + head, kc, :]
                        return e.matmul(ob.ap[0:65, :], lhs, p.ap[:, i * 512:(i + 1) * 512],
                                        start=(kc == 0), stop=(kc == KC - 1))
                    P.op("t", pv, r=[p, vP], w=[ob], dbg=f"pv{hp}_{qc}_{j}_{i}")

            for j in range(NB):
                us = units[3 * j:3 * j + 3]
                sb0 = (3 * j) % 6
                bks_used = [banks[sb0 + i] for i in range(len(us))]
                for i, (kc, head) in enumerate(us):
                    bk = bks_used[i]
                    P.op("t", lambda e, kt=kt, qt=qt, kc=kc, head=head, bk=bk: e.matmul(
                            bk.ap, kt.ap[64 * head:64 * head + 64, kc * 128:(kc + 1) * 128],
                            qt.ap[64 * head:64 * head + 64, :], start=True, stop=True),
                         r=[kt, qt], w=[bk], dbg=f"qk{hp}_{qc}_{j}_{i}")
                if j > 0:
                    emit_pv(j - 1)
                n = len(us) * 512
                p = pt[j % 3]
                P.op("s", lambda e, sb0=sb0, n=n, p=p: e.activation(
                        p.ap[:, 0:n], psall.ap[:, sb0 * 512:sb0 * 512 + n], AF.Exp,
                        bias=negBB.ap[:, 0:1]),
                     r=bks_used + [negBB], w=[p], dbg=f"exp{hp}_{qc}_{j}")
            emit_pv(NB - 1)
            ao = aostage[qc % 2]
            osb = osbuf[qc % 2]
            for head, ob in ((0, obank[0]), (1, obank[1])):
                P.op("v", lambda e, ob=ob, osb=osb, head=head: e.tensor_copy(
                        osb.ap[:, head * 512:(head + 1) * 512], ob.ap[0:65, :]),
                     r=[ob], w=[osb], dbg=f"oev{hp}_{qc}_{head}")
            for head in (0, 1):
                P.op("v", lambda e, osb=osb, head=head: e.reciprocal(
                        rdsb.ap[:, head * 512:(head + 1) * 512],
                        osb.ap[64:65, head * 512:(head + 1) * 512]),
                     r=[osb], w=[rdsb], dbg=f"rd{hp}_{qc}_{head}")
                rb = obank[head]
                P.op("t", lambda e, rb=rb, head=head: e.matmul(
                        rb.ap[0:64, :], X["onesf"].ap[0:1, 0:64],
                        rdsb.ap[:, head * 512:(head + 1) * 512], start=True, stop=True),
                     r=[rdsb, X["onesf"]], w=[rb], dbg=f"rdbc{hp}_{qc}_{head}")
                P.op("v", lambda e, rb=rb, osb=osb, ao=ao, head=head: e.tensor_tensor(
                        ao.ap[64 * head:64 * head + 64, :], osb.ap[0:64, 512 * head:512 * head + 512],
                        rb.ap[0:64, :], ALU.mult),
                     r=[osb, rb], w=[ao], dbg=f"aonorm{hp}_{qc}_{head}")
            P.dma(C["spillout"], [lambda e, ao=ao, hp=hp, qc=qc: e.dma_start(
                    out=aospill[hp * 128:(hp + 1) * 128, qc * 512:(qc + 1) * 512], in_=ao.ap)],
                  r=[ao], w=[bas], dbg=f"aoout{hp}_{qc}")
    A.release()
    A.release()

    ob = emit_outproj_ln(P, A, io, C, X, banks,
                          ("w_out" if layer == 0 else "wfcT"), xin, xout, aospill, bas, bxout)
    A.release()
    return ob




C1R = float(3 * 2 ** 21)
C2R = float(3 * 2 ** 22)
C3R = float(3 * 2 ** 23)

def emit_fp4_quant(P, A, src_buf, dst_buf, rB_buf, sB_buf, tmp, ncols, pcols, src_off=0):
    v, w1, a05, a1 = tmp["v"], tmp["q"], tmp["a"], tmp["b"]
    m2, m4 = tmp["m2"], tmp["m4"]
    for lo in range(0, ncols, pcols):
        hi = min(lo + pcols, ncols)
        n = hi - lo
        s0, s1 = src_off + lo, src_off + hi
        P.op("v", lambda e, s0=s0, s1=s1, lo=lo, hi=hi, n=n: e.tensor_tensor(
                v.ap[:, 0:n], src_buf.ap[:, s0:s1], rB_buf.ap[:, lo:hi], ALU.mult),
             r=[src_buf, rB_buf], w=[v], dbg="q_v")
        P.op("s", lambda e, n=n: e.activation(w1.ap[:, 0:n], v.ap[:, 0:n], AF.Square),
             r=[v], w=[w1], dbg="q_vsq")
        P.op("v", lambda e, n=n: e.tensor_scalar(m2.ap[:, 0:n], w1.ap[:, 0:n], 4.0, None, ALU.is_gt),
             r=[w1], w=[m2], dbg="q_m2")
        P.op("v", lambda e, n=n: e.tensor_scalar(m4.ap[:, 0:n], w1.ap[:, 0:n], 16.0, None, ALU.is_gt),
             r=[w1], w=[m4], dbg="q_m4")
        P.op("v", lambda e, n=n: e.tensor_scalar(a05.ap[:, 0:n], v.ap[:, 0:n], C1R, C1R,
                                                 ALU.add, ALU.subtract),
             r=[v], w=[a05], dbg="q_a05")
        P.op("v", lambda e, n=n: e.tensor_scalar(a1.ap[:, 0:n], v.ap[:, 0:n], C2R, C2R,
                                                 ALU.add, ALU.subtract),
             r=[v], w=[a1], dbg="q_a1")
        P.op("v", lambda e, n=n: e.tensor_scalar(w1.ap[:, 0:n], v.ap[:, 0:n], C3R, C3R,
                                                 ALU.add, ALU.subtract),
             r=[v], w=[w1], dbg="q_a2")
        P.op("v", lambda e, n=n: e.copy_predicated(a05.ap[:, 0:n], m2.ap[:, 0:n], a1.ap[:, 0:n]),
             r=[m2, a1, a05], w=[a05], dbg="q_s1")
        P.op("v", lambda e, n=n: e.copy_predicated(a05.ap[:, 0:n], m4.ap[:, 0:n], w1.ap[:, 0:n]),
             r=[m4, w1, a05], w=[a05], dbg="q_s2")
        P.op("v", lambda e, s0=s0, s1=s1, n=n, lo=lo, hi=hi: e.tensor_tensor(
                dst_buf.ap[:, s0:s1], a05.ap[:, 0:n], sB_buf.ap[:, lo:hi], ALU.mult),
             r=[a05, sB_buf], w=[dst_buf], dbg="q_deq")


def gen_wquant_nat(P, A, io, C, X, banks, wname, wq_dram, nrows, kdim, bwq, tbpair):
    nch = nrows // 128
    kch = kdim // 128
    nblk = kdim // 16
    wnat = [A.alloc(f"wn{wname}{i}", 128, kdim, F32) for i in range(2)]
    sw = [A.alloc(f"sw{wname}{i}", 128, nblk, F32) for i in range(2)]
    rw = [A.alloc(f"rw{wname}{i}", 128, nblk, F32) for i in range(2)]
    swB = [A.alloc(f"swB{wname}{i}", 128, kdim, F32) for i in range(2)]
    rwB = [A.alloc(f"rwB{wname}{i}", 128, kdim, F32) for i in range(2)]
    tmp = [dict([(k, A.alloc(f"wt{wname}{k}{i}", 128, kdim, F32)) for k in ("v", "q", "a", "b")]
                + [("m2", A.alloc(f"wm2{wname}{i}", 128, kdim, U8)),
                   ("m4", A.alloc(f"wm4{wname}{i}", 128, kdim, U8))]) for i in range(2)]
    wqd = [A.alloc(f"wqd{wname}{i}", 128, kdim, BF16) for i in range(2)]
    stg = [A.alloc(f"wstg{wname}{i}", 128, kdim, BF16) for i in range(2)]
    tb0, tb1 = tbpair
    k16 = lambda b: b.ap.rearrange("p (b k) -> p b k", k=16)
    for a in range(nch):
        i = a % 2
        wn = wnat[i]
        P.dma(C["xin"], [lambda e, wn=wn, a=a: e.dma_start(
                out=wn.ap, in_=io[wname][a * 128:(a + 1) * 128, :])],
              w=[wn], dbg=f"wl{wname}{a}")
        P.op("v", lambda e, wn=wn, i=i: e.tensor_reduce(
                sw[i].ap, wn.ap.rearrange("p (b k) -> p b k", k=16), AX.X, ALU.max,
                apply_absolute_value=True),
             r=[wn], w=[sw[i]], dbg=f"ws{wname}{a}")
        P.op("v", lambda e, i=i: e.tensor_scalar(sw[i].ap, sw[i].ap, 1.0 / 6.0, 1e-12,
                                                 ALU.mult, ALU.max),
             r=[sw[i]], w=[sw[i]], dbg=f"ws2{wname}{a}")
        P.op("v", lambda e, i=i: e.reciprocal(rw[i].ap, sw[i].ap),
             r=[sw[i]], w=[rw[i]], dbg=f"wr{wname}{a}")
        P.op("s", lambda e, i=i: e.activation(
                k16(swB[i]), sw[i].ap.unsqueeze(2).broadcast_to([128, nblk, 16]), AF.Copy),
             r=[sw[i]], w=[swB[i]], dbg=f"wsb{wname}{a}")
        P.op("s", lambda e, i=i: e.activation(
                k16(rwB[i]), rw[i].ap.unsqueeze(2).broadcast_to([128, nblk, 16]), AF.Copy),
             r=[rw[i]], w=[rwB[i]], dbg=f"wrb{wname}{a}")
        emit_fp4_quant(P, A, wn, wqd[i], rwB[i], swB[i], tmp[i], kdim, kdim)
        for c in range(kch):
            tb = tb0 if c % 2 == 0 else tb1
            P.op("t", lambda e, c=c, tb=tb, i=i: e.transpose(
                    tb.ap.bitcast(BF16)[:, 0:128], wqd[i].ap[:, c * 128:(c + 1) * 128],
                    X["ident"].ap),
                 r=[wqd[i], X["ident"]], w=[tb], dbg=f"wT{wname}{a}_{c}")
            P.op("s", lambda e, c=c, tb=tb, i=i: e.activation(
                    stg[i].ap[:, c * 128:(c + 1) * 128], tb.ap.bitcast(BF16)[:, 0:128], AF.Copy),
                 r=[tb], w=[stg[i]], dbg=f"wTe{wname}{a}_{c}")
        P.dma(C["spillout"], [lambda e, a=a, i=i: e.dma_start(
                out=wq_dram.rearrange("(c p) n -> p c n", p=128)[:, :, a * 128:(a + 1) * 128],
                in_=stg[i].ap.rearrange("p (c o) -> p c o", o=128))],
              r=[stg[i]], w=[bwq], dbg=f"wst{wname}{a}")
        yield


def pump(main, aux, naux):
    credit = 0.0
    for _ in main:
        credit += naux
        while credit >= 1.0:
            next(aux, None)
            credit -= 1.0
    for _ in aux:
        pass


def build_l0ffn(P, A, io, C, X, banks):
    xin, xout = io["xin"], io["xout"]
    bxout = Buf("xout")
    bwgq, bwuq, bwdq = Buf("wgq"), Buf("wuq"), Buf("wdq")
    bscxs, bscxr = Buf("scxs"), Buf("scxr")
    bschs = [Buf("schs0"), Buf("schs1")]
    bschr = [Buf("schr0"), Buf("schr1")]
    FFC = FF // 128

    A.mark()
    g1 = gen_wquant_nat(P, A, io, C, X, banks, "wg", io["wgq"], FF, D, bwgq,
                        (banks[2], banks[3]))
    g2 = gen_wquant_nat(P, A, io, C, X, banks, "wu", io["wuq"], FF, D, bwuq,
                        (banks[5], banks[6]))
    for _ in g1:
        next(g2, None)
    for _ in g2:
        pass
    A.release()
    A.mark()
    for _ in gen_wquant_nat(P, A, io, C, X, banks, "wd", io["wdq"], D, FF, bwdq,
                            (banks[2], banks[3])):
        pass
    A.release()

    A.mark()
    xr_in = [A.alloc(f"xr{i}", 128, NFC * 512, F32) for i in range(2)]
    xq = [A.alloc(f"xq{i}", 128, NFC * 512, BF16) for i in range(2)]
    sxT = A.alloc("sxT", 128, 4 * 64, F32)
    rxT = A.alloc("rxT", 128, 4 * 64, F32)
    sx_fm = A.alloc("sxfm", 64, 512, F32)
    rx_fm = A.alloc("rxfm", 64, 512, F32)
    rxB = A.alloc("rxB", 128, 4 * 512, F32)
    sxB = A.alloc("sxB", 128, 4 * 512, F32)
    tmpx = {k: A.alloc(f"t{k}", 128, 2 * 512, F32) for k in ("v", "q", "a", "b")}
    tmpx["m2"] = A.alloc("tm2", 128, 2 * 512, U8)
    tmpx["m4"] = A.alloc("tm4", 128, 2 * 512, U8)
    hso = A.alloc("hso", 128, 512, BF16)
    hbuf = [A.alloc(f"hbuf{i}", 128, FFC * 512, BF16) for i in range(2)]
    shT = A.alloc("shT", 128, 4 * 176, F32)
    rhT = A.alloc("rhT", 128, 4 * 176, F32)
    sh_fm1 = A.alloc("shfm1", 128, 512, F32)
    sh_fm2 = A.alloc("shfm2", 48, 512, F32)
    rh_fm1 = A.alloc("rhfm1", 128, 512, F32)
    rh_fm2 = A.alloc("rhfm2", 48, 512, F32)
    rhB = A.alloc("rhB", 128, 4 * 512, F32)
    shB = A.alloc("shB", 128, 4 * 512, F32)
    wgt = [A.alloc(f"wgt{i}", 128, NFC * 128, BF16) for i in range(2)]
    wut = [A.alloc(f"wut{i}", 128, NFC * 128, BF16) for i in range(2)]
    wdt = [A.alloc("wdt0", 128, FFC * 128, BF16)]
    zt = A.alloc("zt2", 128, NFC * 512, F32)
    zsq = A.alloc("zsq2", 128, 512, BF16)
    mu = A.alloc("mu2", 1, 512, F32)
    ms = A.alloc("ms2", 1, 512, F32)
    aa = A.alloc("aa2", 1, 512, F32)
    bb2 = A.alloc("bb22", 1, 512, F32)
    pg, pu = banks[0], banks[1]
    pdb = [banks[2], banks[3]]
    tbb = [banks[4], banks[7]]
    stA, stB = banks[6], banks[5]
    abank, bbank = banks[5], banks[6]

    def gen_front(tc):
        sl = tc % 2
        xr = xr_in[sl]
        hb = hbuf[sl]
        src2 = xin[:, tc * 512:(tc + 1) * 512].rearrange("(c p) t -> p c t", p=128)
        P.dma(C["xin"], [lambda e, xr=xr, src2=src2: e.dma_start(
                out=xr.ap.rearrange("p (c t) -> p c t", c=NFC), in_=src2)],
              w=[xr], dbg=f"xr{tc}")
        for c in range(NFC):
            for tb in range(4):
                bk = tbb[(c * 4 + tb) % 2]
                P.op("t", lambda e, c=c, tb=tb, bk=bk, xr=xr: e.transpose(
                        bk.ap[:, 0:128],
                        xr.ap[:, c * 512 + tb * 128: c * 512 + (tb + 1) * 128], X["identf"].ap),
                     r=[xr, X["identf"]], w=[bk], dbg=f"sxT{tc}_{c}_{tb}")
                P.op("v", lambda e, c=c, tb=tb, bk=bk: e.tensor_reduce(
                        sxT.ap[:, tb * 64 + c * 8: tb * 64 + (c + 1) * 8],
                        bk.ap[:, 0:128].rearrange("p (b k) -> p b k", k=16),
                        AX.X, ALU.max, apply_absolute_value=True),
                     r=[bk], w=[sxT], dbg=f"sxR{tc}_{c}_{tb}")
        P.op("v", lambda e: e.tensor_scalar(sxT.ap, sxT.ap, 1.0 / 6.0, 1e-12, ALU.mult, ALU.max),
             r=[sxT], w=[sxT])
        P.op("v", lambda e: e.reciprocal(rxT.ap, sxT.ap), r=[sxT], w=[rxT])
        for tb in range(4):
            P.op("t", lambda e, tb=tb: e.transpose(
                    tbb[0].ap[:, 0:128][0:64, :], sxT.ap[:, tb * 64:(tb + 1) * 64], X["identf"].ap),
                 r=[sxT, X["identf"]], w=[tbb[0]], dbg=f"sxTb{tc}_{tb}")
            P.op("v", lambda e, tb=tb: e.tensor_copy(sx_fm.ap[:, tb * 128:(tb + 1) * 128],
                                                     tbb[0].ap[0:64, 0:128]),
                 r=[tbb[0]], w=[sx_fm])
            P.op("t", lambda e, tb=tb: e.transpose(
                    tbb[1].ap[:, 0:128][0:64, :], rxT.ap[:, tb * 64:(tb + 1) * 64], X["identf"].ap),
                 r=[rxT, X["identf"]], w=[tbb[1]], dbg=f"rxTb{tc}_{tb}")
            P.op("v", lambda e, tb=tb: e.tensor_copy(rx_fm.ap[:, tb * 128:(tb + 1) * 128],
                                                     tbb[1].ap[0:64, 0:128]),
                 r=[tbb[1]], w=[rx_fm])
        P.dma(C["spillout"], [
            lambda e: e.dma_start(out=io["scx_s"].rearrange("(b c) t -> c b t", b=8),
                                  in_=sx_fm.ap),
            lambda e: e.dma_start(out=io["scx_r"].rearrange("(b c) t -> c b t", b=8),
                                  in_=rx_fm.ap),
        ], r=[sx_fm, rx_fm], w=[bscxs, bscxr], dbg=f"scxout{tc}")
        yield
        for plo in range(0, NFC, 4):
            phi = plo + 4
            P.dma(C["sc"], [
                lambda e, plo=plo, phi=phi: e.dma_start(
                    out=sxB.ap,
                    in_=io["scx_s"].rearrange("(b c) t -> b (c t)", b=8)[:, plo * 512:phi * 512]
                    .unsqueeze(1).broadcast_to([8, 16, (phi - plo) * 512])),
                lambda e, plo=plo, phi=phi: e.dma_start(
                    out=rxB.ap,
                    in_=io["scx_r"].rearrange("(b c) t -> b (c t)", b=8)[:, plo * 512:phi * 512]
                    .unsqueeze(1).broadcast_to([8, 16, (phi - plo) * 512])),
            ], r=[bscxs, bscxr], w=[sxB, rxB], dbg=f"scxrep{tc}_{plo}")
            emit_fp4_quant(P, A, xr, xq[sl], rxB, sxB, tmpx, 4 * 512, 2 * 512, src_off=plo * 512)
            yield
        for j in range(FFC):
            wg_ = wgt[j % 2]
            wu_ = wut[j % 2]
            P.dma(C["w"], [
                lambda e, wg_=wg_, j=j: e.dma_start(
                    out=wg_.ap.rearrange("p (c o) -> p c o", c=NFC),
                    in_=io["wgq"].rearrange("(c p) n -> p c n", p=128)[:, :, j * 128:(j + 1) * 128]),
                lambda e, wu_=wu_, j=j: e.dma_start(
                    out=wu_.ap.rearrange("p (c o) -> p c o", c=NFC),
                    in_=io["wuq"].rearrange("(c p) n -> p c n", p=128)[:, :, j * 128:(j + 1) * 128]),
            ], r=[bwgq, bwuq], w=[wg_, wu_], dbg=f"wgl{tc}_{j}")
            for c in range(NFC):
                P.op("t", lambda e, c=c, wg_=wg_, sl=sl: e.matmul(
                        pg.ap, wg_.ap[:, c * 128:(c + 1) * 128], xq[sl].ap[:, c * 512:(c + 1) * 512],
                        start=(c == 0), stop=(c == NFC - 1)),
                     r=[wg_, xq[sl]], w=[pg], dbg=f"mg{tc}_{j}_{c}")
            for c in range(NFC):
                P.op("t", lambda e, c=c, wu_=wu_, sl=sl: e.matmul(
                        pu.ap, wu_.ap[:, c * 128:(c + 1) * 128], xq[sl].ap[:, c * 512:(c + 1) * 512],
                        start=(c == 0), stop=(c == NFC - 1)),
                     r=[wu_, xq[sl]], w=[pu], dbg=f"mu{tc}_{j}_{c}")
            P.op("s", lambda e: e.activation(hso.ap, pg.ap, AF.Silu), r=[pg], w=[hso], dbg=f"silu{tc}_{j}")
            P.op("v", lambda e, j=j, hb=hb: e.tensor_tensor(
                    hb.ap[:, j * 512:(j + 1) * 512], hso.ap, pu.ap, ALU.mult),
                 r=[hso, pu], w=[hb], dbg=f"hmul{tc}_{j}")
            if j % 2 == 1:
                yield
        for c in range(FFC):
            for tb in range(4):
                bk = tbb[(c * 4 + tb) % 2]
                P.op("t", lambda e, c=c, tb=tb, bk=bk, hb=hb: e.transpose(
                        bk.ap.bitcast(BF16)[:, 0:128],
                        hb.ap[:, c * 512 + tb * 128: c * 512 + (tb + 1) * 128], X["ident"].ap),
                     r=[hb, X["ident"]], w=[bk], dbg=f"shT{tc}_{c}_{tb}")
                P.op("v", lambda e, c=c, tb=tb, bk=bk: e.tensor_reduce(
                        shT.ap[:, tb * 176 + c * 8: tb * 176 + (c + 1) * 8],
                        bk.ap.bitcast(BF16)[:, 0:128].rearrange("p (b k) -> p b k", k=16),
                        AX.X, ALU.max, apply_absolute_value=True),
                     r=[bk], w=[shT], dbg=f"shR{tc}_{c}_{tb}")
            if c % 4 == 3:
                yield
        P.op("v", lambda e: e.tensor_scalar(shT.ap, shT.ap, 1.0 / 6.0, 1e-12, ALU.mult, ALU.max),
             r=[shT], w=[shT])
        P.op("v", lambda e: e.reciprocal(rhT.ap, shT.ap), r=[shT], w=[rhT])
        for tb in range(4):
            P.op("t", lambda e, tb=tb: e.transpose(
                    tbb[0].ap[:, 0:128], shT.ap[:, tb * 176: tb * 176 + 128], X["identf"].ap),
                 r=[shT, X["identf"]], w=[tbb[0]])
            P.op("v", lambda e, tb=tb: e.tensor_copy(sh_fm1.ap[:, tb * 128:(tb + 1) * 128],
                                                     tbb[0].ap[:, 0:128]),
                 r=[tbb[0]], w=[sh_fm1])
            P.op("t", lambda e, tb=tb: e.transpose(
                    tbb[1].ap[:, 0:128][0:48, :], shT.ap[:, tb * 176 + 128: tb * 176 + 176],
                    X["identf"].ap),
                 r=[shT, X["identf"]], w=[tbb[1]])
            P.op("v", lambda e, tb=tb: e.tensor_copy(sh_fm2.ap[:, tb * 128:(tb + 1) * 128],
                                                     tbb[1].ap[0:48, 0:128]),
                 r=[tbb[1]], w=[sh_fm2])
            P.op("t", lambda e, tb=tb: e.transpose(
                    tbb[0].ap[:, 0:128], rhT.ap[:, tb * 176: tb * 176 + 128], X["identf"].ap),
                 r=[rhT, X["identf"]], w=[tbb[0]])
            P.op("v", lambda e, tb=tb: e.tensor_copy(rh_fm1.ap[:, tb * 128:(tb + 1) * 128],
                                                     tbb[0].ap[:, 0:128]),
                 r=[tbb[0]], w=[rh_fm1])
            P.op("t", lambda e, tb=tb: e.transpose(
                    tbb[1].ap[:, 0:128][0:48, :], rhT.ap[:, tb * 176 + 128: tb * 176 + 176],
                    X["identf"].ap),
                 r=[rhT, X["identf"]], w=[tbb[1]])
            P.op("v", lambda e, tb=tb: e.tensor_copy(rh_fm2.ap[:, tb * 128:(tb + 1) * 128],
                                                     tbb[1].ap[0:48, 0:128]),
                 r=[tbb[1]], w=[rh_fm2])
        schs_v = io[f"sch_s{sl}"].rearrange("(b c) t -> c b t", b=8)
        schr_v = io[f"sch_r{sl}"].rearrange("(b c) t -> c b t", b=8)
        P.dma(C["spillout"], [
            lambda e, schs_v=schs_v: e.dma_start(out=schs_v[0:16], in_=sh_fm1.ap),
            lambda e, schs_v=schs_v: e.dma_start(out=schs_v[16:22], in_=sh_fm2.ap),
            lambda e, schr_v=schr_v: e.dma_start(out=schr_v[0:16], in_=rh_fm1.ap),
            lambda e, schr_v=schr_v: e.dma_start(out=schr_v[16:22], in_=rh_fm2.ap),
        ], r=[sh_fm1, sh_fm2, rh_fm1, rh_fm2], w=[bschs[sl], bschr[sl]], dbg=f"schout{tc}")
        yield

    def gen_hq(tc):
        sl = tc % 2
        hb = hbuf[sl]
        for plo in range(0, FFC, 4):
            phi = min(plo + 4, FFC)
            P.dma(C["sc"], [
                lambda e, plo=plo, phi=phi, sl=sl: e.dma_start(
                    out=shB.ap[:, 0:(phi - plo) * 512],
                    in_=io[f"sch_s{sl}"].rearrange("(b c) t -> b (c t)", b=8)[:, plo * 512:phi * 512]
                    .unsqueeze(1).broadcast_to([8, 16, (phi - plo) * 512])),
                lambda e, plo=plo, phi=phi, sl=sl: e.dma_start(
                    out=rhB.ap[:, 0:(phi - plo) * 512],
                    in_=io[f"sch_r{sl}"].rearrange("(b c) t -> b (c t)", b=8)[:, plo * 512:phi * 512]
                    .unsqueeze(1).broadcast_to([8, 16, (phi - plo) * 512])),
            ], r=[bschs[sl], bschr[sl]], w=[shB, rhB], dbg=f"schrep{tc}_{plo}")
            emit_fp4_quant(P, A, hb, hb, rhB, shB, tmpx, (phi - plo) * 512, 2 * 512,
                           src_off=plo * 512)
            yield

    def emit_down(tc):
        sl = tc % 2
        xr = xr_in[sl]
        hb = hbuf[sl]
        for j in range(NFC):
            wd_ = wdt[0]
            P.dma(C["gw"], [lambda e, wd_=wd_, j=j: e.dma_start(
                    out=wd_.ap.rearrange("p (c o) -> p c o", c=FFC),
                    in_=io["wdq"].rearrange("(c p) n -> p c n", p=128)[:, :, j * 128:(j + 1) * 128])],
                  r=[bwdq], w=[wd_], dbg=f"wdl{tc}_{j}")
            pd = pdb[j % 2]
            for c in range(FFC):
                P.op("t", lambda e, c=c, wd_=wd_, pd=pd, hb=hb: e.matmul(
                        pd.ap, wd_.ap[:, c * 128:(c + 1) * 128], hb.ap[:, c * 512:(c + 1) * 512],
                        start=(c == 0), stop=(c == FFC - 1)),
                     r=[wd_, hb], w=[pd], dbg=f"md{tc}_{j}_{c}")
            P.op("v", lambda e, pd=pd, xr=xr, j=j: e.tensor_tensor(
                    zt.ap[:, j * 512:(j + 1) * 512], pd.ap, xr.ap[:, j * 512:(j + 1) * 512], ALU.add),
                 r=[pd, xr], w=[zt], dbg=f"z2{tc}_{j}")
            P.op("s", lambda e, j=j: e.activation(zsq.ap, zt.ap[:, j * 512:(j + 1) * 512], AF.Square),
                 r=[zt], w=[zsq], dbg=f"z2sq{tc}_{j}")
            P.op("t", lambda e, j=j: e.matmul(stB.ap[0:1, :], X["onesb"].ap[:, 0:1], zsq.ap,
                                              start=(j == 0), stop=(j == NFC - 1)),
                 r=[zsq, X["onesb"]], w=[stB], dbg=f"st2b{tc}_{j}")
            P.op("t", lambda e, j=j: e.matmul(stA.ap[0:1, :], X["onesfc"].ap,
                                              zt.ap[:, j * 512:(j + 1) * 512],
                                              start=(j == 0), stop=(j == NFC - 1)),
                 r=[zt, X["onesfc"]], w=[stA], dbg=f"st2a{tc}_{j}")
        P.op("v", lambda e: e.tensor_scalar_mul(mu.ap, stA.ap[0:1, :], 1.0 / D), r=[stA], w=[mu])
        P.op("v", lambda e: e.tensor_scalar_mul(ms.ap, stB.ap[0:1, :], 1.0 / D), r=[stB], w=[ms])
        P.op("v", lambda e: e.tensor_tensor(bb2.ap, mu.ap, mu.ap, ALU.mult), r=[mu], w=[bb2])
        P.op("v", lambda e: e.tensor_tensor(ms.ap, ms.ap, bb2.ap, ALU.subtract), r=[ms, bb2], w=[ms])
        P.op("s", lambda e: e.activation(ms.ap, ms.ap, AF.Sqrt, bias=X["eps"].ap[0:1, :]),
             r=[ms, X["eps"]], w=[ms])
        P.op("v", lambda e: e.reciprocal(aa.ap, ms.ap), r=[ms], w=[aa])
        P.op("v", lambda e: e.tensor_tensor(bb2.ap, mu.ap, aa.ap, ALU.mult), r=[mu, aa], w=[bb2])
        P.op("v", lambda e: e.tensor_scalar_mul(bb2.ap, bb2.ap, -1.0), r=[bb2], w=[bb2])
        P.op("t", lambda e: e.matmul(abank.ap, X["onesf"].ap[0:1, :], aa.ap, start=True, stop=True),
             r=[aa, X["onesf"]], w=[abank], dbg=f"ab2{tc}")
        P.op("t", lambda e: e.matmul(bbank.ap, X["onesf"].ap[0:1, :], bb2.ap, start=True, stop=True),
             r=[bb2, X["onesf"]], w=[bbank], dbg=f"bb2{tc}")
        for j in range(NFC):
            P.op("v", lambda e, j=j: e.tensor_tensor(
                    zt.ap[:, j * 512:(j + 1) * 512], zt.ap[:, j * 512:(j + 1) * 512], abank.ap, ALU.mult),
                 r=[zt, abank], w=[zt], dbg=f"ln2m{tc}_{j}")
            P.op("v", lambda e, j=j: e.tensor_tensor(
                    zt.ap[:, j * 512:(j + 1) * 512], zt.ap[:, j * 512:(j + 1) * 512], bbank.ap, ALU.add),
                 r=[zt, bbank], w=[zt], dbg=f"ln2a{tc}_{j}")
        P.dma(C["xout"], [lambda e, tc=tc: e.dma_start(
                out=xout[:, tc * 512:(tc + 1) * 512].rearrange("(c p) t -> p c t", p=128),
                in_=zt.ap.rearrange("p (c t) -> p c t", c=NFC))],
              r=[zt], w=[bxout], dbg=f"xout2{tc}")

    for tc in range(NTC + 1):
        if 1 <= tc < NTC:
            pump(gen_front(tc), gen_hq(tc - 1), 0.5)
        elif tc < NTC:
            for _ in gen_front(tc):
                pass
        else:
            for _ in gen_hq(tc - 1):
                pass
        if tc >= 1:
            emit_down(tc - 1)
    A.release()
    return bxout


def build_l1attn(P, A, io, C, X, banks):
    xin, xout = io["xin"], io["xout"]
    bxout = Buf("xout3")
    bks, bvs, bas = Buf("kspill3"), Buf("vspill3"), Buf("aospill3")
    bsch = [Buf("scdram0"), Buf("scdram1")]
    badh = [Buf("adram0"), Buf("adram1")]

    A.mark()
    A.mark_top()
    qfm = [A.alloc_top(f"qfm{c}", 128, T, BF16) for c in range(NFC)]
    A.mark()
    xnb = [A.alloc(f"x3n{c}", 128, T, BF16) for c in range(NFC)]
    wqb = [A.alloc(f"w3{i}", 128, NFC * 128, BF16) for i in range(2)]
    kvstage = [A.alloc(f"kvst{i}", 128, 512, BF16) for i in range(2)]
    xin_t = [A.alloc(f"x3t{i}", 128, NFC * 512, F32) for i in range(2)]
    for tc in range(NTC):
        xt = xin_t[tc % 2]
        src = xin[:, tc * 512:(tc + 1) * 512].rearrange("(c p) t -> p c t", p=128)
        P.dma(C["xin"], [lambda e, xt=xt, src=src: e.dma_start(
                out=xt.ap.rearrange("p (c t) -> p c t", c=NFC), in_=src)],
              w=[xt], dbg=f"x3in{tc}")
        for c in range(NFC):
            P.op("v", lambda e, xt=xt, c=c, tc=tc: e.tensor_copy(
                    xnb[c].ap[:, tc * 512:(tc + 1) * 512], xt.ap[:, c * 512:(c + 1) * 512]),
                 r=[xt], w=[xnb[c]], dbg=f"x3c{tc}_{c}")
    for j in range(3 * NFC):
        kind, jj = divmod(j, NFC)
        wq = wqb[j % 2]
        wsrc = io[["wqT", "wkT", "wvT"][kind]].rearrange("(c p) o -> p c o", p=128)[:, :, jj * 128:(jj + 1) * 128]
        P.dma(C["w"], [lambda e, wq=wq, wsrc=wsrc: e.dma_start(
                out=wq.ap.rearrange("p (c o) -> p c o", c=NFC), in_=wsrc)],
              w=[wq], dbg=f"w3{j}")
        if kind == 0:
            P.op("v", lambda e, wq=wq: e.tensor_scalar_mul(wq.ap, wq.ap, 0.125),
                 r=[wq], w=[wq], dbg=f"w3s{j}")
        for tc in range(NTC):
            pb = banks[4 + ((j * NTC + tc) % 4)]
            for c in range(NFC):
                P.op("t", lambda e, wq=wq, c=c, tc=tc, pb=pb: e.matmul(
                        pb.ap, wq.ap[:, c * 128:(c + 1) * 128],
                        xnb[c].ap[:, tc * 512:(tc + 1) * 512],
                        start=(c == 0), stop=(c == NFC - 1)),
                     r=[wq, xnb[c]], w=[pb], dbg=f"p3{j}_{tc}_{c}")
            if kind == 0:
                P.op("s", lambda e, pb=pb, jj=jj, tc=tc: e.activation(
                        qfm[jj].ap[:, tc * 512:(tc + 1) * 512], pb.ap, AF.Copy),
                     r=[pb], w=[qfm[jj]], dbg=f"qe3{j}_{tc}")
            else:
                st = kvstage[tc % 2]
                P.op("s", lambda e, pb=pb, st=st: e.activation(st.ap, pb.ap, AF.Copy),
                     r=[pb], w=[st], dbg=f"kve3{j}_{tc}")
                spill = io["kspill"] if kind == 1 else io["vspill"]
                bsp = bks if kind == 1 else bvs
                P.dma(C["spillout"], [lambda e, st=st, spill=spill, jj=jj, tc=tc: e.dma_start(
                        out=spill[jj * 128:(jj + 1) * 128, tc * 512:(tc + 1) * 512], in_=st.ap)],
                      r=[st], w=[bsp], dbg=f"kvs3{j}_{tc}")
    A.release()
    HT = T // 2
    NTQ = HT // 512
    A.mark()
    kb = [A.alloc(f"kb{i}", 128, HT, BF16) for i in range(2)]
    prod = [A.alloc(f"pr{i}", 128, HT, BF16) for i in range(2)]
    scstg = [A.alloc("scst0", 16, HT, F32)]
    scsb = A.alloc("scsb", 16, 8 * 512, F32)
    asbh = [A.alloc(f"asbh{i}", 16, 8 * 512, BF16) for i in range(2)]
    mx1 = A.alloc("mx1", 16, 2, F32)
    mx2 = A.alloc("mx2", 16, 2, F32)
    sm1 = A.alloc("sm1", 16, 512, F32)
    sm2 = A.alloc("sm2", 16, 512, F32)
    rd = A.alloc("rd", 16, 512, F32)
    rdf = A.alloc("rdf", 16, 512, F32)
    rdb16 = A.alloc("rdb16", 16, 512, BF16)
    mxM = A.alloc("mxM", 16, 2, F32)
    cf = A.alloc("cf", 16, 2, F32)
    scv = io["scdram"].rearrange("p (h t) -> p h t", h=NH)
    adv = io["adram"].rearrange("p (h t) -> p h t", h=NH)

    def gen_scores(h):
        t0 = h * HT
        for hk in range(NH):
            kbt = kb[hk % 2]
            P.dma(C["kin"], [lambda e, kbt=kbt, hk=hk, t0=t0: e.dma_start(
                    out=kbt.ap,
                    in_=io["kspill"][hk * 64:(hk + 1) * 64, t0:t0 + HT]
                    .unsqueeze(0).broadcast_to([2, 64, HT]))],
                  r=[bks], w=[kbt], dbg=f"kb{h}_{hk}")
            for c in range(NFC):
                pr = prod[c % 2]
                P.op("v", lambda e, pr=pr, c=c, kbt=kbt, t0=t0: e.tensor_tensor(
                        pr.ap, qfm[c].ap[:, t0:t0 + HT], kbt.ap, ALU.mult),
                     r=[qfm[c], kbt], w=[pr], dbg=f"pr{h}_{hk}_{c}")
                for tq in range(NTQ):
                    bk = banks[4 * h + tq]
                    P.op("t", lambda e, pr=pr, c=c, tq=tq, bk=bk: e.matmul(
                            bk.ap[0:16, :], X["e2"].ap[:, c * 16:(c + 1) * 16],
                            pr.ap[:, tq * 512:(tq + 1) * 512],
                            start=(c == 0), stop=(c == NFC - 1)),
                         r=[pr, X["e2"]], w=[bk], dbg=f"sc{h}_{hk}_{c}_{tq}")
            st = scstg[0]
            for tq in range(NTQ):
                bk = banks[4 * h + tq]
                P.op("s", lambda e, bk=bk, tq=tq, st=st: e.activation(
                        st.ap[:, tq * 512:(tq + 1) * 512], bk.ap[0:16, :], AF.Copy),
                     r=[bk], w=[st], dbg=f"sce{h}_{hk}_{tq}")
            P.dma(C["spillout"], [lambda e, hk=hk, st=st, t0=t0: e.dma_start(
                    out=scv[:, hk, t0:t0 + HT], in_=st.ap)],
                  r=[st], w=[bsch[h]], dbg=f"scout{h}_{hk}")
            yield

    def gen_softmax(h):
        for tl in range(NTQ):
            tcc = NTQ * h + tl
            tsl = slice(tcc * 512, (tcc + 1) * 512)
            for i, mxb, smb in ((0, mx1, sm1), (1, mx2, sm2)):
                P.dma(C["xin"], [lambda e, i=i, tsl=tsl: e.dma_start(
                        out=scsb.ap.rearrange("p (h t) -> p h t", h=8),
                        in_=scv[:, 8 * i:8 * i + 8, tsl])],
                      r=[bsch[h]], w=[scsb], dbg=f"smin{tcc}_{i}")
                P.op("v", lambda e, mxb=mxb: e.tensor_reduce(
                        mxb.ap[:, 0:1], scsb.ap, AX.X, ALU.max),
                     r=[scsb], w=[mxb], dbg=f"smmax{tcc}_{i}")
                P.op("v", lambda e, mxb=mxb: e.tensor_scalar_mul(
                        mxb.ap[:, 1:2], mxb.ap[:, 0:1], -1.0),
                     r=[mxb], w=[mxb], dbg=f"smneg{tcc}_{i}")
                P.op("s", lambda e, i=i, mxb=mxb: e.activation(
                        asbh[i].ap, scsb.ap, AF.Exp, bias=mxb.ap[:, 1:2]),
                     r=[scsb, mxb], w=[asbh[i]], dbg=f"smexp{tcc}_{i}")
                P.op("v", lambda e, i=i, smb=smb: e.tensor_reduce(
                        smb.ap, asbh[i].ap.rearrange("p (h t) -> p t h", h=8), AX.X, ALU.add),
                     r=[asbh[i]], w=[smb], dbg=f"smsum{tcc}_{i}")
                yield
            P.op("v", lambda e: e.tensor_tensor(
                    mxM.ap[:, 0:1], mx1.ap[:, 0:1], mx2.ap[:, 0:1], ALU.max),
                 r=[mx1, mx2], w=[mxM], dbg=f"smM{tcc}")
            P.op("v", lambda e: e.tensor_scalar_mul(mxM.ap[:, 1:2], mxM.ap[:, 0:1], -1.0),
                 r=[mxM], w=[mxM], dbg=f"smMn{tcc}")
            for i, mxb in ((0, mx1), (1, mx2)):
                P.op("s", lambda e, i=i, mxb=mxb: e.activation(
                        cf.ap[:, i:i + 1], mxb.ap[:, 0:1], AF.Exp, bias=mxM.ap[:, 1:2]),
                     r=[mxb, mxM], w=[cf], dbg=f"smcf{tcc}_{i}")
            for i, smb in ((0, sm1), (1, sm2)):
                P.op("v", lambda e, i=i, smb=smb: e.tensor_scalar_mul(
                        smb.ap, smb.ap, cf.ap[:, i:i + 1]),
                     r=[smb, cf], w=[smb], dbg=f"smcs{tcc}_{i}")
            P.op("v", lambda e: e.tensor_tensor(sm1.ap, sm1.ap, sm2.ap, ALU.add),
                 r=[sm1, sm2], w=[sm1], dbg=f"smsumc{tcc}")
            P.op("v", lambda e: e.reciprocal(rd.ap, sm1.ap), r=[sm1], w=[rd], dbg=f"smrd{tcc}")
            yield
            for i in (0, 1):
                P.op("v", lambda e, i=i: e.tensor_scalar_mul(rdf.ap, rd.ap, cf.ap[:, i:i + 1]),
                     r=[rd, cf], w=[rdf], dbg=f"smri{tcc}_{i}")
                P.op("v", lambda e: e.tensor_copy(rdb16.ap, rdf.ap),
                     r=[rdf], w=[rdb16], dbg=f"smrb{tcc}_{i}")
                P.op("v", lambda e, i=i: e.tensor_tensor(
                        asbh[i].ap.rearrange("p (h t) -> p h t", h=8),
                        asbh[i].ap.rearrange("p (h t) -> p h t", h=8),
                        rdb16.ap.unsqueeze(1).broadcast_to([16, 8, 512]), ALU.mult),
                     r=[asbh[i], rdb16], w=[asbh[i]], dbg=f"smdiv{tcc}_{i}")
                P.dma(C["spillout"], [lambda e, i=i, tsl=tsl: e.dma_start(
                        out=adv[:, 8 * i:8 * i + 8, tsl],
                        in_=asbh[i].ap.rearrange("p (h t) -> p h t", h=8))],
                      r=[asbh[i]], w=[badh[h]], dbg=f"smout{tcc}_{i}")
                yield

    for _ in gen_scores(0):
        pass
    sm0 = gen_softmax(0)
    pump(gen_scores(1), sm0, 1)
    A.release_top()
    A.mark_top()
    vfm2 = [[A.alloc_top(f"vf{h}_{c}", 128, HT, BF16) for c in range(NFC)] for h in range(2)]
    arep = [A.alloc_top(f"ar{i}", 128, HT, BF16) for i in range(6)]
    prod2 = [A.alloc_top(f"p23{i}", 128, HT, BF16) for i in range(5)]
    aostg = [A.alloc_top(f"aos{i}", 64, HT, BF16) for i in range(2)]
    avchan = [C["qin"], C["gw"], C["sdma"]]
    def gen_av(h):
        t0 = h * HT
        vfm = vfm2[h]
        P.dma(C["aoin"],
              [(lambda e, c=c, t0=t0: e.dma_start(
                    out=vfm[c].ap, in_=io["vspill"][c * 128:(c + 1) * 128, t0:t0 + HT]))
               for c in range(NFC)],
              r=[bvs], w=vfm, dbg=f"vres{h}")
        for hq in range(NH):
            for c2 in range(NFC):
                k = hq * NFC + c2
                ar = arep[k % 6]
                P.dma(avchan[k % 3], [lambda e, ar=ar, hq=hq, c2=c2, t0=t0: e.dma_start(
                        out=ar.ap,
                        in_=adv[hq, 2 * c2:2 * c2 + 2, t0:t0 + HT]
                        .unsqueeze(1).broadcast_to([2, 64, HT]))],
                      r=[badh[h]], w=[ar], dbg=f"ar{h}_{hq}_{c2}")
                pr = prod2[k % 5]
                P.op("v", lambda e, pr=pr, ar=ar, c2=c2: e.tensor_tensor(
                        pr.ap, ar.ap, vfm[c2].ap, ALU.mult),
                     r=[ar, vfm[c2]], w=[pr], dbg=f"p2{h}_{hq}_{c2}")
                for tq in range(NTQ):
                    bk = banks[4 * h + tq]
                    P.op("t", lambda e, pr=pr, tq=tq, bk=bk, c2=c2: e.matmul(
                            bk.ap[0:64, :], X["e3"].ap, pr.ap[:, tq * 512:(tq + 1) * 512],
                            start=(c2 == 0), stop=(c2 == NFC - 1)),
                         r=[pr, X["e3"]], w=[bk], dbg=f"av{h}_{hq}_{c2}_{tq}")
            ast = aostg[hq % 2]
            for tq in range(NTQ):
                bk = banks[4 * h + tq]
                P.op("s", lambda e, bk=bk, tq=tq, ast=ast: e.activation(
                        ast.ap[:, tq * 512:(tq + 1) * 512], bk.ap[0:64, :], AF.Copy),
                     r=[bk], w=[ast], dbg=f"aoe{h}_{hq}_{tq}")
            P.dma(C["spillout"], [lambda e, hq=hq, ast=ast, t0=t0: e.dma_start(
                    out=io["aospill"][hq * 64:(hq + 1) * 64, t0:t0 + HT], in_=ast.ap)],
                  r=[ast], w=[bas], dbg=f"ao3{h}_{hq}")
            yield

    pump(gen_av(0), gen_softmax(1), 1)
    for _ in gen_av(1):
        pass
    A.release()
    A.release_top()
    ob = emit_outproj_ln(P, A, io, C, X, banks, "wfcT", xin, xout, io["aospill"], bas, bxout)
    A.release()
    return ob


def build_l1ffn(P, A, io, C, X, banks):
    xin, xout = io["xin"], io["xout"]
    bxout = Buf("xout4")
    bxg = Buf("xgdram")
    bsw1, bsw3, bsw2 = Buf("sw1b"), Buf("sw3b"), Buf("sw2b")
    HSC = HS // 128

    A.mark()
    sb1 = A.alloc("sb1", 128, HSC, F32)
    sb3 = A.alloc("sb3", 128, HSC, F32)
    b1c = A.alloc("b1c", 128, NFC * NE, F32)
    bcomb = A.alloc("bcomb", 128, NFC * NE, F32)
    sb2h = A.alloc("sb2h", 128, NFC, F32)
    P.dma(C["const"], [
        lambda e: e.dma_start(out=sb1.ap, in_=io["sb1r"]),
        lambda e: e.dma_start(out=sb3.ap, in_=io["sb3r"]),
        lambda e: e.dma_start(out=b1c.ap.rearrange("p (h e) -> p h e", e=NE), in_=io["b1r"]),
        lambda e: e.dma_start(out=bcomb.ap.rearrange("p (h e) -> p h e", e=NE), in_=io["b2r"]),
        lambda e: e.dma_start(out=sb2h.ap, in_=io["sb2r"]),
    ], w=[sb1, sb3, b1c, bcomb, sb2h], dbg="biases")
    bcn = A.alloc("bcn", 128, NFC * NE, F32)
    P.op("v", lambda e: e.tensor_scalar_mul(sb2h.ap, sb2h.ap, 0.5), r=[sb2h], w=[sb2h])
    for r in range(NE):
        e0, e1 = (3 * r) % NE, (3 * r + 7) % NE
        for oc in range(NFC):
            P.op("v", lambda e, oc=oc, r=r, e0=e0, e1=e1: e.tensor_tensor(
                    bcn.ap[:, oc * NE + r: oc * NE + r + 1],
                    bcomb.ap[:, oc * NE + e0: oc * NE + e0 + 1],
                    bcomb.ap[:, oc * NE + e1: oc * NE + e1 + 1], ALU.add),
                 r=[bcomb], w=[bcn], dbg=f"bc{r}_{oc}")
    P.op("v", lambda e: e.tensor_scalar_mul(bcn.ap, bcn.ap, 0.25), r=[bcn], w=[bcn])
    for oc in range(NFC):
        P.op("v", lambda e, oc=oc: e.tensor_tensor(
                bcn.ap[:, oc * NE:(oc + 1) * NE],
                bcn.ap[:, oc * NE:(oc + 1) * NE],
                sb2h.ap[:, oc:oc + 1].broadcast_to([128, NE]), ALU.add),
             r=[bcn, sb2h], w=[bcn], dbg=f"bcs{oc}")

    A.mark()
    swst = A.alloc("swst", 128, HS, BF16)
    for (wname, dramb, bflag, fold) in (("sw1", "sw1b", bsw1, 1.0), ("sw3", "sw3b", bsw3, 1.0),
                                        ("sw2", "sw2b", bsw2, 0.5)):
        nr = io[wname].shape[0]
        ncols = io[wname].shape[1]
        for a in range(nr // 128):
            P.dma(C["w"], [lambda e, a=a, wname=wname, ncols=ncols: e.dma_start(
                    out=swst.ap[:, 0:ncols], in_=io[wname][a * 128:(a + 1) * 128, :])],
                  w=[swst], dbg=f"swl{wname}{a}")
            if fold != 1.0:
                P.op("v", lambda e, ncols=ncols, fold=fold: e.tensor_scalar_mul(
                        swst.ap[:, 0:ncols], swst.ap[:, 0:ncols], fold),
                     r=[swst], w=[swst], dbg=f"swf{wname}{a}")
            P.dma(C["spillout"], [lambda e, a=a, dramb=dramb, ncols=ncols: e.dma_start(
                    out=io[dramb][a * 128:(a + 1) * 128, :], in_=swst.ap[:, 0:ncols])],
                  r=[swst], w=[bflag], dbg=f"sws{wname}{a}")
    A.release()

    A.mark()
    xin_t = [A.alloc(f"x4t{i}", 128, NFC * 512, F32) for i in range(2)]
    xgs = [A.alloc(f"xgs{i}", 128, NFC * 512, F32) for i in range(2)]
    for tc in range(NTC):
        xt = xin_t[tc % 2]
        xg_ = xgs[tc % 2]
        src = xin[:, tc * 512:(tc + 1) * 512].rearrange("(c p) t -> p c t", p=128)
        P.dma(C["xin"], [lambda e, xt=xt, src=src: e.dma_start(
                out=xt.ap.rearrange("p (c t) -> p c t", c=NFC), in_=src)],
              w=[xt], dbg=f"x4in{tc}")
        for c in range(NFC):
            P.op("v", lambda e, xt=xt, xg_=xg_, c=c: e.tensor_copy(
                    xg_.ap[:, c * 512:(c + 1) * 512].rearrange("p (r m) -> p r m", r=8),
                    xt.ap[:, c * 512:(c + 1) * 512].rearrange("p (m r) -> p r m", r=8)),
                 r=[xt], w=[xg_], dbg=f"cmp{tc}_{c}")
        for c in range(NFC):
            P.dma(C["spillout"], [lambda e, xg_=xg_, c=c, tc=tc: e.dma_start(
                    out=io["xgdram"][c * 128:(c + 1) * 128, :]
                    .rearrange("p (r m) -> p r m", m=CW)[:, :, tc * 64:(tc + 1) * 64],
                    in_=xg_.ap[:, c * 512:(c + 1) * 512].rearrange("p (r m) -> p r m", r=8))],
                  r=[xg_], w=[bxg], dbg=f"xgo{tc}_{c}")
    A.release()
    xg = [A.alloc(f"xg{c}", 128, T, BF16) for c in range(NFC)]
    es = [A.alloc(f"es{c}", 128, T, BF16) for c in range(NFC)]
    P.dma(C["gw"],
          [(lambda e, c=c: e.dma_start(out=xg[c].ap, in_=io["xgdram"][c * 128:(c + 1) * 128, :]))
           for c in range(NFC)],
          r=[bxg], w=xg, dbg="xgcast")

    A.mark()
    A.mark_top()
    w1b = [A.alloc_top(f"w1b{i}", 128, NFC * D, BF16) for i in range(2)]
    w2b = [A.alloc_top(f"w2b{i}", 128, NFC * D, BF16) for i in range(2)]
    hE = [A.alloc(f"hE{i}", 128, CW, BF16) for i in range(NFC)]
    phb = [banks[0], banks[2]]
    pob = [banks[1], banks[3]]
    written = set()
    for ex in range(NE):
        w1 = w1b[ex % 2]
        w2 = w2b[ex % 2]
        P.dma(C["gw"], [
            lambda e, ex=ex, w1=w1: e.dma_start(
                out=w1.ap.rearrange("p (c o) -> p c o", c=NFC),
                in_=io["w1e"][ex].rearrange("(c p) o -> p c o", p=128)),
        ], w=[w1], dbg=f"we1_{ex}")
        P.dma(C["w"], [
            lambda e, ex=ex, w2=w2: e.dma_start(
                out=w2.ap.rearrange("p (c o) -> p c o", c=NFC),
                in_=io["w2e"][ex].rearrange("(c p) o -> p c o", p=128)),
        ], w=[w2], dbg=f"we2_{ex}")
        P.op("v", lambda e, w2=w2: e.tensor_scalar_mul(w2.ap, w2.ap, 0.25), r=[w2], w=[w2],
             dbg=f"wf{ex}")
        rs = [r for r in range(NE) if (3 * r) % NE == ex or (3 * r + 7) % NE == ex]
        for r in rs:
            for hc in range(NFC):
                ph = phb[hc % 2]
                for c in range(NFC):
                    P.op("t", lambda e, hc=hc, c=c, r=r, w1=w1, ph=ph: e.matmul(
                            ph.ap[:, 0:CW], w1.ap[:, (c * NFC + hc) * 128:(c * NFC + hc + 1) * 128],
                            xg[c].ap[:, r * CW:(r + 1) * CW],
                            start=(c == 0), stop=(c == NFC - 1)),
                         r=[w1, xg[c]], w=[ph], dbg=f"e1_{ex}_{r}_{hc}_{c}")
                P.op("s", lambda e, hc=hc, ex=ex, ph=ph: e.activation(
                        hE[hc].ap, ph.ap[:, 0:CW], AF.Gelu, bias=b1c.ap[:, hc * NE + ex: hc * NE + ex + 1]),
                     r=[ph, b1c], w=[hE[hc]], dbg=f"gel{ex}_{r}_{hc}")
            for oc in range(NFC):
                po = pob[oc % 2]
                for hc in range(NFC):
                    P.op("t", lambda e, oc=oc, hc=hc, w2=w2, po=po: e.matmul(
                            po.ap[:, 0:CW], w2.ap[:, (hc * NFC + oc) * 128:(hc * NFC + oc + 1) * 128],
                            hE[hc].ap, start=(hc == 0), stop=(hc == NFC - 1)),
                         r=[w2, hE[hc]], w=[po], dbg=f"e2_{ex}_{r}_{oc}_{hc}")
                if r not in written:
                    P.op("v", lambda e, oc=oc, r=r, po=po: e.tensor_copy(
                            es[oc].ap[:, r * CW:(r + 1) * CW], po.ap[:, 0:CW]),
                         r=[po], w=[es[oc]], dbg=f"es0_{ex}_{r}_{oc}")
                else:
                    P.op("v", lambda e, oc=oc, r=r, po=po: e.tensor_tensor(
                            es[oc].ap[:, r * CW:(r + 1) * CW], po.ap[:, 0:CW],
                            es[oc].ap[:, r * CW:(r + 1) * CW], ALU.add),
                         r=[po, es[oc]], w=[es[oc]], dbg=f"es1_{ex}_{r}_{oc}")
            written.add(r)
    A.release()
    A.release_top()

    A.mark()
    sw1t = [A.alloc(f"s1t{i}", 128, NFC * 128, BF16) for i in range(2)]
    sw3t = [A.alloc(f"s3t{i}", 128, NFC * 128, BF16) for i in range(2)]
    sw2t = [A.alloc("s2t0", 128, HSC * 128, BF16)]
    mbf = [A.alloc(f"m{j}", 128, CW, BF16) for j in range(HSC)]
    abf = A.alloc("abf", 128, CW, BF16)
    xr4 = A.alloc("xr4", 128, NFC * CW, F32)
    zt = A.alloc("zt4", 128, NFC * CW, F32)
    zsq = A.alloc("zsq4", 128, CW, BF16)
    mu = A.alloc("mu4", 1, CW, F32)
    ms = A.alloc("ms4", 1, CW, F32)
    aa = A.alloc("aa4", 1, CW, F32)
    bb2 = A.alloc("bb24", 1, CW, F32)
    p1b, p3b = banks[0], banks[1]
    stA, stB = banks[6], banks[5]
    abank, bbank = banks[7], banks[4]
    for r in range(NE):
        P.dma(C["xin"], [lambda e, r=r: e.dma_start(
                out=xr4.ap.rearrange("p (c t) -> p c t", c=NFC),
                in_=io["xgdram"][:, r * CW:(r + 1) * CW].rearrange("(c p) t -> p c t", p=128))],
              r=[bxg], w=[xr4], dbg=f"xr4_{r}")
        for j in range(HSC):
            s1 = sw1t[j % 2]
            s3 = sw3t[j % 2]
            P.dma(C["w"], [
                lambda e, s1=s1, j=j: e.dma_start(
                    out=s1.ap.rearrange("p (c o) -> p c o", c=NFC),
                    in_=io["sw1b"].rearrange("(c p) o -> p c o", p=128)[:, :, j * 128:(j + 1) * 128]),
                lambda e, s3=s3, j=j: e.dma_start(
                    out=s3.ap.rearrange("p (c o) -> p c o", c=NFC),
                    in_=io["sw3b"].rearrange("(c p) o -> p c o", p=128)[:, :, j * 128:(j + 1) * 128]),
            ], r=[bsw1, bsw3], w=[s1, s3], dbg=f"swt{r}_{j}")
            for c in range(NFC):
                P.op("t", lambda e, s1=s1, c=c, r=r: e.matmul(
                        p1b.ap[:, 0:CW], s1.ap[:, c * 128:(c + 1) * 128], xg[c].ap[:, r * CW:(r + 1) * CW],
                        start=(c == 0), stop=(c == NFC - 1)),
                     r=[s1, xg[c]], w=[p1b], dbg=f"m1_{r}_{j}_{c}")
            for c in range(NFC):
                P.op("t", lambda e, s3=s3, c=c, r=r: e.matmul(
                        p3b.ap[:, 0:CW], s3.ap[:, c * 128:(c + 1) * 128], xg[c].ap[:, r * CW:(r + 1) * CW],
                        start=(c == 0), stop=(c == NFC - 1)),
                     r=[s3, xg[c]], w=[p3b], dbg=f"m3_{r}_{j}_{c}")
            P.op("s", lambda e, j=j: e.activation(abf.ap, p1b.ap[:, 0:CW], AF.Silu,
                                                  bias=sb1.ap[:, j:j + 1]),
                 r=[p1b, sb1], w=[abf], dbg=f"sil{r}_{j}")
            P.op("v", lambda e, j=j: e.tensor_scalar_add(mbf[j].ap, p3b.ap[:, 0:CW],
                                                         sb3.ap[:, j:j + 1]),
                 r=[p3b, sb3], w=[mbf[j]], dbg=f"u3{r}_{j}")
            P.op("v", lambda e, j=j: e.tensor_tensor(mbf[j].ap, abf.ap, mbf[j].ap, ALU.mult),
                 r=[abf, mbf[j]], w=[mbf[j]], dbg=f"mm{r}_{j}")
        for oc in range(NFC):
            s2 = sw2t[0]
            P.dma(C["gw"], [lambda e, s2=s2, oc=oc: e.dma_start(
                    out=s2.ap.rearrange("p (c o) -> p c o", c=HSC),
                    in_=io["sw2b"].rearrange("(c p) o -> p c o", p=128)[:, :, oc * 128:(oc + 1) * 128])],
                  r=[bsw2], w=[s2], dbg=f"sw2t{r}_{oc}")
            pb = banks[2 + (oc % 2)]
            for j in range(HSC):
                P.op("t", lambda e, s2=s2, j=j, pb=pb: e.matmul(
                        pb.ap[:, 0:CW], s2.ap[:, j * 128:(j + 1) * 128], mbf[j].ap,
                        start=(j == 0), stop=(j == HSC - 1)),
                     r=[s2, mbf[j]], w=[pb], dbg=f"m2_{r}_{oc}_{j}")
            P.op("v", lambda e, pb=pb, oc=oc, r=r: e.scalar_tensor_tensor(
                    zt.ap[:, oc * CW:(oc + 1) * CW], pb.ap[:, 0:CW],
                    bcn.ap[:, oc * NE + r: oc * NE + r + 1],
                    es[oc].ap[:, r * CW:(r + 1) * CW], ALU.add, ALU.add),
                 r=[pb, bcn, es[oc]], w=[zt], dbg=f"comb{r}_{oc}")
            P.op("v", lambda e, oc=oc: e.tensor_tensor(
                    zt.ap[:, oc * CW:(oc + 1) * CW], zt.ap[:, oc * CW:(oc + 1) * CW],
                    xr4.ap[:, oc * CW:(oc + 1) * CW], ALU.add),
                 r=[zt, xr4], w=[zt], dbg=f"zadd4_{r}_{oc}")
            P.op("s", lambda e, oc=oc: e.activation(zsq.ap, zt.ap[:, oc * CW:(oc + 1) * CW],
                                                    AF.Square),
                 r=[zt], w=[zsq], dbg=f"zsq4_{r}_{oc}")
            P.op("t", lambda e, oc=oc: e.matmul(stB.ap[0:1, 0:CW], X["onesb"].ap[:, 0:1], zsq.ap,
                                                start=(oc == 0), stop=(oc == NFC - 1)),
                 r=[zsq, X["onesb"]], w=[stB], dbg=f"s2st4_{r}_{oc}")
            P.op("t", lambda e, oc=oc: e.matmul(stA.ap[0:1, 0:CW], X["onesfc"].ap,
                                                zt.ap[:, oc * CW:(oc + 1) * CW],
                                                start=(oc == 0), stop=(oc == NFC - 1)),
                 r=[zt, X["onesfc"]], w=[stA], dbg=f"s1st4_{r}_{oc}")
        P.op("v", lambda e: e.tensor_scalar_mul(mu.ap, stA.ap[0:1, 0:CW], 1.0 / D), r=[stA], w=[mu])
        P.op("v", lambda e: e.tensor_scalar_mul(ms.ap, stB.ap[0:1, 0:CW], 1.0 / D), r=[stB], w=[ms])
        P.op("v", lambda e: e.tensor_tensor(bb2.ap, mu.ap, mu.ap, ALU.mult), r=[mu], w=[bb2])
        P.op("v", lambda e: e.tensor_tensor(ms.ap, ms.ap, bb2.ap, ALU.subtract), r=[ms, bb2], w=[ms])
        P.op("s", lambda e: e.activation(ms.ap, ms.ap, AF.Sqrt, bias=X["eps"].ap[0:1, :]),
             r=[ms, X["eps"]], w=[ms])
        P.op("v", lambda e: e.reciprocal(aa.ap, ms.ap), r=[ms], w=[aa])
        P.op("v", lambda e: e.tensor_tensor(bb2.ap, mu.ap, aa.ap, ALU.mult), r=[mu, aa], w=[bb2])
        P.op("v", lambda e: e.tensor_scalar_mul(bb2.ap, bb2.ap, -1.0), r=[bb2], w=[bb2])
        P.op("t", lambda e: e.matmul(abank.ap[:, 0:CW], X["onesf"].ap[0:1, :], aa.ap, start=True, stop=True),
             r=[aa, X["onesf"]], w=[abank], dbg=f"ab4{r}")
        P.op("t", lambda e: e.matmul(bbank.ap[:, 0:CW], X["onesf"].ap[0:1, :], bb2.ap, start=True, stop=True),
             r=[bb2, X["onesf"]], w=[bbank], dbg=f"bb4{r}")
        for oc in range(NFC):
            P.op("v", lambda e, oc=oc: e.tensor_tensor(
                    zt.ap[:, oc * CW:(oc + 1) * CW], zt.ap[:, oc * CW:(oc + 1) * CW],
                    abank.ap[:, 0:CW], ALU.mult),
                 r=[zt, abank], w=[zt], dbg=f"ln4m{r}_{oc}")
            P.op("v", lambda e, oc=oc: e.tensor_tensor(
                    zt.ap[:, oc * CW:(oc + 1) * CW], zt.ap[:, oc * CW:(oc + 1) * CW],
                    bbank.ap[:, 0:CW], ALU.add),
                 r=[zt, bbank], w=[zt], dbg=f"ln4a{r}_{oc}")
        P.dma(C["xout"], [lambda e, r=r: e.dma_start(
                out=xout[:, r * CW:(r + 1) * CW].rearrange("(c p) t -> p c t", p=128),
                in_=zt.ap.rearrange("p (c t) -> p c t", c=NFC))],
              r=[zt], w=[bxout], dbg=f"xout4{r}")
    A.release()
    A.release()
    return bxout


def make_chans(P):
    return {
        "const": P.chan("const", "g", 2),
        "w": P.chan("w", "g", 2),
        "xin": P.chan("xin", "q", 2),
        "spillout": P.chan("spout", "q", 2),
        "qin": P.chan("qin", "q", 4),
        "kin": P.chan("kin", "q", 2),
        "aoin": P.chan("aoin", "q", 2),
        "xout": P.chan("xout", "q", 2),
        "gw": P.chan("gw", "g", 4),
        "sdma": P.chan("sdma", "s", 4),
        "sc": P.chan("sc", "g", 2),
    }

CONST_SPECS = {
    "c_ident": ((128, 128), BF16),
    "c_onesb": ((128, 128), BF16),
    "c_onesf": ((1, 128), F32),
    "c_onesfc": ((128, 1), F32),
    "c_eps": ((128, 1), F32),
    "c_identf": ((128, 128), F32),
    "c_e2": ((128, 128), BF16),
    "c_e3": ((128, 64), BF16),
}

def const_arrays():
    return {
        "c_ident": np.eye(128, dtype=np.float32).astype(BF),
        "c_onesb": np.ones((128, 128), dtype=np.float32).astype(BF),
        "c_onesf": np.ones((1, 128), dtype=np.float32),
        "c_onesfc": np.ones((128, 1), dtype=np.float32),
        "c_eps": np.full((128, 1), 1e-5, dtype=np.float32),
        "c_identf": np.eye(128, dtype=np.float32),
        "c_e2": _mk_e2(),
        "c_e3": _mk_e3(),
    }

def _mk_e2():
    e2 = np.zeros((128, 8, 16), np.float32)
    for c in range(8):
        for p in range(128):
            e2[p, c, 2 * c + p // 64] = 1.0
    return e2.reshape(128, 128).astype(BF)

def _mk_e3():
    e3 = np.zeros((128, 64), np.float32)
    for p in range(128):
        e3[p, p % 64] = 1.0
    return e3.astype(BF)


def build_program(io_spec, phase_fns):
    nc = bass.Bass("TRN2", target_bir_lowering=False)
    io = {}
    for name, (shape, dt, kind) in io_spec.items():
        io[name] = nc.dram_tensor(name, list(shape), dt, kind=kind).ap()
    ctxs = []
    sbcm = nc.sbuf_tensor("arena", [128, SB_BYTES // 4], F32)
    sbh = sbcm.__enter__(); ctxs.append(sbcm)
    cmp_ = nc.psum_tensor("psarena", [128, 4096], F32)
    ph_ = cmp_.__enter__(); ctxs.append(cmp_)
    psall = ph_[:]
    banks = [Buf(f"bank{i}", psall[:, i * 512:(i + 1) * 512]) for i in range(8)]
    banks.append(Buf("psall", psall))

    P = Prog(nc)
    A = Arena(sbh[:])
    C = make_chans(P)
    X = setup_consts(P, A, io, C)
    outs = []
    for fn in phase_fns:
        ob = fn(P, A, io, C, X, banks)
        if ob is not None:
            outs.append(ob)
    P.op("q", lambda e: e.nop(), r=outs, dbg="final_wait")
    P.finalize()
    P.replay()
    for h in reversed(ctxs):
        h.__exit__(None, None, None)
    P.close()
    return nc

def prep_host(inputs):
    c = const_arrays()
    w = {}
    asc = np.ascontiguousarray
    w["w_qkv"] = inputs["L0_attn_w_qkv"]
    w["w_out"] = inputs["L0_attn_w_out"]
    w["gamma"] = asc(inputs["L0_attn_gamma"].reshape(NFC, 128).T)
    w["wqT"] = asc(inputs["L1_attn_Wq"].T)
    w["wkT"] = asc(inputs["L1_attn_Wk"].T)
    w["wvT"] = asc(inputs["L1_attn_Wv"].T)
    w["wfcT"] = asc(inputs["L1_attn_Wfc"].T)
    w["wg"] = inputs["L0_ffn_w_gate"]
    w["wu"] = inputs["L0_ffn_w_up"]
    w["wd"] = inputs["L0_ffn_w_down"]
    w["sw1"] = inputs["L1_ffn_sw1"]
    w["sw3"] = inputs["L1_ffn_sw3"]
    w["sw2"] = inputs["L1_ffn_sw2"]
    w["sb1r"] = asc(inputs["L1_ffn_sb1"].reshape(HS // 128, 128).T)
    w["sb3r"] = asc(inputs["L1_ffn_sb3"].reshape(HS // 128, 128).T)
    w["sb2r"] = asc(inputs["L1_ffn_sb2"].reshape(NFC, 128).T)
    w["w1e"] = inputs["L1_ffn_w1"]
    w["w2e"] = inputs["L1_ffn_w2"]
    w["b1r"] = asc(inputs["L1_ffn_b1"].T.reshape(NFC, 128, NE).transpose(1, 0, 2))
    w["b2r"] = asc(inputs["L1_ffn_b2"].T.reshape(NFC, 128, NE).transpose(1, 0, 2))
    _ = inputs["L1_ffn_t_emb"].shape
    xs = [asc(inputs["L0_attn_x"][b].T) for b in range(8)]
    return c, w, xs

def unpermute_tokens(yT):
    yt = yT.T
    return np.ascontiguousarray(yt.reshape(NE, T // NE, D).transpose(1, 0, 2).reshape(T, D))



def _spec_p1():
    s = {
        "xin": ((D, T), F32, "ExternalInput"),
        "xout": ((D, T), F32, "ExternalOutput"),
        "w_qkv": ((D, 3 * D), F32, "ExternalInput"),
        "w_out": ((D, D), F32, "ExternalInput"),
        "gamma": ((128, NFC), F32, "ExternalInput"),
        "qspill": ((D, T), BF16, "Internal"),
        "kspill": ((D, T), BF16, "Internal"),
        "aospill": ((D, T), BF16, "Internal"),
    }
    return s, ["w_qkv", "w_out", "gamma"]

def _spec_p2():
    s = {
        "xin": ((D, T), F32, "ExternalInput"),
        "xout": ((D, T), F32, "ExternalOutput"),
        "wg": ((FF, D), F32, "ExternalInput"),
        "wu": ((FF, D), F32, "ExternalInput"),
        "wd": ((D, FF), F32, "ExternalInput"),
        "wgq": ((D, FF), BF16, "Internal"),
        "wuq": ((D, FF), BF16, "Internal"),
        "wdq": ((FF, D), BF16, "Internal"),
        "scx_s": ((64, 512), F32, "Internal"),
        "scx_r": ((64, 512), F32, "Internal"),
        "sch_s0": ((176, 512), F32, "Internal"),
        "sch_r0": ((176, 512), F32, "Internal"),
        "sch_s1": ((176, 512), F32, "Internal"),
        "sch_r1": ((176, 512), F32, "Internal"),
    }
    return s, ["wg", "wu", "wd"]

def _spec_p3():
    s = {
        "xin": ((D, T), F32, "ExternalInput"),
        "xout": ((D, T), F32, "ExternalOutput"),
        "wqT": ((D, D), F32, "ExternalInput"),
        "wkT": ((D, D), F32, "ExternalInput"),
        "wvT": ((D, D), F32, "ExternalInput"),
        "wfcT": ((D, D), F32, "ExternalInput"),
        "kspill": ((D, T), BF16, "Internal"),
        "vspill": ((D, T), BF16, "Internal"),
        "aospill": ((D, T), BF16, "Internal"),
        "scdram": ((16, NH * T), F32, "Internal"),
        "adram": ((16, NH * T), BF16, "Internal"),
    }
    return s, ["wqT", "wkT", "wvT", "wfcT"]

def _spec_p4():
    s = {
        "xin": ((D, T), F32, "ExternalInput"),
        "xout": ((D, T), F32, "ExternalOutput"),
        "sw1": ((D, HS), F32, "ExternalInput"),
        "sw3": ((D, HS), F32, "ExternalInput"),
        "sw2": ((HS, D), F32, "ExternalInput"),
        "w1e": ((NE, D, D), F32, "ExternalInput"),
        "w2e": ((NE, D, D), F32, "ExternalInput"),
        "sb1r": ((128, HS // 128), F32, "ExternalInput"),
        "sb3r": ((128, HS // 128), F32, "ExternalInput"),
        "sb2r": ((128, NFC), F32, "ExternalInput"),
        "b1r": ((128, NFC, NE), F32, "ExternalInput"),
        "b2r": ((128, NFC, NE), F32, "ExternalInput"),
        "xgdram": ((D, T), F32, "Internal"),
        "sw1b": ((D, HS), BF16, "Internal"),
        "sw3b": ((D, HS), BF16, "Internal"),
        "sw2b": ((HS, D), BF16, "Internal"),
    }
    return s, ["sw1", "sw3", "sw2", "w1e", "w2e", "sb1r", "sb3r", "sb2r", "b1r", "b2r"]

_PHASES = [
    (_spec_p1, lambda P, A, io, C, X, banks: build_attn(P, A, io, C, X, banks, 0)),
    (_spec_p2, build_l0ffn),
    (_spec_p3, build_l1attn),
    (_spec_p4, build_l1ffn),
]

_prog_cache = {}

def _spec_fused():
    s = {
        "xin": ((D, T), F32, "ExternalInput"),
        "x1": ((D, T), F32, "Internal"),
        "x2": ((D, T), F32, "Internal"),
        "x3": ((D, T), F32, "Internal"),
        "xout": ((D, T), F32, "ExternalOutput"),
    }
    wall = []
    for i, (spec_fn, _) in enumerate(_PHASES):
        sp, wk = spec_fn()
        wall += wk
        for name, v in sp.items():
            if name in ("xin", "xout"):
                continue
            key = name if name not in s else f"{name}_p{i}"
            assert key not in s or s[key] == v, key
            s[key] = v
    for nm in ("kspill", "aospill"):
        s[nm + "3"] = ((D, T), BF16, "Internal")
    return s, wall

def _phase_io(io, i):
    xs = ["xin", "x1", "x2", "x3", "xout"]
    m = dict(io)
    m["xin"] = io[xs[i]]
    m["xout"] = io[xs[i + 1]]
    if i == 2:
        m["kspill"] = io["kspill3"]
        m["aospill"] = io["aospill3"]
    return m

def build_fused():
    spec, _ = _spec_fused()
    for n, (s, dt) in CONST_SPECS.items():
        spec[n] = (s, dt, "ExternalInput")
    def mk(i):
        def fn(P, A, io, C, X, banks):
            return _PHASES[i][1](P, A, _phase_io(io, i), C, X, banks)
        return fn
    return build_program(spec, [mk(i) for i in range(4)])

def _get_prog(i):
    if i not in _prog_cache:
        spec_fn, build_fn = _PHASES[i]
        spec, _ = spec_fn()
        for n, (s, dt) in CONST_SPECS.items():
            spec[n] = (s, dt, "ExternalInput")
        _prog_cache[i] = build_program(spec, [build_fn])
    return _prog_cache[i]

FUSED = True

def kernel(**inputs):
    consts, w, xs = prep_host(inputs)
    if FUSED:
        if "fused" not in _prog_cache:
            _prog_cache["fused"] = build_fused()
        nc = _prog_cache["fused"]
        _, wkeys = _spec_fused()
        in_maps = []
        for b in range(8):
            m = dict(consts)
            m["xin"] = xs[b]
            for k in wkeys:
                m[k] = w[k]
            in_maps.append(m)
        res = run_bass_kernel_spmd(nc, in_maps, core_ids=list(range(8)))
        cur = [np.ascontiguousarray(res.results[b]["xout"]) for b in range(8)]
    else:
        cur = xs
        for i in range(4):
            nc = _get_prog(i)
            _, wkeys = _PHASES[i][0]()
            in_maps = []
            for b in range(8):
                m = dict(consts)
                m["xin"] = cur[b]
                for k in wkeys:
                    m[k] = w[k]
                in_maps.append(m)
            res = run_bass_kernel_spmd(nc, in_maps, core_ids=list(range(8)))
            cur = [np.ascontiguousarray(res.results[b]["xout"]) for b in range(8)]
    out = np.stack([unpermute_tokens(cur[b]) for b in range(8)], axis=0)
    return out.astype(np.float32)


# version 24
# speedup vs baseline: 1.1377x; 1.0008x over previous
import numpy as np
import ml_dtypes
import concourse.bass as bass
import concourse.mybir as mybir
from concourse.bass_utils import run_bass_kernel_spmd

F32 = mybir.dt.float32
BF16 = mybir.dt.bfloat16
U8 = mybir.dt.uint8
AF = mybir.ActivationFunctionType
ALU = mybir.AluOpType
AX = mybir.AxisListType
BF = ml_dtypes.bfloat16

T = 4096
D = 1024
NH = 16
HD = 64
FF = 2816
HS = 2048
NE = 8
NFC = D // 128
NTC = T // 512
KC = T // 128
CW = T // NE

def set_T(t):
    global T, NTC, KC, CW
    T = t
    NTC = T // 512
    KC = T // 128
    CW = T // NE
SB_BYTES = 203 * 1024

ENGS = ("t", "v", "s", "g", "q")
ENGMAP = {"t": "tensor", "v": "vector", "s": "scalar", "g": "gpsimd", "q": "sync"}


class Buf:
    __slots__ = ("name", "ap", "aliases", "rng")
    def __init__(self, name, ap=None):
        self.name = name
        self.ap = ap
        self.aliases = []
        self.rng = None

class Op:
    __slots__ = ("eng", "fn", "seq", "waits", "inc_needed", "num", "sem", "incby",
                 "chan", "slot", "dbg", "blast")
    def __init__(self, eng, fn, seq, dbg=""):
        self.eng = eng
        self.fn = fn
        self.seq = seq
        self.waits = []
        self.inc_needed = False
        self.num = None
        self.sem = None
        self.incby = 1
        self.chan = None
        self.slot = None
        self.dbg = dbg
        self.blast = None

class Chan:
    def __init__(self, prog, name, queue, nslots=2):
        self.p = prog
        self.name = name
        self.queue = queue
        self.nslots = nslots
        self.sems = [prog.new_sem(f"{name}{i}") for i in range(nslots)]
        self.count = [0] * nslots
        self.batch = 0
        self.waiters = [[] for _ in range(nslots)]

class Arena:
    def __init__(self, base_ap_f32):
        self.base = base_ap_f32
        self.off = 0
        self.marks = []
        self.top = SB_BYTES
        self.tmarks = []
        self.hi = 0
        self.live = []
        self.freed = []
    def alloc(self, name, rows, cols, dt):
        esz = 4 if dt == F32 else (1 if dt == U8 else 2)
        off = (self.off + 63) & ~63
        nbytes = cols * esz
        assert off + nbytes <= self.top, f"SBUF arena overflow at {name}: {(off+nbytes)/1024:.1f}KB (top {self.top/1024:.1f}KB)"
        self.off = off + nbytes
        self.hi = max(self.hi, self.off)
        ap = self.base.bitcast(dt)[0:rows, off // esz: off // esz + cols]
        b = Buf(name, ap)
        b.rng = (off, off + nbytes)
        b.aliases = [f for f in self.freed if f.rng[0] < b.rng[1] and b.rng[0] < f.rng[1]]
        self.live.append(b)
        return b
    def alloc_top(self, name, rows, cols, dt):
        esz = 4 if dt == F32 else (1 if dt == U8 else 2)
        nbytes = cols * esz
        top = (self.top - nbytes) & ~63
        assert top >= self.off, f"SBUF arena overflow (top) at {name}"
        self.top = top
        ap = self.base.bitcast(dt)[0:rows, top // esz: top // esz + cols]
        b = Buf(name, ap)
        b.rng = (top, top + nbytes)
        b.aliases = [f for f in self.freed if f.rng[0] < b.rng[1] and b.rng[0] < f.rng[1]]
        self.live.append(b)
        return b
    def mark_top(self):
        self.tmarks.append(self.top)
    def release_top(self):
        hi = self.tmarks.pop()
        still = []
        for b in self.live:
            if b.rng[0] >= self.top and b.rng[1] <= hi:
                self.freed.append(b)
            else:
                still.append(b)
        self.live = still
        self.top = hi
    def mark(self):
        self.marks.append(self.off)
    def release(self):
        lo = self.marks.pop()
        still = []
        for b in self.live:
            if b.rng[0] >= lo:
                self.freed.append(b)
            else:
                still.append(b)
        self.live = still
        self.freed = [f for f in self.freed if f.rng[1] > 0]
        self.off = lo

class Prog:
    def __init__(self, nc):
        self.nc = nc
        self.E = {e: [] for e in ENGS}
        self.seq = 0
        self.state = {}
        self.sem_ctx = []
        self.esem = {}
        for e in ENGS:
            self.esem[e] = self.new_sem(f"eng_{e}")
    def new_sem(self, name):
        h = self.nc.semaphore(name=f"sm_{name}")
        sem = h.__enter__()
        self.sem_ctx.append(h)
        return sem
    def chan(self, name, queue, nslots=2):
        return Chan(self, name, queue, nslots)
    def _dep(self, o, producers):
        for p in producers:
            if p is None or p is o:
                continue
            if p.chan is not None and p.blast is not None and p.blast is not p:
                p = p.blast
                if p is o:
                    continue
            if p.eng == o.eng and p.eng == "t":
                continue
            if p not in o.waits:
                o.waits.append(p)
                p.inc_needed = True
                if p.chan is not None:
                    p.chan.waiters[p.slot].append(o)
    def _track(self, o, r, w):
        for b in w:
            if b.aliases and id(b) not in self.state:
                for ab in b.aliases:
                    st = self.state.get(id(ab))
                    if st:
                        self._dep(o, [st["w"]])
                        self._dep(o, st["r"])
                b.aliases = []
        for b in r:
            st = self.state.get(id(b))
            if st:
                self._dep(o, [st["w"]])
        for b in w:
            st = self.state.get(id(b))
            if st:
                self._dep(o, [st["w"]])
                self._dep(o, st["r"])
        for b in r:
            self.state.setdefault(id(b), {"w": None, "r": []})["r"].append(o)
        for b in w:
            self.state[id(b)] = {"w": o, "r": []}
    def op(self, eng, fn, r=(), w=(), dbg=""):
        o = Op(eng, fn, self.seq, dbg); self.seq += 1
        self._track(o, r, w)
        self.E[eng].append(o)
        return o
    def dma(self, chan, fns, r=(), w=(), dbg=""):
        s = chan.batch % chan.nslots
        gate = chan.waiters[s]
        chan.waiters[s] = []
        chan.batch += 1
        sem = chan.sems[s]
        ops = []
        for i, fn in enumerate(fns):
            o = Op(chan.queue, fn, self.seq, dbg); self.seq += 1
            o.sem = sem
            o.incby = 16
            chan.count[s] += 1
            o.num = chan.count[s]
            o.chan = chan
            o.slot = s
            if i == 0:
                for g0 in gate:
                    self._dep(o, [g0])
            ops.append(o)
            self.E[chan.queue].append(o)
        for o in ops:
            o.blast = ops[-1]
        o0, oL = ops[0], ops[-1]
        for b in w:
            if b.aliases and id(b) not in self.state:
                for ab in b.aliases:
                    st = self.state.get(id(ab))
                    if st:
                        self._dep(o0, [st["w"]])
                        self._dep(o0, st["r"])
                b.aliases = []
        for b in r:
            st = self.state.get(id(b))
            if st:
                self._dep(o0, [st["w"]])
        for b in w:
            st = self.state.get(id(b))
            if st:
                self._dep(o0, [st["w"]])
                self._dep(o0, st["r"])
        for b in r:
            self.state.setdefault(id(b), {"w": None, "r": []})["r"].append(oL)
        for b in w:
            self.state[id(b)] = {"w": oL, "r": []}
        return oL
    def finalize(self):
        for e in ("t", "v", "s", "g", "q"):
            n = 0
            for o in self.E[e]:
                if o.sem is None and o.inc_needed:
                    n += 1
                    o.num = n
                    o.sem = self.esem[e]
                    o.incby = 1
    def replay(self):
        nc = self.nc
        with nc.Block() as blk:
            for e in ENGS:
                ops = self.E[e]
                def body(eng, ops=ops):
                    waited = {}
                    for o in ops:
                        for p in o.waits:
                            tgt = p.num * p.incby
                            key = id(p.sem)
                            if waited.get(key, -1) >= tgt:
                                continue
                            waited[key] = tgt
                            eng.wait_ge(p.sem, tgt)
                        inst = o.fn(eng)
                        if o.sem is not None and (o.incby == 16 or o.inc_needed):
                            inst.then_inc(o.sem, o.incby)
                getattr(blk, ENGMAP[e])(body)
    def close(self):
        for h in reversed(self.sem_ctx):
            h.__exit__(None, None, None)



def setup_consts(P, A, io, C):
    X = {}
    X["ident"] = A.alloc("ident", 128, 128, BF16)
    X["onesb"] = A.alloc("onesb", 128, 128, BF16)
    X["onesf"] = A.alloc("onesf", 1, 128, F32)
    X["onesfc"] = A.alloc("onesfc", 128, 1, F32)
    X["eps"] = A.alloc("eps", 128, 1, F32)
    X["identf"] = A.alloc("identf", 128, 128, F32)
    X["e2"] = A.alloc("e2", 128, 128, BF16)
    X["e3"] = A.alloc("e3", 128, 64, BF16)
    ch = C["const"]
    P.dma(ch, [
        lambda e: e.dma_start(out=X["ident"].ap, in_=io["c_ident"]),
        lambda e: e.dma_start(out=X["onesb"].ap, in_=io["c_onesb"]),
        lambda e: e.dma_start(out=X["onesf"].ap, in_=io["c_onesf"]),
        lambda e: e.dma_start(out=X["onesfc"].ap, in_=io["c_onesfc"]),
        lambda e: e.dma_start(out=X["eps"].ap, in_=io["c_eps"]),
        lambda e: e.dma_start(out=X["identf"].ap, in_=io["c_identf"]),
        lambda e: e.dma_start(out=X["e2"].ap, in_=io["c_e2"]),
        lambda e: e.dma_start(out=X["e3"].ap, in_=io["c_e3"]),
    ], w=[X["ident"], X["onesb"], X["onesf"], X["onesfc"], X["eps"], X["identf"], X["e2"], X["e3"]], dbg="consts")
    return X


def emit_outproj_ln(P, A, io, C, X, banks, wname, xin, xout, aospill, bas, bxout):
    A.mark()
    wo = A.alloc("wo", 128, NFC * NFC * 128, BF16)
    P.dma(C["w"], [lambda e: e.dma_start(
            out=wo.ap.rearrange("p (c j o) -> p c j o", c=NFC, j=NFC),
            in_=io[wname].rearrange("(c p) (j o) -> p c j o", p=128, j=NFC))],
          w=[wo], dbg="wo")
    ao_in = [A.alloc(f"aoin{i}", 128, NFC * 512, BF16) for i in range(2)]
    xr_in = [A.alloc(f"xrin{i}", 128, NFC * 512, F32) for i in range(2)]
    ztb = [A.alloc(f"zt{i}", 128, NFC * 512, F32) for i in range(2)]
    zsq = A.alloc("zsq", 128, 512, BF16)
    mu = A.alloc("mu", 1, 512, F32)
    ms = A.alloc("ms", 1, 512, F32)
    aa = A.alloc("aa", 1, 512, F32)
    bb2 = A.alloc("bb2", 1, 512, F32)
    yst = [A.alloc(f"yst{i}", 128, NFC * 512, F32) for i in range(2)]
    stA, stB = banks[6], banks[5]
    abank, bbank = banks[7], banks[3]

    for tc in range(NTC):
        sl = tc % 2
        ai = ao_in[sl]
        xr = xr_in[sl]
        zt = ztb[sl]
        P.dma(C["aoin"], [lambda e, ai=ai, tc=tc: e.dma_start(
                out=ai.ap.rearrange("p (c t) -> p c t", c=NFC),
                in_=aospill[:, tc * 512:(tc + 1) * 512].rearrange("(c p) t -> p c t", p=128))],
              r=[bas], w=[ai], dbg=f"aoin{tc}")
        P.dma(C["xin"], [lambda e, xr=xr, tc=tc: e.dma_start(
                out=xr.ap.rearrange("p (c t) -> p c t", c=NFC),
                in_=xin[:, tc * 512:(tc + 1) * 512].rearrange("(c p) t -> p c t", p=128))],
              w=[xr], dbg=f"xrin{tc}")
        for j in range(NFC):
            pb = banks[(j % 2)]
            for c in range(NFC):
                P.op("t", lambda e, j=j, c=c, ai=ai, pb=pb: e.matmul(
                        pb.ap, wo.ap[:, (c * NFC + j) * 128:(c * NFC + j + 1) * 128],
                        ai.ap[:, c * 512:(c + 1) * 512], start=(c == 0), stop=(c == NFC - 1)),
                     r=[wo, ai], w=[pb], dbg=f"wout{tc}_{j}_{c}")
            P.op("v", lambda e, pb=pb, xr=xr, j=j, zt=zt: e.tensor_tensor(
                    zt.ap[:, j * 512:(j + 1) * 512], pb.ap, xr.ap[:, j * 512:(j + 1) * 512], ALU.add),
                 r=[pb, xr], w=[zt], dbg=f"zadd{tc}_{j}")
            P.op("s", lambda e, j=j, zt=zt: e.activation(zsq.ap, zt.ap[:, j * 512:(j + 1) * 512], AF.Square),
                 r=[zt], w=[zsq], dbg=f"zsq{tc}_{j}")
            P.op("t", lambda e, j=j: e.matmul(stB.ap[0:1, :], X["onesb"].ap[:, 0:1], zsq.ap,
                                              start=(j == 0), stop=(j == NFC - 1)),
                 r=[zsq, X["onesb"]], w=[stB], dbg=f"s2st{tc}_{j}")
            P.op("t", lambda e, j=j, zt=zt: e.matmul(stA.ap[0:1, :], X["onesfc"].ap,
                                              zt.ap[:, j * 512:(j + 1) * 512],
                                              start=(j == 0), stop=(j == NFC - 1)),
                 r=[zt, X["onesfc"]], w=[stA], dbg=f"s1st{tc}_{j}")
        P.op("v", lambda e: e.tensor_scalar_mul(mu.ap, stA.ap[0:1, :], 1.0 / D), r=[stA], w=[mu])
        P.op("v", lambda e: e.tensor_scalar_mul(ms.ap, stB.ap[0:1, :], 1.0 / D), r=[stB], w=[ms])
        P.op("v", lambda e: e.tensor_tensor(bb2.ap, mu.ap, mu.ap, ALU.mult), r=[mu], w=[bb2])
        P.op("v", lambda e: e.tensor_tensor(ms.ap, ms.ap, bb2.ap, ALU.subtract),
             r=[ms, bb2], w=[ms], dbg="var")
        P.op("s", lambda e: e.activation(ms.ap, ms.ap, AF.Sqrt, bias=X["eps"].ap[0:1, :]), r=[ms, X["eps"]], w=[ms])
        P.op("v", lambda e: e.reciprocal(aa.ap, ms.ap), r=[ms], w=[aa])
        P.op("v", lambda e: e.tensor_tensor(bb2.ap, mu.ap, aa.ap, ALU.mult), r=[mu, aa], w=[bb2])
        P.op("v", lambda e: e.tensor_scalar_mul(bb2.ap, bb2.ap, -1.0), r=[bb2], w=[bb2])
        P.op("t", lambda e: e.matmul(abank.ap, X["onesf"].ap[0:1, :], aa.ap, start=True, stop=True),
             r=[aa, X["onesf"]], w=[abank], dbg=f"abc{tc}")
        P.op("t", lambda e: e.matmul(bbank.ap, X["onesf"].ap[0:1, :], bb2.ap, start=True, stop=True),
             r=[bb2, X["onesf"]], w=[bbank], dbg=f"bbc{tc}")
        ys = yst[sl]
        for j in range(NFC):
            P.op("v", lambda e, j=j, ys=ys, zt=zt: e.tensor_tensor(
                    ys.ap[:, j * 512:(j + 1) * 512], zt.ap[:, j * 512:(j + 1) * 512],
                    abank.ap, ALU.mult),
                 r=[zt, abank], w=[ys], dbg=f"lnm{tc}_{j}")
            P.op("v", lambda e, j=j, ys=ys: e.tensor_tensor(
                    ys.ap[:, j * 512:(j + 1) * 512], ys.ap[:, j * 512:(j + 1) * 512],
                    bbank.ap, ALU.add),
                 r=[ys, bbank], w=[ys], dbg=f"lna{tc}_{j}")
        P.dma(C["xout"], [lambda e, ys=ys, tc=tc: e.dma_start(
                out=xout[:, tc * 512:(tc + 1) * 512].rearrange("(c p) t -> p c t", p=128),
                in_=ys.ap.rearrange("p (c t) -> p c t", c=NFC))],
              r=[ys], w=[bxout], dbg=f"xout{tc}")
    A.release()
    return bxout


def build_attn(P, A, io, C, X, banks, layer):
    nc = P.nc
    xin, xout = io["xin"], io["xout"]
    qspill, kspill, aospill = io["qspill"], io["kspill"], io["aospill"]
    bqs, bks, bas = Buf("qspill"), Buf("kspill"), Buf("aospill")
    bxout = Buf("xout")

    A.mark()
    negBB = A.alloc("negBB", 128, NH * NTC, F32)
    g32 = A.alloc("g32", 128, NFC, F32)
    gsb = A.alloc("gsb", 128, NFC, F32)
    qn2m = A.alloc("qn2m", 1, NH * NTC, F32)
    kn2m = A.alloc("kn2m", 1, NH * NTC, F32)
    kn2h = A.alloc("kn2h", 1, NH, F32)
    negB = A.alloc("negB", 1, NH * NTC, F32)
    A.mark()
    vP = A.alloc("vP", 128, NH * KC * 65, BF16)
    A.mark()
    xnb = [A.alloc(f"xn{c}", 128, T, BF16) for c in range(NFC)]

    P.op("v", lambda e: e.memset(vP.ap, 1.0), w=[vP], dbg="vp_ones")
    if layer == 0:
        P.dma(C["const"], [lambda e: e.dma_start(out=gsb.ap, in_=io["gamma"])], w=[gsb])
        P.op("v", lambda e: e.tensor_scalar(g32.ap, gsb.ap, 1.0, 32.0, ALU.add, ALU.mult),
             r=[gsb], w=[g32], dbg="g32")

    A.mark()
    xin_t = [A.alloc(f"xint{i}", 128, NFC * 512, F32) for i in range(2)]
    xsq = [A.alloc(f"xsq{i}", 128, NFC * 512, BF16) for i in range(2)]
    snt = A.alloc("snt", 1, 512, F32)
    stt = A.alloc("stt", 1, 512, F32)
    stat_bank = banks[6]
    bc_bank = banks[7]
    for tc in range(NTC):
        sl = tc % 2
        xt = xin_t[sl]
        src = xin[:, tc * 512:(tc + 1) * 512].rearrange("(c p) t -> p c t", p=128)
        P.dma(C["xin"], [lambda e, xt=xt, src=src: e.dma_start(
                out=xt.ap.rearrange("p (c t) -> p c t", c=NFC), in_=src)],
              w=[xt], dbg=f"xin{tc}")
        if layer == 0:
            xq = xsq[sl]
            P.op("s", lambda e, xt=xt, xq=xq: e.activation(xq.ap, xt.ap, AF.Square),
                 r=[xt], w=[xq], dbg=f"xsq{tc}")
            for c in range(NFC):
                P.op("t", lambda e, xq=xq, c=c: e.matmul(
                        stat_bank.ap[0:1, :], X["onesb"].ap[:, 0:1],
                        xq.ap[:, c * 512:(c + 1) * 512], start=(c == 0), stop=(c == NFC - 1)),
                     r=[xq, X["onesb"]], w=[stat_bank], dbg=f"stat{tc}_{c}")
            P.op("v", lambda e: e.tensor_copy(snt.ap, stat_bank.ap[0:1, :]),
                 r=[stat_bank], w=[snt], dbg=f"snev{tc}")
            P.op("s", lambda e: e.activation(snt.ap, snt.ap, AF.Sqrt),
                 r=[snt], w=[snt], dbg=f"sqrt{tc}")
            P.op("v", lambda e: e.tensor_scalar_max(snt.ap, snt.ap, 1e-12),
                 r=[snt], w=[snt])
            P.op("v", lambda e: e.reciprocal(stt.ap, snt.ap),
                 r=[snt], w=[stt], dbg=f"recip{tc}")
            P.op("t", lambda e: e.matmul(bc_bank.ap, X["onesf"].ap[0:1, :],
                                         stt.ap, start=True, stop=True),
                 r=[stt, X["onesf"]], w=[bc_bank], dbg=f"stokB{tc}")
            for c in range(NFC):
                P.op("v", lambda e, xt=xt, c=c, tc=tc: e.tensor_tensor(
                        xnb[c].ap[:, tc * 512:(tc + 1) * 512],
                        xt.ap[:, c * 512:(c + 1) * 512], bc_bank.ap, ALU.mult),
                     r=[xt, bc_bank], w=[xnb[c]], dbg=f"xn{tc}_{c}")
        else:
            for c in range(NFC):
                P.op("v", lambda e, xt=xt, c=c, tc=tc: e.tensor_copy(
                        xnb[c].ap[:, tc * 512:(tc + 1) * 512], xt.ap[:, c * 512:(c + 1) * 512]),
                     r=[xt], w=[xnb[c]], dbg=f"xcast{tc}_{c}")
    A.release()

    A.mark()
    wqb = [A.alloc(f"wq{i}", 128, NFC * 128, BF16) for i in range(2)]
    qstage = [A.alloc(f"qst{i}", 128, 512, BF16) for i in range(2)]
    sqstage = [A.alloc(f"sqst{i}", 128, 512, BF16) for i in range(2)]
    vstage = [A.alloc(f"vst{i}", 128, T, BF16) for i in range(2)]
    n2bank = banks[5]

    def w_src(j):
        if layer == 0:
            return io["w_qkv"].rearrange("(c p) o -> p c o", p=128)[:, :, j * 128:(j + 1) * 128]
        kind, jj = divmod(j, NFC)
        wsrc = io[["wqT", "wkT", "wvT"][kind]]
        return wsrc.rearrange("(c p) o -> p c o", p=128)[:, :, jj * 128:(jj + 1) * 128]

    for j in range(3 * NFC):
        sl = j % 2
        wq = wqb[sl]
        P.dma(C["w"], [lambda e, wq=wq, j=j: e.dma_start(
                out=wq.ap.rearrange("p (c o) -> p c o", c=NFC), in_=w_src(j))],
              w=[wq], dbg=f"wq{j}")
        kind, jj = divmod(j, NFC)
        if layer == 0:
            isq = (j < NFC)
            for c in range(NFC):
                if isq:
                    P.op("v", lambda e, wq=wq, c=c: e.tensor_scalar(
                            wq.ap[:, c * 128:(c + 1) * 128], wq.ap[:, c * 128:(c + 1) * 128],
                            g32.ap[:, c:c + 1], 0.125, ALU.mult, ALU.mult),
                         r=[wq, g32], w=[wq], dbg=f"wfix{j}_{c}")
                else:
                    P.op("v", lambda e, wq=wq, c=c: e.tensor_scalar_mul(
                            wq.ap[:, c * 128:(c + 1) * 128], wq.ap[:, c * 128:(c + 1) * 128],
                            g32.ap[:, c:c + 1]),
                         r=[wq, g32], w=[wq], dbg=f"wfix{j}_{c}")
        elif j < NFC:
            P.op("v", lambda e, wq=wq: e.tensor_scalar_mul(wq.ap, wq.ap, 0.125),
                 r=[wq], w=[wq], dbg=f"wq_scale{j}")
        for tc in range(NTC):
            pb = banks[(j * NTC + tc) % 2]
            for c in range(NFC):
                P.op("t", lambda e, wq=wq, c=c, tc=tc, pb=pb: e.matmul(
                        pb.ap, wq.ap[:, c * 128:(c + 1) * 128],
                        xnb[c].ap[:, tc * 512:(tc + 1) * 512],
                        start=(c == 0), stop=(c == NFC - 1)),
                     r=[wq, xnb[c]], w=[pb], dbg=f"qkv{j}_{tc}_{c}")
            if kind < 2:
                st = qstage[tc % 2]
                sq = sqstage[tc % 2]
                P.op("s", lambda e, pb=pb, st=st: e.activation(st.ap, pb.ap, AF.Copy),
                     r=[pb], w=[st], dbg=f"qkev{j}_{tc}")
                P.op("s", lambda e, st=st, sq=sq: e.activation(sq.ap, st.ap, AF.Square),
                     r=[st], w=[sq], dbg=f"qsq{j}_{tc}")
                for half in range(2):
                    h = jj * 2 + half
                    P.op("t", lambda e, sq=sq, half=half: e.matmul(
                            n2bank.ap[0:1, :], X["onesb"].ap[64 * half:64 * half + 64, 0:1],
                            sq.ap[64 * half:64 * half + 64, :], start=True, stop=True),
                         r=[sq, X["onesb"]], w=[n2bank], dbg=f"n2{j}_{tc}_{half}")
                    dst = (qn2m if kind == 0 else kn2m)
                    P.op("v", lambda e, dst=dst, h=h, tc=tc: e.reduce_max(
                            dst.ap[:, h * NTC + tc: h * NTC + tc + 1], n2bank.ap[0:1, :], AX.X),
                         r=[n2bank], w=[dst], dbg=f"n2m{j}_{tc}_{half}")
                spill = qspill if kind == 0 else kspill
                bsp = bqs if kind == 0 else bks
                P.dma(C["spillout"], [lambda e, st=st, spill=spill, jj=jj, tc=tc: e.dma_start(
                        out=spill[jj * 128:(jj + 1) * 128, tc * 512:(tc + 1) * 512], in_=st.ap)],
                      r=[st], w=[bsp], dbg=f"spill{j}_{tc}")
            else:
                vs = vstage[jj % 2]
                P.op("v", lambda e, pb=pb, vs=vs, tc=tc: e.tensor_copy(
                        vs.ap[:, tc * 512:(tc + 1) * 512], pb.ap),
                     r=[pb], w=[vs], dbg=f"vev{j}_{tc}")
        if kind == 2:
            vs = vstage[jj % 2]
            for tb in range(KC):
                tb_bank = banks[2 + (tb % 2)]
                P.op("t", lambda e, vs=vs, tb=tb, tb_bank=tb_bank: e.transpose(
                        tb_bank.ap.bitcast(BF16)[:, 0:128], vs.ap[:, tb * 128:(tb + 1) * 128],
                        X["ident"].ap),
                     r=[vs, X["ident"]], w=[tb_bank], dbg=f"vT{jj}_{tb}")
                def cp(e, tb_bank=tb_bank, jj=jj, tb=tb):
                    srcv = tb_bank.ap.bitcast(BF16)[:, 0:128].rearrange("p (h d) -> p h d", h=2)
                    dstv = vP.ap.rearrange("p (h k o) -> p h k o", h=NH, k=KC)[:, 2 * jj:2 * jj + 2, tb, 0:64]
                    return e.tensor_copy(dstv, srcv)
                P.op("v", cp, r=[tb_bank], w=[vP], dbg=f"vPcp{jj}_{tb}")
    A.release()
    A.release()

    bc_bank = banks[7]
    P.op("v", lambda e: e.tensor_reduce(kn2h.ap[:, 0:1], kn2m.ap, AX.X, ALU.max),
         r=[kn2m], w=[kn2h], dbg="kn2h")
    P.op("v", lambda e: e.tensor_reduce(negB.ap[:, 0:1], qn2m.ap, AX.X, ALU.max),
         r=[qn2m], w=[negB], dbg="qn2g")
    P.op("s", lambda e: e.activation(negB.ap[:, 0:1], negB.ap[:, 0:1], AF.Sqrt), r=[negB], w=[negB])
    P.op("s", lambda e: e.activation(kn2h.ap[:, 0:1], kn2h.ap[:, 0:1], AF.Sqrt), r=[kn2h], w=[kn2h])
    P.op("v", lambda e: e.tensor_tensor(negB.ap[:, 0:1], negB.ap[:, 0:1], kn2h.ap[:, 0:1], ALU.mult),
         r=[negB, kn2h], w=[negB], dbg="negB")
    P.op("v", lambda e: e.tensor_scalar_mul(negB.ap[:, 0:1], negB.ap[:, 0:1], -1.0), r=[negB], w=[negB])
    P.op("t", lambda e: e.matmul(bc_bank.ap[:, 0:1], X["onesf"].ap[0:1, :], negB.ap[:, 0:1],
                                 start=True, stop=True),
         r=[negB, X["onesf"]], w=[bc_bank], dbg="negBbc")
    P.op("v", lambda e: e.tensor_copy(negBB.ap[:, 0:1], bc_bank.ap[:, 0:1]), r=[bc_bank], w=[negBB])

    A.mark()
    psall = banks[8]
    ktile = [A.alloc(f"kt{i}", 128, T, BF16) for i in range(2)]
    qtile = [A.alloc(f"qt{i}", 128, 512, BF16) for i in range(4)]
    pt = [A.alloc(f"pt{i}", 128, 1536, BF16) for i in range(3)]
    rdsb = A.alloc("rdsb", 1, 1024, F32)
    rdbb = A.alloc("rdbb", 64, 512, F32)
    aostage = [A.alloc(f"aost{i}", 128, 512, BF16) for i in range(2)]
    osbuf = [A.alloc(f"osb{i}", 65, 1024, F32) for i in range(2)]
    obank = {0: banks[6], 1: banks[7]}

    units = [(kc, head) for kc in range(KC) for head in (0, 1)]
    NB = (len(units) + 2) // 3
    for hp in range(NFC):
        kt = ktile[hp % 2]
        P.dma(C["kin"], [lambda e, kt=kt, hp=hp: e.dma_start(
                out=kt.ap, in_=kspill[hp * 128:(hp + 1) * 128, :])],
              r=[bks], w=[kt], dbg=f"kin{hp}")
        for qc in range(NTC):
            qt = qtile[qc % 4]
            P.dma(C["qin"], [lambda e, qt=qt, hp=hp, qc=qc: e.dma_start(
                    out=qt.ap, in_=qspill[hp * 128:(hp + 1) * 128, qc * 512:(qc + 1) * 512])],
                  r=[bqs], w=[qt], dbg=f"qin{hp}_{qc}")

            def emit_pv(j):
                us = units[3 * j:3 * j + 3]
                p = pt[j % 3]
                for i, (kc, head) in enumerate(us):
                    ob = obank[head]
                    def pv(e, p=p, i=i, kc=kc, head=head, ob=ob, hp=hp):
                        lhs = vP.ap.rearrange("p (h k o) -> p h k o", h=NH, k=KC)[:, 2 * hp + head, kc, :]
                        return e.matmul(ob.ap[0:65, :], lhs, p.ap[:, i * 512:(i + 1) * 512],
                                        start=(kc == 0), stop=(kc == KC - 1))
                    P.op("t", pv, r=[p, vP], w=[ob], dbg=f"pv{hp}_{qc}_{j}_{i}")

            for j in range(NB):
                us = units[3 * j:3 * j + 3]
                sb0 = (3 * j) % 6
                bks_used = [banks[sb0 + i] for i in range(len(us))]
                for i, (kc, head) in enumerate(us):
                    bk = bks_used[i]
                    P.op("t", lambda e, kt=kt, qt=qt, kc=kc, head=head, bk=bk: e.matmul(
                            bk.ap, kt.ap[64 * head:64 * head + 64, kc * 128:(kc + 1) * 128],
                            qt.ap[64 * head:64 * head + 64, :], start=True, stop=True),
                         r=[kt, qt], w=[bk], dbg=f"qk{hp}_{qc}_{j}_{i}")
                if j > 0:
                    emit_pv(j - 1)
                n = len(us) * 512
                p = pt[j % 3]
                P.op("s", lambda e, sb0=sb0, n=n, p=p: e.activation(
                        p.ap[:, 0:n], psall.ap[:, sb0 * 512:sb0 * 512 + n], AF.Exp,
                        bias=negBB.ap[:, 0:1]),
                     r=bks_used + [negBB], w=[p], dbg=f"exp{hp}_{qc}_{j}")
            emit_pv(NB - 1)
            ao = aostage[qc % 2]
            osb = osbuf[qc % 2]
            for head, ob in ((0, obank[0]), (1, obank[1])):
                P.op("v", lambda e, ob=ob, osb=osb, head=head: e.tensor_copy(
                        osb.ap[:, head * 512:(head + 1) * 512], ob.ap[0:65, :]),
                     r=[ob], w=[osb], dbg=f"oev{hp}_{qc}_{head}")
            for head in (0, 1):
                P.op("v", lambda e, osb=osb, head=head: e.reciprocal(
                        rdsb.ap[:, head * 512:(head + 1) * 512],
                        osb.ap[64:65, head * 512:(head + 1) * 512]),
                     r=[osb], w=[rdsb], dbg=f"rd{hp}_{qc}_{head}")
                rb = obank[head]
                P.op("t", lambda e, rb=rb, head=head: e.matmul(
                        rb.ap[0:64, :], X["onesf"].ap[0:1, 0:64],
                        rdsb.ap[:, head * 512:(head + 1) * 512], start=True, stop=True),
                     r=[rdsb, X["onesf"]], w=[rb], dbg=f"rdbc{hp}_{qc}_{head}")
                P.op("v", lambda e, rb=rb, osb=osb, ao=ao, head=head: e.tensor_tensor(
                        ao.ap[64 * head:64 * head + 64, :], osb.ap[0:64, 512 * head:512 * head + 512],
                        rb.ap[0:64, :], ALU.mult),
                     r=[osb, rb], w=[ao], dbg=f"aonorm{hp}_{qc}_{head}")
            P.dma(C["spillout"], [lambda e, ao=ao, hp=hp, qc=qc: e.dma_start(
                    out=aospill[hp * 128:(hp + 1) * 128, qc * 512:(qc + 1) * 512], in_=ao.ap)],
                  r=[ao], w=[bas], dbg=f"aoout{hp}_{qc}")
    A.release()
    A.release()

    ob = emit_outproj_ln(P, A, io, C, X, banks,
                          ("w_out" if layer == 0 else "wfcT"), xin, xout, aospill, bas, bxout)
    A.release()
    return ob




C1R = float(3 * 2 ** 21)
C2R = float(3 * 2 ** 22)
C3R = float(3 * 2 ** 23)

def emit_fp4_quant(P, A, src_buf, dst_buf, rB_buf, sB_buf, tmp, ncols, pcols, src_off=0):
    v, w1, a05, a1 = tmp["v"], tmp["q"], tmp["a"], tmp["b"]
    m2, m4 = tmp["m2"], tmp["m4"]
    for lo in range(0, ncols, pcols):
        hi = min(lo + pcols, ncols)
        n = hi - lo
        s0, s1 = src_off + lo, src_off + hi
        P.op("v", lambda e, s0=s0, s1=s1, lo=lo, hi=hi, n=n: e.tensor_tensor(
                v.ap[:, 0:n], src_buf.ap[:, s0:s1], rB_buf.ap[:, lo:hi], ALU.mult),
             r=[src_buf, rB_buf], w=[v], dbg="q_v")
        P.op("s", lambda e, n=n: e.activation(w1.ap[:, 0:n], v.ap[:, 0:n], AF.Square),
             r=[v], w=[w1], dbg="q_vsq")
        P.op("v", lambda e, n=n: e.tensor_scalar(m2.ap[:, 0:n], w1.ap[:, 0:n], 4.0, None, ALU.is_gt),
             r=[w1], w=[m2], dbg="q_m2")
        P.op("v", lambda e, n=n: e.tensor_scalar(m4.ap[:, 0:n], w1.ap[:, 0:n], 16.0, None, ALU.is_gt),
             r=[w1], w=[m4], dbg="q_m4")
        P.op("v", lambda e, n=n: e.tensor_scalar(a05.ap[:, 0:n], v.ap[:, 0:n], C1R, C1R,
                                                 ALU.add, ALU.subtract),
             r=[v], w=[a05], dbg="q_a05")
        P.op("v", lambda e, n=n: e.tensor_scalar(a1.ap[:, 0:n], v.ap[:, 0:n], C2R, C2R,
                                                 ALU.add, ALU.subtract),
             r=[v], w=[a1], dbg="q_a1")
        P.op("v", lambda e, n=n: e.tensor_scalar(w1.ap[:, 0:n], v.ap[:, 0:n], C3R, C3R,
                                                 ALU.add, ALU.subtract),
             r=[v], w=[w1], dbg="q_a2")
        P.op("v", lambda e, n=n: e.copy_predicated(a05.ap[:, 0:n], m2.ap[:, 0:n], a1.ap[:, 0:n]),
             r=[m2, a1, a05], w=[a05], dbg="q_s1")
        P.op("v", lambda e, n=n: e.copy_predicated(a05.ap[:, 0:n], m4.ap[:, 0:n], w1.ap[:, 0:n]),
             r=[m4, w1, a05], w=[a05], dbg="q_s2")
        P.op("v", lambda e, s0=s0, s1=s1, n=n, lo=lo, hi=hi: e.tensor_tensor(
                dst_buf.ap[:, s0:s1], a05.ap[:, 0:n], sB_buf.ap[:, lo:hi], ALU.mult),
             r=[a05, sB_buf], w=[dst_buf], dbg="q_deq")


def gen_wquant_nat(P, A, io, C, X, banks, wname, wq_dram, nrows, kdim, bwq, tbpair):
    nch = nrows // 128
    kch = kdim // 128
    nblk = kdim // 16
    wnat = [A.alloc(f"wn{wname}{i}", 128, kdim, F32) for i in range(2)]
    sw = [A.alloc(f"sw{wname}{i}", 128, nblk, F32) for i in range(2)]
    rw = [A.alloc(f"rw{wname}{i}", 128, nblk, F32) for i in range(2)]
    swB = [A.alloc(f"swB{wname}{i}", 128, kdim, F32) for i in range(2)]
    rwB = [A.alloc(f"rwB{wname}{i}", 128, kdim, F32) for i in range(2)]
    tmp = [dict([(k, A.alloc(f"wt{wname}{k}{i}", 128, kdim, F32)) for k in ("v", "q", "a", "b")]
                + [("m2", A.alloc(f"wm2{wname}{i}", 128, kdim, U8)),
                   ("m4", A.alloc(f"wm4{wname}{i}", 128, kdim, U8))]) for i in range(2)]
    wqd = [A.alloc(f"wqd{wname}{i}", 128, kdim, BF16) for i in range(2)]
    stg = [A.alloc(f"wstg{wname}{i}", 128, kdim, BF16) for i in range(2)]
    tb0, tb1 = tbpair
    k16 = lambda b: b.ap.rearrange("p (b k) -> p b k", k=16)
    for a in range(nch):
        i = a % 2
        wn = wnat[i]
        P.dma(C["xin"], [lambda e, wn=wn, a=a: e.dma_start(
                out=wn.ap, in_=io[wname][a * 128:(a + 1) * 128, :])],
              w=[wn], dbg=f"wl{wname}{a}")
        P.op("v", lambda e, wn=wn, i=i: e.tensor_reduce(
                sw[i].ap, wn.ap.rearrange("p (b k) -> p b k", k=16), AX.X, ALU.max,
                apply_absolute_value=True),
             r=[wn], w=[sw[i]], dbg=f"ws{wname}{a}")
        P.op("v", lambda e, i=i: e.tensor_scalar(sw[i].ap, sw[i].ap, 1.0 / 6.0, 1e-12,
                                                 ALU.mult, ALU.max),
             r=[sw[i]], w=[sw[i]], dbg=f"ws2{wname}{a}")
        P.op("v", lambda e, i=i: e.reciprocal(rw[i].ap, sw[i].ap),
             r=[sw[i]], w=[rw[i]], dbg=f"wr{wname}{a}")
        P.op("s", lambda e, i=i: e.activation(
                k16(swB[i]), sw[i].ap.unsqueeze(2).broadcast_to([128, nblk, 16]), AF.Copy),
             r=[sw[i]], w=[swB[i]], dbg=f"wsb{wname}{a}")
        P.op("s", lambda e, i=i: e.activation(
                k16(rwB[i]), rw[i].ap.unsqueeze(2).broadcast_to([128, nblk, 16]), AF.Copy),
             r=[rw[i]], w=[rwB[i]], dbg=f"wrb{wname}{a}")
        emit_fp4_quant(P, A, wn, wqd[i], rwB[i], swB[i], tmp[i], kdim, kdim)
        for c in range(kch):
            tb = tb0 if c % 2 == 0 else tb1
            P.op("t", lambda e, c=c, tb=tb, i=i: e.transpose(
                    tb.ap.bitcast(BF16)[:, 0:128], wqd[i].ap[:, c * 128:(c + 1) * 128],
                    X["ident"].ap),
                 r=[wqd[i], X["ident"]], w=[tb], dbg=f"wT{wname}{a}_{c}")
            P.op("s", lambda e, c=c, tb=tb, i=i: e.activation(
                    stg[i].ap[:, c * 128:(c + 1) * 128], tb.ap.bitcast(BF16)[:, 0:128], AF.Copy),
                 r=[tb], w=[stg[i]], dbg=f"wTe{wname}{a}_{c}")
        P.dma(C["spillout"], [lambda e, a=a, i=i: e.dma_start(
                out=wq_dram.rearrange("(c p) n -> p c n", p=128)[:, :, a * 128:(a + 1) * 128],
                in_=stg[i].ap.rearrange("p (c o) -> p c o", o=128))],
              r=[stg[i]], w=[bwq], dbg=f"wst{wname}{a}")
        yield


def pump(main, aux, naux):
    credit = 0.0
    for _ in main:
        credit += naux
        while credit >= 1.0:
            next(aux, None)
            credit -= 1.0
    for _ in aux:
        pass


def build_l0ffn(P, A, io, C, X, banks):
    xin, xout = io["xin"], io["xout"]
    bxout = Buf("xout")
    bwgq, bwuq, bwdq = Buf("wgq"), Buf("wuq"), Buf("wdq")
    bscxs, bscxr = Buf("scxs"), Buf("scxr")
    bschs = [Buf("schs0"), Buf("schs1")]
    bschr = [Buf("schr0"), Buf("schr1")]
    FFC = FF // 128

    A.mark()
    g1 = gen_wquant_nat(P, A, io, C, X, banks, "wg", io["wgq"], FF, D, bwgq,
                        (banks[2], banks[3]))
    g2 = gen_wquant_nat(P, A, io, C, X, banks, "wu", io["wuq"], FF, D, bwuq,
                        (banks[5], banks[6]))
    for _ in g1:
        next(g2, None)
    for _ in g2:
        pass
    A.release()
    A.mark()
    for _ in gen_wquant_nat(P, A, io, C, X, banks, "wd", io["wdq"], D, FF, bwdq,
                            (banks[2], banks[3])):
        pass
    A.release()

    A.mark()
    xr_in = [A.alloc(f"xr{i}", 128, NFC * 512, F32) for i in range(2)]
    xq = [A.alloc(f"xq{i}", 128, NFC * 512, BF16) for i in range(2)]
    sxT = A.alloc("sxT", 128, 4 * 64, F32)
    rxT = A.alloc("rxT", 128, 4 * 64, F32)
    sx_fm = A.alloc("sxfm", 64, 512, F32)
    rx_fm = A.alloc("rxfm", 64, 512, F32)
    rxB = A.alloc("rxB", 128, 4 * 512, F32)
    sxB = A.alloc("sxB", 128, 4 * 512, F32)
    tmpx = {k: A.alloc(f"t{k}", 128, 2 * 512, F32) for k in ("v", "q", "a", "b")}
    tmpx["m2"] = A.alloc("tm2", 128, 2 * 512, U8)
    tmpx["m4"] = A.alloc("tm4", 128, 2 * 512, U8)
    hso = A.alloc("hso", 128, 512, BF16)
    hbuf = [A.alloc(f"hbuf{i}", 128, FFC * 512, BF16) for i in range(2)]
    shT = A.alloc("shT", 128, 4 * 176, F32)
    rhT = A.alloc("rhT", 128, 4 * 176, F32)
    sh_fm1 = A.alloc("shfm1", 128, 512, F32)
    sh_fm2 = A.alloc("shfm2", 48, 512, F32)
    rh_fm1 = A.alloc("rhfm1", 128, 512, F32)
    rh_fm2 = A.alloc("rhfm2", 48, 512, F32)
    rhB = A.alloc("rhB", 128, 4 * 512, F32)
    shB = A.alloc("shB", 128, 4 * 512, F32)
    wgt = [A.alloc(f"wgt{i}", 128, NFC * 128, BF16) for i in range(2)]
    wut = [A.alloc(f"wut{i}", 128, NFC * 128, BF16) for i in range(2)]
    wdt = [A.alloc("wdt0", 128, FFC * 128, BF16)]
    zt = A.alloc("zt2", 128, NFC * 512, F32)
    zsq = A.alloc("zsq2", 128, 512, BF16)
    mu = A.alloc("mu2", 1, 512, F32)
    ms = A.alloc("ms2", 1, 512, F32)
    aa = A.alloc("aa2", 1, 512, F32)
    bb2 = A.alloc("bb22", 1, 512, F32)
    pg, pu = banks[0], banks[1]
    pdb = [banks[2], banks[3]]
    tbb = [banks[4], banks[7]]
    stA, stB = banks[6], banks[5]
    abank, bbank = banks[5], banks[6]

    def gen_front(tc):
        sl = tc % 2
        xr = xr_in[sl]
        hb = hbuf[sl]
        src2 = xin[:, tc * 512:(tc + 1) * 512].rearrange("(c p) t -> p c t", p=128)
        P.dma(C["xin"], [lambda e, xr=xr, src2=src2: e.dma_start(
                out=xr.ap.rearrange("p (c t) -> p c t", c=NFC), in_=src2)],
              w=[xr], dbg=f"xr{tc}")
        for c in range(NFC):
            for tb in range(4):
                bk = tbb[(c * 4 + tb) % 2]
                P.op("t", lambda e, c=c, tb=tb, bk=bk, xr=xr: e.transpose(
                        bk.ap[:, 0:128],
                        xr.ap[:, c * 512 + tb * 128: c * 512 + (tb + 1) * 128], X["identf"].ap),
                     r=[xr, X["identf"]], w=[bk], dbg=f"sxT{tc}_{c}_{tb}")
                P.op("v", lambda e, c=c, tb=tb, bk=bk: e.tensor_reduce(
                        sxT.ap[:, tb * 64 + c * 8: tb * 64 + (c + 1) * 8],
                        bk.ap[:, 0:128].rearrange("p (b k) -> p b k", k=16),
                        AX.X, ALU.max, apply_absolute_value=True),
                     r=[bk], w=[sxT], dbg=f"sxR{tc}_{c}_{tb}")
        P.op("v", lambda e: e.tensor_scalar(sxT.ap, sxT.ap, 1.0 / 6.0, 1e-12, ALU.mult, ALU.max),
             r=[sxT], w=[sxT])
        P.op("v", lambda e: e.reciprocal(rxT.ap, sxT.ap), r=[sxT], w=[rxT])
        for tb in range(4):
            P.op("t", lambda e, tb=tb: e.transpose(
                    tbb[0].ap[:, 0:128][0:64, :], sxT.ap[:, tb * 64:(tb + 1) * 64], X["identf"].ap),
                 r=[sxT, X["identf"]], w=[tbb[0]], dbg=f"sxTb{tc}_{tb}")
            P.op("v", lambda e, tb=tb: e.tensor_copy(sx_fm.ap[:, tb * 128:(tb + 1) * 128],
                                                     tbb[0].ap[0:64, 0:128]),
                 r=[tbb[0]], w=[sx_fm])
            P.op("t", lambda e, tb=tb: e.transpose(
                    tbb[1].ap[:, 0:128][0:64, :], rxT.ap[:, tb * 64:(tb + 1) * 64], X["identf"].ap),
                 r=[rxT, X["identf"]], w=[tbb[1]], dbg=f"rxTb{tc}_{tb}")
            P.op("v", lambda e, tb=tb: e.tensor_copy(rx_fm.ap[:, tb * 128:(tb + 1) * 128],
                                                     tbb[1].ap[0:64, 0:128]),
                 r=[tbb[1]], w=[rx_fm])
        P.dma(C["spillout"], [
            lambda e: e.dma_start(out=io["scx_s"].rearrange("(b c) t -> c b t", b=8),
                                  in_=sx_fm.ap),
            lambda e: e.dma_start(out=io["scx_r"].rearrange("(b c) t -> c b t", b=8),
                                  in_=rx_fm.ap),
        ], r=[sx_fm, rx_fm], w=[bscxs, bscxr], dbg=f"scxout{tc}")
        yield
        for plo in range(0, NFC, 4):
            phi = plo + 4
            P.dma(C["sc"], [
                lambda e, plo=plo, phi=phi: e.dma_start(
                    out=sxB.ap,
                    in_=io["scx_s"].rearrange("(b c) t -> b (c t)", b=8)[:, plo * 512:phi * 512]
                    .unsqueeze(1).broadcast_to([8, 16, (phi - plo) * 512])),
                lambda e, plo=plo, phi=phi: e.dma_start(
                    out=rxB.ap,
                    in_=io["scx_r"].rearrange("(b c) t -> b (c t)", b=8)[:, plo * 512:phi * 512]
                    .unsqueeze(1).broadcast_to([8, 16, (phi - plo) * 512])),
            ], r=[bscxs, bscxr], w=[sxB, rxB], dbg=f"scxrep{tc}_{plo}")
            emit_fp4_quant(P, A, xr, xq[sl], rxB, sxB, tmpx, 4 * 512, 2 * 512, src_off=plo * 512)
            yield
        for j in range(FFC):
            wg_ = wgt[j % 2]
            wu_ = wut[j % 2]
            P.dma(C["w"], [
                lambda e, wg_=wg_, j=j: e.dma_start(
                    out=wg_.ap.rearrange("p (c o) -> p c o", c=NFC),
                    in_=io["wgq"].rearrange("(c p) n -> p c n", p=128)[:, :, j * 128:(j + 1) * 128]),
                lambda e, wu_=wu_, j=j: e.dma_start(
                    out=wu_.ap.rearrange("p (c o) -> p c o", c=NFC),
                    in_=io["wuq"].rearrange("(c p) n -> p c n", p=128)[:, :, j * 128:(j + 1) * 128]),
            ], r=[bwgq, bwuq], w=[wg_, wu_], dbg=f"wgl{tc}_{j}")
            for c in range(NFC):
                P.op("t", lambda e, c=c, wg_=wg_, sl=sl: e.matmul(
                        pg.ap, wg_.ap[:, c * 128:(c + 1) * 128], xq[sl].ap[:, c * 512:(c + 1) * 512],
                        start=(c == 0), stop=(c == NFC - 1)),
                     r=[wg_, xq[sl]], w=[pg], dbg=f"mg{tc}_{j}_{c}")
            for c in range(NFC):
                P.op("t", lambda e, c=c, wu_=wu_, sl=sl: e.matmul(
                        pu.ap, wu_.ap[:, c * 128:(c + 1) * 128], xq[sl].ap[:, c * 512:(c + 1) * 512],
                        start=(c == 0), stop=(c == NFC - 1)),
                     r=[wu_, xq[sl]], w=[pu], dbg=f"mu{tc}_{j}_{c}")
            P.op("s", lambda e: e.activation(hso.ap, pg.ap, AF.Silu), r=[pg], w=[hso], dbg=f"silu{tc}_{j}")
            P.op("v", lambda e, j=j, hb=hb: e.tensor_tensor(
                    hb.ap[:, j * 512:(j + 1) * 512], hso.ap, pu.ap, ALU.mult),
                 r=[hso, pu], w=[hb], dbg=f"hmul{tc}_{j}")
            if j % 2 == 1:
                yield
        for c in range(FFC):
            for tb in range(4):
                bk = tbb[(c * 4 + tb) % 2]
                P.op("t", lambda e, c=c, tb=tb, bk=bk, hb=hb: e.transpose(
                        bk.ap.bitcast(BF16)[:, 0:128],
                        hb.ap[:, c * 512 + tb * 128: c * 512 + (tb + 1) * 128], X["ident"].ap),
                     r=[hb, X["ident"]], w=[bk], dbg=f"shT{tc}_{c}_{tb}")
                P.op("v", lambda e, c=c, tb=tb, bk=bk: e.tensor_reduce(
                        shT.ap[:, tb * 176 + c * 8: tb * 176 + (c + 1) * 8],
                        bk.ap.bitcast(BF16)[:, 0:128].rearrange("p (b k) -> p b k", k=16),
                        AX.X, ALU.max, apply_absolute_value=True),
                     r=[bk], w=[shT], dbg=f"shR{tc}_{c}_{tb}")
            if c % 4 == 3:
                yield
        P.op("v", lambda e: e.tensor_scalar(shT.ap, shT.ap, 1.0 / 6.0, 1e-12, ALU.mult, ALU.max),
             r=[shT], w=[shT])
        P.op("v", lambda e: e.reciprocal(rhT.ap, shT.ap), r=[shT], w=[rhT])
        for tb in range(4):
            P.op("t", lambda e, tb=tb: e.transpose(
                    tbb[0].ap[:, 0:128], shT.ap[:, tb * 176: tb * 176 + 128], X["identf"].ap),
                 r=[shT, X["identf"]], w=[tbb[0]])
            P.op("v", lambda e, tb=tb: e.tensor_copy(sh_fm1.ap[:, tb * 128:(tb + 1) * 128],
                                                     tbb[0].ap[:, 0:128]),
                 r=[tbb[0]], w=[sh_fm1])
            P.op("t", lambda e, tb=tb: e.transpose(
                    tbb[1].ap[:, 0:128][0:48, :], shT.ap[:, tb * 176 + 128: tb * 176 + 176],
                    X["identf"].ap),
                 r=[shT, X["identf"]], w=[tbb[1]])
            P.op("v", lambda e, tb=tb: e.tensor_copy(sh_fm2.ap[:, tb * 128:(tb + 1) * 128],
                                                     tbb[1].ap[0:48, 0:128]),
                 r=[tbb[1]], w=[sh_fm2])
            P.op("t", lambda e, tb=tb: e.transpose(
                    tbb[0].ap[:, 0:128], rhT.ap[:, tb * 176: tb * 176 + 128], X["identf"].ap),
                 r=[rhT, X["identf"]], w=[tbb[0]])
            P.op("v", lambda e, tb=tb: e.tensor_copy(rh_fm1.ap[:, tb * 128:(tb + 1) * 128],
                                                     tbb[0].ap[:, 0:128]),
                 r=[tbb[0]], w=[rh_fm1])
            P.op("t", lambda e, tb=tb: e.transpose(
                    tbb[1].ap[:, 0:128][0:48, :], rhT.ap[:, tb * 176 + 128: tb * 176 + 176],
                    X["identf"].ap),
                 r=[rhT, X["identf"]], w=[tbb[1]])
            P.op("v", lambda e, tb=tb: e.tensor_copy(rh_fm2.ap[:, tb * 128:(tb + 1) * 128],
                                                     tbb[1].ap[0:48, 0:128]),
                 r=[tbb[1]], w=[rh_fm2])
        schs_v = io[f"sch_s{sl}"].rearrange("(b c) t -> c b t", b=8)
        schr_v = io[f"sch_r{sl}"].rearrange("(b c) t -> c b t", b=8)
        P.dma(C["spillout"], [
            lambda e, schs_v=schs_v: e.dma_start(out=schs_v[0:16], in_=sh_fm1.ap),
            lambda e, schs_v=schs_v: e.dma_start(out=schs_v[16:22], in_=sh_fm2.ap),
            lambda e, schr_v=schr_v: e.dma_start(out=schr_v[0:16], in_=rh_fm1.ap),
            lambda e, schr_v=schr_v: e.dma_start(out=schr_v[16:22], in_=rh_fm2.ap),
        ], r=[sh_fm1, sh_fm2, rh_fm1, rh_fm2], w=[bschs[sl], bschr[sl]], dbg=f"schout{tc}")
        yield

    def gen_hq(tc):
        sl = tc % 2
        hb = hbuf[sl]
        for plo in range(0, FFC, 4):
            phi = min(plo + 4, FFC)
            P.dma(C["sc"], [
                lambda e, plo=plo, phi=phi, sl=sl: e.dma_start(
                    out=shB.ap[:, 0:(phi - plo) * 512],
                    in_=io[f"sch_s{sl}"].rearrange("(b c) t -> b (c t)", b=8)[:, plo * 512:phi * 512]
                    .unsqueeze(1).broadcast_to([8, 16, (phi - plo) * 512])),
                lambda e, plo=plo, phi=phi, sl=sl: e.dma_start(
                    out=rhB.ap[:, 0:(phi - plo) * 512],
                    in_=io[f"sch_r{sl}"].rearrange("(b c) t -> b (c t)", b=8)[:, plo * 512:phi * 512]
                    .unsqueeze(1).broadcast_to([8, 16, (phi - plo) * 512])),
            ], r=[bschs[sl], bschr[sl]], w=[shB, rhB], dbg=f"schrep{tc}_{plo}")
            emit_fp4_quant(P, A, hb, hb, rhB, shB, tmpx, (phi - plo) * 512, 2 * 512,
                           src_off=plo * 512)
            yield

    def emit_down(tc):
        sl = tc % 2
        xr = xr_in[sl]
        hb = hbuf[sl]
        for j in range(NFC):
            wd_ = wdt[0]
            P.dma(C["gw"], [lambda e, wd_=wd_, j=j: e.dma_start(
                    out=wd_.ap.rearrange("p (c o) -> p c o", c=FFC),
                    in_=io["wdq"].rearrange("(c p) n -> p c n", p=128)[:, :, j * 128:(j + 1) * 128])],
                  r=[bwdq], w=[wd_], dbg=f"wdl{tc}_{j}")
            pd = pdb[j % 2]
            for c in range(FFC):
                P.op("t", lambda e, c=c, wd_=wd_, pd=pd, hb=hb: e.matmul(
                        pd.ap, wd_.ap[:, c * 128:(c + 1) * 128], hb.ap[:, c * 512:(c + 1) * 512],
                        start=(c == 0), stop=(c == FFC - 1)),
                     r=[wd_, hb], w=[pd], dbg=f"md{tc}_{j}_{c}")
            P.op("v", lambda e, pd=pd, xr=xr, j=j: e.tensor_tensor(
                    zt.ap[:, j * 512:(j + 1) * 512], pd.ap, xr.ap[:, j * 512:(j + 1) * 512], ALU.add),
                 r=[pd, xr], w=[zt], dbg=f"z2{tc}_{j}")
            P.op("s", lambda e, j=j: e.activation(zsq.ap, zt.ap[:, j * 512:(j + 1) * 512], AF.Square),
                 r=[zt], w=[zsq], dbg=f"z2sq{tc}_{j}")
            P.op("t", lambda e, j=j: e.matmul(stB.ap[0:1, :], X["onesb"].ap[:, 0:1], zsq.ap,
                                              start=(j == 0), stop=(j == NFC - 1)),
                 r=[zsq, X["onesb"]], w=[stB], dbg=f"st2b{tc}_{j}")
            P.op("t", lambda e, j=j: e.matmul(stA.ap[0:1, :], X["onesfc"].ap,
                                              zt.ap[:, j * 512:(j + 1) * 512],
                                              start=(j == 0), stop=(j == NFC - 1)),
                 r=[zt, X["onesfc"]], w=[stA], dbg=f"st2a{tc}_{j}")
        P.op("v", lambda e: e.tensor_scalar_mul(mu.ap, stA.ap[0:1, :], 1.0 / D), r=[stA], w=[mu])
        P.op("v", lambda e: e.tensor_scalar_mul(ms.ap, stB.ap[0:1, :], 1.0 / D), r=[stB], w=[ms])
        P.op("v", lambda e: e.tensor_tensor(bb2.ap, mu.ap, mu.ap, ALU.mult), r=[mu], w=[bb2])
        P.op("v", lambda e: e.tensor_tensor(ms.ap, ms.ap, bb2.ap, ALU.subtract), r=[ms, bb2], w=[ms])
        P.op("s", lambda e: e.activation(ms.ap, ms.ap, AF.Sqrt, bias=X["eps"].ap[0:1, :]),
             r=[ms, X["eps"]], w=[ms])
        P.op("v", lambda e: e.reciprocal(aa.ap, ms.ap), r=[ms], w=[aa])
        P.op("v", lambda e: e.tensor_tensor(bb2.ap, mu.ap, aa.ap, ALU.mult), r=[mu, aa], w=[bb2])
        P.op("v", lambda e: e.tensor_scalar_mul(bb2.ap, bb2.ap, -1.0), r=[bb2], w=[bb2])
        P.op("t", lambda e: e.matmul(abank.ap, X["onesf"].ap[0:1, :], aa.ap, start=True, stop=True),
             r=[aa, X["onesf"]], w=[abank], dbg=f"ab2{tc}")
        P.op("t", lambda e: e.matmul(bbank.ap, X["onesf"].ap[0:1, :], bb2.ap, start=True, stop=True),
             r=[bb2, X["onesf"]], w=[bbank], dbg=f"bb2{tc}")
        for j in range(NFC):
            P.op("v", lambda e, j=j: e.tensor_tensor(
                    zt.ap[:, j * 512:(j + 1) * 512], zt.ap[:, j * 512:(j + 1) * 512], abank.ap, ALU.mult),
                 r=[zt, abank], w=[zt], dbg=f"ln2m{tc}_{j}")
            P.op("v", lambda e, j=j: e.tensor_tensor(
                    zt.ap[:, j * 512:(j + 1) * 512], zt.ap[:, j * 512:(j + 1) * 512], bbank.ap, ALU.add),
                 r=[zt, bbank], w=[zt], dbg=f"ln2a{tc}_{j}")
        P.dma(C["xout"], [lambda e, tc=tc: e.dma_start(
                out=xout[:, tc * 512:(tc + 1) * 512].rearrange("(c p) t -> p c t", p=128),
                in_=zt.ap.rearrange("p (c t) -> p c t", c=NFC))],
              r=[zt], w=[bxout], dbg=f"xout2{tc}")

    for tc in range(NTC + 1):
        if 1 <= tc < NTC:
            pump(gen_front(tc), gen_hq(tc - 1), 0.5)
        elif tc < NTC:
            for _ in gen_front(tc):
                pass
        else:
            for _ in gen_hq(tc - 1):
                pass
        if tc >= 1:
            emit_down(tc - 1)
    A.release()
    return bxout


def build_l1attn(P, A, io, C, X, banks):
    xin, xout = io["xin"], io["xout"]
    bxout = Buf("xout3")
    bks, bvs, bas = Buf("kspill3"), Buf("vspill3"), Buf("aospill3")
    bsch = [Buf("scdram0"), Buf("scdram1")]
    badh = [Buf("adram0"), Buf("adram1")]

    A.mark()
    A.mark_top()
    qfm = [A.alloc_top(f"qfm{c}", 128, T, BF16) for c in range(NFC)]
    A.mark()
    xnb = [A.alloc(f"x3n{c}", 128, T, BF16) for c in range(NFC)]
    wqb = [A.alloc(f"w3{i}", 128, NFC * 128, BF16) for i in range(2)]
    kvstage = [A.alloc(f"kvst{i}", 128, 512, BF16) for i in range(2)]
    xin_t = [A.alloc(f"x3t{i}", 128, NFC * 512, F32) for i in range(2)]
    for tc in range(NTC):
        xt = xin_t[tc % 2]
        src = xin[:, tc * 512:(tc + 1) * 512].rearrange("(c p) t -> p c t", p=128)
        P.dma(C["xin"], [lambda e, xt=xt, src=src: e.dma_start(
                out=xt.ap.rearrange("p (c t) -> p c t", c=NFC), in_=src)],
              w=[xt], dbg=f"x3in{tc}")
        for c in range(NFC):
            P.op("v", lambda e, xt=xt, c=c, tc=tc: e.tensor_copy(
                    xnb[c].ap[:, tc * 512:(tc + 1) * 512], xt.ap[:, c * 512:(c + 1) * 512]),
                 r=[xt], w=[xnb[c]], dbg=f"x3c{tc}_{c}")
    for j in range(3 * NFC):
        kind, jj = divmod(j, NFC)
        wq = wqb[j % 2]
        wsrc = io[["wqT", "wkT", "wvT"][kind]].rearrange("(c p) o -> p c o", p=128)[:, :, jj * 128:(jj + 1) * 128]
        P.dma(C["w"], [lambda e, wq=wq, wsrc=wsrc: e.dma_start(
                out=wq.ap.rearrange("p (c o) -> p c o", c=NFC), in_=wsrc)],
              w=[wq], dbg=f"w3{j}")
        if kind == 0:
            P.op("v", lambda e, wq=wq: e.tensor_scalar_mul(wq.ap, wq.ap, 0.125),
                 r=[wq], w=[wq], dbg=f"w3s{j}")
        for tc in range(NTC):
            pb = banks[4 + ((j * NTC + tc) % 4)]
            for c in range(NFC):
                P.op("t", lambda e, wq=wq, c=c, tc=tc, pb=pb: e.matmul(
                        pb.ap, wq.ap[:, c * 128:(c + 1) * 128],
                        xnb[c].ap[:, tc * 512:(tc + 1) * 512],
                        start=(c == 0), stop=(c == NFC - 1)),
                     r=[wq, xnb[c]], w=[pb], dbg=f"p3{j}_{tc}_{c}")
            if kind == 0:
                P.op("s", lambda e, pb=pb, jj=jj, tc=tc: e.activation(
                        qfm[jj].ap[:, tc * 512:(tc + 1) * 512], pb.ap, AF.Copy),
                     r=[pb], w=[qfm[jj]], dbg=f"qe3{j}_{tc}")
            else:
                st = kvstage[tc % 2]
                P.op("s", lambda e, pb=pb, st=st: e.activation(st.ap, pb.ap, AF.Copy),
                     r=[pb], w=[st], dbg=f"kve3{j}_{tc}")
                spill = io["kspill"] if kind == 1 else io["vspill"]
                bsp = bks if kind == 1 else bvs
                P.dma(C["spillout"], [lambda e, st=st, spill=spill, jj=jj, tc=tc: e.dma_start(
                        out=spill[jj * 128:(jj + 1) * 128, tc * 512:(tc + 1) * 512], in_=st.ap)],
                      r=[st], w=[bsp], dbg=f"kvs3{j}_{tc}")
    A.release()
    HT = T // 2
    NTQ = HT // 512
    A.mark()
    kb = [A.alloc(f"kb{i}", 128, HT, BF16) for i in range(3)]
    prod = [A.alloc(f"pr{i}", 128, HT, BF16) for i in range(2)]
    scstg = [A.alloc("scst0", 16, HT, F32)]
    scsb = A.alloc("scsb", 16, 8 * 512, F32)
    asbh = [A.alloc(f"asbh{i}", 16, 8 * 512, BF16) for i in range(2)]
    mx1 = A.alloc("mx1", 16, 2, F32)
    mx2 = A.alloc("mx2", 16, 2, F32)
    sm1 = A.alloc("sm1", 16, 512, F32)
    sm2 = A.alloc("sm2", 16, 512, F32)
    rd = A.alloc("rd", 16, 512, F32)
    rdf = A.alloc("rdf", 16, 512, F32)
    rdb16 = A.alloc("rdb16", 16, 512, BF16)
    mxM = A.alloc("mxM", 16, 2, F32)
    cf = A.alloc("cf", 16, 2, F32)
    scv = io["scdram"].rearrange("p (h t) -> p h t", h=NH)
    adv = io["adram"].rearrange("p (h t) -> p h t", h=NH)

    def gen_scores(h):
        t0 = h * HT
        for hk in range(NH):
            kbt = kb[hk % 3]
            P.dma([C["kin"], C["sdma"]][hk % 2], [lambda e, kbt=kbt, hk=hk, t0=t0: e.dma_start(
                    out=kbt.ap,
                    in_=io["kspill"][hk * 64:(hk + 1) * 64, t0:t0 + HT]
                    .unsqueeze(0).broadcast_to([2, 64, HT]))],
                  r=[bks], w=[kbt], dbg=f"kb{h}_{hk}")
            for c in range(NFC):
                pr = prod[c % 2]
                P.op("v", lambda e, pr=pr, c=c, kbt=kbt, t0=t0: e.tensor_tensor(
                        pr.ap, qfm[c].ap[:, t0:t0 + HT], kbt.ap, ALU.mult),
                     r=[qfm[c], kbt], w=[pr], dbg=f"pr{h}_{hk}_{c}")
                for tq in range(NTQ):
                    bk = banks[4 * h + tq]
                    P.op("t", lambda e, pr=pr, c=c, tq=tq, bk=bk: e.matmul(
                            bk.ap[0:16, :], X["e2"].ap[:, c * 16:(c + 1) * 16],
                            pr.ap[:, tq * 512:(tq + 1) * 512],
                            start=(c == 0), stop=(c == NFC - 1)),
                         r=[pr, X["e2"]], w=[bk], dbg=f"sc{h}_{hk}_{c}_{tq}")
            st = scstg[0]
            for tq in range(NTQ):
                bk = banks[4 * h + tq]
                P.op("s", lambda e, bk=bk, tq=tq, st=st: e.activation(
                        st.ap[:, tq * 512:(tq + 1) * 512], bk.ap[0:16, :], AF.Copy),
                     r=[bk], w=[st], dbg=f"sce{h}_{hk}_{tq}")
            P.dma(C["spillout"], [lambda e, hk=hk, st=st, t0=t0: e.dma_start(
                    out=scv[:, hk, t0:t0 + HT], in_=st.ap)],
                  r=[st], w=[bsch[h]], dbg=f"scout{h}_{hk}")
            yield

    def gen_softmax(h):
        for tl in range(NTQ):
            tcc = NTQ * h + tl
            tsl = slice(tcc * 512, (tcc + 1) * 512)
            for i, mxb, smb in ((0, mx1, sm1), (1, mx2, sm2)):
                P.dma(C["xin"], [lambda e, i=i, tsl=tsl: e.dma_start(
                        out=scsb.ap.rearrange("p (h t) -> p h t", h=8),
                        in_=scv[:, 8 * i:8 * i + 8, tsl])],
                      r=[bsch[h]], w=[scsb], dbg=f"smin{tcc}_{i}")
                P.op("v", lambda e, mxb=mxb: e.tensor_reduce(
                        mxb.ap[:, 0:1], scsb.ap, AX.X, ALU.max),
                     r=[scsb], w=[mxb], dbg=f"smmax{tcc}_{i}")
                P.op("v", lambda e, mxb=mxb: e.tensor_scalar_mul(
                        mxb.ap[:, 1:2], mxb.ap[:, 0:1], -1.0),
                     r=[mxb], w=[mxb], dbg=f"smneg{tcc}_{i}")
                P.op("s", lambda e, i=i, mxb=mxb: e.activation(
                        asbh[i].ap, scsb.ap, AF.Exp, bias=mxb.ap[:, 1:2]),
                     r=[scsb, mxb], w=[asbh[i]], dbg=f"smexp{tcc}_{i}")
                P.op("v", lambda e, i=i, smb=smb: e.tensor_reduce(
                        smb.ap, asbh[i].ap.rearrange("p (h t) -> p t h", h=8), AX.X, ALU.add),
                     r=[asbh[i]], w=[smb], dbg=f"smsum{tcc}_{i}")
                yield
            P.op("v", lambda e: e.tensor_tensor(
                    mxM.ap[:, 0:1], mx1.ap[:, 0:1], mx2.ap[:, 0:1], ALU.max),
                 r=[mx1, mx2], w=[mxM], dbg=f"smM{tcc}")
            P.op("v", lambda e: e.tensor_scalar_mul(mxM.ap[:, 1:2], mxM.ap[:, 0:1], -1.0),
                 r=[mxM], w=[mxM], dbg=f"smMn{tcc}")
            for i, mxb in ((0, mx1), (1, mx2)):
                P.op("s", lambda e, i=i, mxb=mxb: e.activation(
                        cf.ap[:, i:i + 1], mxb.ap[:, 0:1], AF.Exp, bias=mxM.ap[:, 1:2]),
                     r=[mxb, mxM], w=[cf], dbg=f"smcf{tcc}_{i}")
            for i, smb in ((0, sm1), (1, sm2)):
                P.op("v", lambda e, i=i, smb=smb: e.tensor_scalar_mul(
                        smb.ap, smb.ap, cf.ap[:, i:i + 1]),
                     r=[smb, cf], w=[smb], dbg=f"smcs{tcc}_{i}")
            P.op("v", lambda e: e.tensor_tensor(sm1.ap, sm1.ap, sm2.ap, ALU.add),
                 r=[sm1, sm2], w=[sm1], dbg=f"smsumc{tcc}")
            P.op("v", lambda e: e.reciprocal(rd.ap, sm1.ap), r=[sm1], w=[rd], dbg=f"smrd{tcc}")
            yield
            for i in (0, 1):
                P.op("v", lambda e, i=i: e.tensor_scalar_mul(rdf.ap, rd.ap, cf.ap[:, i:i + 1]),
                     r=[rd, cf], w=[rdf], dbg=f"smri{tcc}_{i}")
                P.op("v", lambda e: e.tensor_copy(rdb16.ap, rdf.ap),
                     r=[rdf], w=[rdb16], dbg=f"smrb{tcc}_{i}")
                P.op("v", lambda e, i=i: e.tensor_tensor(
                        asbh[i].ap.rearrange("p (h t) -> p h t", h=8),
                        asbh[i].ap.rearrange("p (h t) -> p h t", h=8),
                        rdb16.ap.unsqueeze(1).broadcast_to([16, 8, 512]), ALU.mult),
                     r=[asbh[i], rdb16], w=[asbh[i]], dbg=f"smdiv{tcc}_{i}")
                P.dma(C["spillout"], [lambda e, i=i, tsl=tsl: e.dma_start(
                        out=adv[:, 8 * i:8 * i + 8, tsl],
                        in_=asbh[i].ap.rearrange("p (h t) -> p h t", h=8))],
                      r=[asbh[i]], w=[badh[h]], dbg=f"smout{tcc}_{i}")
                yield

    for _ in gen_scores(0):
        pass
    sm0 = gen_softmax(0)
    pump(gen_scores(1), sm0, 1)
    A.release_top()
    A.mark_top()
    vfm2 = [[A.alloc_top(f"vf{h}_{c}", 128, HT, BF16) for c in range(NFC)] for h in range(2)]
    arep = [A.alloc_top(f"ar{i}", 128, HT, BF16) for i in range(6)]
    prod2 = [A.alloc_top(f"p23{i}", 128, HT, BF16) for i in range(5)]
    aostg = [A.alloc_top(f"aos{i}", 64, HT, BF16) for i in range(2)]
    avchan = [C["qin"], C["gw"], C["sdma"]]
    def gen_av(h):
        t0 = h * HT
        vfm = vfm2[h]
        P.dma(C["aoin"],
              [(lambda e, c=c, t0=t0: e.dma_start(
                    out=vfm[c].ap, in_=io["vspill"][c * 128:(c + 1) * 128, t0:t0 + HT]))
               for c in range(NFC)],
              r=[bvs], w=vfm, dbg=f"vres{h}")
        for hq in range(NH):
            for c2 in range(NFC):
                k = hq * NFC + c2
                ar = arep[k % 6]
                P.dma(avchan[k % 3], [lambda e, ar=ar, hq=hq, c2=c2, t0=t0: e.dma_start(
                        out=ar.ap,
                        in_=adv[hq, 2 * c2:2 * c2 + 2, t0:t0 + HT]
                        .unsqueeze(1).broadcast_to([2, 64, HT]))],
                      r=[badh[h]], w=[ar], dbg=f"ar{h}_{hq}_{c2}")
                pr = prod2[k % 5]
                P.op("v", lambda e, pr=pr, ar=ar, c2=c2: e.tensor_tensor(
                        pr.ap, ar.ap, vfm[c2].ap, ALU.mult),
                     r=[ar, vfm[c2]], w=[pr], dbg=f"p2{h}_{hq}_{c2}")
                for tq in range(NTQ):
                    bk = banks[4 * h + tq]
                    P.op("t", lambda e, pr=pr, tq=tq, bk=bk, c2=c2: e.matmul(
                            bk.ap[0:64, :], X["e3"].ap, pr.ap[:, tq * 512:(tq + 1) * 512],
                            start=(c2 == 0), stop=(c2 == NFC - 1)),
                         r=[pr, X["e3"]], w=[bk], dbg=f"av{h}_{hq}_{c2}_{tq}")
            ast = aostg[hq % 2]
            for tq in range(NTQ):
                bk = banks[4 * h + tq]
                P.op("s", lambda e, bk=bk, tq=tq, ast=ast: e.activation(
                        ast.ap[:, tq * 512:(tq + 1) * 512], bk.ap[0:64, :], AF.Copy),
                     r=[bk], w=[ast], dbg=f"aoe{h}_{hq}_{tq}")
            P.dma(C["spillout"], [lambda e, hq=hq, ast=ast, t0=t0: e.dma_start(
                    out=io["aospill"][hq * 64:(hq + 1) * 64, t0:t0 + HT], in_=ast.ap)],
                  r=[ast], w=[bas], dbg=f"ao3{h}_{hq}")
            yield

    pump(gen_av(0), gen_softmax(1), 1)
    for _ in gen_av(1):
        pass
    A.release()
    A.release_top()
    ob = emit_outproj_ln(P, A, io, C, X, banks, "wfcT", xin, xout, io["aospill"], bas, bxout)
    A.release()
    return ob


def build_l1ffn(P, A, io, C, X, banks):
    xin, xout = io["xin"], io["xout"]
    bxout = Buf("xout4")
    bxg = Buf("xgdram")
    bsw1, bsw3, bsw2 = Buf("sw1b"), Buf("sw3b"), Buf("sw2b")
    HSC = HS // 128

    A.mark()
    sb1 = A.alloc("sb1", 128, HSC, F32)
    sb3 = A.alloc("sb3", 128, HSC, F32)
    b1c = A.alloc("b1c", 128, NFC * NE, F32)
    bcomb = A.alloc("bcomb", 128, NFC * NE, F32)
    sb2h = A.alloc("sb2h", 128, NFC, F32)
    P.dma(C["const"], [
        lambda e: e.dma_start(out=sb1.ap, in_=io["sb1r"]),
        lambda e: e.dma_start(out=sb3.ap, in_=io["sb3r"]),
        lambda e: e.dma_start(out=b1c.ap.rearrange("p (h e) -> p h e", e=NE), in_=io["b1r"]),
        lambda e: e.dma_start(out=bcomb.ap.rearrange("p (h e) -> p h e", e=NE), in_=io["b2r"]),
        lambda e: e.dma_start(out=sb2h.ap, in_=io["sb2r"]),
    ], w=[sb1, sb3, b1c, bcomb, sb2h], dbg="biases")
    bcn = A.alloc("bcn", 128, NFC * NE, F32)
    P.op("v", lambda e: e.tensor_scalar_mul(sb2h.ap, sb2h.ap, 0.5), r=[sb2h], w=[sb2h])
    for r in range(NE):
        e0, e1 = (3 * r) % NE, (3 * r + 7) % NE
        for oc in range(NFC):
            P.op("v", lambda e, oc=oc, r=r, e0=e0, e1=e1: e.tensor_tensor(
                    bcn.ap[:, oc * NE + r: oc * NE + r + 1],
                    bcomb.ap[:, oc * NE + e0: oc * NE + e0 + 1],
                    bcomb.ap[:, oc * NE + e1: oc * NE + e1 + 1], ALU.add),
                 r=[bcomb], w=[bcn], dbg=f"bc{r}_{oc}")
    P.op("v", lambda e: e.tensor_scalar_mul(bcn.ap, bcn.ap, 0.25), r=[bcn], w=[bcn])
    for oc in range(NFC):
        P.op("v", lambda e, oc=oc: e.tensor_tensor(
                bcn.ap[:, oc * NE:(oc + 1) * NE],
                bcn.ap[:, oc * NE:(oc + 1) * NE],
                sb2h.ap[:, oc:oc + 1].broadcast_to([128, NE]), ALU.add),
             r=[bcn, sb2h], w=[bcn], dbg=f"bcs{oc}")

    A.mark()
    swst = A.alloc("swst", 128, HS, BF16)
    for (wname, dramb, bflag, fold) in (("sw1", "sw1b", bsw1, 1.0), ("sw3", "sw3b", bsw3, 1.0),
                                        ("sw2", "sw2b", bsw2, 0.5)):
        nr = io[wname].shape[0]
        ncols = io[wname].shape[1]
        for a in range(nr // 128):
            P.dma(C["w"], [lambda e, a=a, wname=wname, ncols=ncols: e.dma_start(
                    out=swst.ap[:, 0:ncols], in_=io[wname][a * 128:(a + 1) * 128, :])],
                  w=[swst], dbg=f"swl{wname}{a}")
            if fold != 1.0:
                P.op("v", lambda e, ncols=ncols, fold=fold: e.tensor_scalar_mul(
                        swst.ap[:, 0:ncols], swst.ap[:, 0:ncols], fold),
                     r=[swst], w=[swst], dbg=f"swf{wname}{a}")
            P.dma(C["spillout"], [lambda e, a=a, dramb=dramb, ncols=ncols: e.dma_start(
                    out=io[dramb][a * 128:(a + 1) * 128, :], in_=swst.ap[:, 0:ncols])],
                  r=[swst], w=[bflag], dbg=f"sws{wname}{a}")
    A.release()

    A.mark()
    xin_t = [A.alloc(f"x4t{i}", 128, NFC * 512, F32) for i in range(2)]
    xgs = [A.alloc(f"xgs{i}", 128, NFC * 512, F32) for i in range(2)]
    for tc in range(NTC):
        xt = xin_t[tc % 2]
        xg_ = xgs[tc % 2]
        src = xin[:, tc * 512:(tc + 1) * 512].rearrange("(c p) t -> p c t", p=128)
        P.dma(C["xin"], [lambda e, xt=xt, src=src: e.dma_start(
                out=xt.ap.rearrange("p (c t) -> p c t", c=NFC), in_=src)],
              w=[xt], dbg=f"x4in{tc}")
        for c in range(NFC):
            P.op("v", lambda e, xt=xt, xg_=xg_, c=c: e.tensor_copy(
                    xg_.ap[:, c * 512:(c + 1) * 512].rearrange("p (r m) -> p r m", r=8),
                    xt.ap[:, c * 512:(c + 1) * 512].rearrange("p (m r) -> p r m", r=8)),
                 r=[xt], w=[xg_], dbg=f"cmp{tc}_{c}")
        for c in range(NFC):
            P.dma(C["spillout"], [lambda e, xg_=xg_, c=c, tc=tc: e.dma_start(
                    out=io["xgdram"][c * 128:(c + 1) * 128, :]
                    .rearrange("p (r m) -> p r m", m=CW)[:, :, tc * 64:(tc + 1) * 64],
                    in_=xg_.ap[:, c * 512:(c + 1) * 512].rearrange("p (r m) -> p r m", r=8))],
                  r=[xg_], w=[bxg], dbg=f"xgo{tc}_{c}")
    A.release()
    xg = [A.alloc(f"xg{c}", 128, T, BF16) for c in range(NFC)]
    es = [A.alloc(f"es{c}", 128, T, BF16) for c in range(NFC)]
    P.dma(C["gw"],
          [(lambda e, c=c: e.dma_start(out=xg[c].ap, in_=io["xgdram"][c * 128:(c + 1) * 128, :]))
           for c in range(NFC)],
          r=[bxg], w=xg, dbg="xgcast")

    A.mark()
    A.mark_top()
    w1b = [A.alloc_top(f"w1b{i}", 128, NFC * D, BF16) for i in range(2)]
    w2b = [A.alloc_top(f"w2b{i}", 128, NFC * D, BF16) for i in range(2)]
    hE = [A.alloc(f"hE{i}", 128, CW, BF16) for i in range(NFC)]
    phb = [banks[0], banks[2]]
    pob = [banks[1], banks[3]]
    written = set()
    for ex in range(NE):
        w1 = w1b[ex % 2]
        w2 = w2b[ex % 2]
        P.dma(C["gw"], [
            lambda e, ex=ex, w1=w1: e.dma_start(
                out=w1.ap.rearrange("p (c o) -> p c o", c=NFC),
                in_=io["w1e"][ex].rearrange("(c p) o -> p c o", p=128)),
        ], w=[w1], dbg=f"we1_{ex}")
        P.dma(C["w"], [
            lambda e, ex=ex, w2=w2: e.dma_start(
                out=w2.ap.rearrange("p (c o) -> p c o", c=NFC),
                in_=io["w2e"][ex].rearrange("(c p) o -> p c o", p=128)),
        ], w=[w2], dbg=f"we2_{ex}")
        P.op("v", lambda e, w2=w2: e.tensor_scalar_mul(w2.ap, w2.ap, 0.25), r=[w2], w=[w2],
             dbg=f"wf{ex}")
        rs = [r for r in range(NE) if (3 * r) % NE == ex or (3 * r + 7) % NE == ex]
        for r in rs:
            for hc in range(NFC):
                ph = phb[hc % 2]
                for c in range(NFC):
                    P.op("t", lambda e, hc=hc, c=c, r=r, w1=w1, ph=ph: e.matmul(
                            ph.ap[:, 0:CW], w1.ap[:, (c * NFC + hc) * 128:(c * NFC + hc + 1) * 128],
                            xg[c].ap[:, r * CW:(r + 1) * CW],
                            start=(c == 0), stop=(c == NFC - 1)),
                         r=[w1, xg[c]], w=[ph], dbg=f"e1_{ex}_{r}_{hc}_{c}")
                P.op("s", lambda e, hc=hc, ex=ex, ph=ph: e.activation(
                        hE[hc].ap, ph.ap[:, 0:CW], AF.Gelu, bias=b1c.ap[:, hc * NE + ex: hc * NE + ex + 1]),
                     r=[ph, b1c], w=[hE[hc]], dbg=f"gel{ex}_{r}_{hc}")
            for oc in range(NFC):
                po = pob[oc % 2]
                for hc in range(NFC):
                    P.op("t", lambda e, oc=oc, hc=hc, w2=w2, po=po: e.matmul(
                            po.ap[:, 0:CW], w2.ap[:, (hc * NFC + oc) * 128:(hc * NFC + oc + 1) * 128],
                            hE[hc].ap, start=(hc == 0), stop=(hc == NFC - 1)),
                         r=[w2, hE[hc]], w=[po], dbg=f"e2_{ex}_{r}_{oc}_{hc}")
                if r not in written:
                    P.op("v", lambda e, oc=oc, r=r, po=po: e.tensor_copy(
                            es[oc].ap[:, r * CW:(r + 1) * CW], po.ap[:, 0:CW]),
                         r=[po], w=[es[oc]], dbg=f"es0_{ex}_{r}_{oc}")
                else:
                    P.op("v", lambda e, oc=oc, r=r, po=po: e.tensor_tensor(
                            es[oc].ap[:, r * CW:(r + 1) * CW], po.ap[:, 0:CW],
                            es[oc].ap[:, r * CW:(r + 1) * CW], ALU.add),
                         r=[po, es[oc]], w=[es[oc]], dbg=f"es1_{ex}_{r}_{oc}")
            written.add(r)
    A.release()
    A.release_top()

    A.mark()
    sw1t = [A.alloc(f"s1t{i}", 128, NFC * 128, BF16) for i in range(2)]
    sw3t = [A.alloc(f"s3t{i}", 128, NFC * 128, BF16) for i in range(2)]
    sw2t = [A.alloc("s2t0", 128, HSC * 128, BF16)]
    mbf = [A.alloc(f"m{j}", 128, CW, BF16) for j in range(HSC)]
    abf = A.alloc("abf", 128, CW, BF16)
    xr4 = A.alloc("xr4", 128, NFC * CW, F32)
    zt = A.alloc("zt4", 128, NFC * CW, F32)
    zsq = A.alloc("zsq4", 128, CW, BF16)
    mu = A.alloc("mu4", 1, CW, F32)
    ms = A.alloc("ms4", 1, CW, F32)
    aa = A.alloc("aa4", 1, CW, F32)
    bb2 = A.alloc("bb24", 1, CW, F32)
    p1b, p3b = banks[0], banks[1]
    stA, stB = banks[6], banks[5]
    abank, bbank = banks[7], banks[4]
    for r in range(NE):
        P.dma(C["xin"], [lambda e, r=r: e.dma_start(
                out=xr4.ap.rearrange("p (c t) -> p c t", c=NFC),
                in_=io["xgdram"][:, r * CW:(r + 1) * CW].rearrange("(c p) t -> p c t", p=128))],
              r=[bxg], w=[xr4], dbg=f"xr4_{r}")
        for j in range(HSC):
            s1 = sw1t[j % 2]
            s3 = sw3t[j % 2]
            P.dma(C["w"], [
                lambda e, s1=s1, j=j: e.dma_start(
                    out=s1.ap.rearrange("p (c o) -> p c o", c=NFC),
                    in_=io["sw1b"].rearrange("(c p) o -> p c o", p=128)[:, :, j * 128:(j + 1) * 128]),
                lambda e, s3=s3, j=j: e.dma_start(
                    out=s3.ap.rearrange("p (c o) -> p c o", c=NFC),
                    in_=io["sw3b"].rearrange("(c p) o -> p c o", p=128)[:, :, j * 128:(j + 1) * 128]),
            ], r=[bsw1, bsw3], w=[s1, s3], dbg=f"swt{r}_{j}")
            for c in range(NFC):
                P.op("t", lambda e, s1=s1, c=c, r=r: e.matmul(
                        p1b.ap[:, 0:CW], s1.ap[:, c * 128:(c + 1) * 128], xg[c].ap[:, r * CW:(r + 1) * CW],
                        start=(c == 0), stop=(c == NFC - 1)),
                     r=[s1, xg[c]], w=[p1b], dbg=f"m1_{r}_{j}_{c}")
            for c in range(NFC):
                P.op("t", lambda e, s3=s3, c=c, r=r: e.matmul(
                        p3b.ap[:, 0:CW], s3.ap[:, c * 128:(c + 1) * 128], xg[c].ap[:, r * CW:(r + 1) * CW],
                        start=(c == 0), stop=(c == NFC - 1)),
                     r=[s3, xg[c]], w=[p3b], dbg=f"m3_{r}_{j}_{c}")
            P.op("s", lambda e, j=j: e.activation(abf.ap, p1b.ap[:, 0:CW], AF.Silu,
                                                  bias=sb1.ap[:, j:j + 1]),
                 r=[p1b, sb1], w=[abf], dbg=f"sil{r}_{j}")
            P.op("v", lambda e, j=j: e.tensor_scalar_add(mbf[j].ap, p3b.ap[:, 0:CW],
                                                         sb3.ap[:, j:j + 1]),
                 r=[p3b, sb3], w=[mbf[j]], dbg=f"u3{r}_{j}")
            P.op("v", lambda e, j=j: e.tensor_tensor(mbf[j].ap, abf.ap, mbf[j].ap, ALU.mult),
                 r=[abf, mbf[j]], w=[mbf[j]], dbg=f"mm{r}_{j}")
        for oc in range(NFC):
            s2 = sw2t[0]
            P.dma(C["gw"], [lambda e, s2=s2, oc=oc: e.dma_start(
                    out=s2.ap.rearrange("p (c o) -> p c o", c=HSC),
                    in_=io["sw2b"].rearrange("(c p) o -> p c o", p=128)[:, :, oc * 128:(oc + 1) * 128])],
                  r=[bsw2], w=[s2], dbg=f"sw2t{r}_{oc}")
            pb = banks[2 + (oc % 2)]
            for j in range(HSC):
                P.op("t", lambda e, s2=s2, j=j, pb=pb: e.matmul(
                        pb.ap[:, 0:CW], s2.ap[:, j * 128:(j + 1) * 128], mbf[j].ap,
                        start=(j == 0), stop=(j == HSC - 1)),
                     r=[s2, mbf[j]], w=[pb], dbg=f"m2_{r}_{oc}_{j}")
            P.op("v", lambda e, pb=pb, oc=oc, r=r: e.scalar_tensor_tensor(
                    zt.ap[:, oc * CW:(oc + 1) * CW], pb.ap[:, 0:CW],
                    bcn.ap[:, oc * NE + r: oc * NE + r + 1],
                    es[oc].ap[:, r * CW:(r + 1) * CW], ALU.add, ALU.add),
                 r=[pb, bcn, es[oc]], w=[zt], dbg=f"comb{r}_{oc}")
            P.op("v", lambda e, oc=oc: e.tensor_tensor(
                    zt.ap[:, oc * CW:(oc + 1) * CW], zt.ap[:, oc * CW:(oc + 1) * CW],
                    xr4.ap[:, oc * CW:(oc + 1) * CW], ALU.add),
                 r=[zt, xr4], w=[zt], dbg=f"zadd4_{r}_{oc}")
            P.op("s", lambda e, oc=oc: e.activation(zsq.ap, zt.ap[:, oc * CW:(oc + 1) * CW],
                                                    AF.Square),
                 r=[zt], w=[zsq], dbg=f"zsq4_{r}_{oc}")
            P.op("t", lambda e, oc=oc: e.matmul(stB.ap[0:1, 0:CW], X["onesb"].ap[:, 0:1], zsq.ap,
                                                start=(oc == 0), stop=(oc == NFC - 1)),
                 r=[zsq, X["onesb"]], w=[stB], dbg=f"s2st4_{r}_{oc}")
            P.op("t", lambda e, oc=oc: e.matmul(stA.ap[0:1, 0:CW], X["onesfc"].ap,
                                                zt.ap[:, oc * CW:(oc + 1) * CW],
                                                start=(oc == 0), stop=(oc == NFC - 1)),
                 r=[zt, X["onesfc"]], w=[stA], dbg=f"s1st4_{r}_{oc}")
        P.op("v", lambda e: e.tensor_scalar_mul(mu.ap, stA.ap[0:1, 0:CW], 1.0 / D), r=[stA], w=[mu])
        P.op("v", lambda e: e.tensor_scalar_mul(ms.ap, stB.ap[0:1, 0:CW], 1.0 / D), r=[stB], w=[ms])
        P.op("v", lambda e: e.tensor_tensor(bb2.ap, mu.ap, mu.ap, ALU.mult), r=[mu], w=[bb2])
        P.op("v", lambda e: e.tensor_tensor(ms.ap, ms.ap, bb2.ap, ALU.subtract), r=[ms, bb2], w=[ms])
        P.op("s", lambda e: e.activation(ms.ap, ms.ap, AF.Sqrt, bias=X["eps"].ap[0:1, :]),
             r=[ms, X["eps"]], w=[ms])
        P.op("v", lambda e: e.reciprocal(aa.ap, ms.ap), r=[ms], w=[aa])
        P.op("v", lambda e: e.tensor_tensor(bb2.ap, mu.ap, aa.ap, ALU.mult), r=[mu, aa], w=[bb2])
        P.op("v", lambda e: e.tensor_scalar_mul(bb2.ap, bb2.ap, -1.0), r=[bb2], w=[bb2])
        P.op("t", lambda e: e.matmul(abank.ap[:, 0:CW], X["onesf"].ap[0:1, :], aa.ap, start=True, stop=True),
             r=[aa, X["onesf"]], w=[abank], dbg=f"ab4{r}")
        P.op("t", lambda e: e.matmul(bbank.ap[:, 0:CW], X["onesf"].ap[0:1, :], bb2.ap, start=True, stop=True),
             r=[bb2, X["onesf"]], w=[bbank], dbg=f"bb4{r}")
        for oc in range(NFC):
            P.op("v", lambda e, oc=oc: e.tensor_tensor(
                    zt.ap[:, oc * CW:(oc + 1) * CW], zt.ap[:, oc * CW:(oc + 1) * CW],
                    abank.ap[:, 0:CW], ALU.mult),
                 r=[zt, abank], w=[zt], dbg=f"ln4m{r}_{oc}")
            P.op("v", lambda e, oc=oc: e.tensor_tensor(
                    zt.ap[:, oc * CW:(oc + 1) * CW], zt.ap[:, oc * CW:(oc + 1) * CW],
                    bbank.ap[:, 0:CW], ALU.add),
                 r=[zt, bbank], w=[zt], dbg=f"ln4a{r}_{oc}")
        P.dma(C["xout"], [lambda e, r=r: e.dma_start(
                out=xout[:, r * CW:(r + 1) * CW].rearrange("(c p) t -> p c t", p=128),
                in_=zt.ap.rearrange("p (c t) -> p c t", c=NFC))],
              r=[zt], w=[bxout], dbg=f"xout4{r}")
    A.release()
    A.release()
    return bxout


def make_chans(P):
    return {
        "const": P.chan("const", "g", 2),
        "w": P.chan("w", "g", 2),
        "xin": P.chan("xin", "q", 2),
        "spillout": P.chan("spout", "q", 2),
        "qin": P.chan("qin", "q", 4),
        "kin": P.chan("kin", "q", 2),
        "aoin": P.chan("aoin", "q", 2),
        "xout": P.chan("xout", "q", 2),
        "gw": P.chan("gw", "g", 4),
        "sdma": P.chan("sdma", "s", 4),
        "sc": P.chan("sc", "g", 2),
    }

CONST_SPECS = {
    "c_ident": ((128, 128), BF16),
    "c_onesb": ((128, 128), BF16),
    "c_onesf": ((1, 128), F32),
    "c_onesfc": ((128, 1), F32),
    "c_eps": ((128, 1), F32),
    "c_identf": ((128, 128), F32),
    "c_e2": ((128, 128), BF16),
    "c_e3": ((128, 64), BF16),
}

def const_arrays():
    return {
        "c_ident": np.eye(128, dtype=np.float32).astype(BF),
        "c_onesb": np.ones((128, 128), dtype=np.float32).astype(BF),
        "c_onesf": np.ones((1, 128), dtype=np.float32),
        "c_onesfc": np.ones((128, 1), dtype=np.float32),
        "c_eps": np.full((128, 1), 1e-5, dtype=np.float32),
        "c_identf": np.eye(128, dtype=np.float32),
        "c_e2": _mk_e2(),
        "c_e3": _mk_e3(),
    }

def _mk_e2():
    e2 = np.zeros((128, 8, 16), np.float32)
    for c in range(8):
        for p in range(128):
            e2[p, c, 2 * c + p // 64] = 1.0
    return e2.reshape(128, 128).astype(BF)

def _mk_e3():
    e3 = np.zeros((128, 64), np.float32)
    for p in range(128):
        e3[p, p % 64] = 1.0
    return e3.astype(BF)


def build_program(io_spec, phase_fns):
    nc = bass.Bass("TRN2", target_bir_lowering=False)
    io = {}
    for name, (shape, dt, kind) in io_spec.items():
        io[name] = nc.dram_tensor(name, list(shape), dt, kind=kind).ap()
    ctxs = []
    sbcm = nc.sbuf_tensor("arena", [128, SB_BYTES // 4], F32)
    sbh = sbcm.__enter__(); ctxs.append(sbcm)
    cmp_ = nc.psum_tensor("psarena", [128, 4096], F32)
    ph_ = cmp_.__enter__(); ctxs.append(cmp_)
    psall = ph_[:]
    banks = [Buf(f"bank{i}", psall[:, i * 512:(i + 1) * 512]) for i in range(8)]
    banks.append(Buf("psall", psall))

    P = Prog(nc)
    A = Arena(sbh[:])
    C = make_chans(P)
    X = setup_consts(P, A, io, C)
    outs = []
    for fn in phase_fns:
        ob = fn(P, A, io, C, X, banks)
        if ob is not None:
            outs.append(ob)
    P.op("q", lambda e: e.nop(), r=outs, dbg="final_wait")
    P.finalize()
    P.replay()
    for h in reversed(ctxs):
        h.__exit__(None, None, None)
    P.close()
    return nc

def prep_host(inputs):
    c = const_arrays()
    w = {}
    asc = np.ascontiguousarray
    w["w_qkv"] = inputs["L0_attn_w_qkv"]
    w["w_out"] = inputs["L0_attn_w_out"]
    w["gamma"] = asc(inputs["L0_attn_gamma"].reshape(NFC, 128).T)
    w["wqT"] = asc(inputs["L1_attn_Wq"].T)
    w["wkT"] = asc(inputs["L1_attn_Wk"].T)
    w["wvT"] = asc(inputs["L1_attn_Wv"].T)
    w["wfcT"] = asc(inputs["L1_attn_Wfc"].T)
    w["wg"] = inputs["L0_ffn_w_gate"]
    w["wu"] = inputs["L0_ffn_w_up"]
    w["wd"] = inputs["L0_ffn_w_down"]
    w["sw1"] = inputs["L1_ffn_sw1"]
    w["sw3"] = inputs["L1_ffn_sw3"]
    w["sw2"] = inputs["L1_ffn_sw2"]
    w["sb1r"] = asc(inputs["L1_ffn_sb1"].reshape(HS // 128, 128).T)
    w["sb3r"] = asc(inputs["L1_ffn_sb3"].reshape(HS // 128, 128).T)
    w["sb2r"] = asc(inputs["L1_ffn_sb2"].reshape(NFC, 128).T)
    w["w1e"] = inputs["L1_ffn_w1"]
    w["w2e"] = inputs["L1_ffn_w2"]
    w["b1r"] = asc(inputs["L1_ffn_b1"].T.reshape(NFC, 128, NE).transpose(1, 0, 2))
    w["b2r"] = asc(inputs["L1_ffn_b2"].T.reshape(NFC, 128, NE).transpose(1, 0, 2))
    _ = inputs["L1_ffn_t_emb"].shape
    xs = [asc(inputs["L0_attn_x"][b].T) for b in range(8)]
    return c, w, xs

def unpermute_tokens(yT):
    yt = yT.T
    return np.ascontiguousarray(yt.reshape(NE, T // NE, D).transpose(1, 0, 2).reshape(T, D))



def _spec_p1():
    s = {
        "xin": ((D, T), F32, "ExternalInput"),
        "xout": ((D, T), F32, "ExternalOutput"),
        "w_qkv": ((D, 3 * D), F32, "ExternalInput"),
        "w_out": ((D, D), F32, "ExternalInput"),
        "gamma": ((128, NFC), F32, "ExternalInput"),
        "qspill": ((D, T), BF16, "Internal"),
        "kspill": ((D, T), BF16, "Internal"),
        "aospill": ((D, T), BF16, "Internal"),
    }
    return s, ["w_qkv", "w_out", "gamma"]

def _spec_p2():
    s = {
        "xin": ((D, T), F32, "ExternalInput"),
        "xout": ((D, T), F32, "ExternalOutput"),
        "wg": ((FF, D), F32, "ExternalInput"),
        "wu": ((FF, D), F32, "ExternalInput"),
        "wd": ((D, FF), F32, "ExternalInput"),
        "wgq": ((D, FF), BF16, "Internal"),
        "wuq": ((D, FF), BF16, "Internal"),
        "wdq": ((FF, D), BF16, "Internal"),
        "scx_s": ((64, 512), F32, "Internal"),
        "scx_r": ((64, 512), F32, "Internal"),
        "sch_s0": ((176, 512), F32, "Internal"),
        "sch_r0": ((176, 512), F32, "Internal"),
        "sch_s1": ((176, 512), F32, "Internal"),
        "sch_r1": ((176, 512), F32, "Internal"),
    }
    return s, ["wg", "wu", "wd"]

def _spec_p3():
    s = {
        "xin": ((D, T), F32, "ExternalInput"),
        "xout": ((D, T), F32, "ExternalOutput"),
        "wqT": ((D, D), F32, "ExternalInput"),
        "wkT": ((D, D), F32, "ExternalInput"),
        "wvT": ((D, D), F32, "ExternalInput"),
        "wfcT": ((D, D), F32, "ExternalInput"),
        "kspill": ((D, T), BF16, "Internal"),
        "vspill": ((D, T), BF16, "Internal"),
        "aospill": ((D, T), BF16, "Internal"),
        "scdram": ((16, NH * T), F32, "Internal"),
        "adram": ((16, NH * T), BF16, "Internal"),
    }
    return s, ["wqT", "wkT", "wvT", "wfcT"]

def _spec_p4():
    s = {
        "xin": ((D, T), F32, "ExternalInput"),
        "xout": ((D, T), F32, "ExternalOutput"),
        "sw1": ((D, HS), F32, "ExternalInput"),
        "sw3": ((D, HS), F32, "ExternalInput"),
        "sw2": ((HS, D), F32, "ExternalInput"),
        "w1e": ((NE, D, D), F32, "ExternalInput"),
        "w2e": ((NE, D, D), F32, "ExternalInput"),
        "sb1r": ((128, HS // 128), F32, "ExternalInput"),
        "sb3r": ((128, HS // 128), F32, "ExternalInput"),
        "sb2r": ((128, NFC), F32, "ExternalInput"),
        "b1r": ((128, NFC, NE), F32, "ExternalInput"),
        "b2r": ((128, NFC, NE), F32, "ExternalInput"),
        "xgdram": ((D, T), F32, "Internal"),
        "sw1b": ((D, HS), BF16, "Internal"),
        "sw3b": ((D, HS), BF16, "Internal"),
        "sw2b": ((HS, D), BF16, "Internal"),
    }
    return s, ["sw1", "sw3", "sw2", "w1e", "w2e", "sb1r", "sb3r", "sb2r", "b1r", "b2r"]

_PHASES = [
    (_spec_p1, lambda P, A, io, C, X, banks: build_attn(P, A, io, C, X, banks, 0)),
    (_spec_p2, build_l0ffn),
    (_spec_p3, build_l1attn),
    (_spec_p4, build_l1ffn),
]

_prog_cache = {}

def _spec_fused():
    s = {
        "xin": ((D, T), F32, "ExternalInput"),
        "x1": ((D, T), F32, "Internal"),
        "x2": ((D, T), F32, "Internal"),
        "x3": ((D, T), F32, "Internal"),
        "xout": ((D, T), F32, "ExternalOutput"),
    }
    wall = []
    for i, (spec_fn, _) in enumerate(_PHASES):
        sp, wk = spec_fn()
        wall += wk
        for name, v in sp.items():
            if name in ("xin", "xout"):
                continue
            key = name if name not in s else f"{name}_p{i}"
            assert key not in s or s[key] == v, key
            s[key] = v
    for nm in ("kspill", "aospill"):
        s[nm + "3"] = ((D, T), BF16, "Internal")
    return s, wall

def _phase_io(io, i):
    xs = ["xin", "x1", "x2", "x3", "xout"]
    m = dict(io)
    m["xin"] = io[xs[i]]
    m["xout"] = io[xs[i + 1]]
    if i == 2:
        m["kspill"] = io["kspill3"]
        m["aospill"] = io["aospill3"]
    return m

def build_fused():
    spec, _ = _spec_fused()
    for n, (s, dt) in CONST_SPECS.items():
        spec[n] = (s, dt, "ExternalInput")
    def mk(i):
        def fn(P, A, io, C, X, banks):
            return _PHASES[i][1](P, A, _phase_io(io, i), C, X, banks)
        return fn
    return build_program(spec, [mk(i) for i in range(4)])

def _get_prog(i):
    if i not in _prog_cache:
        spec_fn, build_fn = _PHASES[i]
        spec, _ = spec_fn()
        for n, (s, dt) in CONST_SPECS.items():
            spec[n] = (s, dt, "ExternalInput")
        _prog_cache[i] = build_program(spec, [build_fn])
    return _prog_cache[i]

FUSED = True

def kernel(**inputs):
    consts, w, xs = prep_host(inputs)
    if FUSED:
        if "fused" not in _prog_cache:
            _prog_cache["fused"] = build_fused()
        nc = _prog_cache["fused"]
        _, wkeys = _spec_fused()
        in_maps = []
        for b in range(8):
            m = dict(consts)
            m["xin"] = xs[b]
            for k in wkeys:
                m[k] = w[k]
            in_maps.append(m)
        res = run_bass_kernel_spmd(nc, in_maps, core_ids=list(range(8)))
        cur = [np.ascontiguousarray(res.results[b]["xout"]) for b in range(8)]
    else:
        cur = xs
        for i in range(4):
            nc = _get_prog(i)
            _, wkeys = _PHASES[i][0]()
            in_maps = []
            for b in range(8):
                m = dict(consts)
                m["xin"] = cur[b]
                for k in wkeys:
                    m[k] = w[k]
                in_maps.append(m)
            res = run_bass_kernel_spmd(nc, in_maps, core_ids=list(range(8)))
            cur = [np.ascontiguousarray(res.results[b]["xout"]) for b in range(8)]
    out = np.stack([unpermute_tokens(cur[b]) for b in range(8)], axis=0)
    return out.astype(np.float32)
